# version 05
# baseline (speedup 1.0000x reference)
import contextlib
import os
import numpy as np
import ml_dtypes
import concourse.bass as bass
import concourse.mybir as mybir
from concourse.bass_utils import run_bass_kernel_spmd

F32 = mybir.dt.float32
BF16 = mybir.dt.bfloat16
AF = mybir.ActivationFunctionType
ALU = mybir.AluOpType

NCORES = 8
D = 1024
SEQ = 4096
TSTRIP = 1024
NSTRIP = SEQ // TSTRIP
NT = TSTRIP // 512
FC = D // 128

ENGINES = ("sync", "tensor", "vector", "scalar", "gpsimd")
GDN_DEBUG = False



def tile_w(w):
    K, M = w.shape
    Kc = K // 128
    mw = min(M, 128)
    Mc = M // mw
    return np.ascontiguousarray(
        w.reshape(Kc, 128, Mc, mw).transpose(2, 1, 0, 3))


def fm_of(x):
    T, d = x.shape
    return np.ascontiguousarray(x.T.reshape(d // 128, 128, T).transpose(1, 0, 2))


def cols_of(v):
    return np.ascontiguousarray(v.reshape(-1, 128).T)



class Prog:

    def __init__(self):
        self.q = {e: [] for e in ENGINES}
        self.cnt = dict(dma0=0, dma1=0, dma2=0, dma3=0, stdma=0, te=0, ve=0, sc=0, gp=0)
        self.dma_i = 0

    def wait(self, eng, sem, val):
        if val <= 0:
            return
        self.q[eng].append(lambda E, S, sem=sem, val=val: E.wait_ge(S[sem], val))

    def selfwait(self, eng):
        sem = ENG_SEM[eng]
        self.wait(eng, sem, self.cnt[sem])

    def dma(self, eng, build):
        i = self.dma_i % 4
        self.dma_i += 1
        sem = "dma%d" % i
        self.wait(eng, sem, self.cnt[sem])
        self.emit(eng, build, mark=sem, n=16)
        return (sem, self.cnt[sem])

    def emit(self, eng, build, mark=None, n=1):
        if mark is not None:
            self.cnt[mark] += n

            def f(E, S, build=build, mark=mark, n=n):
                ins = build(E, S)
                ins.then_inc(S[mark], n)
            self.q[eng].append(f)
        else:
            self.q[eng].append(lambda E, S, build=build: build(E, S))

    def replay(self, nc, sems):
        with nc.Block() as block:
            for eng in ENGINES:
                ops = self.q[eng]
                if not ops:
                    continue

                def body(E, ops=ops):
                    for f in ops:
                        f(E, sems)
                getattr(block, eng)(body)


class Ctx:

    def __init__(self, nc, prog):
        self.nc = nc
        self.p = prog
        self.t = {}
        self.colreg = {}
        self.bank_guard = [None] * 4
        self.bank_i = 0
        self.wstage_guard = [None, None]
        self.wbf_guard = {}

    def col(self, name, j=0):
        s, n = self.colreg[name]
        assert 0 <= j < n, (name, j, n)
        return self.t["cols"][:, s + j:s + j + 1]

    def col1(self, name, j=0):
        s, n = self.colreg[name]
        return self.t["cols"][0:1, s + j:s + j + 1]

    def next_bank(self):
        b = self.bank_i % 4
        self.bank_i += 1
        return b

    def guard_bank(self, b):
        g = self.bank_guard[b]
        if g is not None:
            self.p.wait("tensor", g[0], g[1])

    def set_bank_guard(self, b, sem):
        self.bank_guard[b] = (sem, self.p.cnt[sem])


SEMNAMES = ("dma0", "dma1", "dma2", "dma3", "stdma", "te", "ve", "sc", "gp")
ENG_SEM = dict(tensor="te", vector="ve", scalar="sc", gpsimd="gp")



class WSrc:

    def __init__(self, dram_ap, Kc, mw=128):
        self.ap = dram_ap
        self.Kc = Kc
        self.mw = mw

    def emit_load(self, ctx, mc, dest_ap_fn):
        p = ctx.p
        kgran = max(1, 512 // self.mw)
        ng = (self.Kc + kgran - 1) // kgran
        for g in range(ng):
            k0 = g * kgran
            kn = min(kgran, self.Kc - k0)
            half = ctx.stage_i % 2
            ctx.stage_i += 1
            stage = ctx.t["wstage"][:, half * 512:half * 512 + kn * self.mw]
            stage_v = stage.rearrange("p (k f) -> p k f", f=self.mw)
            gg = ctx.wstage_guard[half]
            if gg is not None:
                p.wait("sync", gg[0], gg[1])
            src = self.ap[mc, :, k0:k0 + kn, :]
            dm = p.dma("sync", lambda E, S, d=stage_v, s=src: E.dma_start(out=d, in_=s))
            p.wait("vector", dm[0], dm[1])
            dst = dest_ap_fn(k0, kn)
            p.emit("vector", lambda E, S, d=dst, s=stage_v: E.tensor_copy(d, s), mark="ve")
            ctx.wstage_guard[half] = ("ve", p.cnt["ve"])
        return p.cnt["ve"]


class WSinCos:

    def __init__(self, theta_ap, noise_ap, is_cos, Kc=8, mw=128):
        self.a1 = theta_ap
        self.a2 = noise_ap
        self.is_cos = is_cos
        self.Kc = Kc
        self.mw = mw

    def emit_load(self, ctx, mc, dest_ap_fn):
        p = ctx.p
        kgran = 2
        ng = (self.Kc + kgran - 1) // kgran
        for g in range(ng):
            k0 = g * kgran
            kn = min(kgran, self.Kc - k0)
            half = ctx.stage_i % 2
            ctx.stage_i += 1
            base = half * 512
            st1 = ctx.t["wstage"][:, base:base + kn * 128].rearrange("p (k f) -> p k f", f=128)
            st2 = ctx.t["wstage"][:, base + 256:base + 256 + kn * 128].rearrange("p (k f) -> p k f", f=128)
            gg = ctx.wstage_guard[half]
            if gg is not None:
                p.wait("sync", gg[0], gg[1])
            s1 = self.a1[mc, :, k0:k0 + kn, :]
            s2 = self.a2[mc, :, k0:k0 + kn, :]
            dm1 = p.dma("sync", lambda E, S, d=st1, s=s1: E.dma_start(out=d, in_=s))
            dm2 = p.dma("sync", lambda E, S, d=st2, s=s2: E.dma_start(out=d, in_=s))
            p.wait("vector", dm1[0], dm1[1])
            p.wait("vector", dm2[0], dm2[1])
            p.emit("vector", lambda E, S, a=st1, b=st2: E.tensor_add(a, a, b), mark="ve")
            dst = dest_ap_fn(k0, kn)
            p.wait("scalar", "ve", p.cnt["ve"])
            if self.is_cos:
                p.emit("scalar", lambda E, S, d=dst, s=st1:
                       E.activation(d, s, AF.Sin, bias=ctxcol(ctx, "c_pi2")), mark="sc")
            else:
                p.emit("scalar", lambda E, S, d=dst, s=st1:
                       E.activation(d, s, AF.Sin), mark="sc")
            ctx.wstage_guard[half] = ("sc", p.cnt["sc"])
        return None

    uses_sc = True


def ctxcol(ctx, name, j=0):
    return ctx.col(name, j)



def rhs_barrier(ctx):
    p = ctx.p
    p.wait("tensor", "ve", p.cnt["ve"])
    p.wait("tensor", "sc", p.cnt["sc"])
    p.wait("tensor", "gp", p.cnt["gp"])


def war_barrier(ctx, eng):
    ctx.p.wait(eng, "te", ctx.p.cnt["te"])


def emit_linear_fm(ctx, weights, rhs_fn, out_cb, Mc, T=TSTRIP):
    p = ctx.p
    nw = len(weights)
    Kc = weights[0].Kc
    mw = weights[0].mw
    nt = T // 512
    reg = Kc * mw
    assert nw * 2 * reg <= 8192
    ctx.stage_i = 0
    rhs_barrier(ctx)
    ctx.pass_id = getattr(ctx, "pass_id", 0) + 1
    pass_te0 = p.cnt["te"]

    def wreg(wi, parity):
        off = (wi * 2 + parity) * reg
        return ctx.t["wbf"][:, off:off + reg].rearrange("p (k f) -> p k f", f=mw)

    for mc in range(Mc):
        parity = mc % 2
        for wi, w in enumerate(weights):
            g = ctx.wbf_guard.get((wi, parity))
            if g is None or g[0] != ctx.pass_id:
                g = (ctx.pass_id, "te", pass_te0)
            p.wait("vector", g[1], g[2])
            if getattr(w, "uses_sc", False):
                p.wait("scalar", g[1], g[2])

            def dest_fn(k0, kn, wi=wi, parity=parity):
                return wreg(wi, parity)[:, k0:k0 + kn, :]
            w.emit_load(ctx, mc, dest_fn)
            if getattr(w, "uses_sc", False):
                p.wait("tensor", "sc", p.cnt["sc"])
            else:
                p.wait("tensor", "ve", p.cnt["ve"])
        for t in range(nt):
            banks = []
            for wi, w in enumerate(weights):
                b = ctx.next_bank()
                banks.append(b)
                ctx.guard_bank(b)
                ps = ctx.t["psA"][:, b, 0:512]
                wv = wreg(wi, parity)
                for k in range(Kc):
                    p.emit("tensor",
                           lambda E, S, o=ps[0:mw, :], l=wv[:, k, :], r=rhs_fn(wi, k, t),
                                  st=(k == 0), sp=(k == Kc - 1):
                           E.matmul(o, l, r, start=st, stop=sp),
                           mark="te" if k == Kc - 1 else None)
                out_cb(wi, mc, t, ps[0:mw, :], b, banks)
        ctx.wbf_guard[(0, parity)] = (ctx.pass_id, "te", p.cnt["te"])
        if nw > 1:
            ctx.wbf_guard[(1, parity)] = (ctx.pass_id, "te", p.cnt["te"])



def emit_ln(ctx, src_fn, dst_fn, affine=None, src_bf16=False):
    p = ctx.p
    scr = ctx.t["scr"]
    s2f = ctx.t["scr2"][:].bitcast(F32)
    A_t = scr[:, 0:512]
    B_t = scr[:, 512:1024]
    Cs = scr[:, 1024:1536]
    Ds = scr[:, 1536:2048]
    for t in range(NT):
        war_barrier(ctx, "vector")
        p.selfwait("vector")
        p.selfwait("scalar")
        p.wait("vector", "gp", p.cnt["gp"])
        p.wait("scalar", "gp", p.cnt["gp"])
        cast_marks = []
        if src_bf16:
            statsrc = src_fn
            cast_marks = [p.cnt["ve"]] * FC
        else:
            statsrc = dst_fn
            for fc in range(FC):
                p.emit("vector", lambda E, S, d=dst_fn(fc, t), s=src_fn(fc, t):
                       E.tensor_copy(d, s), mark="ve")
                cast_marks.append(p.cnt["ve"])
        sq_marks = []
        qmm_marks = []
        ps_s = ctx.t["psB"][0:1, 0, 0:512]
        ps_q = ctx.t["psB"][0:1, 1, 0:512]
        for fc in range(FC):
            p.wait("tensor", "ve", cast_marks[fc])
            p.emit("tensor", lambda E, S, o=ps_s, l=ctx.t["cons_bf"][:, 0:1], r=statsrc(fc, t),
                          st=(fc == 0), sp=(fc == FC - 1):
                   E.matmul(o, l, r, start=st, stop=sp),
                   mark="te" if fc == FC - 1 else None)
        smark = p.cnt["te"]
        for fc in range(FC):
            sqslot = ctx.t["scr2"][:, (fc % 2) * 512:(fc % 2) * 512 + 512]
            p.wait("scalar", "ve", cast_marks[fc])
            if fc >= 2:
                p.wait("scalar", "te", qmm_marks[fc - 2])
            p.emit("scalar", lambda E, S, d=sqslot, s=statsrc(fc, t):
                   E.activation(d, s, AF.Square), mark="sc")
            p.wait("tensor", "sc", p.cnt["sc"])
            p.emit("tensor", lambda E, S, o=ps_q, l=ctx.t["cons_bf"][:, 0:1], r=sqslot,
                          st=(fc == 0), sp=(fc == FC - 1):
                   E.matmul(o, l, r, start=st, stop=sp), mark="te")
            qmm_marks.append(p.cnt["te"])
        qmark = p.cnt["te"]
        sq_s = Ds[0:1, 0:512]
        var = A_t[0:1, 0:512]
        a_row = Cs[0:1, 0:512]
        p.wait("scalar", "te", smark)
        p.emit("scalar", lambda E, S, d=sq_s, s=ps_s:
               E.activation(d, s, AF.Square, scale=ctx.col1("c_r1024")), mark="sc")
        p.wait("vector", "te", qmark)
        p.wait("vector", "sc", p.cnt["sc"])
        p.emit("vector", lambda E, S, d=var, q=ps_q, ss=sq_s:
               E.scalar_tensor_tensor(d, q, ctx.col1("c_r1024"), ss, ALU.mult, ALU.subtract),
               mark="ve")
        p.wait("scalar", "ve", p.cnt["ve"])
        p.emit("scalar", lambda E, S, d=sq_s, s=var:
               E.activation(d, s, AF.Ln, bias=ctx.col1("c_eps")), mark="sc")
        p.selfwait("scalar")
        p.emit("scalar", lambda E, S, d=a_row, s=sq_s:
               E.activation(d, s, AF.Exp, scale=ctx.col1("c_neghalf")), mark="sc")
        p.wait("vector", "sc", p.cnt["sc"])
        p.emit("vector", lambda E, S, d=var, s=ps_s, ar=a_row:
               E.scalar_tensor_tensor(d, s, ctx.col1("c_nr1024"), ar, ALU.mult, ALU.mult),
               mark="ve")
        b_row = var
        onesrow_f = ctx.t["cons_f"][0:1, CF_UTRI:CF_UTRI + 128]
        p.wait("tensor", "ve", p.cnt["ve"])
        p.emit("tensor", lambda E, S, o=ctx.t["psB"][:, 0, 0:512], l=onesrow_f, r=a_row:
               E.matmul(o, l, r, start=True, stop=True), mark="te")
        p.emit("tensor", lambda E, S, o=ctx.t["psB"][:, 1, 0:512], l=onesrow_f, r=b_row:
               E.matmul(o, l, r, start=True, stop=True), mark="te")
        p.wait("scalar", "te", p.cnt["te"])
        p.emit("scalar", lambda E, S, d=A_t, s=ctx.t["psB"][:, 0, 0:512]:
               E.activation(d, s, AF.Copy), mark="sc")
        p.emit("scalar", lambda E, S, d=B_t, s=ctx.t["psB"][:, 1, 0:512]:
               E.activation(d, s, AF.Copy), mark="sc")
        bc_mark = p.cnt["sc"]
        p.wait("vector", "sc", bc_mark)
        p.wait("gpsimd", "sc", bc_mark)
        for fc in range(FC):
            if fc > 0:
                p.wait("vector", "gp", p.cnt["gp"])
            p.emit("vector", lambda E, S, d=Cs, s=src_fn(fc, t), a=A_t:
                   E.tensor_mul(d, s, a), mark="ve")
            p.wait("gpsimd", "ve", p.cnt["ve"])
            if affine is None:
                p.emit("gpsimd", lambda E, S, d=dst_fn(fc, t), c=Cs, b=B_t:
                       E.tensor_add(d, c, b), mark="gp")
            else:
                wn, bn = affine
                if fc > 0:
                    p.wait("gpsimd", "ve", p.cnt["ve"])
                p.emit("gpsimd", lambda E, S, d=Ds, c=Cs, b=B_t:
                       E.tensor_add(d, c, b), mark="gp")
                p.wait("vector", "gp", p.cnt["gp"])
                p.emit("vector", lambda E, S, d=dst_fn(fc, t), s=Ds,
                              w=ctx.col(wn, fc), b=ctx.col(bn, fc):
                       E.tensor_scalar(d, s, w, b, ALU.mult, ALU.add), mark="ve")
        ctx.ln_done = ("gp" if affine is None else "ve",
                       p.cnt["gp" if affine is None else "ve"])



def aslot(ctx, arena, slot, t, n=1):
    a = ctx.t[arena]
    return a[:, slot * 1024 + t * 512: slot * 1024 + t * 512 + 512]


def aslot_tokens(ctx, arena, slot, t0, tl):
    a = ctx.t[arena]
    return a[:, slot * 1024 + t0: slot * 1024 + t0 + tl]


def x1_ap(ctx, fc, t):
    a = ctx.t["arenaA"]
    off = 32768 // 2 + (fc * 1024 + t * 512) * 2
    return a[:, off:off + 1024].bitcast(F32)



def emit_l0ffn(ctx, W):
    p = ctx.p
    silu_slot = [ctx.t["scr2"][:, i * 512:(i + 1) * 512] for i in range(2)]

    def rhs_z(wi, k, t):
        return aslot(ctx, "arenaA", k, t)

    def cb_w12(wi, mc, t, ps, b, banks):
        if wi == 0:
            p.wait("scalar", "te", p.cnt["te"])
            p.wait("scalar", "ve", p.cnt["ve"])
            p.emit("scalar", lambda E, S, d=silu_slot[t], s=ps:
                   E.activation(d, s, AF.Silu, scale=ctx.col("l0f_s1", mc)), mark="sc")
            ctx.set_bank_guard(b, "sc")
        else:
            p.wait("vector", "te", p.cnt["te"])
            p.wait("vector", "sc", p.cnt["sc"])
            dst = aslot(ctx, "arenaB", mc, t)
            p.emit("vector", lambda E, S, d=dst, s=ps, sl=silu_slot[t]:
                   E.scalar_tensor_tensor(d, s, ctx.col("l0f_s2", mc), sl, ALU.mult, ALU.mult),
                   mark="ve")
            ctx.set_bank_guard(b, "ve")
            ctx.set_bank_guard(banks[0], "ve")

    emit_linear_fm(ctx, [W["w1"], W["w2"]], rhs_z, cb_w12, Mc=32)

    def rhs_a(wi, k, t):
        return aslot(ctx, "arenaB", k, t)

    def cb_h1(wi, mc, t, ps, b, banks):
        p.wait("scalar", "te", p.cnt["te"])
        dst = aslot(ctx, "arenaA", mc, t)
        p.emit("scalar", lambda E, S, d=dst, s=ps:
               E.activation(d, s, AF.Gelu, scale=ctx.col("l0f_h1s", mc)), mark="sc")
        ctx.set_bank_guard(b, "sc")

    emit_linear_fm(ctx, [W["h1"]], rhs_a, cb_h1, Mc=32)

    def rhs_b(wi, k, t):
        return aslot(ctx, "arenaA", k, t)

    def cb_h2(wi, mc, t, ps, b, banks):
        p.wait("scalar", "te", p.cnt["te"])
        dst = aslot(ctx, "arenaB", mc, t)
        p.emit("scalar", lambda E, S, d=dst, s=ps:
               E.activation(d, s, AF.Gelu, scale=ctx.col("l0f_h2s", mc)), mark="sc")
        ctx.set_bank_guard(b, "sc")

    emit_linear_fm(ctx, [W["h2"]], rhs_b, cb_h2, Mc=32)

    def rhs_c(wi, k, t):
        return aslot(ctx, "arenaB", k, t)

    def cb_w3(wi, mc, t, ps, b, banks):
        p.wait("vector", "te", p.cnt["te"])
        R = ctx.t["R"][:, mc, t * 512:(t + 1) * 512]
        p.emit("vector", lambda E, S, d=R, s=ps:
               E.scalar_tensor_tensor(d, s, ctx.col("l0f_hs3", mc), d, ALU.mult, ALU.add),
               mark="ve")
        ctx.set_bank_guard(b, "ve")

    emit_linear_fm(ctx, [W["w3"]], rhs_c, cb_w3, Mc=8)



def emit_l1ffn(ctx, W):
    p = ctx.p
    scr = ctx.t["scr"]
    Cs, Ds = scr[:, 1024:1536], scr[:, 1536:2048]

    def rhs_z(wi, k, t):
        return aslot(ctx, "arenaA", k, t)

    def cb(wi, mc, t, ps, b, banks):
        if wi == 0:
            return
        p.wait("scalar", "te", p.cnt["te"])
        p.wait("scalar", "ve", p.cnt["ve"])
        ps_re = ctx.t["psA"][:, banks[0], 0:512]
        p.emit("scalar", lambda E, S, d=Cs, s=ps_re:
               E.activation(d, s, AF.Square), mark="sc")
        p.emit("scalar", lambda E, S, d=Ds, s=ps:
               E.activation(d, s, AF.Square), mark="sc")
        p.wait("vector", "sc", p.cnt["sc"])
        p.emit("vector", lambda E, S, a=Cs, bb=Ds: E.tensor_add(a, a, bb), mark="ve")
        p.wait("scalar", "ve", p.cnt["ve"])
        p.emit("scalar", lambda E, S, d=Ds, s=Cs:
               E.activation(d, s, AF.Sqrt, scale=ctx.col("c_quarter")), mark="sc")
        R = ctx.t["R"][:, mc, t * 512:(t + 1) * 512]
        p.wait("vector", "sc", p.cnt["sc"])
        p.emit("vector", lambda E, S, d=R, s=Ds: E.tensor_add(d, d, s), mark="ve")
        ctx.set_bank_guard(b, "ve")
        ctx.set_bank_guard(banks[0], "sc")

    emit_linear_fm(ctx, [W["cos"], W["sin"]], rhs_z, cb, Mc=8)



BLOCKS = ("l0ffn_a", "l0attn", "l0ffn_b", "l1ffn_a", "l1attn", "l1ffn_b")


def make_consts():
    i128 = np.arange(128)
    ident_f = np.eye(128, dtype=np.float32)
    utri_f = (i128[:, None] <= i128[None, :]).astype(np.float32)
    bf = ml_dtypes.bfloat16
    ident_b = ident_f.astype(bf)
    utri_b = utri_f.astype(bf)
    ones_b = np.ones((128, 128), bf)
    maskLneg = -np.tril(np.ones((128, 128), np.float32), -1)
    maskUneg = -np.triu(np.ones((128, 128), np.float32), 1)
    knsel = np.zeros((128, 128), np.float32)
    for fc in range(8):
        for j in range(16):
            if j // 2 == fc:
                rows = slice((j % 2) * 64, (j % 2) * 64 + 64)
                knsel[rows, fc * 16 + j] = 1.0
    BIG = -1e30
    mb_le = np.where(i128[:, None] <= i128[None, :], 0.0, BIG).astype(np.float32)
    mb_lt = np.where(i128[:, None] < i128[None, :], 0.0, BIG).astype(np.float32)
    mb_gt = np.where(i128[:, None] > i128[None, :], 0.0, BIG).astype(np.float32)
    ones_f = np.ones((128, 128), np.float32)
    cons_f = np.concatenate([ident_f, utri_f, ones_f, mb_le, mb_lt, mb_gt], axis=1)
    cons_b = np.concatenate([np.ones((128, 1), np.float32), ident_f, utri_f,
                             np.ones((128, 128), np.float32), maskLneg, maskUneg,
                             knsel], axis=1).astype(bf)
    return cons_f, cons_b

CB_ONESCOL = 0
CB_IDENT = 1
CB_UTRI = 129
CB_ONES = 257
CB_MASKL = 385
CB_MASKU = 513
CB_KNSEL = 641
CF_IDENT = 0
CF_UTRI = 128
CF_ONES = 256
CF_MBLE = 384
CF_MBLT = 512
CF_MBGT = 640


def build_cols(inputs):
    reg = {}
    colsl = []

    def add(name, arr):
        arr = np.asarray(arr, np.float32)
        if arr.ndim == 1:
            if arr.size % 128 == 0:
                c = cols_of(arr)
            else:
                raise ValueError(name)
        else:
            c = arr
        reg[name] = (sum(x.shape[1] for x in colsl), c.shape[1])
        colsl.append(c)

    def addc(name, v):
        add(name, np.full((128, 1), v, np.float32))

    add("l0f_s1", inputs["L0_ffn_w1_s"])
    add("l0f_s2", inputs["L0_ffn_w2_s"])
    add("l0f_h1s", inputs["L0_ffn_h1_s"])
    add("l0f_h2s", inputs["L0_ffn_h2_s"])
    add("l0f_s3", inputs["L0_ffn_w3_s"])
    add("l0f_hs3", inputs["L0_ffn_w3_s"])
    add("l0a_n1w", inputs["L0_attn_norm1_w"])
    add("l0a_n1b", inputs["L0_attn_norm1_b"])
    add("l0a_n2w", inputs["L0_attn_norm2_w"])
    add("l0a_n2b", inputs["L0_attn_norm2_b"])
    add("l0a_gb", inputs["L0_attn_gate_b"])
    add("l0a_ob", inputs["L0_attn_out_b"])
    add("l0a_mb1", inputs["L0_attn_mlp_b1"])
    add("l0a_mb2", inputs["L0_attn_mlp_b2"])
    add("l1a_ln1g", inputs["L1_attn_ln1_g"])
    add("l1a_ln1b", inputs["L1_attn_ln1_b"])
    add("l1a_ln2g", inputs["L1_attn_ln2_g"])
    add("l1a_ln2b", inputs["L1_attn_ln2_b"])
    add("l1a_bo", inputs["L1_attn_bo"])
    add("l1a_b1", inputs["L1_attn_b1"])
    add("l1a_b2", inputs["L1_attn_b2"])
    addc("c_1", 1.0)
    addc("c_quarter", 0.25)
    addc("c_eighth", 0.125)
    addc("c_eps", 1e-5)
    addc("c_r1024", 1.0 / 1024.0)
    addc("c_nr1024", -1.0 / 1024.0)
    addc("c_neg1", -1.0)
    addc("c_neghalf", -0.5)
    addc("c_1e6", 1e-6)
    addc("c_pi2", float(np.pi / 2))
    addc("c_half", 0.5)
    cols = np.concatenate(colsl, axis=1)
    return cols, reg


def prep_host(inputs):
    h = {}
    h["W_l0f_w1"] = tile_w(np.asarray(inputs["L0_ffn_w1_w"]).T)
    h["W_l0f_w2"] = tile_w(np.asarray(inputs["L0_ffn_w2_w"]).T)
    h["W_l0f_h1"] = tile_w(np.asarray(inputs["L0_ffn_h1_w"]).T)
    h["W_l0f_h2"] = tile_w(np.asarray(inputs["L0_ffn_h2_w"]).T)
    h["W_l0f_w3"] = tile_w(np.asarray(inputs["L0_ffn_w3_w"]).T)
    h["W_l0a_gate"] = tile_w(np.asarray(inputs["L0_attn_gate_w"]))
    h["W_l0a_state"] = np.ascontiguousarray(np.asarray(inputs["L0_attn_state_w"]))
    h["W_l0a_out"] = tile_w(np.asarray(inputs["L0_attn_out_w"]))
    h["W_l0a_mlp1"] = tile_w(np.asarray(inputs["L0_attn_mlp_w1"]))
    h["W_l0a_mlp2"] = tile_w(np.asarray(inputs["L0_attn_mlp_w2"]))
    h["W_l1a_q"] = tile_w(np.asarray(inputs["L1_attn_Wq"]))
    h["W_l1a_k"] = tile_w(np.asarray(inputs["L1_attn_Wk"]))
    h["W_l1a_v"] = tile_w(np.asarray(inputs["L1_attn_Wv"]))
    h["W_l1a_b"] = tile_w(np.asarray(inputs["L1_attn_Wb"]))
    h["W_l1a_g"] = tile_w(np.asarray(inputs["L1_attn_Wg"]))
    h["W_l1a_o"] = tile_w(np.asarray(inputs["L1_attn_Wo"]))
    h["W_l1a_m1"] = tile_w(np.asarray(inputs["L1_attn_w1"]))
    h["W_l1a_m2"] = tile_w(np.asarray(inputs["L1_attn_w2"]))
    h["W_l1f_th"] = tile_w(np.asarray(inputs["L1_ffn_theta"]))
    h["W_l1f_thn"] = tile_w(np.asarray(inputs["L1_ffn_theta_noise"]))
    h["W_l1f_ph"] = tile_w(np.asarray(inputs["L1_ffn_phi"]))
    h["W_l1f_phn"] = tile_w(np.asarray(inputs["L1_ffn_phi_noise"]))
    cols, reg = build_cols(inputs)
    h["cols_in"] = cols
    h["rows_in"] = np.asarray(inputs["L0_attn_state_b"], np.float32)[None, :]
    cf, cb = make_consts()
    h["cons_f"] = cf
    h["cons_b"] = cb
    return h, reg


def declare_drams(nc, h, blocks, fused, rin_shape):
    d = {}
    for k, v in h.items():
        d[k] = nc.dram_tensor(k, list(v.shape), BF16 if v.dtype == ml_dtypes.bfloat16 else F32,
                              kind="ExternalInput").ap()
    d["x_fm"] = nc.dram_tensor("x_fm", list(rin_shape), F32, kind="ExternalInput").ap()
    d["y_fm"] = nc.dram_tensor("y_fm", list(rin_shape), F32, kind="ExternalOutput").ap()
    return d


def alloc_tensors(ctx, es):
    nc = ctx.nc
    t = ctx.t
    t["R"] = es.enter_context(nc.sbuf_tensor("sb_R", [128, 8, 1024], F32))
    t["arenaA"] = es.enter_context(nc.sbuf_tensor("sb_arenaA", [128, 32768], BF16))
    t["arenaB"] = es.enter_context(nc.sbuf_tensor("sb_arenaB", [128, 32768], BF16))
    t["wstage"] = es.enter_context(nc.sbuf_tensor("sb_wstage", [128, 1024], F32))
    t["wbf"] = es.enter_context(nc.sbuf_tensor("sb_wbf", [128, 8192], BF16))
    t["scr"] = es.enter_context(nc.sbuf_tensor("sb_scr", [128, 2048], F32))
    t["scr2"] = es.enter_context(nc.sbuf_tensor("sb_scr2", [128, 3584], BF16))
    t["cols"] = es.enter_context(nc.sbuf_tensor("sb_cols", [128, 384], F32))
    t["cons_f"] = es.enter_context(nc.sbuf_tensor("sb_cons_f", [128, 768], F32))
    t["cons_bf"] = es.enter_context(nc.sbuf_tensor("sb_cons_bf", [128, 769], BF16))
    t["Sstate"] = es.enter_context(nc.sbuf_tensor("sb_Sstate", [128, 8, 64], F32))
    t["rows_bf"] = es.enter_context(nc.sbuf_tensor("sb_rows_bf", [1, 1040], BF16))
    t["carry"] = es.enter_context(nc.sbuf_tensor("sb_carry", [1, 1024], BF16))
    t["wbar"] = es.enter_context(nc.sbuf_tensor("sb_wbar", [128, 18], F32))
    t["psA"] = es.enter_context(nc.psum_tensor("sb_psA", [128, 4, 512], F32))
    t["psB"] = es.enter_context(nc.psum_tensor("sb_psB", [128, 3, 512], F32))
    t["psC"] = es.enter_context(nc.psum_tensor("sb_psC", [128, 1024], BF16))


def emit_init(ctx, d):
    p = ctx.p
    rows = ctx.t["scr"][0:1, 0:1024]
    dms = []
    for name, dst in [("cols_in", "cols"), ("cons_f", "cons_f"), ("cons_b", "cons_bf")]:
        nn = d[name].shape[1]
        dms.append(p.dma("sync", lambda E, S, dd=ctx.t[dst][:, 0:nn], s=d[name]:
                         E.dma_start(out=dd, in_=s)))
    dms.append(p.dma("sync", lambda E, S, dd=rows, s=d["rows_in"]: E.dma_start(out=dd, in_=s)))
    for sem, val in dms:
        p.wait("vector", sem, val)
    s, n = ctx.colreg["l0f_hs3"]
    p.emit("vector", lambda E, S: E.tensor_scalar_mul(
        ctx.t["cols"][:, s:s + n], ctx.t["cols"][:, s:s + n], 0.5), mark="ve")
    p.emit("vector", lambda E, S, rows=rows: E.tensor_copy(ctx.t["rows_bf"][:, 0:1024], rows),
           mark="ve")
    bbar_f = ctx.t["scr"][0:1, 1024:1025]
    p.emit("vector", lambda E, S, rows=rows, d=bbar_f: E.reduce_sum(d, rows, axis=mybir.AxisListType.X),
           mark="ve")
    p.selfwait("vector")
    p.emit("vector", lambda E, S, d=ctx.t["rows_bf"][0:1, 1024:1025], s=bbar_f:
           E.tensor_scalar_mul(d, s, 1.0 / 1024.0), mark="ve")
    p.emit("vector", lambda E, S: E.memset(ctx.t["carry"][:], 0.0), mark="ve")
    p.emit("vector", lambda E, S: E.memset(ctx.t["scr2"][:], 0.0), mark="ve")
    p.emit("vector", lambda E, S: E.memset(ctx.t["arenaA"][:], 0.0), mark="ve")
    p.emit("vector", lambda E, S: E.memset(ctx.t["arenaB"][:], 0.0), mark="ve")
    p.selfwait("vector")
    if os.environ.get("BASS_SIM_FP64") == "1" or os.environ.get("KMK_SIM") == "1":
        p.emit("vector", lambda E, S: E.memset(ctx.t["psC"][:], 0.0), mark="ve")
        p.emit("vector", lambda E, S: E.memset(ctx.t["psB"][:], 0.0), mark="ve")
    p.emit("vector", lambda E, S: E.memset(ctx.t["scr"][:], 0.0), mark="ve")
    p.emit("vector", lambda E, S: E.memset(ctx.t["Sstate"][:], 0.0), mark="ve")
    p.emit("vector", lambda E, S: E.memset(ctx.t["wbar"][:], 0.0), mark="ve")
    ctx.init_done = p.cnt["ve"]


def emit_strip(ctx, d, W, strip, blocks):
    p = ctx.p
    t0 = strip * TSTRIP
    ctx.rload = p.dma("sync", lambda E, S: E.dma_start(
        out=ctx.t["R"][:], in_=d["x_fm"][:, :, t0:t0 + TSTRIP]))

    def Rsrc(fc, t):
        return ctx.t["R"][:, fc, t * 512:(t + 1) * 512]

    def zdst(fc, t):
        return aslot(ctx, "arenaA", fc, t)

    first = True
    for blk in blocks:
        if first:
            p.wait("vector", ctx.rload[0], ctx.rload[1])
            first = False
        if blk in ("l0ffn_a", "l0ffn_b"):
            emit_ln(ctx, Rsrc, zdst, affine=None)
            emit_l0ffn(ctx, {k: W[k] for k in ("w1", "w2", "h1", "h2", "w3")})
        elif blk in ("l1ffn_a", "l1ffn_b"):
            emit_ln(ctx, Rsrc, zdst, affine=None)
            emit_l1ffn(ctx, {"cos": W["cos"], "sin": W["sin"]})
        elif blk == "l0attn":
            emit_ln(ctx, Rsrc, zdst, affine=None)
            emit_l0attn(ctx, W)
        elif blk == "l1attn":
            emit_ln(ctx, Rsrc, zdst, affine=None)
            emit_l1attn(ctx, W)
        elif blk == "lntest":
            emit_ln(ctx, Rsrc, zdst, affine=None)
            for t in range(NT):
                for fc in range(FC):
                    p.wait("vector", "gp", p.cnt["gp"])
                    p.emit("vector", lambda E, S, d=Rsrc(fc, t), s=zdst(fc, t):
                           E.tensor_copy(d, s), mark="ve")
        elif blk == "lnafftest":
            emit_ln(ctx, Rsrc, zdst, affine=("l0a_n1w", "l0a_n1b"))
            for t in range(NT):
                for fc in range(FC):
                    p.wait("vector", "sc", p.cnt["sc"])
                    p.emit("vector", lambda E, S, d=Rsrc(fc, t), s=zdst(fc, t):
                           E.tensor_copy(d, s), mark="ve")
    p.wait("gpsimd", "ve", p.cnt["ve"])
    p.wait("gpsimd", "sc", p.cnt["sc"])
    p.wait("gpsimd", "stdma", p.cnt["stdma"])
    p.emit("gpsimd", lambda E, S: E.dma_start(
        out=d["y_fm"][:, :, t0:t0 + TSTRIP], in_=ctx.t["R"][:]), mark="stdma", n=16)
    p.wait("sync", "stdma", p.cnt["stdma"])


def make_wsrcs(ctx, d):
    return dict(
        w1=WSrc(d["W_l0f_w1"], 8), w2=WSrc(d["W_l0f_w2"], 8),
        h1=WSrc(d["W_l0f_h1"], 32), h2=WSrc(d["W_l0f_h2"], 32),
        w3=WSrc(d["W_l0f_w3"], 32),
        gate=WSrc(d["W_l0a_gate"], 8), out=WSrc(d["W_l0a_out"], 8),
        mlp1=WSrc(d["W_l0a_mlp1"], 8), mlp2=WSrc(d["W_l0a_mlp2"], 32),
        state_nat=d["W_l0a_state"],
        q=WSrc(d["W_l1a_q"], 8), k=WSrc(d["W_l1a_k"], 8), v=WSrc(d["W_l1a_v"], 8),
        wb=WSrc(d["W_l1a_b"], 8, mw=16), wg=WSrc(d["W_l1a_g"], 8, mw=16),
        wo=WSrc(d["W_l1a_o"], 8), m1=WSrc(d["W_l1a_m1"], 8), m2=WSrc(d["W_l1a_m2"], 32),
        cos=WSinCos(d["W_l1f_th"], d["W_l1f_thn"], True),
        sin=WSinCos(d["W_l1f_ph"], d["W_l1f_phn"], False),
    )


def build_program(h, reg, blocks, n_strips=NSTRIP):
    nc = bass.Bass("TRN2", target_bir_lowering=False)
    d = declare_drams(nc, h, blocks, True, [128, 8, SEQ])
    prog = Prog()
    ctx = Ctx(nc, prog)
    ctx.colreg = reg
    es = contextlib.ExitStack()
    with es:
        alloc_tensors(ctx, es)
        sems = {}
        for sn in SEMNAMES:
            sems[sn] = es.enter_context(nc.semaphore(sn))
        emit_init(ctx, d)
        W = make_wsrcs(ctx, d)
        for strip in range(n_strips):
            emit_strip(ctx, d, W, strip, blocks)
        prog.wait("gpsimd", "stdma", prog.cnt["stdma"])
        prog.replay(nc, sems)
    return nc


def kernel(**inputs):
    h, reg = prep_host(inputs)
    x = np.asarray(inputs["L0_attn_x"])
    nc = build_program(h, reg, list(BLOCKS))
    in_maps = []
    for c in range(NCORES):
        m = dict(h)
        m["x_fm"] = fm_of(x[c])
        in_maps.append(m)
    res = run_bass_kernel_spmd(nc, in_maps, core_ids=list(range(NCORES)))
    out = np.empty((NCORES, SEQ, D), np.float32)
    for c in range(NCORES):
        y = res.results[c]["y_fm"]
        out[c] = y.transpose(1, 0, 2).reshape(D, SEQ).T
    return out


def emit_tok_state_pass(ctx, W, srcs, dsts, seed_row, wbar=True):
    p = ctx.p
    rhs_barrier(ctx)
    ctx.pass_id = getattr(ctx, "pass_id", 0) + 1
    pass_te0 = p.cnt["te"]
    wk = [ctx.t["wbf"][:, k * 1024:(k + 1) * 1024] for k in range(8)]
    ctx.stage_i = 0
    for k in range(8):
        for half in range(2):
            hs = ctx.stage_i % 2
            ctx.stage_i += 1
            stage = ctx.t["wstage"][:, hs * 512:(hs + 1) * 512]
            gg = ctx.wstage_guard[hs]
            if gg is not None:
                p.wait("sync", gg[0], gg[1])
            src = W[k * 128:(k + 1) * 128, half * 512:(half + 1) * 512]
            dm = p.dma("sync", lambda E, S, d=stage, s=src: E.dma_start(out=d, in_=s))
            p.wait("vector", dm[0], dm[1])
            if k == 0 and half == 0:
                p.wait("vector", "te", pass_te0)
            p.emit("vector", lambda E, S, d=wk[k][:, half * 512:(half + 1) * 512], s=stage:
                   E.tensor_copy(d, s), mark="ve")
            ctx.wstage_guard[hs] = ("ve", p.cnt["ve"])
        if wbar:
            p.selfwait("vector")
            p.emit("vector", lambda E, S, d=ctx.t["wbar"][:, k:k + 1], s=wk[k]:
                   E.reduce_sum(d, s, axis=mybir.AxisListType.X), mark="ve")
    if wbar:
        p.selfwait("vector")
        p.emit("vector", lambda E, S: E.tensor_scalar_mul(
            ctx.t["wbar"][:, 0:8], ctx.t["wbar"][:, 0:8], 1.0 / 1024.0), mark="ve")
        p.selfwait("vector")
        p.emit("vector", lambda E, S: E.tensor_copy(
            ctx.t["scr2"][:, 3520:3528], ctx.t["wbar"][:, 0:8]), mark="ve")
    cast_done = p.cnt["ve"]
    p.wait("tensor", "ve", cast_done)
    for tc in range(8):
        for si, src_fn in enumerate(srcs):
            for half in range(2):
                b = ctx.next_bank()
                ctx.guard_bank(b)
                ps = ctx.t["psA"][:, b, 0:512]
                p.emit("tensor", lambda E, S, o=ps, r=seed_row(half):
                       E.matmul(o, ctx.t["cons_bf"][0:1, CB_ONES:CB_ONES + 128], r,
                                start=True, stop=False))
                for k in range(8):
                    p.emit("tensor", lambda E, S, o=ps, l=src_fn(k, tc),
                                  r=wk[k][:, half * 512:(half + 1) * 512], sp=(k == 7):
                           E.matmul(o, l, r, start=False, stop=sp),
                           mark="te" if k == 7 else None)
                p.wait("scalar", "te", p.cnt["te"])
                p.emit("scalar", lambda E, S, d=dsts[si](tc, half), s=ps:
                       E.activation(d, s, AF.Copy), mark="sc")
                ctx.set_bank_guard(b, "sc")


def emit_l0attn(ctx, W):
    p = ctx.p

    def z_ap(fc, t):
        return aslot(ctx, "arenaA", fc, t)

    def z_tok(kc, tc):
        return ctx.t["arenaA"][:, kc * 1024 + tc * 128: kc * 1024 + tc * 128 + 128]

    war_barrier(ctx, "vector")
    p.selfwait("vector")
    p.wait("vector", "gp", p.cnt["gp"])
    for t in range(NT):
        for fc in range(FC):
            p.emit("vector", lambda E, S, d=ctx.t["R"][:, fc, t * 512:(t + 1) * 512],
                          s=z_ap(fc, t): E.tensor_add(d, d, s), mark="ve")

    def xn1_ap(fc, t):
        return aslot(ctx, "arenaB", fc, t)
    emit_ln(ctx, z_ap, xn1_ap, affine=("l0a_n1w", "l0a_n1b"), src_bf16=True)

    def rhs_xn1(wi, k, t):
        return aslot(ctx, "arenaB", k, t)

    def cb_gate(wi, mc, t, ps, b, banks):
        p.wait("scalar", "te", p.cnt["te"])
        dst = aslot(ctx, "arenaB", 8 + mc, t)
        p.emit("scalar", lambda E, S, d=dst, s=ps:
               E.activation(d, s, AF.Silu, bias=ctx.col("l0a_gb", mc)), mark="sc")
        ctx.set_bank_guard(b, "sc")

    emit_linear_fm(ctx, [W["gate"]], rhs_xn1, cb_gate, Mc=16)

    war_barrier(ctx, "vector")
    p.wait("vector", "sc", p.cnt["sc"])
    for t in range(NT):
        for fc in range(FC):
            p.emit("vector", lambda E, S, d=aslot(ctx, "arenaA", 8 + fc, t),
                          a=aslot(ctx, "arenaB", fc, t), g=aslot(ctx, "arenaB", 8 + fc, t):
                   E.tensor_mul(d, a, g), mark="ve")

    def xg_tok(kc, tc):
        return ctx.t["arenaA"][:, (8 + kc) * 1024 + tc * 128: (8 + kc) * 1024 + tc * 128 + 128]

    def gg_tok(kc, tc):
        return ctx.t["arenaB"][:, (16 + kc) * 1024 + tc * 128: (16 + kc) * 1024 + tc * 128 + 128]

    def u_dst(tc, half):
        return ctx.t["arenaB"][:, (24 + tc) * 1024 + half * 512: (24 + tc) * 1024 + half * 512 + 512]

    def bm_dst(tc, half):
        return ctx.t["arenaA"][:, (16 + tc) * 1024 + half * 512: (16 + tc) * 1024 + half * 512 + 512]

    def seed_row(half):
        return ctx.t["rows_bf"][0:1, half * 512:(half + 1) * 512]

    emit_tok_state_pass(ctx, W["state_nat"], [xg_tok, gg_tok], [u_dst, bm_dst], seed_row)

    scr = ctx.t["scr"]
    dcol = scr[:, 1536:1544]
    ps_d = ctx.t["psB"][:, 2, 0:1]
    for tc in range(8):
        p.wait("tensor", "sc", p.cnt["sc"])
        p.emit("tensor", lambda E, S, o=ps_d:
               E.matmul(o, ctx.t["cons_bf"][0:1, CB_ONES:CB_ONES + 128],
                        ctx.t["rows_bf"][0:1, 1024:1025], start=True, stop=False))
        wb_bf = ctx.t["scr2"][:, 3520:3528]
        for k in range(8):
            p.emit("tensor", lambda E, S, o=ps_d, l=xn1_tok(ctx, k, tc),
                          r=wb_bf[:, k:k + 1], sp=(k == 7):
                   E.matmul(o, l, r, start=False, stop=sp), mark="te" if k == 7 else None)
        p.wait("scalar", "te", p.cnt["te"])
        p.emit("scalar", lambda E, S, d=dcol[:, tc:tc + 1], s=ps_d:
               E.activation(d, s, AF.Copy), mark="sc")
    t1 = scr[:, 1544:1552]
    t2 = scr[:, 1552:1560]
    t3 = scr[:, 1560:1568]
    p.selfwait("scalar")
    p.emit("scalar", lambda E, S: E.activation(t1, dcol, AF.Abs), mark="sc")
    p.selfwait("scalar")
    p.emit("scalar", lambda E, S: E.activation(t2, t1, AF.Exp, scale=ctx.col("c_neg1")), mark="sc")
    p.selfwait("scalar")
    p.emit("scalar", lambda E, S: E.activation(t3, t2, AF.Ln, bias=ctx.col("c_1")), mark="sc")
    p.wait("vector", "sc", p.cnt["sc"])
    p.selfwait("vector")
    p.emit("vector", lambda E, S: E.tensor_relu(t1, dcol), mark="ve")
    p.selfwait("vector")
    p.emit("vector", lambda E, S: E.tensor_add(dcol, t1, t3), mark="ve")

    p.wait("vector", "sc", p.cnt["sc"])
    for tc in range(8):
        dbu = ctx.t["arenaA"][:, (24 + tc) * 1024: (25 + tc) * 1024]
        u = ctx.t["arenaB"][:, (24 + tc) * 1024: (25 + tc) * 1024]
        bm = ctx.t["arenaA"][:, (16 + tc) * 1024: (17 + tc) * 1024]
        p.emit("vector", lambda E, S, d=dbu, a=u, b=bm: E.tensor_mul(d, a, b), mark="ve")
        p.selfwait("vector")
        p.emit("vector", lambda E, S, d=dbu, c=dcol[:, tc:tc + 1]:
               E.tensor_scalar_mul(d, d, c), mark="ve")

    consf = ctx.t["cons_f"]
    utri_f = consf[:, CF_UTRI:CF_UTRI + 128]
    ident_f = consf[:, CF_IDENT:CF_IDENT + 128]
    onesrow_f = consf[0:1, CF_UTRI:CF_UTRI + 128]
    utri_b = ctx.t["cons_bf"][:, CB_UTRI:CB_UTRI + 128]
    scr2 = ctx.t["scr2"]
    M_sb = scr2[:, 2048:2176]
    coef = scr2[0:1, 2176:2304]
    dcol_sb = scr[:, 1568:1569]
    drow_sb = scr[0:1, 1569:1697]
    rhs_barrier(ctx)
    for tc in range(8):
        ps_dc = ctx.t["psB"][:, 2, 0:1]
        ps_dr = ctx.t["psB"][0:1, 2, 128:256]
        ps_bc = ctx.t["psB"][:, 2, 256:384]
        p.wait("tensor", "ve", p.cnt["ve"])
        p.wait("tensor", "sc", p.cnt["sc"])
        p.emit("tensor", lambda E, S, o=ps_dc, r=dcol[:, tc:tc + 1]:
               E.matmul(o, utri_f, r, start=True, stop=True), mark="te")
        p.wait("scalar", "te", p.cnt["te"])
        p.selfwait("scalar")
        p.emit("scalar", lambda E, S, d=dcol_sb, s=ps_dc:
               E.activation(d, s, AF.Copy), mark="sc")
        p.wait("tensor", "sc", p.cnt["sc"])
        p.emit("tensor", lambda E, S, o=ps_dr, i=dcol_sb:
               E.transpose(o, i, ident_f), mark="te")
        p.wait("scalar", "te", p.cnt["te"])
        p.selfwait("scalar")
        p.emit("scalar", lambda E, S, d=drow_sb, s=ps_dr:
               E.activation(d, s, AF.Copy), mark="sc")
        p.wait("tensor", "sc", p.cnt["sc"])
        p.emit("tensor", lambda E, S, o=ps_bc, r=drow_sb:
               E.matmul(o, onesrow_f, r, start=True, stop=True), mark="te")
        p.wait("vector", "te", p.cnt["te"])
        p.selfwait("vector")
        marg = scr[:, 0:128]
        p.emit("vector", lambda E, S, d=marg, s=ps_bc, m=consf[:, CF_MBLE:CF_MBLE + 128]:
               E.tensor_sub(d, s, m), mark="ve")
        p.wait("scalar", "ve", p.cnt["ve"])
        p.wait("scalar", "te", p.cnt["te"])
        p.selfwait("scalar")
        p.emit("scalar", lambda E, S, d=M_sb, s=marg:
               E.activation(d, s, AF.Exp, scale=ctx.col("c_neg1"), bias=dcol_sb), mark="sc")
        p.selfwait("scalar")
        p.emit("scalar", lambda E, S, d=coef, s=ps_dr:
               E.activation(d, s, AF.Exp, scale=ctx.col1("c_neg1")), mark="sc")
        p.wait("vector", "sc", p.cnt["sc"])
        b1 = ctx.next_bank()
        b2 = ctx.next_bank()
        ctx.guard_bank(b1)
        ctx.guard_bank(b2)
        p.wait("tensor", "ve", p.cnt["ve"])
        p.wait("tensor", "sc", p.cnt["sc"])
        for fc in range(8):
            bk = b1 if fc < 4 else b2
            reg = ctx.t["psA"][:, bk, (fc % 4) * 128:(fc % 4) * 128 + 128]
            dbu = ctx.t["arenaA"][:, (24 + tc) * 1024 + fc * 128: (24 + tc) * 1024 + fc * 128 + 128]
            p.emit("tensor", lambda E, S, o=reg, l=dbu, m=M_sb:
                   E.matmul(o, l, m, start=True, stop=False))
            p.emit("tensor", lambda E, S, o=reg, l=ctx.t["carry"][0:1, fc * 128:(fc + 1) * 128],
                          r=coef: E.matmul(o, l, r, start=False, stop=True),
                   mark="te" if fc == 7 else None)
        ymark = p.cnt["te"]
        ps_c0 = ctx.t["psB"][0:1, 0, 0:512]
        ps_c1 = ctx.t["psB"][0:1, 1, 0:512]
        for half in range(2):
            psc = ps_c0 if half == 0 else ps_c1
            dbuh = ctx.t["arenaA"][:, (24 + tc) * 1024 + half * 512: (24 + tc) * 1024 + half * 512 + 512]
            p.emit("tensor", lambda E, S, o=psc, l=M_sb[:, 127:128], r=dbuh:
                   E.matmul(o, l, r, start=True, stop=False))
            p.emit("tensor", lambda E, S, o=psc, l=coef[0:1, 127:128],
                          r=ctx.t["carry"][0:1, half * 512:(half + 1) * 512]:
                   E.matmul(o, l, r, start=False, stop=True), mark="te")
        p.wait("scalar", "te", p.cnt["te"])
        p.selfwait("scalar")
        p.emit("scalar", lambda E, S, d=ctx.t["carry"][0:1, 0:512], s=ps_c0:
               E.activation(d, s, AF.Copy), mark="sc")
        p.emit("scalar", lambda E, S, d=ctx.t["carry"][0:1, 512:1024], s=ps_c1:
               E.activation(d, s, AF.Copy), mark="sc")
        for i, bk in enumerate((b1, b2)):
            ydst = ctx.t["arenaB"][:, 0:8 * 1024].rearrange("p (f c) -> p f c", c=1024)[
                :, i * 4:(i + 1) * 4, tc * 128:(tc + 1) * 128]
            ysrc = ctx.t["psA"][:, bk, 0:512].rearrange("p (f c) -> p f c", c=128)
            p.emit("scalar", lambda E, S, d=ydst, s=ysrc:
                   E.activation(d, s, AF.Copy), mark="sc")
        ctx.set_bank_guard(b1, "sc")
        ctx.set_bank_guard(b2, "sc")

    def rhs_y(wi, k, t):
        return aslot(ctx, "arenaB", k, t)

    def cb_out(wi, mc, t, ps, b, banks):
        p.wait("vector", "te", p.cnt["te"])
        p.selfwait("vector")
        R = ctx.t["R"][:, mc, t * 512:(t + 1) * 512]
        p.emit("vector", lambda E, S, d=R, s=ps, c=ctx.col("l0a_ob", mc):
               E.scalar_tensor_tensor(d, s, c, d, ALU.add, ALU.add), mark="ve")
        p.emit("vector", lambda E, S, d=x1_ap(ctx, mc, t), s=ps,
                      c=ctx.col("l0a_ob", mc), z=z_ap(mc, t):
               E.scalar_tensor_tensor(d, s, c, z, ALU.add, ALU.add), mark="ve")
        ctx.set_bank_guard(b, "ve")

    emit_linear_fm(ctx, [W["out"]], rhs_y, cb_out, Mc=8)

    def x1src(fc, t):
        return x1_ap(ctx, fc, t)

    def xn2_ap(fc, t):
        return aslot(ctx, "arenaB", fc, t)
    emit_ln(ctx, x1src, xn2_ap, affine=("l0a_n2w", "l0a_n2b"))

    def rhs_xn2(wi, k, t):
        return aslot(ctx, "arenaB", k, t)

    def h_ap(kc, t):
        if kc < 16:
            return aslot(ctx, "arenaA", kc, t)
        return aslot(ctx, "arenaB", 8 + (kc - 16), t)

    def cb_m1(wi, mc, t, ps, b, banks):
        p.wait("scalar", "te", p.cnt["te"])
        p.emit("scalar", lambda E, S, d=h_ap(mc, t), s=ps:
               E.activation(d, s, AF.Gelu, bias=ctx.col("l0a_mb1", mc)), mark="sc")
        ctx.set_bank_guard(b, "sc")

    emit_linear_fm(ctx, [W["mlp1"]], rhs_xn2, cb_m1, Mc=32)

    def rhs_h(wi, k, t):
        return h_ap(k, t)

    def cb_m2(wi, mc, t, ps, b, banks):
        p.wait("vector", "te", p.cnt["te"])
        p.selfwait("vector")
        R = ctx.t["R"][:, mc, t * 512:(t + 1) * 512]
        p.emit("vector", lambda E, S, d=R, s=ps, c=ctx.col("l0a_mb2", mc):
               E.scalar_tensor_tensor(d, s, c, d, ALU.add, ALU.add), mark="ve")
        ctx.set_bank_guard(b, "ve")

    emit_linear_fm(ctx, [W["mlp2"]], rhs_h, cb_m2, Mc=8)


def xn1_tok(ctx, kc, tc):
    return ctx.t["arenaB"][:, kc * 1024 + tc * 128: kc * 1024 + tc * 128 + 128]


def emit_l1attn(ctx, W):
    p = ctx.p
    scr = ctx.t["scr"]
    scr2 = ctx.t["scr2"]
    s2f = scr2[:].bitcast(F32)
    consb = ctx.t["cons_bf"]
    consf = ctx.t["cons_f"]
    utri_f = consf[:, CF_UTRI:CF_UTRI + 128]
    ident_f = consf[:, CF_IDENT:CF_IDENT + 128]
    onesrow_f = consf[0:1, CF_UTRI:CF_UTRI + 128]
    ident_b = consb[:, CB_IDENT:CB_IDENT + 128]
    ones_b = consb[:, CB_ONES:CB_ONES + 128]
    utri_b = consb[:, CB_UTRI:CB_UTRI + 128]
    maskL = consb[:, CB_MASKL:CB_MASKL + 128]
    maskU = consb[:, CB_MASKU:CB_MASKU + 128]

    def z_ap(fc, t):
        return aslot(ctx, "arenaA", fc, t)

    war_barrier(ctx, "vector")
    p.selfwait("vector")
    p.wait("vector", "gp", p.cnt["gp"])
    for t in range(NT):
        for fc in range(FC):
            p.emit("vector", lambda E, S, d=ctx.t["R"][:, fc, t * 512:(t + 1) * 512],
                          s=z_ap(fc, t): E.tensor_add(d, d, s), mark="ve")

    def xn1_ap(fc, t):
        return aslot(ctx, "arenaB", fc, t)
    emit_ln(ctx, z_ap, xn1_ap, affine=("l1a_ln1g", "l1a_ln1b"), src_bf16=True)

    def rhs_xn1(wi, k, t):
        return aslot(ctx, "arenaB", k, t)

    def mk_cb(base_slot, scale_col):
        def cb(wi, mc, t, ps, b, banks):
            p.wait("scalar", "te", p.cnt["te"])
            dst = aslot(ctx, "arenaA", base_slot + mc, t)
            if scale_col is None:
                p.emit("scalar", lambda E, S, d=dst, s=ps:
                       E.activation(d, s, AF.Copy), mark="sc")
            else:
                p.emit("scalar", lambda E, S, d=dst, s=ps:
                       E.activation(d, s, AF.Copy, scale=ctx.col(scale_col)), mark="sc")
            ctx.set_bank_guard(b, "sc")
        return cb

    emit_linear_fm(ctx, [W["q"]], rhs_xn1, mk_cb(8, "c_eighth"), Mc=8)
    emit_linear_fm(ctx, [W["k"]], rhs_xn1, mk_cb(16, None), Mc=8)
    emit_linear_fm(ctx, [W["v"]], rhs_xn1, mk_cb(24, None), Mc=8)

    lgT = lambda tc: scr[:, 448 + tc * 16: 448 + tc * 16 + 16]
    bT = lambda tc: scr[:, 576 + tc * 16: 576 + tc * 16 + 16]
    knT = lambda tc: scr[:, 704 + tc * 16: 704 + tc * 16 + 16]
    rowtmp = scr2[0:16, 0:512]

    def cb_bg(wi, mc, t, ps, b, banks):
        p.wait("scalar", "te", p.cnt["te"])
        p.wait("scalar", "ve", p.cnt["ve"])
        p.selfwait("scalar")
        p.emit("scalar", lambda E, S, d=rowtmp, s=ps:
               E.activation(d, s, AF.Sigmoid), mark="sc")
        ctx.set_bank_guard(b, "sc")
        for i in range(4):
            tc = t * 4 + i
            pc = ctx.t["psC"][:, i * 16:(i + 1) * 16]
            p.wait("tensor", "sc", p.cnt["sc"])
            p.emit("tensor", lambda E, S, o=pc, s=rowtmp[:, i * 128:(i + 1) * 128]:
                   E.transpose(o, s, ident_b[0:16, 0:16]), mark="te")
            p.wait("scalar", "te", p.cnt["te"])
            p.selfwait("scalar")
            if wi == 1:
                p.emit("scalar", lambda E, S, d=lgT(tc), s=pc:
                       E.activation(d, s, AF.Ln), mark="sc")
            else:
                p.emit("scalar", lambda E, S, d=bT(tc), s=pc:
                       E.activation(d, s, AF.Copy), mark="sc")

    emit_linear_fm(ctx, [W["wb"], W["wg"]], rhs_xn1, cb_bg, Mc=1)

    ksq = [scr2[:, 0:512], scr2[:, 512:1024]]
    ps_kn = ctx.t["psB"][0:16, 0, 0:512]
    for t in range(NT):
        p.wait("scalar", "ve", p.cnt["ve"])
        p.wait("scalar", "te", p.cnt["te"])
        for fc in range(FC):
            p.selfwait("scalar")
            if fc >= 2:
                p.wait("scalar", "te", p.cnt["te"])
            p.emit("scalar", lambda E, S, d=ksq[fc % 2], s=aslot(ctx, "arenaA", 16 + fc, t):
                   E.activation(d, s, AF.Square), mark="sc")
            p.wait("tensor", "sc", p.cnt["sc"])
            p.emit("tensor", lambda E, S, o=ps_kn, l=consb[:, CB_KNSEL + fc * 16:CB_KNSEL + fc * 16 + 16],
                          r=ksq[fc % 2], st=(fc == 0), sp=(fc == FC - 1):
                   E.matmul(o, l, r, start=st, stop=sp), mark="te")
        p.wait("scalar", "te", p.cnt["te"])
        p.selfwait("scalar")
        p.emit("scalar", lambda E, S, d=rowtmp, s=ps_kn:
               E.activation(d, s, AF.Copy), mark="sc")
        for i in range(4):
            tc = t * 4 + i
            pc = ctx.t["psC"][:, i * 16:(i + 1) * 16]
            p.wait("tensor", "sc", p.cnt["sc"])
            p.emit("tensor", lambda E, S, o=pc, s=rowtmp[:, i * 128:(i + 1) * 128]:
                   E.transpose(o, s, ident_b[0:16, 0:16]), mark="te")
            p.wait("scalar", "te", p.cnt["te"])
            p.selfwait("scalar")
            p.emit("scalar", lambda E, S, d=knT(tc), s=pc:
                   E.activation(d, s, AF.Sqrt), mark="sc")
            p.wait("vector", "sc", p.cnt["sc"])
            p.selfwait("vector")
            p.emit("vector", lambda E, S, d=knT(tc):
                   E.tensor_scalar_add(d, d, 1e-6), mark="ve")
            p.selfwait("vector")
            p.emit("vector", lambda E, S, d=knT(tc), s=knT(tc):
                   E.reciprocal(d, s), mark="ve")

    cumlg = scr[:, 832:848]
    gamT = scr[:, 848:864]
    negcl = scr[:, 864:880]
    cumlnb = scr[:, 880:896]
    lnb = scr[:, 896:912]
    negb = scr[:, 912:928]
    gCc = scr[:, 928:944]
    c_sb = scr[:, 944:945]
    gCrow = scr[0:1, 945:961]
    lcrow = scr[0:1, 961:977]
    sC = scr[:, 977:993]
    f32a = [scr[:, 0:64], scr[:, 1024:1088]]
    f32b = [scr[:, 64:192], scr[:, 1088:1216]]
    diagf = [scr[:, 192:320], scr[:, 1216:1344]]
    argf = [scr[:, 320:448], scr[:, 1344:1472]]

    def _pt(base):
        off = [0]
        def take(n):
            o = base + off[0]
            off[0] += n
            return scr2[:, o:o + n]
        return take
    tk = [None, None]
    tiles = [{}, {}]
    for hp in range(2):
        take = _pt(hp * 1216)
        tiles[hp] = dict(
            kh=take(128), Adec=take(128), qh=take(128), B=take(128), BT=take(128),
            Mkq=take(128), U=take(64), kt=take(128), Rm=take(64), S0b=take(64),
            diagb=take(128))
    Opair = scr2[:, 2432:2560]
    vT_t = scr2[:, 2560:2688]

    psA_ = ctx.t["psA"]
    psB_ = ctx.t["psB"]
    P0 = dict(bc=psB_[:, 1, 0:128], gram=psB_[:, 1, 128:256],
              bu=psB_[:, 2, 368:432], b2=psB_[:, 0, 0:128], bt2=psB_[:, 0, 128:256],
              w0=psB_[:, 2, 176:240], O=psB_[:, 2, 240:304], P=psB_[:, 2, 304:368])
    P1 = dict(bc=psA_[:, 2, 0:128], gram=psA_[:, 2, 128:256],
              bu=psA_[:, 1, 192:256], b2=psA_[:, 0, 0:128], bt2=psA_[:, 0, 128:256],
              w0=psA_[:, 1, 0:64], O=psA_[:, 1, 64:128], P=psA_[:, 1, 128:192])
    PS = [P0, P1]
    ps_cg = psB_[:, 2, 0:16]
    ps_cgT = psB_[0:16, 2, 16:144]
    ps_g1 = psB_[0:1, 2, 144:160]
    ps_gc = psB_[:, 2, 160:176]
    pc_kt = [ctx.t["psC"][:, 256:384], ctx.t["psC"][:, 512:640]]
    pc_ot = ctx.t["psC"][:, 384:512]
    onesf = consf[:, CF_ONES:CF_ONES + 128]
    mb_gt = consf[:, CF_MBGT:CF_MBGT + 128]
    mb_lt = consf[:, CF_MBLT:CF_MBLT + 128]
    mb_le = consf[:, CF_MBLE:CF_MBLE + 128]

    Sst = ctx.t["Sstate"]

    for t in range(NT):
        for ci in range(4):
            tc = t * 4 + ci
            rhs_barrier(ctx)
            for b in range(4):
                ctx.guard_bank(b)
            p.emit("tensor", lambda E, S, o=ps_cg, r=lgT(tc):
                   E.matmul(o, utri_f, r, start=True, stop=True), mark="te")
            p.wait("scalar", "te", p.cnt["te"])
            p.selfwait("scalar")
            p.emit("scalar", lambda E, S, d=cumlg, s=ps_cg:
                   E.activation(d, s, AF.Copy), mark="sc")
            p.selfwait("scalar")
            p.emit("scalar", lambda E, S, d=gamT, s=cumlg:
                   E.activation(d, s, AF.Exp), mark="sc")
            p.selfwait("scalar")
            p.emit("scalar", lambda E, S, d=lnb, s=bT(tc):
                   E.activation(d, s, AF.Ln), mark="sc")
            p.wait("vector", "sc", p.cnt["sc"])
            p.selfwait("vector")
            p.emit("vector", lambda E, S, d=negcl, s=cumlg:
                   E.tensor_scalar_mul(d, s, -1.0), mark="ve")
            p.emit("vector", lambda E, S, d=cumlnb, a=cumlg, b=lnb:
                   E.tensor_add(d, a, b), mark="ve")
            p.emit("vector", lambda E, S, d=negb, s=bT(tc):
                   E.tensor_scalar_mul(d, s, -1.0), mark="ve")
            p.wait("tensor", "sc", p.cnt["sc"])
            p.wait("tensor", "ve", p.cnt["ve"])
            p.emit("tensor", lambda E, S, o=ps_cgT, s=cumlg:
                   E.transpose(o, s, ident_f), mark="te")
            p.wait("scalar", "te", p.cnt["te"])
            p.selfwait("scalar")
            p.emit("scalar", lambda E, S, d=c_sb[0:16, :], s=ps_cgT[0:16, 127:128]:
                   E.activation(d, s, AF.Copy), mark="sc")
            p.wait("tensor", "sc", p.cnt["sc"])
            p.emit("tensor", lambda E, S, o=ps_g1, s=c_sb[0:16, :]:
                   E.transpose(o, s, ident_f[0:16, 0:16]), mark="te")
            p.wait("scalar", "te", p.cnt["te"])
            p.selfwait("scalar")
            p.emit("scalar", lambda E, S, d=gCrow, s=ps_g1:
                   E.activation(d, s, AF.Exp), mark="sc")
            p.emit("scalar", lambda E, S, d=lcrow, s=ps_g1:
                   E.activation(d, s, AF.Copy), mark="sc")
            p.wait("tensor", "sc", p.cnt["sc"])
            p.emit("tensor", lambda E, S, o=ps_gc, r=gCrow:
                   E.matmul(o, onesrow_f, r, start=True, stop=True), mark="te")
            p.wait("scalar", "te", p.cnt["te"])
            p.selfwait("scalar")
            p.emit("scalar", lambda E, S, d=gCc, s=ps_gc:
                   E.activation(d, s, AF.Copy), mark="sc")
            p.wait("tensor", "sc", p.cnt["sc"])
            p.emit("tensor", lambda E, S, o=ps_gc, r=lcrow:
                   E.matmul(o, onesrow_f, r, start=True, stop=True), mark="te")
            p.wait("vector", "te", p.cnt["te"])
            p.selfwait("vector")
            p.emit("vector", lambda E, S, d=sC, s=ps_gc, c=cumlg:
                   E.tensor_sub(d, s, c), mark="ve")
            p.wait("scalar", "ve", p.cnt["ve"])
            p.selfwait("scalar")
            p.emit("scalar", lambda E, S, d=sC, s=sC:
                   E.activation(d, s, AF.Exp), mark="sc")

            for fc in range(FC):
                vsl = ctx.t["arenaA"][:, (24 + fc) * 1024 + tc * 128: (24 + fc) * 1024 + tc * 128 + 128]
                qsl = ctx.t["arenaA"][:, (8 + fc) * 1024 + tc * 128: (8 + fc) * 1024 + tc * 128 + 128]
                ksl = ctx.t["arenaA"][:, (16 + fc) * 1024 + tc * 128: (16 + fc) * 1024 + tc * 128 + 128]
                p.wait("tensor", "sc", p.cnt["sc"])
                p.wait("tensor", "ve", p.cnt["ve"])
                p.emit("tensor", lambda E, S, o=pc_kt[0], s=vsl:
                       E.transpose(o, s, ident_b), mark="te")
                p.wait("scalar", "te", p.cnt["te"])
                p.selfwait("scalar")
                p.emit("scalar", lambda E, S, d=vT_t, s=pc_kt[0]:
                       E.activation(d, s, AF.Copy), mark="sc")
                vT_done = p.cnt["sc"]

                def mk(hp):
                    h = 2 * fc + hp
                    b0 = 64 * hp
                    return dict(h=h, b0=b0, T=tiles[hp], R=PS[hp],
                                ksl_h=ksl[b0:b0 + 64, :], qsl_h=qsl[b0:b0 + 64, :],
                                S_h=Sst[b0:b0 + 64, fc, :],
                                fa=f32a[hp], fb=f32b[hp], df=diagf[hp], af=argf[hp],
                                pk=pc_kt[hp])
                H = [mk(0), mk(1)]

                def s_copy(c, dtile_key, coltile, srct):
                    T, R = c["T"], c["R"]
                    p.wait("vector", "sc", p.cnt["sc"])
                    p.selfwait("vector")
                    p.wait("vector", "te", p.cnt["te"])
                    p.emit("vector", lambda E, S, d=T["diagb"], cc=coltile[:, c["h"]:c["h"] + 1]:
                           E.tensor_scalar_mul(d, ident_b, cc), mark="ve")
                    p.wait("tensor", "ve", p.cnt["ve"])
                    p.wait("tensor", "sc", p.cnt["sc"])
                    p.emit("tensor", lambda E, S, o=R["bc"], dd=T["diagb"]:
                           E.matmul(o, ones_b, dd, start=True, stop=True), mark="te")
                    p.wait("vector", "te", p.cnt["te"])
                    p.selfwait("vector")
                    p.emit("vector", lambda E, S, d=T[dtile_key][c["b0"]:c["b0"] + 64, :], s=srct,
                                  bc=R["bc"][c["b0"]:c["b0"] + 64, :]:
                           E.tensor_mul(d, s, bc), mark="ve")

                def s_decay(c, dstkey, content_col, bias_col, mask):
                    T = c["T"]
                    p.wait("vector", "sc", p.cnt["sc"])
                    p.selfwait("vector")
                    p.wait("vector", "te", p.cnt["te"])
                    p.emit("vector", lambda E, S, d=c["df"], cc=content_col[:, c["h"]:c["h"] + 1]:
                           E.tensor_scalar_mul(d, ident_f, cc), mark="ve")
                    p.wait("tensor", "ve", p.cnt["ve"])
                    p.wait("tensor", "sc", p.cnt["sc"])
                    p.emit("tensor", lambda E, S, o=c["R"]["bc"], dd=c["df"]:
                           E.matmul(o, onesf, dd, start=True, stop=True), mark="te")
                    p.wait("vector", "te", p.cnt["te"])
                    p.selfwait("vector")
                    p.emit("vector", lambda E, S, d=c["af"], s=c["R"]["bc"], m=mask:
                           E.tensor_add(d, s, m), mark="ve")
                    p.wait("scalar", "ve", p.cnt["ve"])
                    p.selfwait("scalar")
                    p.emit("scalar", lambda E, S, d=T[dstkey], s=c["af"],
                                  b=bias_col[:, c["h"]:c["h"] + 1]:
                           E.activation(d, s, AF.Exp, bias=b), mark="sc")

                for c in H:
                    s_copy(c, "kh", knT(tc), c["ksl_h"])
                for c in H:
                    s_copy(c, "qh", gamT, c["qsl_h"])
                for c in H:
                    s_decay(c, "Adec", negcl, cumlg, mb_gt)
                for c in H:
                    s_decay(c, "BT", cumlnb, negcl, mb_lt)
                for c in H:
                    s_decay(c, "Mkq", cumlg, negcl, mb_le)
                for c in H:
                    T, R = c["T"], c["R"]
                    b0 = c["b0"]
                    p.wait("tensor", "ve", p.cnt["ve"])
                    p.wait("tensor", "sc", p.cnt["sc"])
                    p.emit("tensor", lambda E, S, o=R["gram"], l=T["kh"][b0:b0 + 64, :], r=T["kh"][b0:b0 + 64, :]:
                           E.matmul(o, l, r, start=True, stop=True), mark="te")
                    c["gram_mark"] = p.cnt["te"]
                for c in H:
                    T, R = c["T"], c["R"]
                    p.wait("vector", "te", c["gram_mark"])
                    p.selfwait("vector")
                    p.emit("vector", lambda E, S, d=T["B"], s=R["gram"], cc=negb[:, c["h"]:c["h"] + 1], a=T["Adec"]:
                           E.scalar_tensor_tensor(d, s, cc, a, ALU.mult, ALU.mult), mark="ve")
                    p.emit("vector", lambda E, S, d=T["BT"], s=R["gram"], a=T["BT"]:
                           E.scalar_tensor_tensor(d, s, -1.0, a, ALU.mult, ALU.mult), mark="ve")
                    c["bbt_mark"] = p.cnt["ve"]
                for c in H:
                    T, R = c["T"], c["R"]
                    b0 = c["b0"]
                    p.wait("tensor", "ve", c["bbt_mark"])
                    p.emit("tensor", lambda E, S, o=R["gram"], l=T["kh"][b0:b0 + 64, :], r=c["qsl_h"]:
                           E.matmul(o, l, r, start=True, stop=True), mark="te")
                    c["gram2_mark"] = p.cnt["te"]
                for c in H:
                    T, R = c["T"], c["R"]
                    p.wait("vector", "te", c["gram2_mark"])
                    p.selfwait("vector")
                    p.emit("vector", lambda E, S, d=T["Mkq"], s=R["gram"], a=T["Mkq"]:
                           E.tensor_mul(d, s, a), mark="ve")
                    p.emit("vector", lambda E, S, d=T["S0b"][c["b0"]:c["b0"] + 64, :], s=c["S_h"]:
                           E.tensor_copy(d, s), mark="ve")
                    c["s0_mark"] = p.cnt["ve"]
                for c in H:
                    T, R = c["T"], c["R"]
                    b0 = c["b0"]
                    p.wait("tensor", "ve", c["s0_mark"])
                    p.emit("tensor", lambda E, S, o=R["w0"], l=T["kh"][b0:b0 + 64, :], r=T["S0b"][b0:b0 + 64, :]:
                           E.matmul(o, l, r, start=True, stop=True), mark="te")
                    c["w0_mark"] = p.cnt["te"]
                for c in H:
                    T = c["T"]
                    p.wait("vector", "te", c["w0_mark"])
                    p.wait("vector", "sc", vT_done)
                    p.selfwait("vector")
                    p.emit("vector", lambda E, S, d=c["fa"], s=c["R"]["w0"], cc=gamT[:, c["h"]:c["h"] + 1]:
                           E.tensor_scalar_mul(d, s, cc), mark="ve")
                    p.selfwait("vector")
                    p.emit("vector", lambda E, S, d=T["Rm"], s=vT_t[:, c["b0"]:c["b0"] + 64],
                                  cc=bT(tc)[:, c["h"]:c["h"] + 1], b=c["fa"]:
                           E.scalar_tensor_tensor(d, s, cc, b, ALU.mult, ALU.subtract), mark="ve")
                    p.selfwait("vector")
                    p.emit("vector", lambda E, S, d=T["U"], s=T["Rm"]: E.tensor_copy(d, s), mark="ve")
                    c["u_mark"] = p.cnt["ve"]
                for it in range(7):
                    for c in H:
                        T, R = c["T"], c["R"]
                        p.wait("tensor", "ve", c["u_mark"])
                        if it > 0:
                            p.wait("tensor", "sc", c["evac_mark"])
                        if it < 6:
                            p.emit("tensor", lambda E, S, o=R["bu"], l=T["BT"], r=T["U"]:
                                   E.matmul(o, l, r, start=True, stop=True))
                            p.emit("tensor", lambda E, S, o=R["b2"], l=T["BT"], r=T["B"]:
                                   E.matmul(o, l, r, start=True, stop=True))
                            p.emit("tensor", lambda E, S, o=R["bt2"], l=T["B"], r=T["BT"]:
                                   E.matmul(o, l, r, start=True, stop=True), mark="te")
                        else:
                            p.emit("tensor", lambda E, S, o=R["bu"], l=T["BT"], r=T["U"]:
                                   E.matmul(o, l, r, start=True, stop=True), mark="te")
                        c["mm_mark"] = p.cnt["te"]
                    for c in H:
                        T, R = c["T"], c["R"]
                        p.wait("vector", "te", c["mm_mark"])
                        p.selfwait("vector")
                        p.emit("vector", lambda E, S, d=T["U"], s=R["bu"]:
                               E.tensor_add(d, d, s), mark="ve")
                        c["u_mark"] = p.cnt["ve"]
                        if it < 6:
                            p.wait("scalar", "te", c["mm_mark"])
                            p.selfwait("scalar")
                            p.emit("scalar", lambda E, S, d=T["B"], s=R["b2"]:
                                   E.activation(d, s, AF.Copy), mark="sc")
                            p.emit("scalar", lambda E, S, d=T["BT"], s=R["bt2"]:
                                   E.activation(d, s, AF.Copy), mark="sc")
                            c["evac_mark"] = p.cnt["sc"]
                for c in H:
                    T, R = c["T"], c["R"]
                    b0 = c["b0"]
                    p.wait("tensor", "ve", c["u_mark"])
                    p.emit("tensor", lambda E, S, o=R["O"], l=T["Mkq"], r=T["U"]:
                           E.matmul(o, l, r, start=True, stop=False))
                    p.emit("tensor", lambda E, S, o=R["O"], l=T["qh"][b0:b0 + 64, :], r=T["S0b"][b0:b0 + 64, :]:
                           E.matmul(o, l, r, start=False, stop=True), mark="te")
                    c["o_mark"] = p.cnt["te"]
                for c in H:
                    p.wait("scalar", "te", c["o_mark"])
                    p.selfwait("scalar")
                    p.emit("scalar", lambda E, S, d=Opair[:, c["b0"]:c["b0"] + 64], s=c["R"]["O"]:
                           E.activation(d, s, AF.Copy), mark="sc")
                for c in H:
                    T = c["T"]
                    p.wait("tensor", "ve", p.cnt["ve"])
                    p.wait("tensor", "sc", p.cnt["sc"])
                    p.emit("tensor", lambda E, S, o=c["pk"], s=T["kh"]:
                           E.transpose(o, s, ident_b), mark="te")
                    c["kt_mark"] = p.cnt["te"]
                    p.wait("vector", "te", c["kt_mark"])
                    p.selfwait("vector")
                    p.emit("vector", lambda E, S, d=T["kt"], s=c["pk"], cc=sC[:, c["h"]:c["h"] + 1]:
                           E.tensor_scalar_mul(d, s, cc), mark="ve")
                    c["ktd_mark"] = p.cnt["ve"]
                for c in H:
                    T, R = c["T"], c["R"]
                    p.wait("tensor", "ve", c["ktd_mark"])
                    p.emit("tensor", lambda E, S, o=R["P"], l=T["kt"], r=T["U"]:
                           E.matmul(o, l, r, start=True, stop=True), mark="te")
                    c["p_mark"] = p.cnt["te"]
                for c in H:
                    b0 = c["b0"]
                    p.wait("vector", "te", c["p_mark"])
                    p.selfwait("vector")
                    p.emit("vector", lambda E, S, d=c["fb"][b0:b0 + 64, 0:64], a=c["S_h"],
                                  cc=gCc[b0:b0 + 64, c["h"]:c["h"] + 1]:
                           E.tensor_scalar_mul(d, a, cc), mark="ve")
                    p.selfwait("vector")
                    p.emit("vector", lambda E, S, d=c["S_h"], a=c["fb"][b0:b0 + 64, 0:64],
                                  b=c["R"]["P"][b0:b0 + 64, :]:
                           E.tensor_add(d, a, b), mark="ve")
                p.wait("tensor", "sc", p.cnt["sc"])
                p.wait("tensor", "ve", p.cnt["ve"])
                p.emit("tensor", lambda E, S, o=pc_ot, s=Opair:
                       E.transpose(o, s, ident_b), mark="te")
                p.wait("scalar", "te", p.cnt["te"])
                p.selfwait("scalar")
                odst = ctx.t["arenaB"][:, (8 + fc) * 1024 + tc * 128: (8 + fc) * 1024 + tc * 128 + 128]
                p.emit("scalar", lambda E, S, d=odst, s=pc_ot:
                       E.activation(d, s, AF.Copy), mark="sc")

    if GDN_DEBUG:
        return
    def rhs_o(wi, k, t):
        return aslot(ctx, "arenaB", 8 + k, t)

    def cb_wo(wi, mc, t, ps, b, banks):
        p.wait("vector", "te", p.cnt["te"])
        p.selfwait("vector")
        R = ctx.t["R"][:, mc, t * 512:(t + 1) * 512]
        p.emit("vector", lambda E, S, d=R, s=ps, c=ctx.col("l1a_bo", mc):
               E.scalar_tensor_tensor(d, s, c, d, ALU.add, ALU.add), mark="ve")
        p.emit("vector", lambda E, S, d=x1_ap(ctx, mc, t), s=ps,
                      c=ctx.col("l1a_bo", mc), z=z_ap(mc, t):
               E.scalar_tensor_tensor(d, s, c, z, ALU.add, ALU.add), mark="ve")
        ctx.set_bank_guard(b, "ve")

    emit_linear_fm(ctx, [W["wo"]], rhs_o, cb_wo, Mc=8)

    def x1src(fc, t):
        return x1_ap(ctx, fc, t)

    def xn2_ap(fc, t):
        return aslot(ctx, "arenaB", fc, t)
    emit_ln(ctx, x1src, xn2_ap, affine=("l1a_ln2g", "l1a_ln2b"))

    def rhs_xn2(wi, k, t):
        return aslot(ctx, "arenaB", k, t)

    def h_ap(kc, t):
        if kc < 16:
            return aslot(ctx, "arenaA", kc, t)
        return aslot(ctx, "arenaB", 8 + (kc - 16), t)

    def cb_m1(wi, mc, t, ps, b, banks):
        p.wait("scalar", "te", p.cnt["te"])
        p.emit("scalar", lambda E, S, d=h_ap(mc, t), s=ps:
               E.activation(d, s, AF.Gelu, bias=ctx.col("l1a_b1", mc)), mark="sc")
        ctx.set_bank_guard(b, "sc")

    emit_linear_fm(ctx, [W["m1"]], rhs_xn2, cb_m1, Mc=32)

    def rhs_h(wi, k, t):
        return h_ap(k, t)

    def cb_m2(wi, mc, t, ps, b, banks):
        p.wait("vector", "te", p.cnt["te"])
        p.selfwait("vector")
        R = ctx.t["R"][:, mc, t * 512:(t + 1) * 512]
        p.emit("vector", lambda E, S, d=R, s=ps, c=ctx.col("l1a_b2", mc):
               E.scalar_tensor_tensor(d, s, c, d, ALU.add, ALU.add), mark="ve")
        ctx.set_bank_guard(b, "ve")

    emit_linear_fm(ctx, [W["m2"]], rhs_h, cb_m2, Mc=8)


# version 07
# speedup vs baseline: 1.0400x; 1.0400x over previous
import contextlib
import os
import numpy as np
import ml_dtypes
import concourse.bass as bass
import concourse.mybir as mybir
from concourse.bass_utils import run_bass_kernel_spmd

F32 = mybir.dt.float32
BF16 = mybir.dt.bfloat16
AF = mybir.ActivationFunctionType
ALU = mybir.AluOpType

NCORES = 8
D = 1024
SEQ = 4096
TSTRIP = 1024
NSTRIP = SEQ // TSTRIP
NT = TSTRIP // 512
FC = D // 128

ENGINES = ("sync", "tensor", "vector", "scalar", "gpsimd")
GDN_DEBUG = False



def tile_w(w):
    K, M = w.shape
    Kc = K // 128
    mw = min(M, 128)
    Mc = M // mw
    return np.ascontiguousarray(
        w.reshape(Kc, 128, Mc, mw).transpose(2, 1, 0, 3))


def fm_of(x):
    T, d = x.shape
    return np.ascontiguousarray(x.T.reshape(d // 128, 128, T).transpose(1, 0, 2))


def cols_of(v):
    return np.ascontiguousarray(v.reshape(-1, 128).T)



class Prog:

    def __init__(self):
        self.q = {e: [] for e in ENGINES}
        self.cnt = dict(dma0=0, dma1=0, dma2=0, dma3=0, stdma=0, te=0, ve=0, sc=0, gp=0)
        self.dma_i = 0

    def wait(self, eng, sem, val):
        if val <= 0:
            return
        self.q[eng].append(lambda E, S, sem=sem, val=val: E.wait_ge(S[sem], val))

    def selfwait(self, eng):
        sem = ENG_SEM[eng]
        self.wait(eng, sem, self.cnt[sem])

    def dma(self, eng, build):
        i = self.dma_i % 4
        self.dma_i += 1
        sem = "dma%d" % i
        self.wait(eng, sem, self.cnt[sem])
        self.emit(eng, build, mark=sem, n=16)
        return (sem, self.cnt[sem])

    def emit(self, eng, build, mark=None, n=1):
        if mark is not None:
            self.cnt[mark] += n

            def f(E, S, build=build, mark=mark, n=n):
                ins = build(E, S)
                ins.then_inc(S[mark], n)
            self.q[eng].append(f)
        else:
            self.q[eng].append(lambda E, S, build=build: build(E, S))

    def replay(self, nc, sems):
        with nc.Block() as block:
            for eng in ENGINES:
                ops = self.q[eng]
                if not ops:
                    continue

                def body(E, ops=ops):
                    for f in ops:
                        f(E, sems)
                getattr(block, eng)(body)


class Ctx:

    def __init__(self, nc, prog):
        self.nc = nc
        self.p = prog
        self.t = {}
        self.colreg = {}
        self.bank_guard = [None] * 4
        self.bank_i = 0
        self.wstage_guard = [None, None]
        self.wbf_guard = {}

    def col(self, name, j=0):
        s, n = self.colreg[name]
        assert 0 <= j < n, (name, j, n)
        return self.t["cols"][:, s + j:s + j + 1]

    def col1(self, name, j=0):
        s, n = self.colreg[name]
        return self.t["cols"][0:1, s + j:s + j + 1]

    def next_bank(self):
        b = self.bank_i % 4
        self.bank_i += 1
        return b

    def guard_bank(self, b):
        g = self.bank_guard[b]
        if g is not None:
            self.p.wait("tensor", g[0], g[1])

    def set_bank_guard(self, b, sem):
        self.bank_guard[b] = (sem, self.p.cnt[sem])


SEMNAMES = ("dma0", "dma1", "dma2", "dma3", "stdma", "te", "ve", "sc", "gp")
ENG_SEM = dict(tensor="te", vector="ve", scalar="sc", gpsimd="gp")



class WSrc:

    def __init__(self, dram_ap, Kc, mw=128):
        self.ap = dram_ap
        self.Kc = Kc
        self.mw = mw

    def emit_load(self, ctx, mc, dest_ap_fn):
        p = ctx.p
        kgran = max(1, 512 // self.mw)
        ng = (self.Kc + kgran - 1) // kgran
        for g in range(ng):
            k0 = g * kgran
            kn = min(kgran, self.Kc - k0)
            half = ctx.stage_i % 2
            ctx.stage_i += 1
            stage = ctx.t["wstage"][:, half * 512:half * 512 + kn * self.mw]
            stage_v = stage.rearrange("p (k f) -> p k f", f=self.mw)
            gg = ctx.wstage_guard[half]
            if gg is not None:
                p.wait("sync", gg[0], gg[1])
            src = self.ap[mc, :, k0:k0 + kn, :]
            dm = p.dma("sync", lambda E, S, d=stage_v, s=src: E.dma_start(out=d, in_=s))
            p.wait("vector", dm[0], dm[1])
            dst = dest_ap_fn(k0, kn)
            p.emit("vector", lambda E, S, d=dst, s=stage_v: E.tensor_copy(d, s), mark="ve")
            ctx.wstage_guard[half] = ("ve", p.cnt["ve"])
        return p.cnt["ve"]


class WSinCos:

    def __init__(self, theta_ap, noise_ap, is_cos, Kc=8, mw=128):
        self.a1 = theta_ap
        self.a2 = noise_ap
        self.is_cos = is_cos
        self.Kc = Kc
        self.mw = mw

    def emit_load(self, ctx, mc, dest_ap_fn):
        p = ctx.p
        kgran = 2
        ng = (self.Kc + kgran - 1) // kgran
        for g in range(ng):
            k0 = g * kgran
            kn = min(kgran, self.Kc - k0)
            half = ctx.stage_i % 2
            ctx.stage_i += 1
            base = half * 512
            st1 = ctx.t["wstage"][:, base:base + kn * 128].rearrange("p (k f) -> p k f", f=128)
            st2 = ctx.t["wstage"][:, base + 256:base + 256 + kn * 128].rearrange("p (k f) -> p k f", f=128)
            gg = ctx.wstage_guard[half]
            if gg is not None:
                p.wait("sync", gg[0], gg[1])
            s1 = self.a1[mc, :, k0:k0 + kn, :]
            s2 = self.a2[mc, :, k0:k0 + kn, :]
            dm1 = p.dma("sync", lambda E, S, d=st1, s=s1: E.dma_start(out=d, in_=s))
            dm2 = p.dma("sync", lambda E, S, d=st2, s=s2: E.dma_start(out=d, in_=s))
            p.wait("vector", dm1[0], dm1[1])
            p.wait("vector", dm2[0], dm2[1])
            p.emit("vector", lambda E, S, a=st1, b=st2: E.tensor_add(a, a, b), mark="ve")
            dst = dest_ap_fn(k0, kn)
            p.wait("scalar", "ve", p.cnt["ve"])
            if self.is_cos:
                p.emit("scalar", lambda E, S, d=dst, s=st1:
                       E.activation(d, s, AF.Sin, bias=ctxcol(ctx, "c_pi2")), mark="sc")
            else:
                p.emit("scalar", lambda E, S, d=dst, s=st1:
                       E.activation(d, s, AF.Sin), mark="sc")
            ctx.wstage_guard[half] = ("sc", p.cnt["sc"])
        return None

    uses_sc = True


def ctxcol(ctx, name, j=0):
    return ctx.col(name, j)



def rhs_barrier(ctx):
    p = ctx.p
    p.wait("tensor", "ve", p.cnt["ve"])
    p.wait("tensor", "sc", p.cnt["sc"])
    p.wait("tensor", "gp", p.cnt["gp"])


def war_barrier(ctx, eng):
    ctx.p.wait(eng, "te", ctx.p.cnt["te"])


def emit_linear_fm(ctx, weights, rhs_fn, out_cb, Mc, T=TSTRIP):
    p = ctx.p
    nw = len(weights)
    Kc = weights[0].Kc
    mw = weights[0].mw
    nt = T // 512
    reg = Kc * mw
    assert nw * 2 * reg <= 8192
    ctx.stage_i = 0
    rhs_barrier(ctx)
    ctx.pass_id = getattr(ctx, "pass_id", 0) + 1
    pass_te0 = p.cnt["te"]

    def wreg(wi, parity):
        off = (wi * 2 + parity) * reg
        return ctx.t["wbf"][:, off:off + reg].rearrange("p (k f) -> p k f", f=mw)

    for mc in range(Mc):
        parity = mc % 2
        for wi, w in enumerate(weights):
            g = ctx.wbf_guard.get((wi, parity))
            if g is None or g[0] != ctx.pass_id:
                g = (ctx.pass_id, "te", pass_te0)
            p.wait("vector", g[1], g[2])
            if getattr(w, "uses_sc", False):
                p.wait("scalar", g[1], g[2])

            def dest_fn(k0, kn, wi=wi, parity=parity):
                return wreg(wi, parity)[:, k0:k0 + kn, :]
            w.emit_load(ctx, mc, dest_fn)
            if getattr(w, "uses_sc", False):
                p.wait("tensor", "sc", p.cnt["sc"])
            else:
                p.wait("tensor", "ve", p.cnt["ve"])
        for t in range(nt):
            banks = []
            for wi, w in enumerate(weights):
                b = ctx.next_bank()
                banks.append(b)
                ctx.guard_bank(b)
                ps = ctx.t["psA"][:, b, 0:512]
                wv = wreg(wi, parity)
                for k in range(Kc):
                    p.emit("tensor",
                           lambda E, S, o=ps[0:mw, :], l=wv[:, k, :], r=rhs_fn(wi, k, t),
                                  st=(k == 0), sp=(k == Kc - 1):
                           E.matmul(o, l, r, start=st, stop=sp),
                           mark="te" if k == Kc - 1 else None)
                out_cb(wi, mc, t, ps[0:mw, :], b, banks)
        ctx.wbf_guard[(0, parity)] = (ctx.pass_id, "te", p.cnt["te"])
        if nw > 1:
            ctx.wbf_guard[(1, parity)] = (ctx.pass_id, "te", p.cnt["te"])



def emit_ln(ctx, src_fn, dst_fn, affine=None, src_bf16=False):
    p = ctx.p
    scr = ctx.t["scr"]
    s2f = ctx.t["scr2"][:].bitcast(F32)
    A_t = scr[:, 0:512]
    B_t = scr[:, 512:1024]
    Cs = scr[:, 1024:1536]
    Ds = scr[:, 1536:2048]
    for t in range(NT):
        war_barrier(ctx, "vector")
        p.selfwait("vector")
        p.selfwait("scalar")
        p.wait("vector", "gp", p.cnt["gp"])
        p.wait("scalar", "gp", p.cnt["gp"])
        cast_marks = []
        if src_bf16:
            statsrc = src_fn
            cast_marks = [p.cnt["ve"]] * FC
        else:
            statsrc = dst_fn
            for fc in range(FC):
                p.emit("vector", lambda E, S, d=dst_fn(fc, t), s=src_fn(fc, t):
                       E.tensor_copy(d, s), mark="ve")
                cast_marks.append(p.cnt["ve"])
        sq_marks = []
        qmm_marks = []
        ps_s = ctx.t["psB"][0:1, 0, 0:512]
        ps_q = ctx.t["psB"][0:1, 1, 0:512]
        for fc in range(FC):
            p.wait("tensor", "ve", cast_marks[fc])
            p.emit("tensor", lambda E, S, o=ps_s, l=ctx.t["cons_bf"][:, 0:1], r=statsrc(fc, t),
                          st=(fc == 0), sp=(fc == FC - 1):
                   E.matmul(o, l, r, start=st, stop=sp),
                   mark="te" if fc == FC - 1 else None)
        smark = p.cnt["te"]
        for fc in range(FC):
            sqslot = ctx.t["scr2"][:, (fc % 2) * 512:(fc % 2) * 512 + 512]
            p.wait("scalar", "ve", cast_marks[fc])
            if fc >= 2:
                p.wait("scalar", "te", qmm_marks[fc - 2])
            p.emit("scalar", lambda E, S, d=sqslot, s=statsrc(fc, t):
                   E.activation(d, s, AF.Square), mark="sc")
            p.wait("tensor", "sc", p.cnt["sc"])
            p.emit("tensor", lambda E, S, o=ps_q, l=ctx.t["cons_bf"][:, 0:1], r=sqslot,
                          st=(fc == 0), sp=(fc == FC - 1):
                   E.matmul(o, l, r, start=st, stop=sp), mark="te")
            qmm_marks.append(p.cnt["te"])
        qmark = p.cnt["te"]
        sq_s = Ds[0:1, 0:512]
        var = A_t[0:1, 0:512]
        a_row = Cs[0:1, 0:512]
        p.wait("scalar", "te", smark)
        p.emit("scalar", lambda E, S, d=sq_s, s=ps_s:
               E.activation(d, s, AF.Square, scale=ctx.col1("c_r1024")), mark="sc")
        p.wait("vector", "te", qmark)
        p.wait("vector", "sc", p.cnt["sc"])
        p.emit("vector", lambda E, S, d=var, q=ps_q, ss=sq_s:
               E.scalar_tensor_tensor(d, q, ctx.col1("c_r1024"), ss, ALU.mult, ALU.subtract),
               mark="ve")
        p.wait("scalar", "ve", p.cnt["ve"])
        p.emit("scalar", lambda E, S, d=sq_s, s=var:
               E.activation(d, s, AF.Ln, bias=ctx.col1("c_eps")), mark="sc")
        p.selfwait("scalar")
        p.emit("scalar", lambda E, S, d=a_row, s=sq_s:
               E.activation(d, s, AF.Exp, scale=ctx.col1("c_neghalf")), mark="sc")
        p.wait("vector", "sc", p.cnt["sc"])
        p.emit("vector", lambda E, S, d=var, s=ps_s, ar=a_row:
               E.scalar_tensor_tensor(d, s, ctx.col1("c_nr1024"), ar, ALU.mult, ALU.mult),
               mark="ve")
        b_row = var
        onesrow_f = ctx.t["cons_f"][0:1, CF_UTRI:CF_UTRI + 128]
        p.wait("tensor", "ve", p.cnt["ve"])
        p.emit("tensor", lambda E, S, o=ctx.t["psB"][:, 0, 0:512], l=onesrow_f, r=a_row:
               E.matmul(o, l, r, start=True, stop=True), mark="te")
        p.emit("tensor", lambda E, S, o=ctx.t["psB"][:, 1, 0:512], l=onesrow_f, r=b_row:
               E.matmul(o, l, r, start=True, stop=True), mark="te")
        p.wait("scalar", "te", p.cnt["te"])
        p.emit("scalar", lambda E, S, d=A_t, s=ctx.t["psB"][:, 0, 0:512]:
               E.activation(d, s, AF.Copy), mark="sc")
        p.emit("scalar", lambda E, S, d=B_t, s=ctx.t["psB"][:, 1, 0:512]:
               E.activation(d, s, AF.Copy), mark="sc")
        bc_mark = p.cnt["sc"]
        p.wait("vector", "sc", bc_mark)
        p.wait("gpsimd", "sc", bc_mark)
        for fc in range(FC):
            if fc > 0:
                p.wait("vector", "gp", p.cnt["gp"])
            p.emit("vector", lambda E, S, d=Cs, s=src_fn(fc, t), a=A_t:
                   E.tensor_mul(d, s, a), mark="ve")
            p.wait("gpsimd", "ve", p.cnt["ve"])
            if affine is None:
                p.emit("gpsimd", lambda E, S, d=dst_fn(fc, t), c=Cs, b=B_t:
                       E.tensor_add(d, c, b), mark="gp")
            else:
                wn, bn = affine
                if fc > 0:
                    p.wait("gpsimd", "ve", p.cnt["ve"])
                p.emit("gpsimd", lambda E, S, d=Ds, c=Cs, b=B_t:
                       E.tensor_add(d, c, b), mark="gp")
                p.wait("vector", "gp", p.cnt["gp"])
                p.emit("vector", lambda E, S, d=dst_fn(fc, t), s=Ds,
                              w=ctx.col(wn, fc), b=ctx.col(bn, fc):
                       E.tensor_scalar(d, s, w, b, ALU.mult, ALU.add), mark="ve")
        ctx.ln_done = ("gp" if affine is None else "ve",
                       p.cnt["gp" if affine is None else "ve"])



def aslot(ctx, arena, slot, t, n=1):
    a = ctx.t[arena]
    return a[:, slot * 1024 + t * 512: slot * 1024 + t * 512 + 512]


def aslot_tokens(ctx, arena, slot, t0, tl):
    a = ctx.t[arena]
    return a[:, slot * 1024 + t0: slot * 1024 + t0 + tl]


def x1_ap(ctx, fc, t):
    a = ctx.t["arenaA"]
    off = 32768 // 2 + (fc * 1024 + t * 512) * 2
    return a[:, off:off + 1024].bitcast(F32)



def emit_l0ffn(ctx, W):
    p = ctx.p
    silu_slot = [ctx.t["scr2"][:, i * 512:(i + 1) * 512] for i in range(2)]

    def rhs_z(wi, k, t):
        return aslot(ctx, "arenaA", k, t)

    def cb_w12(wi, mc, t, ps, b, banks):
        if wi == 0:
            p.wait("scalar", "te", p.cnt["te"])
            p.wait("scalar", "ve", p.cnt["ve"])
            p.emit("scalar", lambda E, S, d=silu_slot[t], s=ps:
                   E.activation(d, s, AF.Silu, scale=ctx.col("l0f_s1", mc)), mark="sc")
            ctx.set_bank_guard(b, "sc")
        else:
            p.wait("vector", "te", p.cnt["te"])
            p.wait("vector", "sc", p.cnt["sc"])
            dst = aslot(ctx, "arenaB", mc, t)
            p.emit("vector", lambda E, S, d=dst, s=ps, sl=silu_slot[t]:
                   E.scalar_tensor_tensor(d, s, ctx.col("l0f_s2", mc), sl, ALU.mult, ALU.mult),
                   mark="ve")
            ctx.set_bank_guard(b, "ve")
            ctx.set_bank_guard(banks[0], "ve")

    emit_linear_fm(ctx, [W["w1"], W["w2"]], rhs_z, cb_w12, Mc=32)

    def rhs_a(wi, k, t):
        return aslot(ctx, "arenaB", k, t)

    def cb_h1(wi, mc, t, ps, b, banks):
        p.wait("scalar", "te", p.cnt["te"])
        dst = aslot(ctx, "arenaA", mc, t)
        p.emit("scalar", lambda E, S, d=dst, s=ps:
               E.activation(d, s, AF.Gelu, scale=ctx.col("l0f_h1s", mc)), mark="sc")
        ctx.set_bank_guard(b, "sc")

    emit_linear_fm(ctx, [W["h1"]], rhs_a, cb_h1, Mc=32)

    def rhs_b(wi, k, t):
        return aslot(ctx, "arenaA", k, t)

    def cb_h2(wi, mc, t, ps, b, banks):
        p.wait("scalar", "te", p.cnt["te"])
        dst = aslot(ctx, "arenaB", mc, t)
        p.emit("scalar", lambda E, S, d=dst, s=ps:
               E.activation(d, s, AF.Gelu, scale=ctx.col("l0f_h2s", mc)), mark="sc")
        ctx.set_bank_guard(b, "sc")

    emit_linear_fm(ctx, [W["h2"]], rhs_b, cb_h2, Mc=32)

    def rhs_c(wi, k, t):
        return aslot(ctx, "arenaB", k, t)

    def cb_w3(wi, mc, t, ps, b, banks):
        p.wait("vector", "te", p.cnt["te"])
        R = ctx.t["R"][:, mc, t * 512:(t + 1) * 512]
        p.emit("vector", lambda E, S, d=R, s=ps:
               E.scalar_tensor_tensor(d, s, ctx.col("l0f_hs3", mc), d, ALU.mult, ALU.add),
               mark="ve")
        ctx.set_bank_guard(b, "ve")

    emit_linear_fm(ctx, [W["w3"]], rhs_c, cb_w3, Mc=8)



def emit_l1ffn(ctx, W):
    p = ctx.p
    scr = ctx.t["scr"]
    Cs, Ds = scr[:, 1024:1536], scr[:, 1536:2048]

    def rhs_z(wi, k, t):
        return aslot(ctx, "arenaA", k, t)

    def mag_ap(mc, t):
        a = ctx.t["arenaB"]
        off = (mc * 1024 + t * 512) * 2
        return a[:, off:off + 1024].bitcast(F32)

    def cb(wi, mc, t, ps, b, banks):
        if wi == 0:
            return
        p.wait("scalar", "te", p.cnt["te"])
        p.wait("scalar", "ve", p.cnt["ve"])
        ps_re = ctx.t["psA"][:, banks[0], 0:512]
        p.emit("scalar", lambda E, S, d=Cs, s=ps_re:
               E.activation(d, s, AF.Square), mark="sc")
        p.emit("scalar", lambda E, S, d=Ds, s=ps:
               E.activation(d, s, AF.Square), mark="sc")
        p.wait("vector", "sc", p.cnt["sc"])
        p.emit("vector", lambda E, S, d=mag_ap(mc, t), a=Cs, bb=Ds:
               E.tensor_add(d, a, bb), mark="ve")
        ctx.set_bank_guard(b, "ve")
        ctx.set_bank_guard(banks[0], "sc")

    emit_linear_fm(ctx, [W["cos"], W["sin"]], rhs_z, cb, Mc=8)
    p.wait("scalar", "ve", p.cnt["ve"])
    for mc in range(8):
        for t in range(NT):
            p.wait("scalar", "ve", p.cnt["ve"])
            p.selfwait("scalar")
            p.emit("scalar", lambda E, S, d=Ds, s=mag_ap(mc, t):
                   E.activation(d, s, AF.Sqrt, scale=ctx.col("c_quarter")), mark="sc")
            p.wait("vector", "sc", p.cnt["sc"])
            p.selfwait("vector")
            R = ctx.t["R"][:, mc, t * 512:(t + 1) * 512]
            p.emit("vector", lambda E, S, d=R, s=Ds: E.tensor_add(d, d, s), mark="ve")



BLOCKS = ("l0ffn_a", "l0attn", "l0ffn_b", "l1ffn_a", "l1attn", "l1ffn_b")


def make_consts():
    i128 = np.arange(128)
    ident_f = np.eye(128, dtype=np.float32)
    utri_f = (i128[:, None] <= i128[None, :]).astype(np.float32)
    bf = ml_dtypes.bfloat16
    ident_b = ident_f.astype(bf)
    utri_b = utri_f.astype(bf)
    ones_b = np.ones((128, 128), bf)
    maskLneg = -np.tril(np.ones((128, 128), np.float32), -1)
    maskUneg = -np.triu(np.ones((128, 128), np.float32), 1)
    knsel = np.zeros((128, 128), np.float32)
    for fc in range(8):
        for j in range(16):
            if j // 2 == fc:
                rows = slice((j % 2) * 64, (j % 2) * 64 + 64)
                knsel[rows, fc * 16 + j] = 1.0
    BIG = -1e30
    mb_le = np.where(i128[:, None] <= i128[None, :], 0.0, BIG).astype(np.float32)
    mb_lt = np.where(i128[:, None] < i128[None, :], 0.0, BIG).astype(np.float32)
    mb_gt = np.where(i128[:, None] > i128[None, :], 0.0, BIG).astype(np.float32)
    ones_f = np.ones((128, 128), np.float32)
    cons_f = np.concatenate([ident_f, utri_f, ones_f, mb_le, mb_lt, mb_gt], axis=1)
    cons_b = np.concatenate([np.ones((128, 1), np.float32), ident_f, utri_f,
                             np.ones((128, 128), np.float32), maskLneg, maskUneg,
                             knsel], axis=1).astype(bf)
    return cons_f, cons_b

CB_ONESCOL = 0
CB_IDENT = 1
CB_UTRI = 129
CB_ONES = 257
CB_MASKL = 385
CB_MASKU = 513
CB_KNSEL = 641
CF_IDENT = 0
CF_UTRI = 128
CF_ONES = 256
CF_MBLE = 384
CF_MBLT = 512
CF_MBGT = 640


def build_cols(inputs):
    reg = {}
    colsl = []

    def add(name, arr):
        arr = np.asarray(arr, np.float32)
        if arr.ndim == 1:
            if arr.size % 128 == 0:
                c = cols_of(arr)
            else:
                raise ValueError(name)
        else:
            c = arr
        reg[name] = (sum(x.shape[1] for x in colsl), c.shape[1])
        colsl.append(c)

    def addc(name, v):
        add(name, np.full((128, 1), v, np.float32))

    add("l0f_s1", inputs["L0_ffn_w1_s"])
    add("l0f_s2", inputs["L0_ffn_w2_s"])
    add("l0f_h1s", inputs["L0_ffn_h1_s"])
    add("l0f_h2s", inputs["L0_ffn_h2_s"])
    add("l0f_s3", inputs["L0_ffn_w3_s"])
    add("l0f_hs3", inputs["L0_ffn_w3_s"])
    add("l0a_n1w", inputs["L0_attn_norm1_w"])
    add("l0a_n1b", inputs["L0_attn_norm1_b"])
    add("l0a_n2w", inputs["L0_attn_norm2_w"])
    add("l0a_n2b", inputs["L0_attn_norm2_b"])
    add("l0a_gb", inputs["L0_attn_gate_b"])
    add("l0a_ob", inputs["L0_attn_out_b"])
    add("l0a_mb1", inputs["L0_attn_mlp_b1"])
    add("l0a_mb2", inputs["L0_attn_mlp_b2"])
    add("l1a_ln1g", inputs["L1_attn_ln1_g"])
    add("l1a_ln1b", inputs["L1_attn_ln1_b"])
    add("l1a_ln2g", inputs["L1_attn_ln2_g"])
    add("l1a_ln2b", inputs["L1_attn_ln2_b"])
    add("l1a_bo", inputs["L1_attn_bo"])
    add("l1a_b1", inputs["L1_attn_b1"])
    add("l1a_b2", inputs["L1_attn_b2"])
    addc("c_1", 1.0)
    addc("c_quarter", 0.25)
    addc("c_eighth", 0.125)
    addc("c_eps", 1e-5)
    addc("c_r1024", 1.0 / 1024.0)
    addc("c_nr1024", -1.0 / 1024.0)
    addc("c_neg1", -1.0)
    addc("c_neghalf", -0.5)
    addc("c_1e6", 1e-6)
    addc("c_pi2", float(np.pi / 2))
    addc("c_half", 0.5)
    cols = np.concatenate(colsl, axis=1)
    return cols, reg


def prep_host(inputs):
    h = {}
    h["W_l0f_w1"] = tile_w(np.asarray(inputs["L0_ffn_w1_w"]).T)
    h["W_l0f_w2"] = tile_w(np.asarray(inputs["L0_ffn_w2_w"]).T)
    h["W_l0f_h1"] = tile_w(np.asarray(inputs["L0_ffn_h1_w"]).T)
    h["W_l0f_h2"] = tile_w(np.asarray(inputs["L0_ffn_h2_w"]).T)
    h["W_l0f_w3"] = tile_w(np.asarray(inputs["L0_ffn_w3_w"]).T)
    h["W_l0a_gate"] = tile_w(np.asarray(inputs["L0_attn_gate_w"]))
    h["W_l0a_state"] = np.ascontiguousarray(np.asarray(inputs["L0_attn_state_w"]))
    h["W_l0a_out"] = tile_w(np.asarray(inputs["L0_attn_out_w"]))
    h["W_l0a_mlp1"] = tile_w(np.asarray(inputs["L0_attn_mlp_w1"]))
    h["W_l0a_mlp2"] = tile_w(np.asarray(inputs["L0_attn_mlp_w2"]))
    h["W_l1a_q"] = tile_w(np.asarray(inputs["L1_attn_Wq"]))
    h["W_l1a_k"] = tile_w(np.asarray(inputs["L1_attn_Wk"]))
    h["W_l1a_v"] = tile_w(np.asarray(inputs["L1_attn_Wv"]))
    h["W_l1a_b"] = tile_w(np.asarray(inputs["L1_attn_Wb"]))
    h["W_l1a_g"] = tile_w(np.asarray(inputs["L1_attn_Wg"]))
    h["W_l1a_o"] = tile_w(np.asarray(inputs["L1_attn_Wo"]))
    h["W_l1a_m1"] = tile_w(np.asarray(inputs["L1_attn_w1"]))
    h["W_l1a_m2"] = tile_w(np.asarray(inputs["L1_attn_w2"]))
    h["W_l1f_th"] = tile_w(np.asarray(inputs["L1_ffn_theta"]))
    h["W_l1f_thn"] = tile_w(np.asarray(inputs["L1_ffn_theta_noise"]))
    h["W_l1f_ph"] = tile_w(np.asarray(inputs["L1_ffn_phi"]))
    h["W_l1f_phn"] = tile_w(np.asarray(inputs["L1_ffn_phi_noise"]))
    cols, reg = build_cols(inputs)
    h["cols_in"] = cols
    h["rows_in"] = np.asarray(inputs["L0_attn_state_b"], np.float32)[None, :]
    cf, cb = make_consts()
    h["cons_f"] = cf
    h["cons_b"] = cb
    return h, reg


def declare_drams(nc, h, blocks, fused, rin_shape):
    d = {}
    for k, v in h.items():
        d[k] = nc.dram_tensor(k, list(v.shape), BF16 if v.dtype == ml_dtypes.bfloat16 else F32,
                              kind="ExternalInput").ap()
    d["x_fm"] = nc.dram_tensor("x_fm", list(rin_shape), F32, kind="ExternalInput").ap()
    d["y_fm"] = nc.dram_tensor("y_fm", list(rin_shape), F32, kind="ExternalOutput").ap()
    return d


def alloc_tensors(ctx, es):
    nc = ctx.nc
    t = ctx.t
    t["R"] = es.enter_context(nc.sbuf_tensor("sb_R", [128, 8, 1024], F32))
    t["arenaA"] = es.enter_context(nc.sbuf_tensor("sb_arenaA", [128, 32768], BF16))
    t["arenaB"] = es.enter_context(nc.sbuf_tensor("sb_arenaB", [128, 32768], BF16))
    t["wstage"] = es.enter_context(nc.sbuf_tensor("sb_wstage", [128, 1024], F32))
    t["wbf"] = es.enter_context(nc.sbuf_tensor("sb_wbf", [128, 8192], BF16))
    t["scr"] = es.enter_context(nc.sbuf_tensor("sb_scr", [128, 2048], F32))
    t["scr2"] = es.enter_context(nc.sbuf_tensor("sb_scr2", [128, 3584], BF16))
    t["cols"] = es.enter_context(nc.sbuf_tensor("sb_cols", [128, 384], F32))
    t["cons_f"] = es.enter_context(nc.sbuf_tensor("sb_cons_f", [128, 768], F32))
    t["cons_bf"] = es.enter_context(nc.sbuf_tensor("sb_cons_bf", [128, 769], BF16))
    t["Sstate"] = es.enter_context(nc.sbuf_tensor("sb_Sstate", [128, 8, 64], F32))
    t["rows_bf"] = es.enter_context(nc.sbuf_tensor("sb_rows_bf", [1, 1040], BF16))
    t["carry"] = es.enter_context(nc.sbuf_tensor("sb_carry", [1, 1024], BF16))
    t["wbar"] = es.enter_context(nc.sbuf_tensor("sb_wbar", [128, 18], F32))
    t["psA"] = es.enter_context(nc.psum_tensor("sb_psA", [128, 4, 512], F32))
    t["psB"] = es.enter_context(nc.psum_tensor("sb_psB", [128, 3, 512], F32))
    t["psC"] = es.enter_context(nc.psum_tensor("sb_psC", [128, 1024], BF16))


def emit_init(ctx, d):
    p = ctx.p
    rows = ctx.t["scr"][0:1, 0:1024]
    dms = []
    for name, dst in [("cols_in", "cols"), ("cons_f", "cons_f"), ("cons_b", "cons_bf")]:
        nn = d[name].shape[1]
        dms.append(p.dma("sync", lambda E, S, dd=ctx.t[dst][:, 0:nn], s=d[name]:
                         E.dma_start(out=dd, in_=s)))
    dms.append(p.dma("sync", lambda E, S, dd=rows, s=d["rows_in"]: E.dma_start(out=dd, in_=s)))
    for sem, val in dms:
        p.wait("vector", sem, val)
    s, n = ctx.colreg["l0f_hs3"]
    p.emit("vector", lambda E, S: E.tensor_scalar_mul(
        ctx.t["cols"][:, s:s + n], ctx.t["cols"][:, s:s + n], 0.5), mark="ve")
    p.emit("vector", lambda E, S, rows=rows: E.tensor_copy(ctx.t["rows_bf"][:, 0:1024], rows),
           mark="ve")
    bbar_f = ctx.t["scr"][0:1, 1024:1025]
    p.emit("vector", lambda E, S, rows=rows, d=bbar_f: E.reduce_sum(d, rows, axis=mybir.AxisListType.X),
           mark="ve")
    p.selfwait("vector")
    p.emit("vector", lambda E, S, d=ctx.t["rows_bf"][0:1, 1024:1025], s=bbar_f:
           E.tensor_scalar_mul(d, s, 1.0 / 1024.0), mark="ve")
    p.emit("vector", lambda E, S: E.memset(ctx.t["carry"][:], 0.0), mark="ve")
    p.emit("vector", lambda E, S: E.memset(ctx.t["scr2"][:], 0.0), mark="ve")
    p.emit("vector", lambda E, S: E.memset(ctx.t["arenaA"][:], 0.0), mark="ve")
    p.emit("vector", lambda E, S: E.memset(ctx.t["arenaB"][:], 0.0), mark="ve")
    p.selfwait("vector")
    if os.environ.get("BASS_SIM_FP64") == "1" or os.environ.get("KMK_SIM") == "1":
        p.emit("vector", lambda E, S: E.memset(ctx.t["psC"][:], 0.0), mark="ve")
        p.emit("vector", lambda E, S: E.memset(ctx.t["psB"][:], 0.0), mark="ve")
    p.emit("vector", lambda E, S: E.memset(ctx.t["scr"][:], 0.0), mark="ve")
    p.emit("vector", lambda E, S: E.memset(ctx.t["Sstate"][:], 0.0), mark="ve")
    p.emit("vector", lambda E, S: E.memset(ctx.t["wbar"][:], 0.0), mark="ve")
    ctx.init_done = p.cnt["ve"]


def emit_strip(ctx, d, W, strip, blocks):
    p = ctx.p
    t0 = strip * TSTRIP
    ctx.rload = p.dma("sync", lambda E, S: E.dma_start(
        out=ctx.t["R"][:], in_=d["x_fm"][:, :, t0:t0 + TSTRIP]))

    def Rsrc(fc, t):
        return ctx.t["R"][:, fc, t * 512:(t + 1) * 512]

    def zdst(fc, t):
        return aslot(ctx, "arenaA", fc, t)

    first = True
    for blk in blocks:
        if first:
            p.wait("vector", ctx.rload[0], ctx.rload[1])
            first = False
        if blk in ("l0ffn_a", "l0ffn_b"):
            emit_ln(ctx, Rsrc, zdst, affine=None)
            emit_l0ffn(ctx, {k: W[k] for k in ("w1", "w2", "h1", "h2", "w3")})
        elif blk in ("l1ffn_a", "l1ffn_b"):
            emit_ln(ctx, Rsrc, zdst, affine=None)
            emit_l1ffn(ctx, {"cos": W["cos"], "sin": W["sin"]})
        elif blk == "l0attn":
            emit_ln(ctx, Rsrc, zdst, affine=None)
            emit_l0attn(ctx, W)
        elif blk == "l1attn":
            emit_ln(ctx, Rsrc, zdst, affine=None)
            emit_l1attn(ctx, W)
        elif blk == "lntest":
            emit_ln(ctx, Rsrc, zdst, affine=None)
            for t in range(NT):
                for fc in range(FC):
                    p.wait("vector", "gp", p.cnt["gp"])
                    p.emit("vector", lambda E, S, d=Rsrc(fc, t), s=zdst(fc, t):
                           E.tensor_copy(d, s), mark="ve")
        elif blk == "lnafftest":
            emit_ln(ctx, Rsrc, zdst, affine=("l0a_n1w", "l0a_n1b"))
            for t in range(NT):
                for fc in range(FC):
                    p.wait("vector", "sc", p.cnt["sc"])
                    p.emit("vector", lambda E, S, d=Rsrc(fc, t), s=zdst(fc, t):
                           E.tensor_copy(d, s), mark="ve")
    p.wait("gpsimd", "ve", p.cnt["ve"])
    p.wait("gpsimd", "sc", p.cnt["sc"])
    p.wait("gpsimd", "stdma", p.cnt["stdma"])
    p.emit("gpsimd", lambda E, S: E.dma_start(
        out=d["y_fm"][:, :, t0:t0 + TSTRIP], in_=ctx.t["R"][:]), mark="stdma", n=16)
    p.wait("sync", "stdma", p.cnt["stdma"])


def make_wsrcs(ctx, d):
    return dict(
        w1=WSrc(d["W_l0f_w1"], 8), w2=WSrc(d["W_l0f_w2"], 8),
        h1=WSrc(d["W_l0f_h1"], 32), h2=WSrc(d["W_l0f_h2"], 32),
        w3=WSrc(d["W_l0f_w3"], 32),
        gate=WSrc(d["W_l0a_gate"], 8), out=WSrc(d["W_l0a_out"], 8),
        mlp1=WSrc(d["W_l0a_mlp1"], 8), mlp2=WSrc(d["W_l0a_mlp2"], 32),
        state_nat=d["W_l0a_state"],
        q=WSrc(d["W_l1a_q"], 8), k=WSrc(d["W_l1a_k"], 8), v=WSrc(d["W_l1a_v"], 8),
        wb=WSrc(d["W_l1a_b"], 8, mw=16), wg=WSrc(d["W_l1a_g"], 8, mw=16),
        wo=WSrc(d["W_l1a_o"], 8), m1=WSrc(d["W_l1a_m1"], 8), m2=WSrc(d["W_l1a_m2"], 32),
        cos=WSinCos(d["W_l1f_th"], d["W_l1f_thn"], True),
        sin=WSinCos(d["W_l1f_ph"], d["W_l1f_phn"], False),
    )


def build_program(h, reg, blocks, n_strips=NSTRIP):
    nc = bass.Bass("TRN2", target_bir_lowering=False)
    d = declare_drams(nc, h, blocks, True, [128, 8, SEQ])
    prog = Prog()
    ctx = Ctx(nc, prog)
    ctx.colreg = reg
    es = contextlib.ExitStack()
    with es:
        alloc_tensors(ctx, es)
        sems = {}
        for sn in SEMNAMES:
            sems[sn] = es.enter_context(nc.semaphore(sn))
        emit_init(ctx, d)
        W = make_wsrcs(ctx, d)
        for strip in range(n_strips):
            emit_strip(ctx, d, W, strip, blocks)
        prog.wait("gpsimd", "stdma", prog.cnt["stdma"])
        prog.replay(nc, sems)
    return nc


def kernel(**inputs):
    h, reg = prep_host(inputs)
    x = np.asarray(inputs["L0_attn_x"])
    nc = build_program(h, reg, list(BLOCKS))
    in_maps = []
    for c in range(NCORES):
        m = dict(h)
        m["x_fm"] = fm_of(x[c])
        in_maps.append(m)
    res = run_bass_kernel_spmd(nc, in_maps, core_ids=list(range(NCORES)))
    out = np.empty((NCORES, SEQ, D), np.float32)
    for c in range(NCORES):
        y = res.results[c]["y_fm"]
        out[c] = y.transpose(1, 0, 2).reshape(D, SEQ).T
    return out


def emit_tok_state_pass(ctx, W, srcs, dsts, seed_row, wbar=True):
    p = ctx.p
    rhs_barrier(ctx)
    ctx.pass_id = getattr(ctx, "pass_id", 0) + 1
    pass_te0 = p.cnt["te"]
    wk = [ctx.t["wbf"][:, k * 1024:(k + 1) * 1024] for k in range(8)]
    ctx.stage_i = 0
    for k in range(8):
        for half in range(2):
            hs = ctx.stage_i % 2
            ctx.stage_i += 1
            stage = ctx.t["wstage"][:, hs * 512:(hs + 1) * 512]
            gg = ctx.wstage_guard[hs]
            if gg is not None:
                p.wait("sync", gg[0], gg[1])
            src = W[k * 128:(k + 1) * 128, half * 512:(half + 1) * 512]
            dm = p.dma("sync", lambda E, S, d=stage, s=src: E.dma_start(out=d, in_=s))
            p.wait("vector", dm[0], dm[1])
            if k == 0 and half == 0:
                p.wait("vector", "te", pass_te0)
            p.emit("vector", lambda E, S, d=wk[k][:, half * 512:(half + 1) * 512], s=stage:
                   E.tensor_copy(d, s), mark="ve")
            ctx.wstage_guard[hs] = ("ve", p.cnt["ve"])
        if wbar:
            p.selfwait("vector")
            p.emit("vector", lambda E, S, d=ctx.t["wbar"][:, k:k + 1], s=wk[k]:
                   E.reduce_sum(d, s, axis=mybir.AxisListType.X), mark="ve")
    if wbar:
        p.selfwait("vector")
        p.emit("vector", lambda E, S: E.tensor_scalar_mul(
            ctx.t["wbar"][:, 0:8], ctx.t["wbar"][:, 0:8], 1.0 / 1024.0), mark="ve")
        p.selfwait("vector")
        p.emit("vector", lambda E, S: E.tensor_copy(
            ctx.t["scr2"][:, 3520:3528], ctx.t["wbar"][:, 0:8]), mark="ve")
    cast_done = p.cnt["ve"]
    p.wait("tensor", "ve", cast_done)
    for tc in range(8):
        for si, src_fn in enumerate(srcs):
            for half in range(2):
                b = ctx.next_bank()
                ctx.guard_bank(b)
                ps = ctx.t["psA"][:, b, 0:512]
                p.emit("tensor", lambda E, S, o=ps, r=seed_row(half):
                       E.matmul(o, ctx.t["cons_bf"][0:1, CB_ONES:CB_ONES + 128], r,
                                start=True, stop=False))
                for k in range(8):
                    p.emit("tensor", lambda E, S, o=ps, l=src_fn(k, tc),
                                  r=wk[k][:, half * 512:(half + 1) * 512], sp=(k == 7):
                           E.matmul(o, l, r, start=False, stop=sp),
                           mark="te" if k == 7 else None)
                p.wait("scalar", "te", p.cnt["te"])
                p.emit("scalar", lambda E, S, d=dsts[si](tc, half), s=ps:
                       E.activation(d, s, AF.Copy), mark="sc")
                ctx.set_bank_guard(b, "sc")


def emit_l0attn(ctx, W):
    p = ctx.p

    def z_ap(fc, t):
        return aslot(ctx, "arenaA", fc, t)

    def z_tok(kc, tc):
        return ctx.t["arenaA"][:, kc * 1024 + tc * 128: kc * 1024 + tc * 128 + 128]

    war_barrier(ctx, "vector")
    p.selfwait("vector")
    p.wait("vector", "gp", p.cnt["gp"])
    for t in range(NT):
        for fc in range(FC):
            p.emit("vector", lambda E, S, d=ctx.t["R"][:, fc, t * 512:(t + 1) * 512],
                          s=z_ap(fc, t): E.tensor_add(d, d, s), mark="ve")

    def xn1_ap(fc, t):
        return aslot(ctx, "arenaB", fc, t)
    emit_ln(ctx, z_ap, xn1_ap, affine=("l0a_n1w", "l0a_n1b"), src_bf16=True)

    def rhs_xn1(wi, k, t):
        return aslot(ctx, "arenaB", k, t)

    def cb_gate(wi, mc, t, ps, b, banks):
        p.wait("scalar", "te", p.cnt["te"])
        dst = aslot(ctx, "arenaB", 8 + mc, t)
        p.emit("scalar", lambda E, S, d=dst, s=ps:
               E.activation(d, s, AF.Silu, bias=ctx.col("l0a_gb", mc)), mark="sc")
        ctx.set_bank_guard(b, "sc")

    emit_linear_fm(ctx, [W["gate"]], rhs_xn1, cb_gate, Mc=16)

    war_barrier(ctx, "vector")
    p.wait("vector", "sc", p.cnt["sc"])
    for t in range(NT):
        for fc in range(FC):
            p.emit("vector", lambda E, S, d=aslot(ctx, "arenaA", 8 + fc, t),
                          a=aslot(ctx, "arenaB", fc, t), g=aslot(ctx, "arenaB", 8 + fc, t):
                   E.tensor_mul(d, a, g), mark="ve")

    def xg_tok(kc, tc):
        return ctx.t["arenaA"][:, (8 + kc) * 1024 + tc * 128: (8 + kc) * 1024 + tc * 128 + 128]

    def gg_tok(kc, tc):
        return ctx.t["arenaB"][:, (16 + kc) * 1024 + tc * 128: (16 + kc) * 1024 + tc * 128 + 128]

    def u_dst(tc, half):
        return ctx.t["arenaB"][:, (24 + tc) * 1024 + half * 512: (24 + tc) * 1024 + half * 512 + 512]

    def bm_dst(tc, half):
        return ctx.t["arenaA"][:, (16 + tc) * 1024 + half * 512: (16 + tc) * 1024 + half * 512 + 512]

    def seed_row(half):
        return ctx.t["rows_bf"][0:1, half * 512:(half + 1) * 512]

    emit_tok_state_pass(ctx, W["state_nat"], [xg_tok, gg_tok], [u_dst, bm_dst], seed_row)

    scr = ctx.t["scr"]
    dcol = scr[:, 1536:1544]
    ps_d = ctx.t["psB"][:, 2, 0:1]
    for tc in range(8):
        p.wait("tensor", "sc", p.cnt["sc"])
        p.emit("tensor", lambda E, S, o=ps_d:
               E.matmul(o, ctx.t["cons_bf"][0:1, CB_ONES:CB_ONES + 128],
                        ctx.t["rows_bf"][0:1, 1024:1025], start=True, stop=False))
        wb_bf = ctx.t["scr2"][:, 3520:3528]
        for k in range(8):
            p.emit("tensor", lambda E, S, o=ps_d, l=xn1_tok(ctx, k, tc),
                          r=wb_bf[:, k:k + 1], sp=(k == 7):
                   E.matmul(o, l, r, start=False, stop=sp), mark="te" if k == 7 else None)
        p.wait("scalar", "te", p.cnt["te"])
        p.emit("scalar", lambda E, S, d=dcol[:, tc:tc + 1], s=ps_d:
               E.activation(d, s, AF.Copy), mark="sc")
    t1 = scr[:, 1544:1552]
    t2 = scr[:, 1552:1560]
    t3 = scr[:, 1560:1568]
    p.selfwait("scalar")
    p.emit("scalar", lambda E, S: E.activation(t1, dcol, AF.Abs), mark="sc")
    p.selfwait("scalar")
    p.emit("scalar", lambda E, S: E.activation(t2, t1, AF.Exp, scale=ctx.col("c_neg1")), mark="sc")
    p.selfwait("scalar")
    p.emit("scalar", lambda E, S: E.activation(t3, t2, AF.Ln, bias=ctx.col("c_1")), mark="sc")
    p.wait("vector", "sc", p.cnt["sc"])
    p.selfwait("vector")
    p.emit("vector", lambda E, S: E.tensor_relu(t1, dcol), mark="ve")
    p.selfwait("vector")
    p.emit("vector", lambda E, S: E.tensor_add(dcol, t1, t3), mark="ve")

    p.wait("vector", "sc", p.cnt["sc"])
    for tc in range(8):
        dbu = ctx.t["arenaA"][:, (24 + tc) * 1024: (25 + tc) * 1024]
        u = ctx.t["arenaB"][:, (24 + tc) * 1024: (25 + tc) * 1024]
        bm = ctx.t["arenaA"][:, (16 + tc) * 1024: (17 + tc) * 1024]
        p.emit("vector", lambda E, S, d=dbu, a=u, b=bm: E.tensor_mul(d, a, b), mark="ve")
        p.selfwait("vector")
        p.emit("vector", lambda E, S, d=dbu, c=dcol[:, tc:tc + 1]:
               E.tensor_scalar_mul(d, d, c), mark="ve")

    consf = ctx.t["cons_f"]
    utri_f = consf[:, CF_UTRI:CF_UTRI + 128]
    ident_f = consf[:, CF_IDENT:CF_IDENT + 128]
    onesrow_f = consf[0:1, CF_UTRI:CF_UTRI + 128]
    utri_b = ctx.t["cons_bf"][:, CB_UTRI:CB_UTRI + 128]
    scr2 = ctx.t["scr2"]
    M_sb = scr2[:, 2048:2176]
    coef = scr2[0:1, 2176:2304]
    dcol_sb = scr[:, 1568:1569]
    drow_sb = scr[0:1, 1569:1697]
    rhs_barrier(ctx)
    for tc in range(8):
        ps_dc = ctx.t["psB"][:, 2, 0:1]
        ps_dr = ctx.t["psB"][0:1, 2, 128:256]
        ps_bc = ctx.t["psB"][:, 2, 256:384]
        p.wait("tensor", "ve", p.cnt["ve"])
        p.wait("tensor", "sc", p.cnt["sc"])
        p.emit("tensor", lambda E, S, o=ps_dc, r=dcol[:, tc:tc + 1]:
               E.matmul(o, utri_f, r, start=True, stop=True), mark="te")
        p.wait("scalar", "te", p.cnt["te"])
        p.selfwait("scalar")
        p.emit("scalar", lambda E, S, d=dcol_sb, s=ps_dc:
               E.activation(d, s, AF.Copy), mark="sc")
        p.wait("tensor", "sc", p.cnt["sc"])
        p.emit("tensor", lambda E, S, o=ps_dr, i=dcol_sb:
               E.transpose(o, i, ident_f), mark="te")
        p.wait("scalar", "te", p.cnt["te"])
        p.selfwait("scalar")
        p.emit("scalar", lambda E, S, d=drow_sb, s=ps_dr:
               E.activation(d, s, AF.Copy), mark="sc")
        p.wait("tensor", "sc", p.cnt["sc"])
        p.emit("tensor", lambda E, S, o=ps_bc, r=drow_sb:
               E.matmul(o, onesrow_f, r, start=True, stop=True), mark="te")
        p.wait("vector", "te", p.cnt["te"])
        p.selfwait("vector")
        marg = scr[:, 0:128]
        p.emit("vector", lambda E, S, d=marg, s=ps_bc, m=consf[:, CF_MBLE:CF_MBLE + 128]:
               E.tensor_sub(d, s, m), mark="ve")
        p.wait("scalar", "ve", p.cnt["ve"])
        p.wait("scalar", "te", p.cnt["te"])
        p.selfwait("scalar")
        p.emit("scalar", lambda E, S, d=M_sb, s=marg:
               E.activation(d, s, AF.Exp, scale=ctx.col("c_neg1"), bias=dcol_sb), mark="sc")
        p.selfwait("scalar")
        p.emit("scalar", lambda E, S, d=coef, s=ps_dr:
               E.activation(d, s, AF.Exp, scale=ctx.col1("c_neg1")), mark="sc")
        p.wait("vector", "sc", p.cnt["sc"])
        b1 = ctx.next_bank()
        b2 = ctx.next_bank()
        ctx.guard_bank(b1)
        ctx.guard_bank(b2)
        p.wait("tensor", "ve", p.cnt["ve"])
        p.wait("tensor", "sc", p.cnt["sc"])
        for fc in range(8):
            bk = b1 if fc < 4 else b2
            reg = ctx.t["psA"][:, bk, (fc % 4) * 128:(fc % 4) * 128 + 128]
            dbu = ctx.t["arenaA"][:, (24 + tc) * 1024 + fc * 128: (24 + tc) * 1024 + fc * 128 + 128]
            p.emit("tensor", lambda E, S, o=reg, l=dbu, m=M_sb:
                   E.matmul(o, l, m, start=True, stop=False))
            p.emit("tensor", lambda E, S, o=reg, l=ctx.t["carry"][0:1, fc * 128:(fc + 1) * 128],
                          r=coef: E.matmul(o, l, r, start=False, stop=True),
                   mark="te" if fc == 7 else None)
        ymark = p.cnt["te"]
        ps_c0 = ctx.t["psB"][0:1, 0, 0:512]
        ps_c1 = ctx.t["psB"][0:1, 1, 0:512]
        for half in range(2):
            psc = ps_c0 if half == 0 else ps_c1
            dbuh = ctx.t["arenaA"][:, (24 + tc) * 1024 + half * 512: (24 + tc) * 1024 + half * 512 + 512]
            p.emit("tensor", lambda E, S, o=psc, l=M_sb[:, 127:128], r=dbuh:
                   E.matmul(o, l, r, start=True, stop=False))
            p.emit("tensor", lambda E, S, o=psc, l=coef[0:1, 127:128],
                          r=ctx.t["carry"][0:1, half * 512:(half + 1) * 512]:
                   E.matmul(o, l, r, start=False, stop=True), mark="te")
        p.wait("scalar", "te", p.cnt["te"])
        p.selfwait("scalar")
        p.emit("scalar", lambda E, S, d=ctx.t["carry"][0:1, 0:512], s=ps_c0:
               E.activation(d, s, AF.Copy), mark="sc")
        p.emit("scalar", lambda E, S, d=ctx.t["carry"][0:1, 512:1024], s=ps_c1:
               E.activation(d, s, AF.Copy), mark="sc")
        for i, bk in enumerate((b1, b2)):
            ydst = ctx.t["arenaB"][:, 0:8 * 1024].rearrange("p (f c) -> p f c", c=1024)[
                :, i * 4:(i + 1) * 4, tc * 128:(tc + 1) * 128]
            ysrc = ctx.t["psA"][:, bk, 0:512].rearrange("p (f c) -> p f c", c=128)
            p.emit("scalar", lambda E, S, d=ydst, s=ysrc:
                   E.activation(d, s, AF.Copy), mark="sc")
        ctx.set_bank_guard(b1, "sc")
        ctx.set_bank_guard(b2, "sc")

    def rhs_y(wi, k, t):
        return aslot(ctx, "arenaB", k, t)

    def cb_out(wi, mc, t, ps, b, banks):
        p.wait("vector", "te", p.cnt["te"])
        p.selfwait("vector")
        R = ctx.t["R"][:, mc, t * 512:(t + 1) * 512]
        p.emit("vector", lambda E, S, d=R, s=ps, c=ctx.col("l0a_ob", mc):
               E.scalar_tensor_tensor(d, s, c, d, ALU.add, ALU.add), mark="ve")
        p.emit("vector", lambda E, S, d=x1_ap(ctx, mc, t), s=ps,
                      c=ctx.col("l0a_ob", mc), z=z_ap(mc, t):
               E.scalar_tensor_tensor(d, s, c, z, ALU.add, ALU.add), mark="ve")
        ctx.set_bank_guard(b, "ve")

    emit_linear_fm(ctx, [W["out"]], rhs_y, cb_out, Mc=8)

    def x1src(fc, t):
        return x1_ap(ctx, fc, t)

    def xn2_ap(fc, t):
        return aslot(ctx, "arenaB", fc, t)
    emit_ln(ctx, x1src, xn2_ap, affine=("l0a_n2w", "l0a_n2b"))

    def rhs_xn2(wi, k, t):
        return aslot(ctx, "arenaB", k, t)

    def h_ap(kc, t):
        if kc < 16:
            return aslot(ctx, "arenaA", kc, t)
        return aslot(ctx, "arenaB", 8 + (kc - 16), t)

    def cb_m1(wi, mc, t, ps, b, banks):
        p.wait("scalar", "te", p.cnt["te"])
        p.emit("scalar", lambda E, S, d=h_ap(mc, t), s=ps:
               E.activation(d, s, AF.Gelu, bias=ctx.col("l0a_mb1", mc)), mark="sc")
        ctx.set_bank_guard(b, "sc")

    emit_linear_fm(ctx, [W["mlp1"]], rhs_xn2, cb_m1, Mc=32)

    def rhs_h(wi, k, t):
        return h_ap(k, t)

    def cb_m2(wi, mc, t, ps, b, banks):
        p.wait("vector", "te", p.cnt["te"])
        p.selfwait("vector")
        R = ctx.t["R"][:, mc, t * 512:(t + 1) * 512]
        p.emit("vector", lambda E, S, d=R, s=ps, c=ctx.col("l0a_mb2", mc):
               E.scalar_tensor_tensor(d, s, c, d, ALU.add, ALU.add), mark="ve")
        ctx.set_bank_guard(b, "ve")

    emit_linear_fm(ctx, [W["mlp2"]], rhs_h, cb_m2, Mc=8)


def xn1_tok(ctx, kc, tc):
    return ctx.t["arenaB"][:, kc * 1024 + tc * 128: kc * 1024 + tc * 128 + 128]


def emit_l1attn(ctx, W):
    p = ctx.p
    scr = ctx.t["scr"]
    scr2 = ctx.t["scr2"]
    s2f = scr2[:].bitcast(F32)
    consb = ctx.t["cons_bf"]
    consf = ctx.t["cons_f"]
    utri_f = consf[:, CF_UTRI:CF_UTRI + 128]
    ident_f = consf[:, CF_IDENT:CF_IDENT + 128]
    onesrow_f = consf[0:1, CF_UTRI:CF_UTRI + 128]
    ident_b = consb[:, CB_IDENT:CB_IDENT + 128]
    ones_b = consb[:, CB_ONES:CB_ONES + 128]
    utri_b = consb[:, CB_UTRI:CB_UTRI + 128]
    maskL = consb[:, CB_MASKL:CB_MASKL + 128]
    maskU = consb[:, CB_MASKU:CB_MASKU + 128]

    def z_ap(fc, t):
        return aslot(ctx, "arenaA", fc, t)

    war_barrier(ctx, "vector")
    p.selfwait("vector")
    p.wait("vector", "gp", p.cnt["gp"])
    for t in range(NT):
        for fc in range(FC):
            p.emit("vector", lambda E, S, d=ctx.t["R"][:, fc, t * 512:(t + 1) * 512],
                          s=z_ap(fc, t): E.tensor_add(d, d, s), mark="ve")

    def xn1_ap(fc, t):
        return aslot(ctx, "arenaB", fc, t)
    emit_ln(ctx, z_ap, xn1_ap, affine=("l1a_ln1g", "l1a_ln1b"), src_bf16=True)

    def rhs_xn1(wi, k, t):
        return aslot(ctx, "arenaB", k, t)

    def mk_cb(base_slot, scale_col):
        def cb(wi, mc, t, ps, b, banks):
            p.wait("scalar", "te", p.cnt["te"])
            dst = aslot(ctx, "arenaA", base_slot + mc, t)
            if scale_col is None:
                p.emit("scalar", lambda E, S, d=dst, s=ps:
                       E.activation(d, s, AF.Copy), mark="sc")
            else:
                p.emit("scalar", lambda E, S, d=dst, s=ps:
                       E.activation(d, s, AF.Copy, scale=ctx.col(scale_col)), mark="sc")
            ctx.set_bank_guard(b, "sc")
        return cb

    emit_linear_fm(ctx, [W["q"]], rhs_xn1, mk_cb(8, "c_eighth"), Mc=8)
    emit_linear_fm(ctx, [W["k"]], rhs_xn1, mk_cb(16, None), Mc=8)
    emit_linear_fm(ctx, [W["v"]], rhs_xn1, mk_cb(24, None), Mc=8)

    lgT = lambda tc: scr[:, 448 + tc * 16: 448 + tc * 16 + 16]
    bT = lambda tc: scr[:, 576 + tc * 16: 576 + tc * 16 + 16]
    knT = lambda tc: scr[:, 704 + tc * 16: 704 + tc * 16 + 16]
    rowtmp = scr2[0:16, 0:512]
    r1 = scr[0:16, 832:1344]
    r2 = scr[0:16, 1344:1856]
    rowf = scr[0:16, 0:512]

    def cb_bg(wi, mc, t, ps, b, banks):
        p.wait("scalar", "te", p.cnt["te"])
        p.wait("scalar", "ve", p.cnt["ve"])
        p.selfwait("scalar")
        if wi == 0:
            p.emit("scalar", lambda E, S, d=r1, s=ps:
                   E.activation(d, s, AF.Exp, scale=ctx.t["cols"][0:16,
                       ctx.colreg["c_neg1"][0]:ctx.colreg["c_neg1"][0] + 1]), mark="sc")
            p.wait("vector", "sc", p.cnt["sc"])
            p.wait("vector", "te", p.cnt["te"])
            p.selfwait("vector")
            p.emit("vector", lambda E, S, d=r1: E.tensor_scalar_add(d, d, 1.0), mark="ve")
            p.selfwait("vector")
            p.emit("vector", lambda E, S, d=r2, s=r1: E.reciprocal(d, s), mark="ve")
            p.wait("tensor", "ve", p.cnt["ve"])
        else:
            p.emit("scalar", lambda E, S, d=r1, s=ps:
                   E.activation(d, s, AF.Abs), mark="sc")
            p.selfwait("scalar")
            p.emit("scalar", lambda E, S, d=r2, s=r1:
                   E.activation(d, s, AF.Exp, scale=ctx.t["cols"][0:16,
                       ctx.colreg["c_neg1"][0]:ctx.colreg["c_neg1"][0] + 1]), mark="sc")
            p.selfwait("scalar")
            p.emit("scalar", lambda E, S, d=r1, s=r2:
                   E.activation(d, s, AF.Ln, bias=ctx.t["cols"][0:16,
                       ctx.colreg["c_1"][0]:ctx.colreg["c_1"][0] + 1]), mark="sc")
            p.wait("vector", "sc", p.cnt["sc"])
            p.wait("vector", "te", p.cnt["te"])
            p.selfwait("vector")
            p.emit("vector", lambda E, S, d=r2, s=ps:
                   E.tensor_scalar(d, s, -1.0, 0.0, ALU.mult, ALU.max), mark="ve")
            p.selfwait("vector")
            p.emit("vector", lambda E, S, d=rowf, a=r1, bb=r2:
                   E.scalar_tensor_tensor(d, a, -1.0, bb, ALU.mult, ALU.subtract), mark="ve")
            p.wait("tensor", "ve", p.cnt["ve"])
        ctx.set_bank_guard(b, "ve")
        for i in range(4):
            tc = t * 4 + i
            p.wait("tensor", "sc", p.cnt["sc"])
            if wi == 1:
                pf = ctx.t["psB"][:, 0, i * 16:(i + 1) * 16]
                p.emit("tensor", lambda E, S, o=pf, s=rowf[:, i * 128:(i + 1) * 128]:
                       E.transpose(o, s, ident_f[0:16, 0:16]), mark="te")
                p.wait("scalar", "te", p.cnt["te"])
                p.selfwait("scalar")
                p.emit("scalar", lambda E, S, d=lgT(tc), s=pf:
                       E.activation(d, s, AF.Copy), mark="sc")
            else:
                pf = ctx.t["psB"][:, 0, 64 + i * 16:64 + (i + 1) * 16]
                p.emit("tensor", lambda E, S, o=pf, s=r2[:, i * 128:(i + 1) * 128]:
                       E.transpose(o, s, ident_f[0:16, 0:16]), mark="te")
                p.wait("scalar", "te", p.cnt["te"])
                p.selfwait("scalar")
                p.emit("scalar", lambda E, S, d=bT(tc), s=pf:
                       E.activation(d, s, AF.Copy), mark="sc")

    emit_linear_fm(ctx, [W["wb"], W["wg"]], rhs_xn1, cb_bg, Mc=1)

    ksq = [scr2[:, 0:512], scr2[:, 512:1024]]
    ps_kn = ctx.t["psB"][0:16, 0, 0:512]
    for t in range(NT):
        p.wait("scalar", "ve", p.cnt["ve"])
        p.wait("scalar", "te", p.cnt["te"])
        for fc in range(FC):
            p.selfwait("scalar")
            if fc >= 2:
                p.wait("scalar", "te", p.cnt["te"])
            p.emit("scalar", lambda E, S, d=ksq[fc % 2], s=aslot(ctx, "arenaA", 16 + fc, t):
                   E.activation(d, s, AF.Square), mark="sc")
            p.wait("tensor", "sc", p.cnt["sc"])
            p.emit("tensor", lambda E, S, o=ps_kn, l=consb[:, CB_KNSEL + fc * 16:CB_KNSEL + fc * 16 + 16],
                          r=ksq[fc % 2], st=(fc == 0), sp=(fc == FC - 1):
                   E.matmul(o, l, r, start=st, stop=sp), mark="te")
        p.wait("scalar", "te", p.cnt["te"])
        p.selfwait("scalar")
        p.emit("scalar", lambda E, S, d=rowtmp, s=ps_kn:
               E.activation(d, s, AF.Copy), mark="sc")
        for i in range(4):
            tc = t * 4 + i
            pc = ctx.t["psC"][:, i * 16:(i + 1) * 16]
            p.wait("tensor", "sc", p.cnt["sc"])
            p.emit("tensor", lambda E, S, o=pc, s=rowtmp[:, i * 128:(i + 1) * 128]:
                   E.transpose(o, s, ident_b[0:16, 0:16]), mark="te")
            p.wait("scalar", "te", p.cnt["te"])
            p.selfwait("scalar")
            p.emit("scalar", lambda E, S, d=scr[:, 192:208], s=pc:
                   E.activation(d, s, AF.Ln), mark="sc")
            p.selfwait("scalar")
            p.emit("scalar", lambda E, S, d=knT(tc), s=scr[:, 192:208]:
                   E.activation(d, s, AF.Exp, scale=ctx.col("c_half")), mark="sc")
            p.wait("vector", "sc", p.cnt["sc"])
            p.selfwait("vector")
            p.emit("vector", lambda E, S, d=knT(tc):
                   E.tensor_scalar_add(d, d, 1e-6), mark="ve")
            p.selfwait("vector")
            p.emit("vector", lambda E, S, d=knT(tc), s=knT(tc):
                   E.reciprocal(d, s), mark="ve")

    cumlg = scr[:, 832:848]
    gamT = scr[:, 848:864]
    negcl = scr[:, 864:880]
    cumlnb = scr[:, 880:896]
    lnb = scr[:, 896:912]
    negb = scr[:, 912:928]
    gCc = scr[:, 928:944]
    c_sb = scr[:, 944:945]
    gCrow = scr[0:1, 945:961]
    lcrow = scr[0:1, 961:977]
    sC = scr[:, 977:993]
    f32a = [scr[:, 0:64], scr[:, 1024:1088]]
    f32b = [scr[:, 64:192], scr[:, 1088:1216]]
    diagf = [scr[:, 192:320], scr[:, 1216:1344]]
    argf = [scr[:, 320:448], scr[:, 1344:1472]]

    def _pt(base):
        off = [0]
        def take(n):
            o = base + off[0]
            off[0] += n
            return scr2[:, o:o + n]
        return take
    tk = [None, None]
    tiles = [{}, {}]
    for hp in range(2):
        take = _pt(hp * 1216)
        tiles[hp] = dict(
            kh=take(128), Adec=take(128), qh=take(128), B=take(128), BT=take(128),
            Mkq=take(128), U=take(64), kt=take(128), Rm=take(64), S0b=take(64),
            diagb=take(128))
    Opair = scr2[:, 2432:2560]
    vT_t = scr2[:, 2560:2688]

    psA_ = ctx.t["psA"]
    psB_ = ctx.t["psB"]
    P0 = dict(bc=psB_[:, 1, 0:128], gram=psB_[:, 1, 128:256],
              bu=psB_[:, 2, 368:432], b2=psB_[:, 0, 0:128], bt2=psB_[:, 0, 128:256],
              w0=psB_[:, 2, 176:240], O=psB_[:, 2, 240:304], P=psB_[:, 2, 304:368])
    P1 = dict(bc=psA_[:, 2, 0:128], gram=psA_[:, 2, 128:256],
              bu=psA_[:, 1, 192:256], b2=psA_[:, 0, 0:128], bt2=psA_[:, 0, 128:256],
              w0=psA_[:, 1, 0:64], O=psA_[:, 1, 64:128], P=psA_[:, 1, 128:192])
    PS = [P0, P1]
    ps_cg = psB_[:, 2, 0:16]
    ps_cgT = psB_[0:16, 2, 16:144]
    ps_g1 = psB_[0:1, 2, 144:160]
    ps_gc = psB_[:, 2, 160:176]
    pc_kt = [ctx.t["psC"][:, 256:384], ctx.t["psC"][:, 512:640]]
    pc_ot = ctx.t["psC"][:, 384:512]
    onesf = consf[:, CF_ONES:CF_ONES + 128]
    mb_gt = consf[:, CF_MBGT:CF_MBGT + 128]
    mb_lt = consf[:, CF_MBLT:CF_MBLT + 128]
    mb_le = consf[:, CF_MBLE:CF_MBLE + 128]

    Sst = ctx.t["Sstate"]

    for t in range(NT):
        for ci in range(4):
            tc = t * 4 + ci
            rhs_barrier(ctx)
            for b in range(4):
                ctx.guard_bank(b)
            p.emit("tensor", lambda E, S, o=ps_cg, r=lgT(tc):
                   E.matmul(o, utri_f, r, start=True, stop=True), mark="te")
            p.wait("scalar", "te", p.cnt["te"])
            p.selfwait("scalar")
            p.emit("scalar", lambda E, S, d=cumlg, s=ps_cg:
                   E.activation(d, s, AF.Copy), mark="sc")
            p.selfwait("scalar")
            p.emit("scalar", lambda E, S, d=gamT, s=cumlg:
                   E.activation(d, s, AF.Exp), mark="sc")
            p.selfwait("scalar")
            p.emit("scalar", lambda E, S, d=lnb, s=bT(tc):
                   E.activation(d, s, AF.Ln), mark="sc")
            p.wait("vector", "sc", p.cnt["sc"])
            p.selfwait("vector")
            p.emit("vector", lambda E, S, d=negcl, s=cumlg:
                   E.tensor_scalar_mul(d, s, -1.0), mark="ve")
            p.emit("vector", lambda E, S, d=cumlnb, a=cumlg, b=lnb:
                   E.tensor_add(d, a, b), mark="ve")
            p.emit("vector", lambda E, S, d=negb, s=bT(tc):
                   E.tensor_scalar_mul(d, s, -1.0), mark="ve")
            p.wait("tensor", "sc", p.cnt["sc"])
            p.wait("tensor", "ve", p.cnt["ve"])
            p.emit("tensor", lambda E, S, o=ps_cgT, s=cumlg:
                   E.transpose(o, s, ident_f), mark="te")
            p.wait("scalar", "te", p.cnt["te"])
            p.selfwait("scalar")
            p.emit("scalar", lambda E, S, d=c_sb[0:16, :], s=ps_cgT[0:16, 127:128]:
                   E.activation(d, s, AF.Copy), mark="sc")
            p.wait("tensor", "sc", p.cnt["sc"])
            p.emit("tensor", lambda E, S, o=ps_g1, s=c_sb[0:16, :]:
                   E.transpose(o, s, ident_f[0:16, 0:16]), mark="te")
            p.wait("scalar", "te", p.cnt["te"])
            p.selfwait("scalar")
            p.emit("scalar", lambda E, S, d=gCrow, s=ps_g1:
                   E.activation(d, s, AF.Exp), mark="sc")
            p.emit("scalar", lambda E, S, d=lcrow, s=ps_g1:
                   E.activation(d, s, AF.Copy), mark="sc")
            p.wait("tensor", "sc", p.cnt["sc"])
            p.emit("tensor", lambda E, S, o=ps_gc, r=gCrow:
                   E.matmul(o, onesrow_f, r, start=True, stop=True), mark="te")
            p.wait("scalar", "te", p.cnt["te"])
            p.selfwait("scalar")
            p.emit("scalar", lambda E, S, d=gCc, s=ps_gc:
                   E.activation(d, s, AF.Copy), mark="sc")
            p.wait("tensor", "sc", p.cnt["sc"])
            p.emit("tensor", lambda E, S, o=ps_gc, r=lcrow:
                   E.matmul(o, onesrow_f, r, start=True, stop=True), mark="te")
            p.wait("vector", "te", p.cnt["te"])
            p.selfwait("vector")
            p.emit("vector", lambda E, S, d=sC, s=ps_gc, c=cumlg:
                   E.tensor_sub(d, s, c), mark="ve")
            p.wait("scalar", "ve", p.cnt["ve"])
            p.selfwait("scalar")
            p.emit("scalar", lambda E, S, d=sC, s=sC:
                   E.activation(d, s, AF.Exp), mark="sc")

            for fc in range(FC):
                vsl = ctx.t["arenaA"][:, (24 + fc) * 1024 + tc * 128: (24 + fc) * 1024 + tc * 128 + 128]
                qsl = ctx.t["arenaA"][:, (8 + fc) * 1024 + tc * 128: (8 + fc) * 1024 + tc * 128 + 128]
                ksl = ctx.t["arenaA"][:, (16 + fc) * 1024 + tc * 128: (16 + fc) * 1024 + tc * 128 + 128]
                p.wait("tensor", "sc", p.cnt["sc"])
                p.wait("tensor", "ve", p.cnt["ve"])
                p.emit("tensor", lambda E, S, o=pc_kt[0], s=vsl:
                       E.transpose(o, s, ident_b), mark="te")
                p.wait("scalar", "te", p.cnt["te"])
                p.selfwait("scalar")
                p.emit("scalar", lambda E, S, d=vT_t, s=pc_kt[0]:
                       E.activation(d, s, AF.Copy), mark="sc")
                vT_done = p.cnt["sc"]

                def mk(hp):
                    h = 2 * fc + hp
                    b0 = 64 * hp
                    te0 = p.cnt["te"]
                    ve0 = p.cnt["ve"]
                    return dict(h=h, b0=b0, T=tiles[hp], R=PS[hp],
                                ksl_h=ksl[b0:b0 + 64, :], qsl_h=qsl[b0:b0 + 64, :],
                                S_h=Sst[b0:b0 + 64, fc, :],
                                fa=f32a[hp], fb=f32b[hp], df=diagf[hp], af=argf[hp],
                                pk=pc_kt[hp],
                                m_bc=te0, m_bcread=ve0, m_diag=ve0, m_arg=ve0, m_exp=0)
                H = [mk(0), mk(1)]

                def s_copy(c, dtile_key, coltile, srct):
                    T, R = c["T"], c["R"]
                    p.wait("vector", "te", c["m_bc"])
                    p.selfwait("vector")
                    p.emit("vector", lambda E, S, d=T["diagb"], cc=coltile[:, c["h"]:c["h"] + 1]:
                           E.tensor_scalar_mul(d, ident_b, cc), mark="ve")
                    c["m_diag"] = p.cnt["ve"]
                    p.wait("tensor", "ve", max(c["m_diag"], c["m_bcread"]))
                    p.emit("tensor", lambda E, S, o=R["bc"], dd=T["diagb"]:
                           E.matmul(o, ones_b, dd, start=True, stop=True), mark="te")
                    c["m_bc"] = p.cnt["te"]
                    p.wait("vector", "te", c["m_bc"])
                    p.selfwait("vector")
                    p.emit("vector", lambda E, S, d=T[dtile_key][c["b0"]:c["b0"] + 64, :], s=srct,
                                  bc=R["bc"][c["b0"]:c["b0"] + 64, :]:
                           E.tensor_mul(d, s, bc), mark="ve")
                    c["m_bcread"] = p.cnt["ve"]
                    c[f"m_{dtile_key}"] = p.cnt["ve"]

                def s_decay(c, dstkey, content_col, bias_col, mask):
                    T = c["T"]
                    p.wait("vector", "te", c["m_bc"])
                    p.selfwait("vector")
                    p.emit("vector", lambda E, S, d=c["df"], cc=content_col[:, c["h"]:c["h"] + 1]:
                           E.tensor_scalar_mul(d, ident_f, cc), mark="ve")
                    c["m_diag"] = p.cnt["ve"]
                    p.wait("tensor", "ve", max(c["m_diag"], c["m_bcread"]))
                    p.emit("tensor", lambda E, S, o=c["R"]["bc"], dd=c["df"]:
                           E.matmul(o, onesf, dd, start=True, stop=True), mark="te")
                    c["m_bc"] = p.cnt["te"]
                    p.wait("vector", "te", c["m_bc"])
                    p.wait("vector", "sc", c["m_exp"])
                    p.selfwait("vector")
                    p.emit("vector", lambda E, S, d=c["af"], s=c["R"]["bc"], m=mask:
                           E.tensor_add(d, s, m), mark="ve")
                    c["m_bcread"] = p.cnt["ve"]
                    c["m_arg"] = p.cnt["ve"]
                    p.wait("scalar", "ve", c["m_arg"])
                    p.selfwait("scalar")
                    p.emit("scalar", lambda E, S, d=T[dstkey], s=c["af"],
                                  b=bias_col[:, c["h"]:c["h"] + 1]:
                           E.activation(d, s, AF.Exp, bias=b), mark="sc")
                    c["m_exp"] = p.cnt["sc"]
                    c[f"m_{dstkey}"] = p.cnt["sc"]

                for c in H:
                    s_copy(c, "kh", knT(tc), c["ksl_h"])
                for c in H:
                    s_copy(c, "qh", gamT, c["qsl_h"])
                for c in H:
                    s_decay(c, "Adec", negcl, cumlg, mb_gt)
                for c in H:
                    s_decay(c, "BT", cumlnb, negcl, mb_lt)
                for c in H:
                    s_decay(c, "Mkq", cumlg, negcl, mb_le)
                for c in H:
                    T, R = c["T"], c["R"]
                    b0 = c["b0"]
                    p.wait("tensor", "ve", max(c["m_kh"], c["m_bcread"]))
                    p.emit("tensor", lambda E, S, o=R["gram"], l=T["kh"][b0:b0 + 64, :], r=T["kh"][b0:b0 + 64, :]:
                           E.matmul(o, l, r, start=True, stop=True), mark="te")
                    c["gram_mark"] = p.cnt["te"]
                for c in H:
                    T, R = c["T"], c["R"]
                    p.wait("vector", "te", c["gram_mark"])
                    p.wait("vector", "sc", max(c["m_Adec"], c["m_BT"]))
                    p.selfwait("vector")
                    p.emit("vector", lambda E, S, d=T["B"], s=R["gram"], cc=negb[:, c["h"]:c["h"] + 1], a=T["Adec"]:
                           E.scalar_tensor_tensor(d, s, cc, a, ALU.mult, ALU.mult), mark="ve")
                    p.emit("vector", lambda E, S, d=T["BT"], s=R["gram"], a=T["BT"]:
                           E.scalar_tensor_tensor(d, s, -1.0, a, ALU.mult, ALU.mult), mark="ve")
                    c["bbt_mark"] = p.cnt["ve"]
                for c in H:
                    T, R = c["T"], c["R"]
                    b0 = c["b0"]
                    p.wait("tensor", "ve", c["bbt_mark"])
                    p.emit("tensor", lambda E, S, o=R["gram"], l=T["kh"][b0:b0 + 64, :], r=c["qsl_h"]:
                           E.matmul(o, l, r, start=True, stop=True), mark="te")
                    c["gram2_mark"] = p.cnt["te"]
                for c in H:
                    T, R = c["T"], c["R"]
                    p.wait("vector", "te", c["gram2_mark"])
                    p.wait("vector", "sc", c["m_Mkq"])
                    p.selfwait("vector")
                    p.emit("vector", lambda E, S, d=T["Mkq"], s=R["gram"], a=T["Mkq"]:
                           E.tensor_mul(d, s, a), mark="ve")
                    p.emit("vector", lambda E, S, d=T["S0b"][c["b0"]:c["b0"] + 64, :], s=c["S_h"]:
                           E.tensor_copy(d, s), mark="ve")
                    c["s0_mark"] = p.cnt["ve"]
                for c in H:
                    T, R = c["T"], c["R"]
                    b0 = c["b0"]
                    p.wait("tensor", "ve", c["s0_mark"])
                    p.emit("tensor", lambda E, S, o=R["w0"], l=T["kh"][b0:b0 + 64, :], r=T["S0b"][b0:b0 + 64, :]:
                           E.matmul(o, l, r, start=True, stop=True), mark="te")
                    c["w0_mark"] = p.cnt["te"]
                for c in H:
                    T = c["T"]
                    p.wait("vector", "te", c["w0_mark"])
                    p.wait("vector", "sc", vT_done)
                    p.selfwait("vector")
                    p.emit("vector", lambda E, S, d=c["fa"], s=c["R"]["w0"], cc=gamT[:, c["h"]:c["h"] + 1]:
                           E.tensor_scalar_mul(d, s, cc), mark="ve")
                    p.selfwait("vector")
                    p.emit("vector", lambda E, S, d=T["Rm"], s=vT_t[:, c["b0"]:c["b0"] + 64],
                                  cc=bT(tc)[:, c["h"]:c["h"] + 1], b=c["fa"]:
                           E.scalar_tensor_tensor(d, s, cc, b, ALU.mult, ALU.subtract), mark="ve")
                    p.selfwait("vector")
                    p.emit("vector", lambda E, S, d=T["U"], s=T["Rm"]: E.tensor_copy(d, s), mark="ve")
                    c["u_mark"] = p.cnt["ve"]

                for it in range(7):
                    for c in H:
                        T, R = c["T"], c["R"]
                        p.wait("tensor", "ve", c["u_mark"])
                        if it > 0:
                            p.wait("tensor", "sc", c["evac_mark"])
                        if it < 6:
                            p.emit("tensor", lambda E, S, o=R["bu"], l=T["BT"], r=T["U"]:
                                   E.matmul(o, l, r, start=True, stop=True))
                            p.emit("tensor", lambda E, S, o=R["b2"], l=T["BT"], r=T["B"]:
                                   E.matmul(o, l, r, start=True, stop=True))
                            p.emit("tensor", lambda E, S, o=R["bt2"], l=T["B"], r=T["BT"]:
                                   E.matmul(o, l, r, start=True, stop=True), mark="te")
                        else:
                            p.emit("tensor", lambda E, S, o=R["bu"], l=T["BT"], r=T["U"]:
                                   E.matmul(o, l, r, start=True, stop=True), mark="te")
                        c["mm_mark"] = p.cnt["te"]
                    for c in H:
                        T, R = c["T"], c["R"]
                        p.wait("vector", "te", c["mm_mark"])
                        p.selfwait("vector")
                        p.emit("vector", lambda E, S, d=T["U"], s=R["bu"]:
                               E.tensor_add(d, d, s), mark="ve")
                        c["u_mark"] = p.cnt["ve"]
                        if it < 6:
                            p.wait("scalar", "te", c["mm_mark"])
                            p.selfwait("scalar")
                            p.emit("scalar", lambda E, S, d=T["B"], s=R["b2"]:
                                   E.activation(d, s, AF.Copy), mark="sc")
                            p.emit("scalar", lambda E, S, d=T["BT"], s=R["bt2"]:
                                   E.activation(d, s, AF.Copy), mark="sc")
                            c["evac_mark"] = p.cnt["sc"]
                for c in H:
                    T, R = c["T"], c["R"]
                    b0 = c["b0"]
                    p.wait("tensor", "ve", c["u_mark"])
                    p.emit("tensor", lambda E, S, o=R["O"], l=T["Mkq"], r=T["U"]:
                           E.matmul(o, l, r, start=True, stop=False))
                    p.emit("tensor", lambda E, S, o=R["O"], l=T["qh"][b0:b0 + 64, :], r=T["S0b"][b0:b0 + 64, :]:
                           E.matmul(o, l, r, start=False, stop=True), mark="te")
                    c["o_mark"] = p.cnt["te"]
                for c in H:
                    p.wait("scalar", "te", c["o_mark"])
                    p.selfwait("scalar")
                    p.emit("scalar", lambda E, S, d=Opair[:, c["b0"]:c["b0"] + 64], s=c["R"]["O"]:
                           E.activation(d, s, AF.Copy), mark="sc")
                for c in H:
                    T = c["T"]
                    p.wait("tensor", "ve", p.cnt["ve"])
                    p.wait("tensor", "sc", p.cnt["sc"])
                    p.emit("tensor", lambda E, S, o=c["pk"], s=T["kh"]:
                           E.transpose(o, s, ident_b), mark="te")
                    c["kt_mark"] = p.cnt["te"]
                    p.wait("vector", "te", c["kt_mark"])
                    p.selfwait("vector")
                    p.emit("vector", lambda E, S, d=T["kt"], s=c["pk"], cc=sC[:, c["h"]:c["h"] + 1]:
                           E.tensor_scalar_mul(d, s, cc), mark="ve")
                    c["ktd_mark"] = p.cnt["ve"]
                for c in H:
                    T, R = c["T"], c["R"]
                    p.wait("tensor", "ve", c["ktd_mark"])
                    p.emit("tensor", lambda E, S, o=R["P"], l=T["kt"], r=T["U"]:
                           E.matmul(o, l, r, start=True, stop=True), mark="te")
                    c["p_mark"] = p.cnt["te"]
                for c in H:
                    b0 = c["b0"]
                    p.wait("vector", "te", c["p_mark"])
                    p.selfwait("vector")
                    p.emit("vector", lambda E, S, d=c["fb"][b0:b0 + 64, 0:64], a=c["S_h"],
                                  cc=gCc[b0:b0 + 64, c["h"]:c["h"] + 1]:
                           E.tensor_scalar_mul(d, a, cc), mark="ve")
                    p.selfwait("vector")
                    p.emit("vector", lambda E, S, d=c["S_h"], a=c["fb"][b0:b0 + 64, 0:64],
                                  b=c["R"]["P"][b0:b0 + 64, :]:
                           E.tensor_add(d, a, b), mark="ve")
                p.wait("tensor", "sc", p.cnt["sc"])
                p.wait("tensor", "ve", p.cnt["ve"])
                p.emit("tensor", lambda E, S, o=pc_ot, s=Opair:
                       E.transpose(o, s, ident_b), mark="te")
                p.wait("scalar", "te", p.cnt["te"])
                p.selfwait("scalar")
                odst = ctx.t["arenaB"][:, (8 + fc) * 1024 + tc * 128: (8 + fc) * 1024 + tc * 128 + 128]
                p.emit("scalar", lambda E, S, d=odst, s=pc_ot:
                       E.activation(d, s, AF.Copy), mark="sc")

    if GDN_DEBUG:
        return
    def rhs_o(wi, k, t):
        return aslot(ctx, "arenaB", 8 + k, t)

    def cb_wo(wi, mc, t, ps, b, banks):
        p.wait("vector", "te", p.cnt["te"])
        p.selfwait("vector")
        R = ctx.t["R"][:, mc, t * 512:(t + 1) * 512]
        p.emit("vector", lambda E, S, d=R, s=ps, c=ctx.col("l1a_bo", mc):
               E.scalar_tensor_tensor(d, s, c, d, ALU.add, ALU.add), mark="ve")
        p.emit("vector", lambda E, S, d=x1_ap(ctx, mc, t), s=ps,
                      c=ctx.col("l1a_bo", mc), z=z_ap(mc, t):
               E.scalar_tensor_tensor(d, s, c, z, ALU.add, ALU.add), mark="ve")
        ctx.set_bank_guard(b, "ve")

    emit_linear_fm(ctx, [W["wo"]], rhs_o, cb_wo, Mc=8)

    def x1src(fc, t):
        return x1_ap(ctx, fc, t)

    def xn2_ap(fc, t):
        return aslot(ctx, "arenaB", fc, t)
    emit_ln(ctx, x1src, xn2_ap, affine=("l1a_ln2g", "l1a_ln2b"))

    def rhs_xn2(wi, k, t):
        return aslot(ctx, "arenaB", k, t)

    def h_ap(kc, t):
        if kc < 16:
            return aslot(ctx, "arenaA", kc, t)
        return aslot(ctx, "arenaB", 8 + (kc - 16), t)

    def cb_m1(wi, mc, t, ps, b, banks):
        p.wait("scalar", "te", p.cnt["te"])
        p.emit("scalar", lambda E, S, d=h_ap(mc, t), s=ps:
               E.activation(d, s, AF.Gelu, bias=ctx.col("l1a_b1", mc)), mark="sc")
        ctx.set_bank_guard(b, "sc")

    emit_linear_fm(ctx, [W["m1"]], rhs_xn2, cb_m1, Mc=32)

    def rhs_h(wi, k, t):
        return h_ap(k, t)

    def cb_m2(wi, mc, t, ps, b, banks):
        p.wait("vector", "te", p.cnt["te"])
        p.selfwait("vector")
        R = ctx.t["R"][:, mc, t * 512:(t + 1) * 512]
        p.emit("vector", lambda E, S, d=R, s=ps, c=ctx.col("l1a_b2", mc):
               E.scalar_tensor_tensor(d, s, c, d, ALU.add, ALU.add), mark="ve")
        ctx.set_bank_guard(b, "ve")

    emit_linear_fm(ctx, [W["m2"]], rhs_h, cb_m2, Mc=8)


# version 08
# speedup vs baseline: 1.0560x; 1.0154x over previous
import contextlib
import os
import numpy as np
import ml_dtypes
import concourse.bass as bass
import concourse.mybir as mybir
from concourse.bass_utils import run_bass_kernel_spmd

F32 = mybir.dt.float32
BF16 = mybir.dt.bfloat16
AF = mybir.ActivationFunctionType
ALU = mybir.AluOpType

NCORES = 8
D = 1024
SEQ = 4096
TSTRIP = 1024
NSTRIP = SEQ // TSTRIP
NT = TSTRIP // 512
FC = D // 128

ENGINES = ("sync", "tensor", "vector", "scalar", "gpsimd")
GDN_DEBUG = False



def tile_w(w):
    K, M = w.shape
    Kc = K // 128
    mw = min(M, 128)
    Mc = M // mw
    return np.ascontiguousarray(
        w.reshape(Kc, 128, Mc, mw).transpose(2, 1, 0, 3))


def fm_of(x):
    T, d = x.shape
    return np.ascontiguousarray(x.T.reshape(d // 128, 128, T).transpose(1, 0, 2))


def cols_of(v):
    return np.ascontiguousarray(v.reshape(-1, 128).T)



class Prog:

    def __init__(self):
        self.q = {e: [] for e in ENGINES}
        self.cnt = dict(dma0=0, dma1=0, dma2=0, dma3=0, stdma=0, te=0, ve=0, sc=0, gp=0)
        self.dma_i = 0

    def wait(self, eng, sem, val):
        if val <= 0:
            return
        self.q[eng].append(lambda E, S, sem=sem, val=val: E.wait_ge(S[sem], val))

    def selfwait(self, eng):
        sem = ENG_SEM[eng]
        self.wait(eng, sem, self.cnt[sem])

    def dma(self, eng, build):
        i = self.dma_i % 4
        self.dma_i += 1
        sem = "dma%d" % i
        self.wait(eng, sem, self.cnt[sem])
        self.emit(eng, build, mark=sem, n=16)
        return (sem, self.cnt[sem])

    def emit(self, eng, build, mark=None, n=1):
        if mark is not None:
            self.cnt[mark] += n

            def f(E, S, build=build, mark=mark, n=n):
                ins = build(E, S)
                ins.then_inc(S[mark], n)
            self.q[eng].append(f)
        else:
            self.q[eng].append(lambda E, S, build=build: build(E, S))

    def replay(self, nc, sems):
        with nc.Block() as block:
            for eng in ENGINES:
                ops = self.q[eng]
                if not ops:
                    continue

                def body(E, ops=ops):
                    for f in ops:
                        f(E, sems)
                getattr(block, eng)(body)


class Ctx:

    def __init__(self, nc, prog):
        self.nc = nc
        self.p = prog
        self.t = {}
        self.colreg = {}
        self.bank_guard = [None] * 4
        self.bank_i = 0
        self.wstage_guard = [None, None]
        self.wbf_guard = {}

    def col(self, name, j=0):
        s, n = self.colreg[name]
        assert 0 <= j < n, (name, j, n)
        return self.t["cols"][:, s + j:s + j + 1]

    def col1(self, name, j=0):
        s, n = self.colreg[name]
        return self.t["cols"][0:1, s + j:s + j + 1]

    def next_bank(self):
        b = self.bank_i % 4
        self.bank_i += 1
        return b

    def guard_bank(self, b):
        g = self.bank_guard[b]
        if g is not None:
            self.p.wait("tensor", g[0], g[1])

    def set_bank_guard(self, b, sem):
        self.bank_guard[b] = (sem, self.p.cnt[sem])


SEMNAMES = ("dma0", "dma1", "dma2", "dma3", "stdma", "te", "ve", "sc", "gp")
ENG_SEM = dict(tensor="te", vector="ve", scalar="sc", gpsimd="gp")



class WSrc:

    def __init__(self, dram_ap, Kc, mw=128):
        self.ap = dram_ap
        self.Kc = Kc
        self.mw = mw

    def emit_load(self, ctx, mc, dest_ap_fn):
        p = ctx.p
        kgran = max(1, 512 // self.mw)
        ng = (self.Kc + kgran - 1) // kgran
        for g in range(ng):
            k0 = g * kgran
            kn = min(kgran, self.Kc - k0)
            half = ctx.stage_i % 2
            ctx.stage_i += 1
            stage = ctx.t["wstage"][:, half * 512:half * 512 + kn * self.mw]
            stage_v = stage.rearrange("p (k f) -> p k f", f=self.mw)
            gg = ctx.wstage_guard[half]
            if gg is not None:
                p.wait("sync", gg[0], gg[1])
            src = self.ap[mc, :, k0:k0 + kn, :]
            dm = p.dma("sync", lambda E, S, d=stage_v, s=src: E.dma_start(out=d, in_=s))
            p.wait("vector", dm[0], dm[1])
            dst = dest_ap_fn(k0, kn)
            p.emit("vector", lambda E, S, d=dst, s=stage_v: E.tensor_copy(d, s), mark="ve")
            ctx.wstage_guard[half] = ("ve", p.cnt["ve"])
        return p.cnt["ve"]


class WSinCos:

    def __init__(self, theta_ap, noise_ap, is_cos, Kc=8, mw=128):
        self.a1 = theta_ap
        self.a2 = noise_ap
        self.is_cos = is_cos
        self.Kc = Kc
        self.mw = mw

    def emit_load(self, ctx, mc, dest_ap_fn):
        p = ctx.p
        kgran = 2
        ng = (self.Kc + kgran - 1) // kgran
        for g in range(ng):
            k0 = g * kgran
            kn = min(kgran, self.Kc - k0)
            half = ctx.stage_i % 2
            ctx.stage_i += 1
            base = half * 512
            st1 = ctx.t["wstage"][:, base:base + kn * 128].rearrange("p (k f) -> p k f", f=128)
            st2 = ctx.t["wstage"][:, base + 256:base + 256 + kn * 128].rearrange("p (k f) -> p k f", f=128)
            gg = ctx.wstage_guard[half]
            if gg is not None:
                p.wait("sync", gg[0], gg[1])
            s1 = self.a1[mc, :, k0:k0 + kn, :]
            s2 = self.a2[mc, :, k0:k0 + kn, :]
            dm1 = p.dma("sync", lambda E, S, d=st1, s=s1: E.dma_start(out=d, in_=s))
            dm2 = p.dma("sync", lambda E, S, d=st2, s=s2: E.dma_start(out=d, in_=s))
            p.wait("vector", dm1[0], dm1[1])
            p.wait("vector", dm2[0], dm2[1])
            p.emit("vector", lambda E, S, a=st1, b=st2: E.tensor_add(a, a, b), mark="ve")
            dst = dest_ap_fn(k0, kn)
            p.wait("scalar", "ve", p.cnt["ve"])
            if self.is_cos:
                p.emit("scalar", lambda E, S, d=dst, s=st1:
                       E.activation(d, s, AF.Sin, bias=ctxcol(ctx, "c_pi2")), mark="sc")
            else:
                p.emit("scalar", lambda E, S, d=dst, s=st1:
                       E.activation(d, s, AF.Sin), mark="sc")
            ctx.wstage_guard[half] = ("sc", p.cnt["sc"])
        return None

    uses_sc = True


def ctxcol(ctx, name, j=0):
    return ctx.col(name, j)



def rhs_barrier(ctx):
    p = ctx.p
    p.wait("tensor", "ve", p.cnt["ve"])
    p.wait("tensor", "sc", p.cnt["sc"])
    p.wait("tensor", "gp", p.cnt["gp"])


def war_barrier(ctx, eng):
    ctx.p.wait(eng, "te", ctx.p.cnt["te"])


def emit_linear_fm(ctx, weights, rhs_fn, out_cb, Mc, T=TSTRIP):
    p = ctx.p
    nw = len(weights)
    Kc = weights[0].Kc
    mw = weights[0].mw
    nt = T // 512
    reg = Kc * mw
    assert nw * 2 * reg <= 8192
    ctx.stage_i = 0
    rhs_barrier(ctx)
    ctx.pass_id = getattr(ctx, "pass_id", 0) + 1
    pass_te0 = p.cnt["te"]

    def wreg(wi, parity):
        off = (wi * 2 + parity) * reg
        return ctx.t["wbf"][:, off:off + reg].rearrange("p (k f) -> p k f", f=mw)

    for mc in range(Mc):
        parity = mc % 2
        for wi, w in enumerate(weights):
            g = ctx.wbf_guard.get((wi, parity))
            if g is None or g[0] != ctx.pass_id:
                g = (ctx.pass_id, "te", pass_te0)
            p.wait("vector", g[1], g[2])
            if getattr(w, "uses_sc", False):
                p.wait("scalar", g[1], g[2])

            def dest_fn(k0, kn, wi=wi, parity=parity):
                return wreg(wi, parity)[:, k0:k0 + kn, :]
            w.emit_load(ctx, mc, dest_fn)
            if getattr(w, "uses_sc", False):
                p.wait("tensor", "sc", p.cnt["sc"])
            else:
                p.wait("tensor", "ve", p.cnt["ve"])
        for t in range(nt):
            banks = []
            for wi, w in enumerate(weights):
                b = ctx.next_bank()
                banks.append(b)
                ctx.guard_bank(b)
                ps = ctx.t["psA"][:, b, 0:512]
                wv = wreg(wi, parity)
                for k in range(Kc):
                    p.emit("tensor",
                           lambda E, S, o=ps[0:mw, :], l=wv[:, k, :], r=rhs_fn(wi, k, t),
                                  st=(k == 0), sp=(k == Kc - 1):
                           E.matmul(o, l, r, start=st, stop=sp),
                           mark="te" if k == Kc - 1 else None)
                out_cb(wi, mc, t, ps[0:mw, :], b, banks)
        ctx.wbf_guard[(0, parity)] = (ctx.pass_id, "te", p.cnt["te"])
        if nw > 1:
            ctx.wbf_guard[(1, parity)] = (ctx.pass_id, "te", p.cnt["te"])



def emit_ln(ctx, src_fn, dst_fn, affine=None, src_bf16=False):
    p = ctx.p
    scr = ctx.t["scr"]
    scr2 = ctx.t["scr2"]
    A_t = scr[:, 0:512]
    B_t = scr[:, 512:1024]
    Cs = scr[:, 1024:1536]
    Ds = scr[:, 1536:2048]
    Csl = [Cs, Ds]
    for t in range(NT):
        war_barrier(ctx, "vector")
        p.selfwait("vector")
        p.selfwait("scalar")
        p.wait("vector", "gp", p.cnt["gp"])
        p.wait("scalar", "gp", p.cnt["gp"])
        cast_marks = []
        if src_bf16:
            statsrc = src_fn
            cast_marks = [p.cnt["ve"]] * FC
        else:
            statsrc = dst_fn
            for fc in range(FC):
                p.emit("vector", lambda E, S, d=dst_fn(fc, t), s=src_fn(fc, t):
                       E.tensor_copy(d, s), mark="ve")
                cast_marks.append(p.cnt["ve"])
        sq_marks = []
        qmm_marks = []
        ps_s = ctx.t["psB"][0:1, 0, 0:512]
        ps_q = ctx.t["psB"][0:1, 1, 0:512]
        for fc in range(FC):
            p.wait("tensor", "ve", cast_marks[fc])
            p.emit("tensor", lambda E, S, o=ps_s, l=ctx.t["cons_bf"][:, 0:1], r=statsrc(fc, t),
                          st=(fc == 0), sp=(fc == FC - 1):
                   E.matmul(o, l, r, start=st, stop=sp),
                   mark="te" if fc == FC - 1 else None)
        smark = p.cnt["te"]
        for fc in range(FC):
            sqslot = ctx.t["scr2"][:, (fc % 2) * 512:(fc % 2) * 512 + 512]
            p.wait("scalar", "ve", cast_marks[fc])
            if fc >= 2:
                p.wait("scalar", "te", qmm_marks[fc - 2])
            p.emit("scalar", lambda E, S, d=sqslot, s=statsrc(fc, t):
                   E.activation(d, s, AF.Square), mark="sc")
            p.wait("tensor", "sc", p.cnt["sc"])
            p.emit("tensor", lambda E, S, o=ps_q, l=ctx.t["cons_bf"][:, 0:1], r=sqslot,
                          st=(fc == 0), sp=(fc == FC - 1):
                   E.matmul(o, l, r, start=st, stop=sp), mark="te")
            qmm_marks.append(p.cnt["te"])
        qmark = p.cnt["te"]
        ve2_mark = [0]
        sq_s = Ds[0:1, 0:512]
        var = B_t[0:1, 0:512]
        a_row = Cs[0:1, 0:512]
        p.wait("scalar", "te", smark)
        p.emit("scalar", lambda E, S, d=sq_s, s=ps_s:
               E.activation(d, s, AF.Square, scale=ctx.col1("c_r1024")), mark="sc")
        p.wait("vector", "te", qmark)
        p.wait("vector", "sc", p.cnt["sc"])
        p.emit("vector", lambda E, S, d=var, q=ps_q, ss=sq_s:
               E.scalar_tensor_tensor(d, q, ctx.col1("c_r1024"), ss, ALU.mult, ALU.subtract),
               mark="ve")
        p.wait("scalar", "ve", p.cnt["ve"])
        p.emit("scalar", lambda E, S, d=sq_s, s=var:
               E.activation(d, s, AF.Ln, bias=ctx.col1("c_eps")), mark="sc")
        p.selfwait("scalar")
        p.emit("scalar", lambda E, S, d=a_row, s=sq_s:
               E.activation(d, s, AF.Exp, scale=ctx.col1("c_neghalf")), mark="sc")
        p.wait("vector", "sc", p.cnt["sc"])
        p.emit("vector", lambda E, S, d=var, s=ps_s, ar=a_row:
               E.scalar_tensor_tensor(d, s, ctx.col1("c_nr1024"), ar, ALU.mult, ALU.mult),
               mark="ve")
        b_row = var
        onesrow_f = ctx.t["cons_f"][0:1, CF_UTRI:CF_UTRI + 128]
        p.wait("tensor", "ve", p.cnt["ve"])
        p.emit("tensor", lambda E, S, o=ctx.t["psB"][:, 0, 0:512], l=onesrow_f, r=a_row:
               E.matmul(o, l, r, start=True, stop=True), mark="te")
        p.emit("tensor", lambda E, S, o=ctx.t["psB"][:, 1, 0:512], l=onesrow_f, r=b_row:
               E.matmul(o, l, r, start=True, stop=True), mark="te")
        p.wait("scalar", "te", p.cnt["te"])
        p.emit("scalar", lambda E, S, d=A_t, s=ctx.t["psB"][:, 0, 0:512]:
               E.activation(d, s, AF.Copy), mark="sc")
        p.emit("scalar", lambda E, S, d=B_t, s=ctx.t["psB"][:, 1, 0:512]:
               E.activation(d, s, AF.Copy), mark="sc")
        bc_mark = p.cnt["sc"]
        p.wait("vector", "sc", bc_mark)
        p.wait("gpsimd", "sc", bc_mark)
        gp_marks = [0, 0]
        for fc in range(FC):
            sl = fc % 2
            if affine is None:
                if fc >= 2:
                    p.wait("vector", "gp", gp_marks[sl])
                p.emit("vector", lambda E, S, d=Csl[sl], s=src_fn(fc, t), a=A_t:
                       E.tensor_mul(d, s, a), mark="ve")
                p.wait("gpsimd", "ve", p.cnt["ve"])
                p.emit("gpsimd", lambda E, S, d=dst_fn(fc, t), c=Csl[sl], b=B_t:
                       E.tensor_add(d, c, b), mark="gp")
                gp_marks[sl] = p.cnt["gp"]
            else:
                wn, bn = affine
                if fc > 0:
                    p.wait("vector", "gp", p.cnt["gp"])
                p.emit("vector", lambda E, S, d=Cs, s=src_fn(fc, t), a=A_t:
                       E.tensor_mul(d, s, a), mark="ve")
                p.wait("gpsimd", "ve", p.cnt["ve"])
                if fc > 0:
                    p.wait("gpsimd", "ve", ve2_mark[0])
                p.emit("gpsimd", lambda E, S, d=Ds, c=Cs, b=B_t:
                       E.tensor_add(d, c, b), mark="gp")
                p.wait("vector", "gp", p.cnt["gp"])
                p.emit("vector", lambda E, S, d=dst_fn(fc, t), s=Ds,
                              w=ctx.col(wn, fc), b=ctx.col(bn, fc):
                       E.tensor_scalar(d, s, w, b, ALU.mult, ALU.add), mark="ve")
                ve2_mark[0] = p.cnt["ve"]
        ctx.ln_done = ("gp" if affine is None else "ve",
                       p.cnt["gp" if affine is None else "ve"])



def aslot(ctx, arena, slot, t, n=1):
    a = ctx.t[arena]
    return a[:, slot * 1024 + t * 512: slot * 1024 + t * 512 + 512]


def aslot_tokens(ctx, arena, slot, t0, tl):
    a = ctx.t[arena]
    return a[:, slot * 1024 + t0: slot * 1024 + t0 + tl]


def x1_ap(ctx, fc, t):
    a = ctx.t["arenaA"]
    off = 32768 // 2 + (fc * 1024 + t * 512) * 2
    return a[:, off:off + 1024].bitcast(F32)



def emit_l0ffn(ctx, W):
    p = ctx.p
    silu_slot = [ctx.t["scr2"][:, i * 512:(i + 1) * 512] for i in range(2)]

    def rhs_z(wi, k, t):
        return aslot(ctx, "arenaA", k, t)

    def cb_w12(wi, mc, t, ps, b, banks):
        if wi == 0:
            p.wait("scalar", "te", p.cnt["te"])
            p.wait("scalar", "ve", p.cnt["ve"])
            p.emit("scalar", lambda E, S, d=silu_slot[t], s=ps:
                   E.activation(d, s, AF.Silu, scale=ctx.col("l0f_s1", mc)), mark="sc")
            ctx.set_bank_guard(b, "sc")
        else:
            p.wait("vector", "te", p.cnt["te"])
            p.wait("vector", "sc", p.cnt["sc"])
            dst = aslot(ctx, "arenaB", mc, t)
            p.emit("vector", lambda E, S, d=dst, s=ps, sl=silu_slot[t]:
                   E.scalar_tensor_tensor(d, s, ctx.col("l0f_s2", mc), sl, ALU.mult, ALU.mult),
                   mark="ve")
            ctx.set_bank_guard(b, "ve")
            ctx.set_bank_guard(banks[0], "ve")

    emit_linear_fm(ctx, [W["w1"], W["w2"]], rhs_z, cb_w12, Mc=32)

    def rhs_a(wi, k, t):
        return aslot(ctx, "arenaB", k, t)

    def cb_h1(wi, mc, t, ps, b, banks):
        p.wait("scalar", "te", p.cnt["te"])
        dst = aslot(ctx, "arenaA", mc, t)
        p.emit("scalar", lambda E, S, d=dst, s=ps:
               E.activation(d, s, AF.Gelu, scale=ctx.col("l0f_h1s", mc)), mark="sc")
        ctx.set_bank_guard(b, "sc")

    emit_linear_fm(ctx, [W["h1"]], rhs_a, cb_h1, Mc=32)

    def rhs_b(wi, k, t):
        return aslot(ctx, "arenaA", k, t)

    def cb_h2(wi, mc, t, ps, b, banks):
        p.wait("scalar", "te", p.cnt["te"])
        dst = aslot(ctx, "arenaB", mc, t)
        p.emit("scalar", lambda E, S, d=dst, s=ps:
               E.activation(d, s, AF.Gelu, scale=ctx.col("l0f_h2s", mc)), mark="sc")
        ctx.set_bank_guard(b, "sc")

    emit_linear_fm(ctx, [W["h2"]], rhs_b, cb_h2, Mc=32)

    def rhs_c(wi, k, t):
        return aslot(ctx, "arenaB", k, t)

    def cb_w3(wi, mc, t, ps, b, banks):
        p.wait("vector", "te", p.cnt["te"])
        R = ctx.t["R"][:, mc, t * 512:(t + 1) * 512]
        p.emit("vector", lambda E, S, d=R, s=ps:
               E.scalar_tensor_tensor(d, s, ctx.col("l0f_hs3", mc), d, ALU.mult, ALU.add),
               mark="ve")
        ctx.set_bank_guard(b, "ve")

    emit_linear_fm(ctx, [W["w3"]], rhs_c, cb_w3, Mc=8)



def emit_l1ffn(ctx, W):
    p = ctx.p
    scr = ctx.t["scr"]
    Cs, Ds = scr[:, 1024:1536], scr[:, 1536:2048]

    def rhs_z(wi, k, t):
        return aslot(ctx, "arenaA", k, t)

    def mag_ap(mc, t):
        a = ctx.t["arenaB"]
        off = (mc * 1024 + t * 512) * 2
        return a[:, off:off + 1024].bitcast(F32)

    def cb(wi, mc, t, ps, b, banks):
        if wi == 0:
            return
        p.wait("scalar", "te", p.cnt["te"])
        p.wait("scalar", "ve", p.cnt["ve"])
        ps_re = ctx.t["psA"][:, banks[0], 0:512]
        p.emit("scalar", lambda E, S, d=Cs, s=ps_re:
               E.activation(d, s, AF.Square), mark="sc")
        p.emit("scalar", lambda E, S, d=Ds, s=ps:
               E.activation(d, s, AF.Square), mark="sc")
        p.wait("vector", "sc", p.cnt["sc"])
        p.emit("vector", lambda E, S, d=mag_ap(mc, t), a=Cs, bb=Ds:
               E.tensor_add(d, a, bb), mark="ve")
        ctx.set_bank_guard(b, "ve")
        ctx.set_bank_guard(banks[0], "sc")

    emit_linear_fm(ctx, [W["cos"], W["sin"]], rhs_z, cb, Mc=8)
    p.wait("scalar", "ve", p.cnt["ve"])
    for mc in range(8):
        for t in range(NT):
            p.wait("scalar", "ve", p.cnt["ve"])
            p.selfwait("scalar")
            p.emit("scalar", lambda E, S, d=Ds, s=mag_ap(mc, t):
                   E.activation(d, s, AF.Sqrt, scale=ctx.col("c_quarter")), mark="sc")
            p.wait("vector", "sc", p.cnt["sc"])
            p.selfwait("vector")
            R = ctx.t["R"][:, mc, t * 512:(t + 1) * 512]
            p.emit("vector", lambda E, S, d=R, s=Ds: E.tensor_add(d, d, s), mark="ve")



BLOCKS = ("l0ffn_a", "l0attn", "l0ffn_b", "l1ffn_a", "l1attn", "l1ffn_b")


def make_consts():
    i128 = np.arange(128)
    ident_f = np.eye(128, dtype=np.float32)
    utri_f = (i128[:, None] <= i128[None, :]).astype(np.float32)
    bf = ml_dtypes.bfloat16
    ident_b = ident_f.astype(bf)
    utri_b = utri_f.astype(bf)
    ones_b = np.ones((128, 128), bf)
    maskLneg = -np.tril(np.ones((128, 128), np.float32), -1)
    maskUneg = -np.triu(np.ones((128, 128), np.float32), 1)
    knsel = np.zeros((128, 128), np.float32)
    for fc in range(8):
        for j in range(16):
            if j // 2 == fc:
                rows = slice((j % 2) * 64, (j % 2) * 64 + 64)
                knsel[rows, fc * 16 + j] = 1.0
    BIG = -1e30
    mb_le = np.where(i128[:, None] <= i128[None, :], 0.0, BIG).astype(np.float32)
    mb_lt = np.where(i128[:, None] < i128[None, :], 0.0, BIG).astype(np.float32)
    mb_gt = np.where(i128[:, None] > i128[None, :], 0.0, BIG).astype(np.float32)
    ones_f = np.ones((128, 128), np.float32)
    cons_f = np.concatenate([ident_f, utri_f, ones_f, mb_le, mb_lt, mb_gt], axis=1)
    cons_b = np.concatenate([np.ones((128, 1), np.float32), ident_f, utri_f,
                             np.ones((128, 128), np.float32), maskLneg, maskUneg,
                             knsel], axis=1).astype(bf)
    return cons_f, cons_b

CB_ONESCOL = 0
CB_IDENT = 1
CB_UTRI = 129
CB_ONES = 257
CB_MASKL = 385
CB_MASKU = 513
CB_KNSEL = 641
CF_IDENT = 0
CF_UTRI = 128
CF_ONES = 256
CF_MBLE = 384
CF_MBLT = 512
CF_MBGT = 640


def build_cols(inputs):
    reg = {}
    colsl = []

    def add(name, arr):
        arr = np.asarray(arr, np.float32)
        if arr.ndim == 1:
            if arr.size % 128 == 0:
                c = cols_of(arr)
            else:
                raise ValueError(name)
        else:
            c = arr
        reg[name] = (sum(x.shape[1] for x in colsl), c.shape[1])
        colsl.append(c)

    def addc(name, v):
        add(name, np.full((128, 1), v, np.float32))

    add("l0f_s1", inputs["L0_ffn_w1_s"])
    add("l0f_s2", inputs["L0_ffn_w2_s"])
    add("l0f_h1s", inputs["L0_ffn_h1_s"])
    add("l0f_h2s", inputs["L0_ffn_h2_s"])
    add("l0f_s3", inputs["L0_ffn_w3_s"])
    add("l0f_hs3", inputs["L0_ffn_w3_s"])
    add("l0a_n1w", inputs["L0_attn_norm1_w"])
    add("l0a_n1b", inputs["L0_attn_norm1_b"])
    add("l0a_n2w", inputs["L0_attn_norm2_w"])
    add("l0a_n2b", inputs["L0_attn_norm2_b"])
    add("l0a_gb", inputs["L0_attn_gate_b"])
    add("l0a_ob", inputs["L0_attn_out_b"])
    add("l0a_mb1", inputs["L0_attn_mlp_b1"])
    add("l0a_mb2", inputs["L0_attn_mlp_b2"])
    add("l1a_ln1g", inputs["L1_attn_ln1_g"])
    add("l1a_ln1b", inputs["L1_attn_ln1_b"])
    add("l1a_ln2g", inputs["L1_attn_ln2_g"])
    add("l1a_ln2b", inputs["L1_attn_ln2_b"])
    add("l1a_bo", inputs["L1_attn_bo"])
    add("l1a_b1", inputs["L1_attn_b1"])
    add("l1a_b2", inputs["L1_attn_b2"])
    addc("c_1", 1.0)
    addc("c_quarter", 0.25)
    addc("c_eighth", 0.125)
    addc("c_eps", 1e-5)
    addc("c_r1024", 1.0 / 1024.0)
    addc("c_nr1024", -1.0 / 1024.0)
    addc("c_neg1", -1.0)
    addc("c_neghalf", -0.5)
    addc("c_1e6", 1e-6)
    addc("c_pi2", float(np.pi / 2))
    addc("c_half", 0.5)
    cols = np.concatenate(colsl, axis=1)
    return cols, reg


def prep_host(inputs):
    h = {}
    h["W_l0f_w1"] = tile_w(np.asarray(inputs["L0_ffn_w1_w"]).T)
    h["W_l0f_w2"] = tile_w(np.asarray(inputs["L0_ffn_w2_w"]).T)
    h["W_l0f_h1"] = tile_w(np.asarray(inputs["L0_ffn_h1_w"]).T)
    h["W_l0f_h2"] = tile_w(np.asarray(inputs["L0_ffn_h2_w"]).T)
    h["W_l0f_w3"] = tile_w(np.asarray(inputs["L0_ffn_w3_w"]).T)
    h["W_l0a_gate"] = tile_w(np.asarray(inputs["L0_attn_gate_w"]))
    h["W_l0a_state"] = np.ascontiguousarray(np.asarray(inputs["L0_attn_state_w"]))
    h["W_l0a_out"] = tile_w(np.asarray(inputs["L0_attn_out_w"]))
    h["W_l0a_mlp1"] = tile_w(np.asarray(inputs["L0_attn_mlp_w1"]))
    h["W_l0a_mlp2"] = tile_w(np.asarray(inputs["L0_attn_mlp_w2"]))
    h["W_l1a_q"] = tile_w(np.asarray(inputs["L1_attn_Wq"]))
    h["W_l1a_k"] = tile_w(np.asarray(inputs["L1_attn_Wk"]))
    h["W_l1a_v"] = tile_w(np.asarray(inputs["L1_attn_Wv"]))
    h["W_l1a_b"] = tile_w(np.asarray(inputs["L1_attn_Wb"]))
    h["W_l1a_g"] = tile_w(np.asarray(inputs["L1_attn_Wg"]))
    h["W_l1a_o"] = tile_w(np.asarray(inputs["L1_attn_Wo"]))
    h["W_l1a_m1"] = tile_w(np.asarray(inputs["L1_attn_w1"]))
    h["W_l1a_m2"] = tile_w(np.asarray(inputs["L1_attn_w2"]))
    h["W_l1f_th"] = tile_w(np.asarray(inputs["L1_ffn_theta"]))
    h["W_l1f_thn"] = tile_w(np.asarray(inputs["L1_ffn_theta_noise"]))
    h["W_l1f_ph"] = tile_w(np.asarray(inputs["L1_ffn_phi"]))
    h["W_l1f_phn"] = tile_w(np.asarray(inputs["L1_ffn_phi_noise"]))
    cols, reg = build_cols(inputs)
    h["cols_in"] = cols
    h["rows_in"] = np.asarray(inputs["L0_attn_state_b"], np.float32)[None, :]
    cf, cb = make_consts()
    h["cons_f"] = cf
    h["cons_b"] = cb
    return h, reg


def declare_drams(nc, h, blocks, fused, rin_shape):
    d = {}
    for k, v in h.items():
        d[k] = nc.dram_tensor(k, list(v.shape), BF16 if v.dtype == ml_dtypes.bfloat16 else F32,
                              kind="ExternalInput").ap()
    d["x_fm"] = nc.dram_tensor("x_fm", list(rin_shape), F32, kind="ExternalInput").ap()
    d["y_fm"] = nc.dram_tensor("y_fm", list(rin_shape), F32, kind="ExternalOutput").ap()
    return d


def alloc_tensors(ctx, es):
    nc = ctx.nc
    t = ctx.t
    t["R"] = es.enter_context(nc.sbuf_tensor("sb_R", [128, 8, 1024], F32))
    t["arenaA"] = es.enter_context(nc.sbuf_tensor("sb_arenaA", [128, 32768], BF16))
    t["arenaB"] = es.enter_context(nc.sbuf_tensor("sb_arenaB", [128, 32768], BF16))
    t["wstage"] = es.enter_context(nc.sbuf_tensor("sb_wstage", [128, 1024], F32))
    t["wbf"] = es.enter_context(nc.sbuf_tensor("sb_wbf", [128, 8192], BF16))
    t["scr"] = es.enter_context(nc.sbuf_tensor("sb_scr", [128, 2048], F32))
    t["scr2"] = es.enter_context(nc.sbuf_tensor("sb_scr2", [128, 3584], BF16))
    t["cols"] = es.enter_context(nc.sbuf_tensor("sb_cols", [128, 384], F32))
    t["cons_f"] = es.enter_context(nc.sbuf_tensor("sb_cons_f", [128, 768], F32))
    t["cons_bf"] = es.enter_context(nc.sbuf_tensor("sb_cons_bf", [128, 769], BF16))
    t["Sstate"] = es.enter_context(nc.sbuf_tensor("sb_Sstate", [128, 8, 64], F32))
    t["rows_bf"] = es.enter_context(nc.sbuf_tensor("sb_rows_bf", [1, 1040], BF16))
    t["carry"] = es.enter_context(nc.sbuf_tensor("sb_carry", [1, 1024], BF16))
    t["wbar"] = es.enter_context(nc.sbuf_tensor("sb_wbar", [128, 18], F32))
    t["psA"] = es.enter_context(nc.psum_tensor("sb_psA", [128, 4, 512], F32))
    t["psB"] = es.enter_context(nc.psum_tensor("sb_psB", [128, 3, 512], F32))
    t["psC"] = es.enter_context(nc.psum_tensor("sb_psC", [128, 1024], BF16))


def emit_init(ctx, d):
    p = ctx.p
    rows = ctx.t["scr"][0:1, 0:1024]
    dms = []
    for name, dst in [("cols_in", "cols"), ("cons_f", "cons_f"), ("cons_b", "cons_bf")]:
        nn = d[name].shape[1]
        dms.append(p.dma("sync", lambda E, S, dd=ctx.t[dst][:, 0:nn], s=d[name]:
                         E.dma_start(out=dd, in_=s)))
    dms.append(p.dma("sync", lambda E, S, dd=rows, s=d["rows_in"]: E.dma_start(out=dd, in_=s)))
    for sem, val in dms:
        p.wait("vector", sem, val)
    s, n = ctx.colreg["l0f_hs3"]
    p.emit("vector", lambda E, S: E.tensor_scalar_mul(
        ctx.t["cols"][:, s:s + n], ctx.t["cols"][:, s:s + n], 0.5), mark="ve")
    p.emit("vector", lambda E, S, rows=rows: E.tensor_copy(ctx.t["rows_bf"][:, 0:1024], rows),
           mark="ve")
    bbar_f = ctx.t["scr"][0:1, 1024:1025]
    p.emit("vector", lambda E, S, rows=rows, d=bbar_f: E.reduce_sum(d, rows, axis=mybir.AxisListType.X),
           mark="ve")
    p.selfwait("vector")
    p.emit("vector", lambda E, S, d=ctx.t["rows_bf"][0:1, 1024:1025], s=bbar_f:
           E.tensor_scalar_mul(d, s, 1.0 / 1024.0), mark="ve")
    p.emit("vector", lambda E, S: E.memset(ctx.t["carry"][:], 0.0), mark="ve")
    p.emit("vector", lambda E, S: E.memset(ctx.t["scr2"][:], 0.0), mark="ve")
    p.emit("vector", lambda E, S: E.memset(ctx.t["arenaA"][:], 0.0), mark="ve")
    p.emit("vector", lambda E, S: E.memset(ctx.t["arenaB"][:], 0.0), mark="ve")
    p.selfwait("vector")
    if os.environ.get("BASS_SIM_FP64") == "1" or os.environ.get("KMK_SIM") == "1":
        p.emit("vector", lambda E, S: E.memset(ctx.t["psC"][:], 0.0), mark="ve")
        p.emit("vector", lambda E, S: E.memset(ctx.t["psB"][:], 0.0), mark="ve")
    p.emit("vector", lambda E, S: E.memset(ctx.t["scr"][:], 0.0), mark="ve")
    p.emit("vector", lambda E, S: E.memset(ctx.t["Sstate"][:], 0.0), mark="ve")
    p.emit("vector", lambda E, S: E.memset(ctx.t["wbar"][:], 0.0), mark="ve")
    ctx.init_done = p.cnt["ve"]


def emit_strip(ctx, d, W, strip, blocks):
    p = ctx.p
    t0 = strip * TSTRIP
    ctx.rload = p.dma("sync", lambda E, S: E.dma_start(
        out=ctx.t["R"][:], in_=d["x_fm"][:, :, t0:t0 + TSTRIP]))

    def Rsrc(fc, t):
        return ctx.t["R"][:, fc, t * 512:(t + 1) * 512]

    def zdst(fc, t):
        return aslot(ctx, "arenaA", fc, t)

    first = True
    for blk in blocks:
        if first:
            p.wait("vector", ctx.rload[0], ctx.rload[1])
            first = False
        if blk in ("l0ffn_a", "l0ffn_b"):
            emit_ln(ctx, Rsrc, zdst, affine=None)
            emit_l0ffn(ctx, {k: W[k] for k in ("w1", "w2", "h1", "h2", "w3")})
        elif blk in ("l1ffn_a", "l1ffn_b"):
            emit_ln(ctx, Rsrc, zdst, affine=None)
            emit_l1ffn(ctx, {"cos": W["cos"], "sin": W["sin"]})
        elif blk == "l0attn":
            emit_ln(ctx, Rsrc, zdst, affine=None)
            emit_l0attn(ctx, W)
        elif blk == "l1attn":
            emit_ln(ctx, Rsrc, zdst, affine=None)
            emit_l1attn(ctx, W)
        elif blk == "lntest":
            emit_ln(ctx, Rsrc, zdst, affine=None)
            for t in range(NT):
                for fc in range(FC):
                    p.wait("vector", "gp", p.cnt["gp"])
                    p.emit("vector", lambda E, S, d=Rsrc(fc, t), s=zdst(fc, t):
                           E.tensor_copy(d, s), mark="ve")
        elif blk == "lnafftest":
            emit_ln(ctx, Rsrc, zdst, affine=("l0a_n1w", "l0a_n1b"))
            p.selfwait("vector")
            for t in range(NT):
                for fc in range(FC):
                    p.selfwait("vector")
                    p.emit("vector", lambda E, S, d=Rsrc(fc, t), s=zdst(fc, t):
                           E.tensor_copy(d, s), mark="ve")
    p.wait("gpsimd", "ve", p.cnt["ve"])
    p.wait("gpsimd", "sc", p.cnt["sc"])
    p.wait("gpsimd", "stdma", p.cnt["stdma"])
    p.emit("gpsimd", lambda E, S: E.dma_start(
        out=d["y_fm"][:, :, t0:t0 + TSTRIP], in_=ctx.t["R"][:]), mark="stdma", n=16)
    p.wait("sync", "stdma", p.cnt["stdma"])


def make_wsrcs(ctx, d):
    return dict(
        w1=WSrc(d["W_l0f_w1"], 8), w2=WSrc(d["W_l0f_w2"], 8),
        h1=WSrc(d["W_l0f_h1"], 32), h2=WSrc(d["W_l0f_h2"], 32),
        w3=WSrc(d["W_l0f_w3"], 32),
        gate=WSrc(d["W_l0a_gate"], 8), out=WSrc(d["W_l0a_out"], 8),
        mlp1=WSrc(d["W_l0a_mlp1"], 8), mlp2=WSrc(d["W_l0a_mlp2"], 32),
        state_nat=d["W_l0a_state"],
        q=WSrc(d["W_l1a_q"], 8), k=WSrc(d["W_l1a_k"], 8), v=WSrc(d["W_l1a_v"], 8),
        wb=WSrc(d["W_l1a_b"], 8, mw=16), wg=WSrc(d["W_l1a_g"], 8, mw=16),
        wo=WSrc(d["W_l1a_o"], 8), m1=WSrc(d["W_l1a_m1"], 8), m2=WSrc(d["W_l1a_m2"], 32),
        cos=WSinCos(d["W_l1f_th"], d["W_l1f_thn"], True),
        sin=WSinCos(d["W_l1f_ph"], d["W_l1f_phn"], False),
    )


def build_program(h, reg, blocks, n_strips=NSTRIP):
    nc = bass.Bass("TRN2", target_bir_lowering=False)
    d = declare_drams(nc, h, blocks, True, [128, 8, SEQ])
    prog = Prog()
    ctx = Ctx(nc, prog)
    ctx.colreg = reg
    es = contextlib.ExitStack()
    with es:
        alloc_tensors(ctx, es)
        sems = {}
        for sn in SEMNAMES:
            sems[sn] = es.enter_context(nc.semaphore(sn))
        emit_init(ctx, d)
        W = make_wsrcs(ctx, d)
        for strip in range(n_strips):
            emit_strip(ctx, d, W, strip, blocks)
        prog.wait("gpsimd", "stdma", prog.cnt["stdma"])
        prog.replay(nc, sems)
    return nc


def kernel(**inputs):
    h, reg = prep_host(inputs)
    x = np.asarray(inputs["L0_attn_x"])
    nc = build_program(h, reg, list(BLOCKS))
    in_maps = []
    for c in range(NCORES):
        m = dict(h)
        m["x_fm"] = fm_of(x[c])
        in_maps.append(m)
    res = run_bass_kernel_spmd(nc, in_maps, core_ids=list(range(NCORES)))
    out = np.empty((NCORES, SEQ, D), np.float32)
    for c in range(NCORES):
        y = res.results[c]["y_fm"]
        out[c] = y.transpose(1, 0, 2).reshape(D, SEQ).T
    return out


def emit_tok_state_pass(ctx, W, srcs, dsts, seed_row, wbar=True):
    p = ctx.p
    rhs_barrier(ctx)
    ctx.pass_id = getattr(ctx, "pass_id", 0) + 1
    pass_te0 = p.cnt["te"]
    wk = [ctx.t["wbf"][:, k * 1024:(k + 1) * 1024] for k in range(8)]
    ctx.stage_i = 0
    for k in range(8):
        for half in range(2):
            hs = ctx.stage_i % 2
            ctx.stage_i += 1
            stage = ctx.t["wstage"][:, hs * 512:(hs + 1) * 512]
            gg = ctx.wstage_guard[hs]
            if gg is not None:
                p.wait("sync", gg[0], gg[1])
            src = W[k * 128:(k + 1) * 128, half * 512:(half + 1) * 512]
            dm = p.dma("sync", lambda E, S, d=stage, s=src: E.dma_start(out=d, in_=s))
            p.wait("vector", dm[0], dm[1])
            if k == 0 and half == 0:
                p.wait("vector", "te", pass_te0)
            p.emit("vector", lambda E, S, d=wk[k][:, half * 512:(half + 1) * 512], s=stage:
                   E.tensor_copy(d, s), mark="ve")
            ctx.wstage_guard[hs] = ("ve", p.cnt["ve"])
        if wbar:
            p.selfwait("vector")
            p.emit("vector", lambda E, S, d=ctx.t["wbar"][:, k:k + 1], s=wk[k]:
                   E.reduce_sum(d, s, axis=mybir.AxisListType.X), mark="ve")
    if wbar:
        p.selfwait("vector")
        p.emit("vector", lambda E, S: E.tensor_scalar_mul(
            ctx.t["wbar"][:, 0:8], ctx.t["wbar"][:, 0:8], 1.0 / 1024.0), mark="ve")
        p.selfwait("vector")
        p.emit("vector", lambda E, S: E.tensor_copy(
            ctx.t["scr2"][:, 3520:3528], ctx.t["wbar"][:, 0:8]), mark="ve")
    cast_done = p.cnt["ve"]
    p.wait("tensor", "ve", cast_done)
    for tc in range(8):
        for si, src_fn in enumerate(srcs):
            for half in range(2):
                b = ctx.next_bank()
                ctx.guard_bank(b)
                ps = ctx.t["psA"][:, b, 0:512]
                p.emit("tensor", lambda E, S, o=ps, r=seed_row(half):
                       E.matmul(o, ctx.t["cons_bf"][0:1, CB_ONES:CB_ONES + 128], r,
                                start=True, stop=False))
                for k in range(8):
                    p.emit("tensor", lambda E, S, o=ps, l=src_fn(k, tc),
                                  r=wk[k][:, half * 512:(half + 1) * 512], sp=(k == 7):
                           E.matmul(o, l, r, start=False, stop=sp),
                           mark="te" if k == 7 else None)
                p.wait("scalar", "te", p.cnt["te"])
                p.emit("scalar", lambda E, S, d=dsts[si](tc, half), s=ps:
                       E.activation(d, s, AF.Copy), mark="sc")
                ctx.set_bank_guard(b, "sc")


def emit_l0attn(ctx, W):
    p = ctx.p

    def z_ap(fc, t):
        return aslot(ctx, "arenaA", fc, t)

    def z_tok(kc, tc):
        return ctx.t["arenaA"][:, kc * 1024 + tc * 128: kc * 1024 + tc * 128 + 128]

    war_barrier(ctx, "vector")
    p.selfwait("vector")
    p.wait("vector", "gp", p.cnt["gp"])
    for t in range(NT):
        for fc in range(FC):
            p.emit("vector", lambda E, S, d=ctx.t["R"][:, fc, t * 512:(t + 1) * 512],
                          s=z_ap(fc, t): E.tensor_add(d, d, s), mark="ve")

    def xn1_ap(fc, t):
        return aslot(ctx, "arenaB", fc, t)
    emit_ln(ctx, z_ap, xn1_ap, affine=("l0a_n1w", "l0a_n1b"), src_bf16=True)

    def rhs_xn1(wi, k, t):
        return aslot(ctx, "arenaB", k, t)

    def cb_gate(wi, mc, t, ps, b, banks):
        p.wait("scalar", "te", p.cnt["te"])
        dst = aslot(ctx, "arenaB", 8 + mc, t)
        p.emit("scalar", lambda E, S, d=dst, s=ps:
               E.activation(d, s, AF.Silu, bias=ctx.col("l0a_gb", mc)), mark="sc")
        ctx.set_bank_guard(b, "sc")

    emit_linear_fm(ctx, [W["gate"]], rhs_xn1, cb_gate, Mc=16)

    war_barrier(ctx, "vector")
    p.wait("vector", "sc", p.cnt["sc"])
    for t in range(NT):
        for fc in range(FC):
            p.emit("vector", lambda E, S, d=aslot(ctx, "arenaA", 8 + fc, t),
                          a=aslot(ctx, "arenaB", fc, t), g=aslot(ctx, "arenaB", 8 + fc, t):
                   E.tensor_mul(d, a, g), mark="ve")

    def xg_tok(kc, tc):
        return ctx.t["arenaA"][:, (8 + kc) * 1024 + tc * 128: (8 + kc) * 1024 + tc * 128 + 128]

    def gg_tok(kc, tc):
        return ctx.t["arenaB"][:, (16 + kc) * 1024 + tc * 128: (16 + kc) * 1024 + tc * 128 + 128]

    def u_dst(tc, half):
        return ctx.t["arenaB"][:, (24 + tc) * 1024 + half * 512: (24 + tc) * 1024 + half * 512 + 512]

    def bm_dst(tc, half):
        return ctx.t["arenaA"][:, (16 + tc) * 1024 + half * 512: (16 + tc) * 1024 + half * 512 + 512]

    def seed_row(half):
        return ctx.t["rows_bf"][0:1, half * 512:(half + 1) * 512]

    emit_tok_state_pass(ctx, W["state_nat"], [xg_tok, gg_tok], [u_dst, bm_dst], seed_row)

    scr = ctx.t["scr"]
    dcol = scr[:, 1536:1544]
    ps_d = ctx.t["psB"][:, 2, 0:1]
    for tc in range(8):
        p.wait("tensor", "sc", p.cnt["sc"])
        p.emit("tensor", lambda E, S, o=ps_d:
               E.matmul(o, ctx.t["cons_bf"][0:1, CB_ONES:CB_ONES + 128],
                        ctx.t["rows_bf"][0:1, 1024:1025], start=True, stop=False))
        wb_bf = ctx.t["scr2"][:, 3520:3528]
        for k in range(8):
            p.emit("tensor", lambda E, S, o=ps_d, l=xn1_tok(ctx, k, tc),
                          r=wb_bf[:, k:k + 1], sp=(k == 7):
                   E.matmul(o, l, r, start=False, stop=sp), mark="te" if k == 7 else None)
        p.wait("scalar", "te", p.cnt["te"])
        p.emit("scalar", lambda E, S, d=dcol[:, tc:tc + 1], s=ps_d:
               E.activation(d, s, AF.Copy), mark="sc")
    t1 = scr[:, 1544:1552]
    t2 = scr[:, 1552:1560]
    t3 = scr[:, 1560:1568]
    p.selfwait("scalar")
    p.emit("scalar", lambda E, S: E.activation(t1, dcol, AF.Abs), mark="sc")
    p.selfwait("scalar")
    p.emit("scalar", lambda E, S: E.activation(t2, t1, AF.Exp, scale=ctx.col("c_neg1")), mark="sc")
    p.selfwait("scalar")
    p.emit("scalar", lambda E, S: E.activation(t3, t2, AF.Ln, bias=ctx.col("c_1")), mark="sc")
    p.wait("vector", "sc", p.cnt["sc"])
    p.selfwait("vector")
    p.emit("vector", lambda E, S: E.tensor_relu(t1, dcol), mark="ve")
    p.selfwait("vector")
    p.emit("vector", lambda E, S: E.tensor_add(dcol, t1, t3), mark="ve")

    p.wait("vector", "sc", p.cnt["sc"])
    for tc in range(8):
        dbu = ctx.t["arenaA"][:, (24 + tc) * 1024: (25 + tc) * 1024]
        u = ctx.t["arenaB"][:, (24 + tc) * 1024: (25 + tc) * 1024]
        bm = ctx.t["arenaA"][:, (16 + tc) * 1024: (17 + tc) * 1024]
        p.emit("vector", lambda E, S, d=dbu, a=u, b=bm: E.tensor_mul(d, a, b), mark="ve")
        p.selfwait("vector")
        p.emit("vector", lambda E, S, d=dbu, c=dcol[:, tc:tc + 1]:
               E.tensor_scalar_mul(d, d, c), mark="ve")

    consf = ctx.t["cons_f"]
    utri_f = consf[:, CF_UTRI:CF_UTRI + 128]
    ident_f = consf[:, CF_IDENT:CF_IDENT + 128]
    onesrow_f = consf[0:1, CF_UTRI:CF_UTRI + 128]
    utri_b = ctx.t["cons_bf"][:, CB_UTRI:CB_UTRI + 128]
    scr2 = ctx.t["scr2"]
    M_sb = scr2[:, 2048:2176]
    coef = scr2[0:1, 2176:2304]
    dcol_sb = scr[:, 1568:1569]
    drow_sb = scr[0:1, 1569:1697]
    rhs_barrier(ctx)
    for tc in range(8):
        ps_dc = ctx.t["psB"][:, 2, 0:1]
        ps_dr = ctx.t["psB"][0:1, 2, 128:256]
        ps_bc = ctx.t["psB"][:, 2, 256:384]
        p.wait("tensor", "ve", p.cnt["ve"])
        p.wait("tensor", "sc", p.cnt["sc"])
        p.emit("tensor", lambda E, S, o=ps_dc, r=dcol[:, tc:tc + 1]:
               E.matmul(o, utri_f, r, start=True, stop=True), mark="te")
        p.wait("scalar", "te", p.cnt["te"])
        p.selfwait("scalar")
        p.emit("scalar", lambda E, S, d=dcol_sb, s=ps_dc:
               E.activation(d, s, AF.Copy), mark="sc")
        p.wait("tensor", "sc", p.cnt["sc"])
        p.emit("tensor", lambda E, S, o=ps_dr, i=dcol_sb:
               E.transpose(o, i, ident_f), mark="te")
        p.wait("scalar", "te", p.cnt["te"])
        p.selfwait("scalar")
        p.emit("scalar", lambda E, S, d=drow_sb, s=ps_dr:
               E.activation(d, s, AF.Copy), mark="sc")
        p.wait("tensor", "sc", p.cnt["sc"])
        p.emit("tensor", lambda E, S, o=ps_bc, r=drow_sb:
               E.matmul(o, onesrow_f, r, start=True, stop=True), mark="te")
        p.wait("vector", "te", p.cnt["te"])
        p.selfwait("vector")
        marg = scr[:, 0:128]
        p.emit("vector", lambda E, S, d=marg, s=ps_bc, m=consf[:, CF_MBLE:CF_MBLE + 128]:
               E.tensor_sub(d, s, m), mark="ve")
        p.wait("scalar", "ve", p.cnt["ve"])
        p.wait("scalar", "te", p.cnt["te"])
        p.selfwait("scalar")
        p.emit("scalar", lambda E, S, d=M_sb, s=marg:
               E.activation(d, s, AF.Exp, scale=ctx.col("c_neg1"), bias=dcol_sb), mark="sc")
        p.selfwait("scalar")
        p.emit("scalar", lambda E, S, d=coef, s=ps_dr:
               E.activation(d, s, AF.Exp, scale=ctx.col1("c_neg1")), mark="sc")
        p.wait("vector", "sc", p.cnt["sc"])
        b1 = ctx.next_bank()
        b2 = ctx.next_bank()
        ctx.guard_bank(b1)
        ctx.guard_bank(b2)
        p.wait("tensor", "ve", p.cnt["ve"])
        p.wait("tensor", "sc", p.cnt["sc"])
        for fc in range(8):
            bk = b1 if fc < 4 else b2
            reg = ctx.t["psA"][:, bk, (fc % 4) * 128:(fc % 4) * 128 + 128]
            dbu = ctx.t["arenaA"][:, (24 + tc) * 1024 + fc * 128: (24 + tc) * 1024 + fc * 128 + 128]
            p.emit("tensor", lambda E, S, o=reg, l=dbu, m=M_sb:
                   E.matmul(o, l, m, start=True, stop=False))
            p.emit("tensor", lambda E, S, o=reg, l=ctx.t["carry"][0:1, fc * 128:(fc + 1) * 128],
                          r=coef: E.matmul(o, l, r, start=False, stop=True),
                   mark="te" if fc == 7 else None)
        ymark = p.cnt["te"]
        ps_c0 = ctx.t["psB"][0:1, 0, 0:512]
        ps_c1 = ctx.t["psB"][0:1, 1, 0:512]
        for half in range(2):
            psc = ps_c0 if half == 0 else ps_c1
            dbuh = ctx.t["arenaA"][:, (24 + tc) * 1024 + half * 512: (24 + tc) * 1024 + half * 512 + 512]
            p.emit("tensor", lambda E, S, o=psc, l=M_sb[:, 127:128], r=dbuh:
                   E.matmul(o, l, r, start=True, stop=False))
            p.emit("tensor", lambda E, S, o=psc, l=coef[0:1, 127:128],
                          r=ctx.t["carry"][0:1, half * 512:(half + 1) * 512]:
                   E.matmul(o, l, r, start=False, stop=True), mark="te")
        p.wait("scalar", "te", p.cnt["te"])
        p.selfwait("scalar")
        p.emit("scalar", lambda E, S, d=ctx.t["carry"][0:1, 0:512], s=ps_c0:
               E.activation(d, s, AF.Copy), mark="sc")
        p.emit("scalar", lambda E, S, d=ctx.t["carry"][0:1, 512:1024], s=ps_c1:
               E.activation(d, s, AF.Copy), mark="sc")
        for i, bk in enumerate((b1, b2)):
            ydst = ctx.t["arenaB"][:, 0:8 * 1024].rearrange("p (f c) -> p f c", c=1024)[
                :, i * 4:(i + 1) * 4, tc * 128:(tc + 1) * 128]
            ysrc = ctx.t["psA"][:, bk, 0:512].rearrange("p (f c) -> p f c", c=128)
            p.emit("scalar", lambda E, S, d=ydst, s=ysrc:
                   E.activation(d, s, AF.Copy), mark="sc")
        ctx.set_bank_guard(b1, "sc")
        ctx.set_bank_guard(b2, "sc")

    def rhs_y(wi, k, t):
        return aslot(ctx, "arenaB", k, t)

    def cb_out(wi, mc, t, ps, b, banks):
        p.wait("vector", "te", p.cnt["te"])
        p.selfwait("vector")
        R = ctx.t["R"][:, mc, t * 512:(t + 1) * 512]
        p.emit("vector", lambda E, S, d=R, s=ps, c=ctx.col("l0a_ob", mc):
               E.scalar_tensor_tensor(d, s, c, d, ALU.add, ALU.add), mark="ve")
        p.emit("vector", lambda E, S, d=x1_ap(ctx, mc, t), s=ps,
                      c=ctx.col("l0a_ob", mc), z=z_ap(mc, t):
               E.scalar_tensor_tensor(d, s, c, z, ALU.add, ALU.add), mark="ve")
        ctx.set_bank_guard(b, "ve")

    emit_linear_fm(ctx, [W["out"]], rhs_y, cb_out, Mc=8)

    def x1src(fc, t):
        return x1_ap(ctx, fc, t)

    def xn2_ap(fc, t):
        return aslot(ctx, "arenaB", fc, t)
    emit_ln(ctx, x1src, xn2_ap, affine=("l0a_n2w", "l0a_n2b"))

    def rhs_xn2(wi, k, t):
        return aslot(ctx, "arenaB", k, t)

    def h_ap(kc, t):
        if kc < 16:
            return aslot(ctx, "arenaA", kc, t)
        return aslot(ctx, "arenaB", 8 + (kc - 16), t)

    def cb_m1(wi, mc, t, ps, b, banks):
        p.wait("scalar", "te", p.cnt["te"])
        p.emit("scalar", lambda E, S, d=h_ap(mc, t), s=ps:
               E.activation(d, s, AF.Gelu, bias=ctx.col("l0a_mb1", mc)), mark="sc")
        ctx.set_bank_guard(b, "sc")

    emit_linear_fm(ctx, [W["mlp1"]], rhs_xn2, cb_m1, Mc=32)

    def rhs_h(wi, k, t):
        return h_ap(k, t)

    def cb_m2(wi, mc, t, ps, b, banks):
        p.wait("vector", "te", p.cnt["te"])
        p.selfwait("vector")
        R = ctx.t["R"][:, mc, t * 512:(t + 1) * 512]
        p.emit("vector", lambda E, S, d=R, s=ps, c=ctx.col("l0a_mb2", mc):
               E.scalar_tensor_tensor(d, s, c, d, ALU.add, ALU.add), mark="ve")
        ctx.set_bank_guard(b, "ve")

    emit_linear_fm(ctx, [W["mlp2"]], rhs_h, cb_m2, Mc=8)


def xn1_tok(ctx, kc, tc):
    return ctx.t["arenaB"][:, kc * 1024 + tc * 128: kc * 1024 + tc * 128 + 128]


def emit_l1attn(ctx, W):
    p = ctx.p
    scr = ctx.t["scr"]
    scr2 = ctx.t["scr2"]
    s2f = scr2[:].bitcast(F32)
    consb = ctx.t["cons_bf"]
    consf = ctx.t["cons_f"]
    utri_f = consf[:, CF_UTRI:CF_UTRI + 128]
    ident_f = consf[:, CF_IDENT:CF_IDENT + 128]
    onesrow_f = consf[0:1, CF_UTRI:CF_UTRI + 128]
    ident_b = consb[:, CB_IDENT:CB_IDENT + 128]
    ones_b = consb[:, CB_ONES:CB_ONES + 128]
    utri_b = consb[:, CB_UTRI:CB_UTRI + 128]
    maskL = consb[:, CB_MASKL:CB_MASKL + 128]
    maskU = consb[:, CB_MASKU:CB_MASKU + 128]

    def z_ap(fc, t):
        return aslot(ctx, "arenaA", fc, t)

    war_barrier(ctx, "vector")
    p.selfwait("vector")
    p.wait("vector", "gp", p.cnt["gp"])
    for t in range(NT):
        for fc in range(FC):
            p.emit("vector", lambda E, S, d=ctx.t["R"][:, fc, t * 512:(t + 1) * 512],
                          s=z_ap(fc, t): E.tensor_add(d, d, s), mark="ve")

    def xn1_ap(fc, t):
        return aslot(ctx, "arenaB", fc, t)
    emit_ln(ctx, z_ap, xn1_ap, affine=("l1a_ln1g", "l1a_ln1b"), src_bf16=True)

    def rhs_xn1(wi, k, t):
        return aslot(ctx, "arenaB", k, t)

    def mk_cb(base_slot, scale_col):
        def cb(wi, mc, t, ps, b, banks):
            p.wait("scalar", "te", p.cnt["te"])
            dst = aslot(ctx, "arenaA", base_slot + mc, t)
            if scale_col is None:
                p.emit("scalar", lambda E, S, d=dst, s=ps:
                       E.activation(d, s, AF.Copy), mark="sc")
            else:
                p.emit("scalar", lambda E, S, d=dst, s=ps:
                       E.activation(d, s, AF.Copy, scale=ctx.col(scale_col)), mark="sc")
            ctx.set_bank_guard(b, "sc")
        return cb

    emit_linear_fm(ctx, [W["q"]], rhs_xn1, mk_cb(8, "c_eighth"), Mc=8)
    emit_linear_fm(ctx, [W["k"]], rhs_xn1, mk_cb(16, None), Mc=8)
    emit_linear_fm(ctx, [W["v"]], rhs_xn1, mk_cb(24, None), Mc=8)

    lgT = lambda tc: scr[:, 448 + tc * 16: 448 + tc * 16 + 16]
    bT = lambda tc: scr[:, 576 + tc * 16: 576 + tc * 16 + 16]
    knT = lambda tc: scr[:, 704 + tc * 16: 704 + tc * 16 + 16]
    rowtmp = scr2[0:16, 0:512]
    r1 = scr[0:16, 832:1344]
    r2 = scr[0:16, 1344:1856]
    rowf = scr[0:16, 0:512]

    def cb_bg(wi, mc, t, ps, b, banks):
        p.wait("scalar", "te", p.cnt["te"])
        p.wait("scalar", "ve", p.cnt["ve"])
        p.selfwait("scalar")
        if wi == 0:
            p.emit("scalar", lambda E, S, d=r1, s=ps:
                   E.activation(d, s, AF.Exp, scale=ctx.t["cols"][0:16,
                       ctx.colreg["c_neg1"][0]:ctx.colreg["c_neg1"][0] + 1]), mark="sc")
            p.wait("vector", "sc", p.cnt["sc"])
            p.wait("vector", "te", p.cnt["te"])
            p.selfwait("vector")
            p.emit("vector", lambda E, S, d=r1: E.tensor_scalar_add(d, d, 1.0), mark="ve")
            p.selfwait("vector")
            p.emit("vector", lambda E, S, d=r2, s=r1: E.reciprocal(d, s), mark="ve")
            p.wait("tensor", "ve", p.cnt["ve"])
        else:
            p.emit("scalar", lambda E, S, d=r1, s=ps:
                   E.activation(d, s, AF.Abs), mark="sc")
            p.selfwait("scalar")
            p.emit("scalar", lambda E, S, d=r2, s=r1:
                   E.activation(d, s, AF.Exp, scale=ctx.t["cols"][0:16,
                       ctx.colreg["c_neg1"][0]:ctx.colreg["c_neg1"][0] + 1]), mark="sc")
            p.selfwait("scalar")
            p.emit("scalar", lambda E, S, d=r1, s=r2:
                   E.activation(d, s, AF.Ln, bias=ctx.t["cols"][0:16,
                       ctx.colreg["c_1"][0]:ctx.colreg["c_1"][0] + 1]), mark="sc")
            p.wait("vector", "sc", p.cnt["sc"])
            p.wait("vector", "te", p.cnt["te"])
            p.selfwait("vector")
            p.emit("vector", lambda E, S, d=r2, s=ps:
                   E.tensor_scalar(d, s, -1.0, 0.0, ALU.mult, ALU.max), mark="ve")
            p.selfwait("vector")
            p.emit("vector", lambda E, S, d=rowf, a=r1, bb=r2:
                   E.scalar_tensor_tensor(d, a, -1.0, bb, ALU.mult, ALU.subtract), mark="ve")
            p.wait("tensor", "ve", p.cnt["ve"])
        ctx.set_bank_guard(b, "ve")
        for i in range(4):
            tc = t * 4 + i
            p.wait("tensor", "sc", p.cnt["sc"])
            if wi == 1:
                pf = ctx.t["psB"][:, 0, i * 16:(i + 1) * 16]
                p.emit("tensor", lambda E, S, o=pf, s=rowf[:, i * 128:(i + 1) * 128]:
                       E.transpose(o, s, ident_f[0:16, 0:16]), mark="te")
                p.wait("scalar", "te", p.cnt["te"])
                p.selfwait("scalar")
                p.emit("scalar", lambda E, S, d=lgT(tc), s=pf:
                       E.activation(d, s, AF.Copy), mark="sc")
            else:
                pf = ctx.t["psB"][:, 0, 64 + i * 16:64 + (i + 1) * 16]
                p.emit("tensor", lambda E, S, o=pf, s=r2[:, i * 128:(i + 1) * 128]:
                       E.transpose(o, s, ident_f[0:16, 0:16]), mark="te")
                p.wait("scalar", "te", p.cnt["te"])
                p.selfwait("scalar")
                p.emit("scalar", lambda E, S, d=bT(tc), s=pf:
                       E.activation(d, s, AF.Copy), mark="sc")

    emit_linear_fm(ctx, [W["wb"], W["wg"]], rhs_xn1, cb_bg, Mc=1)

    ksq = [scr2[:, 0:512], scr2[:, 512:1024]]
    ps_kn = ctx.t["psB"][0:16, 0, 0:512]
    for t in range(NT):
        p.wait("scalar", "ve", p.cnt["ve"])
        p.wait("scalar", "te", p.cnt["te"])
        for fc in range(FC):
            p.selfwait("scalar")
            if fc >= 2:
                p.wait("scalar", "te", p.cnt["te"])
            p.emit("scalar", lambda E, S, d=ksq[fc % 2], s=aslot(ctx, "arenaA", 16 + fc, t):
                   E.activation(d, s, AF.Square), mark="sc")
            p.wait("tensor", "sc", p.cnt["sc"])
            p.emit("tensor", lambda E, S, o=ps_kn, l=consb[:, CB_KNSEL + fc * 16:CB_KNSEL + fc * 16 + 16],
                          r=ksq[fc % 2], st=(fc == 0), sp=(fc == FC - 1):
                   E.matmul(o, l, r, start=st, stop=sp), mark="te")
        p.wait("scalar", "te", p.cnt["te"])
        p.selfwait("scalar")
        p.emit("scalar", lambda E, S, d=rowtmp, s=ps_kn:
               E.activation(d, s, AF.Copy), mark="sc")
        for i in range(4):
            tc = t * 4 + i
            pc = ctx.t["psC"][:, i * 16:(i + 1) * 16]
            p.wait("tensor", "sc", p.cnt["sc"])
            p.emit("tensor", lambda E, S, o=pc, s=rowtmp[:, i * 128:(i + 1) * 128]:
                   E.transpose(o, s, ident_b[0:16, 0:16]), mark="te")
            p.wait("scalar", "te", p.cnt["te"])
            p.selfwait("scalar")
            p.emit("scalar", lambda E, S, d=scr[:, 192:208], s=pc:
                   E.activation(d, s, AF.Ln), mark="sc")
            p.selfwait("scalar")
            p.emit("scalar", lambda E, S, d=knT(tc), s=scr[:, 192:208]:
                   E.activation(d, s, AF.Exp, scale=ctx.col("c_half")), mark="sc")
            p.wait("vector", "sc", p.cnt["sc"])
            p.selfwait("vector")
            p.emit("vector", lambda E, S, d=knT(tc):
                   E.tensor_scalar_add(d, d, 1e-6), mark="ve")
            p.selfwait("vector")
            p.emit("vector", lambda E, S, d=knT(tc), s=knT(tc):
                   E.reciprocal(d, s), mark="ve")

    cumlg = scr[:, 832:848]
    gamT = scr[:, 848:864]
    negcl = scr[:, 864:880]
    cumlnb = scr[:, 880:896]
    lnb = scr[:, 896:912]
    negb = scr[:, 912:928]
    gCc = scr[:, 928:944]
    c_sb = scr[:, 944:945]
    gCrow = scr[0:1, 945:961]
    lcrow = scr[0:1, 961:977]
    sC = scr[:, 977:993]
    f32a = [scr[:, 0:64], scr[:, 1024:1088]]
    f32b = [scr[:, 64:192], scr[:, 1088:1216]]
    diagf = [scr[:, 192:320], scr[:, 1216:1344]]
    argf = [scr[:, 320:448], scr[:, 1344:1472]]

    def _pt(base):
        off = [0]
        def take(n):
            o = base + off[0]
            off[0] += n
            return scr2[:, o:o + n]
        return take
    tk = [None, None]
    tiles = [{}, {}]
    for hp in range(2):
        take = _pt(hp * 1216)
        tiles[hp] = dict(
            kh=take(128), Adec=take(128), qh=take(128), B=take(128), BT=take(128),
            Mkq=take(128), U=take(64), kt=take(128), Rm=take(64), S0b=take(64),
            diagb=take(128))
    Opair = scr2[:, 2432:2560]
    vT_t = scr2[:, 2560:2688]

    psA_ = ctx.t["psA"]
    psB_ = ctx.t["psB"]
    P0 = dict(bc=psB_[:, 1, 0:128], gram=psB_[:, 1, 128:256],
              bu=psB_[:, 2, 368:432], b2=psB_[:, 0, 0:128], bt2=psB_[:, 0, 128:256],
              w0=psB_[:, 2, 176:240], O=psB_[:, 2, 240:304], P=psB_[:, 2, 304:368])
    P1 = dict(bc=psA_[:, 2, 0:128], gram=psA_[:, 2, 128:256],
              bu=psA_[:, 1, 192:256], b2=psA_[:, 0, 0:128], bt2=psA_[:, 0, 128:256],
              w0=psA_[:, 1, 0:64], O=psA_[:, 1, 64:128], P=psA_[:, 1, 128:192])
    PS = [P0, P1]
    ps_cg = psB_[:, 2, 0:16]
    ps_cgT = psB_[0:16, 2, 16:144]
    ps_g1 = psB_[0:1, 2, 144:160]
    ps_gc = psB_[:, 2, 160:176]
    pc_kt = [ctx.t["psC"][:, 256:384], ctx.t["psC"][:, 512:640]]
    pc_ot = ctx.t["psC"][:, 384:512]
    onesf = consf[:, CF_ONES:CF_ONES + 128]
    mb_gt = consf[:, CF_MBGT:CF_MBGT + 128]
    mb_lt = consf[:, CF_MBLT:CF_MBLT + 128]
    mb_le = consf[:, CF_MBLE:CF_MBLE + 128]

    Sst = ctx.t["Sstate"]

    for t in range(NT):
        for ci in range(4):
            tc = t * 4 + ci
            rhs_barrier(ctx)
            for b in range(4):
                ctx.guard_bank(b)
            p.emit("tensor", lambda E, S, o=ps_cg, r=lgT(tc):
                   E.matmul(o, utri_f, r, start=True, stop=True), mark="te")
            p.wait("scalar", "te", p.cnt["te"])
            p.selfwait("scalar")
            p.emit("scalar", lambda E, S, d=cumlg, s=ps_cg:
                   E.activation(d, s, AF.Copy), mark="sc")
            p.selfwait("scalar")
            p.emit("scalar", lambda E, S, d=gamT, s=cumlg:
                   E.activation(d, s, AF.Exp), mark="sc")
            p.selfwait("scalar")
            p.emit("scalar", lambda E, S, d=lnb, s=bT(tc):
                   E.activation(d, s, AF.Ln), mark="sc")
            p.wait("vector", "sc", p.cnt["sc"])
            p.selfwait("vector")
            p.emit("vector", lambda E, S, d=negcl, s=cumlg:
                   E.tensor_scalar_mul(d, s, -1.0), mark="ve")
            p.emit("vector", lambda E, S, d=cumlnb, a=cumlg, b=lnb:
                   E.tensor_add(d, a, b), mark="ve")
            p.emit("vector", lambda E, S, d=negb, s=bT(tc):
                   E.tensor_scalar_mul(d, s, -1.0), mark="ve")
            p.wait("tensor", "sc", p.cnt["sc"])
            p.wait("tensor", "ve", p.cnt["ve"])
            p.emit("tensor", lambda E, S, o=ps_cgT, s=cumlg:
                   E.transpose(o, s, ident_f), mark="te")
            p.wait("scalar", "te", p.cnt["te"])
            p.selfwait("scalar")
            p.emit("scalar", lambda E, S, d=c_sb[0:16, :], s=ps_cgT[0:16, 127:128]:
                   E.activation(d, s, AF.Copy), mark="sc")
            p.wait("tensor", "sc", p.cnt["sc"])
            p.emit("tensor", lambda E, S, o=ps_g1, s=c_sb[0:16, :]:
                   E.transpose(o, s, ident_f[0:16, 0:16]), mark="te")
            p.wait("scalar", "te", p.cnt["te"])
            p.selfwait("scalar")
            p.emit("scalar", lambda E, S, d=gCrow, s=ps_g1:
                   E.activation(d, s, AF.Exp), mark="sc")
            p.emit("scalar", lambda E, S, d=lcrow, s=ps_g1:
                   E.activation(d, s, AF.Copy), mark="sc")
            p.wait("tensor", "sc", p.cnt["sc"])
            p.emit("tensor", lambda E, S, o=ps_gc, r=gCrow:
                   E.matmul(o, onesrow_f, r, start=True, stop=True), mark="te")
            p.wait("scalar", "te", p.cnt["te"])
            p.selfwait("scalar")
            p.emit("scalar", lambda E, S, d=gCc, s=ps_gc:
                   E.activation(d, s, AF.Copy), mark="sc")
            p.wait("tensor", "sc", p.cnt["sc"])
            p.emit("tensor", lambda E, S, o=ps_gc, r=lcrow:
                   E.matmul(o, onesrow_f, r, start=True, stop=True), mark="te")
            p.wait("vector", "te", p.cnt["te"])
            p.selfwait("vector")
            p.emit("vector", lambda E, S, d=sC, s=ps_gc, c=cumlg:
                   E.tensor_sub(d, s, c), mark="ve")
            p.wait("scalar", "ve", p.cnt["ve"])
            p.selfwait("scalar")
            p.emit("scalar", lambda E, S, d=sC, s=sC:
                   E.activation(d, s, AF.Exp), mark="sc")

            for fc in range(FC):
                vsl = ctx.t["arenaA"][:, (24 + fc) * 1024 + tc * 128: (24 + fc) * 1024 + tc * 128 + 128]
                qsl = ctx.t["arenaA"][:, (8 + fc) * 1024 + tc * 128: (8 + fc) * 1024 + tc * 128 + 128]
                ksl = ctx.t["arenaA"][:, (16 + fc) * 1024 + tc * 128: (16 + fc) * 1024 + tc * 128 + 128]
                p.wait("tensor", "sc", p.cnt["sc"])
                p.wait("tensor", "ve", p.cnt["ve"])
                p.emit("tensor", lambda E, S, o=pc_kt[0], s=vsl:
                       E.transpose(o, s, ident_b), mark="te")
                p.wait("scalar", "te", p.cnt["te"])
                p.selfwait("scalar")
                p.emit("scalar", lambda E, S, d=vT_t, s=pc_kt[0]:
                       E.activation(d, s, AF.Copy), mark="sc")
                vT_done = p.cnt["sc"]

                def mk(hp):
                    h = 2 * fc + hp
                    b0 = 64 * hp
                    te0 = p.cnt["te"]
                    ve0 = p.cnt["ve"]
                    return dict(h=h, b0=b0, T=tiles[hp], R=PS[hp],
                                ksl_h=ksl[b0:b0 + 64, :], qsl_h=qsl[b0:b0 + 64, :],
                                S_h=Sst[b0:b0 + 64, fc, :],
                                fa=f32a[hp], fb=f32b[hp], df=diagf[hp], af=argf[hp],
                                pk=pc_kt[hp],
                                m_bc=te0, m_bcread=ve0, m_diag=ve0, m_arg=ve0, m_exp=0)
                H = [mk(0), mk(1)]

                def s_copy(c, dtile_key, coltile, srct):
                    T, R = c["T"], c["R"]
                    p.wait("vector", "te", c["m_bc"])
                    p.selfwait("vector")
                    p.emit("vector", lambda E, S, d=T["diagb"], cc=coltile[:, c["h"]:c["h"] + 1]:
                           E.tensor_scalar_mul(d, ident_b, cc), mark="ve")
                    c["m_diag"] = p.cnt["ve"]
                    p.wait("tensor", "ve", max(c["m_diag"], c["m_bcread"]))
                    p.emit("tensor", lambda E, S, o=R["bc"], dd=T["diagb"]:
                           E.matmul(o, ones_b, dd, start=True, stop=True), mark="te")
                    c["m_bc"] = p.cnt["te"]
                    p.wait("vector", "te", c["m_bc"])
                    p.selfwait("vector")
                    p.emit("vector", lambda E, S, d=T[dtile_key][c["b0"]:c["b0"] + 64, :], s=srct,
                                  bc=R["bc"][c["b0"]:c["b0"] + 64, :]:
                           E.tensor_mul(d, s, bc), mark="ve")
                    c["m_bcread"] = p.cnt["ve"]
                    c[f"m_{dtile_key}"] = p.cnt["ve"]

                def s_decay(c, dstkey, content_col, bias_col, mask):
                    T = c["T"]
                    p.wait("vector", "te", c["m_bc"])
                    p.selfwait("vector")
                    p.emit("vector", lambda E, S, d=c["df"], cc=content_col[:, c["h"]:c["h"] + 1]:
                           E.tensor_scalar_mul(d, ident_f, cc), mark="ve")
                    c["m_diag"] = p.cnt["ve"]
                    p.wait("tensor", "ve", max(c["m_diag"], c["m_bcread"]))
                    p.emit("tensor", lambda E, S, o=c["R"]["bc"], dd=c["df"]:
                           E.matmul(o, onesf, dd, start=True, stop=True), mark="te")
                    c["m_bc"] = p.cnt["te"]
                    p.wait("vector", "te", c["m_bc"])
                    p.wait("vector", "sc", c["m_exp"])
                    p.selfwait("vector")
                    p.emit("vector", lambda E, S, d=c["af"], s=c["R"]["bc"], m=mask:
                           E.tensor_add(d, s, m), mark="ve")
                    c["m_bcread"] = p.cnt["ve"]
                    c["m_arg"] = p.cnt["ve"]
                    p.wait("scalar", "ve", c["m_arg"])
                    p.selfwait("scalar")
                    p.emit("scalar", lambda E, S, d=T[dstkey], s=c["af"],
                                  b=bias_col[:, c["h"]:c["h"] + 1]:
                           E.activation(d, s, AF.Exp, bias=b), mark="sc")
                    c["m_exp"] = p.cnt["sc"]
                    c[f"m_{dstkey}"] = p.cnt["sc"]

                for c in H:
                    s_copy(c, "kh", knT(tc), c["ksl_h"])
                for c in H:
                    s_copy(c, "qh", gamT, c["qsl_h"])
                for c in H:
                    s_decay(c, "Adec", negcl, cumlg, mb_gt)
                for c in H:
                    s_decay(c, "BT", cumlnb, negcl, mb_lt)
                for c in H:
                    s_decay(c, "Mkq", cumlg, negcl, mb_le)
                for c in H:
                    T, R = c["T"], c["R"]
                    b0 = c["b0"]
                    p.wait("tensor", "ve", max(c["m_kh"], c["m_bcread"]))
                    p.emit("tensor", lambda E, S, o=R["gram"], l=T["kh"][b0:b0 + 64, :], r=T["kh"][b0:b0 + 64, :]:
                           E.matmul(o, l, r, start=True, stop=True), mark="te")
                    c["gram_mark"] = p.cnt["te"]
                for c in H:
                    T, R = c["T"], c["R"]
                    p.wait("vector", "te", c["gram_mark"])
                    p.wait("vector", "sc", max(c["m_Adec"], c["m_BT"]))
                    p.selfwait("vector")
                    p.emit("vector", lambda E, S, d=T["B"], s=R["gram"], cc=negb[:, c["h"]:c["h"] + 1], a=T["Adec"]:
                           E.scalar_tensor_tensor(d, s, cc, a, ALU.mult, ALU.mult), mark="ve")
                    p.emit("vector", lambda E, S, d=T["BT"], s=R["gram"], a=T["BT"]:
                           E.scalar_tensor_tensor(d, s, -1.0, a, ALU.mult, ALU.mult), mark="ve")
                    c["bbt_mark"] = p.cnt["ve"]
                for c in H:
                    T, R = c["T"], c["R"]
                    b0 = c["b0"]
                    p.wait("tensor", "ve", c["bbt_mark"])
                    p.emit("tensor", lambda E, S, o=R["gram"], l=T["kh"][b0:b0 + 64, :], r=c["qsl_h"]:
                           E.matmul(o, l, r, start=True, stop=True), mark="te")
                    c["gram2_mark"] = p.cnt["te"]
                for c in H:
                    T, R = c["T"], c["R"]
                    p.wait("vector", "te", c["gram2_mark"])
                    p.wait("vector", "sc", c["m_Mkq"])
                    p.selfwait("vector")
                    p.emit("vector", lambda E, S, d=T["Mkq"], s=R["gram"], a=T["Mkq"]:
                           E.tensor_mul(d, s, a), mark="ve")
                    p.emit("vector", lambda E, S, d=T["S0b"][c["b0"]:c["b0"] + 64, :], s=c["S_h"]:
                           E.tensor_copy(d, s), mark="ve")
                    c["s0_mark"] = p.cnt["ve"]
                for c in H:
                    T, R = c["T"], c["R"]
                    b0 = c["b0"]
                    p.wait("tensor", "ve", c["s0_mark"])
                    p.emit("tensor", lambda E, S, o=R["w0"], l=T["kh"][b0:b0 + 64, :], r=T["S0b"][b0:b0 + 64, :]:
                           E.matmul(o, l, r, start=True, stop=True), mark="te")
                    c["w0_mark"] = p.cnt["te"]
                for c in H:
                    T = c["T"]
                    p.wait("vector", "te", c["w0_mark"])
                    p.wait("vector", "sc", vT_done)
                    p.selfwait("vector")
                    p.emit("vector", lambda E, S, d=c["fa"], s=c["R"]["w0"], cc=gamT[:, c["h"]:c["h"] + 1]:
                           E.tensor_scalar_mul(d, s, cc), mark="ve")
                    p.selfwait("vector")
                    p.emit("vector", lambda E, S, d=T["Rm"], s=vT_t[:, c["b0"]:c["b0"] + 64],
                                  cc=bT(tc)[:, c["h"]:c["h"] + 1], b=c["fa"]:
                           E.scalar_tensor_tensor(d, s, cc, b, ALU.mult, ALU.subtract), mark="ve")
                    p.selfwait("vector")
                    p.emit("vector", lambda E, S, d=T["U"], s=T["Rm"]: E.tensor_copy(d, s), mark="ve")
                    c["u_mark"] = p.cnt["ve"]

                for it in range(7):
                    for c in H:
                        T, R = c["T"], c["R"]
                        p.wait("tensor", "ve", c["u_mark"])
                        if it > 0:
                            p.wait("tensor", "sc", c["evac_mark"])
                        if it < 6:
                            p.emit("tensor", lambda E, S, o=R["bu"], l=T["BT"], r=T["U"]:
                                   E.matmul(o, l, r, start=True, stop=True))
                            p.emit("tensor", lambda E, S, o=R["b2"], l=T["BT"], r=T["B"]:
                                   E.matmul(o, l, r, start=True, stop=True))
                            p.emit("tensor", lambda E, S, o=R["bt2"], l=T["B"], r=T["BT"]:
                                   E.matmul(o, l, r, start=True, stop=True), mark="te")
                        else:
                            p.emit("tensor", lambda E, S, o=R["bu"], l=T["BT"], r=T["U"]:
                                   E.matmul(o, l, r, start=True, stop=True), mark="te")
                        c["mm_mark"] = p.cnt["te"]
                    for c in H:
                        T, R = c["T"], c["R"]
                        p.wait("vector", "te", c["mm_mark"])
                        p.selfwait("vector")
                        p.emit("vector", lambda E, S, d=T["U"], s=R["bu"]:
                               E.tensor_add(d, d, s), mark="ve")
                        c["u_mark"] = p.cnt["ve"]
                        if it < 6:
                            p.wait("scalar", "te", c["mm_mark"])
                            p.selfwait("scalar")
                            p.emit("scalar", lambda E, S, d=T["B"], s=R["b2"]:
                                   E.activation(d, s, AF.Copy), mark="sc")
                            p.emit("scalar", lambda E, S, d=T["BT"], s=R["bt2"]:
                                   E.activation(d, s, AF.Copy), mark="sc")
                            c["evac_mark"] = p.cnt["sc"]
                for c in H:
                    T, R = c["T"], c["R"]
                    b0 = c["b0"]
                    p.wait("tensor", "ve", c["u_mark"])
                    p.emit("tensor", lambda E, S, o=R["O"], l=T["Mkq"], r=T["U"]:
                           E.matmul(o, l, r, start=True, stop=False))
                    p.emit("tensor", lambda E, S, o=R["O"], l=T["qh"][b0:b0 + 64, :], r=T["S0b"][b0:b0 + 64, :]:
                           E.matmul(o, l, r, start=False, stop=True), mark="te")
                    c["o_mark"] = p.cnt["te"]
                for c in H:
                    p.wait("scalar", "te", c["o_mark"])
                    p.selfwait("scalar")
                    p.emit("scalar", lambda E, S, d=Opair[:, c["b0"]:c["b0"] + 64], s=c["R"]["O"]:
                           E.activation(d, s, AF.Copy), mark="sc")
                for c in H:
                    T = c["T"]
                    p.wait("tensor", "ve", p.cnt["ve"])
                    p.wait("tensor", "sc", p.cnt["sc"])
                    p.emit("tensor", lambda E, S, o=c["pk"], s=T["kh"]:
                           E.transpose(o, s, ident_b), mark="te")
                    c["kt_mark"] = p.cnt["te"]
                    p.wait("vector", "te", c["kt_mark"])
                    p.selfwait("vector")
                    p.emit("vector", lambda E, S, d=T["kt"], s=c["pk"], cc=sC[:, c["h"]:c["h"] + 1]:
                           E.tensor_scalar_mul(d, s, cc), mark="ve")
                    c["ktd_mark"] = p.cnt["ve"]
                for c in H:
                    T, R = c["T"], c["R"]
                    p.wait("tensor", "ve", c["ktd_mark"])
                    p.emit("tensor", lambda E, S, o=R["P"], l=T["kt"], r=T["U"]:
                           E.matmul(o, l, r, start=True, stop=True), mark="te")
                    c["p_mark"] = p.cnt["te"]
                for c in H:
                    b0 = c["b0"]
                    p.wait("vector", "te", c["p_mark"])
                    p.selfwait("vector")
                    p.emit("vector", lambda E, S, d=c["fb"][b0:b0 + 64, 0:64], a=c["S_h"],
                                  cc=gCc[b0:b0 + 64, c["h"]:c["h"] + 1]:
                           E.tensor_scalar_mul(d, a, cc), mark="ve")
                    p.selfwait("vector")
                    p.emit("vector", lambda E, S, d=c["S_h"], a=c["fb"][b0:b0 + 64, 0:64],
                                  b=c["R"]["P"][b0:b0 + 64, :]:
                           E.tensor_add(d, a, b), mark="ve")
                p.wait("tensor", "sc", p.cnt["sc"])
                p.wait("tensor", "ve", p.cnt["ve"])
                p.emit("tensor", lambda E, S, o=pc_ot, s=Opair:
                       E.transpose(o, s, ident_b), mark="te")
                p.wait("scalar", "te", p.cnt["te"])
                p.selfwait("scalar")
                odst = ctx.t["arenaB"][:, (8 + fc) * 1024 + tc * 128: (8 + fc) * 1024 + tc * 128 + 128]
                p.emit("scalar", lambda E, S, d=odst, s=pc_ot:
                       E.activation(d, s, AF.Copy), mark="sc")

    if GDN_DEBUG:
        return
    def rhs_o(wi, k, t):
        return aslot(ctx, "arenaB", 8 + k, t)

    def cb_wo(wi, mc, t, ps, b, banks):
        p.wait("vector", "te", p.cnt["te"])
        p.selfwait("vector")
        R = ctx.t["R"][:, mc, t * 512:(t + 1) * 512]
        p.emit("vector", lambda E, S, d=R, s=ps, c=ctx.col("l1a_bo", mc):
               E.scalar_tensor_tensor(d, s, c, d, ALU.add, ALU.add), mark="ve")
        p.emit("vector", lambda E, S, d=x1_ap(ctx, mc, t), s=ps,
                      c=ctx.col("l1a_bo", mc), z=z_ap(mc, t):
               E.scalar_tensor_tensor(d, s, c, z, ALU.add, ALU.add), mark="ve")
        ctx.set_bank_guard(b, "ve")

    emit_linear_fm(ctx, [W["wo"]], rhs_o, cb_wo, Mc=8)

    def x1src(fc, t):
        return x1_ap(ctx, fc, t)

    def xn2_ap(fc, t):
        return aslot(ctx, "arenaB", fc, t)
    emit_ln(ctx, x1src, xn2_ap, affine=("l1a_ln2g", "l1a_ln2b"))

    def rhs_xn2(wi, k, t):
        return aslot(ctx, "arenaB", k, t)

    def h_ap(kc, t):
        if kc < 16:
            return aslot(ctx, "arenaA", kc, t)
        return aslot(ctx, "arenaB", 8 + (kc - 16), t)

    def cb_m1(wi, mc, t, ps, b, banks):
        p.wait("scalar", "te", p.cnt["te"])
        p.emit("scalar", lambda E, S, d=h_ap(mc, t), s=ps:
               E.activation(d, s, AF.Gelu, bias=ctx.col("l1a_b1", mc)), mark="sc")
        ctx.set_bank_guard(b, "sc")

    emit_linear_fm(ctx, [W["m1"]], rhs_xn2, cb_m1, Mc=32)

    def rhs_h(wi, k, t):
        return h_ap(k, t)

    def cb_m2(wi, mc, t, ps, b, banks):
        p.wait("vector", "te", p.cnt["te"])
        p.selfwait("vector")
        R = ctx.t["R"][:, mc, t * 512:(t + 1) * 512]
        p.emit("vector", lambda E, S, d=R, s=ps, c=ctx.col("l1a_b2", mc):
               E.scalar_tensor_tensor(d, s, c, d, ALU.add, ALU.add), mark="ve")
        ctx.set_bank_guard(b, "ve")

    emit_linear_fm(ctx, [W["m2"]], rhs_h, cb_m2, Mc=8)


# version 09
# speedup vs baseline: 1.0722x; 1.0153x over previous
import contextlib
import os
import numpy as np
import ml_dtypes
import concourse.bass as bass
import concourse.mybir as mybir
from concourse.bass_utils import run_bass_kernel_spmd

F32 = mybir.dt.float32
BF16 = mybir.dt.bfloat16
AF = mybir.ActivationFunctionType
ALU = mybir.AluOpType

NCORES = 8
D = 1024
SEQ = 4096
TSTRIP = 1024
NSTRIP = SEQ // TSTRIP
NT = TSTRIP // 512
FC = D // 128

ENGINES = ("sync", "tensor", "vector", "scalar", "gpsimd")
GDN_DEBUG = False



def tile_w(w):
    K, M = w.shape
    Kc = K // 128
    mw = min(M, 128)
    Mc = M // mw
    return np.ascontiguousarray(
        w.reshape(Kc, 128, Mc, mw).transpose(2, 1, 0, 3))


def fm_of(x):
    T, d = x.shape
    return np.ascontiguousarray(x.T.reshape(d // 128, 128, T).transpose(1, 0, 2))


def cols_of(v):
    return np.ascontiguousarray(v.reshape(-1, 128).T)



class Prog:

    def __init__(self):
        self.q = {e: [] for e in ENGINES}
        self.cnt = dict(dma0=0, dma1=0, dma2=0, dma3=0, stdma=0, te=0, ve=0, sc=0, gp=0)
        self.dma_i = 0

    def wait(self, eng, sem, val):
        if val <= 0:
            return
        self.q[eng].append(lambda E, S, sem=sem, val=val: E.wait_ge(S[sem], val))

    def selfwait(self, eng):
        sem = ENG_SEM[eng]
        self.wait(eng, sem, self.cnt[sem])

    def dma(self, eng, build):
        i = self.dma_i % 4
        self.dma_i += 1
        sem = "dma%d" % i
        self.wait(eng, sem, self.cnt[sem])
        self.emit(eng, build, mark=sem, n=16)
        return (sem, self.cnt[sem])

    def emit(self, eng, build, mark=None, n=1):
        if mark is not None:
            self.cnt[mark] += n

            def f(E, S, build=build, mark=mark, n=n):
                ins = build(E, S)
                ins.then_inc(S[mark], n)
            self.q[eng].append(f)
        else:
            self.q[eng].append(lambda E, S, build=build: build(E, S))

    def replay(self, nc, sems):
        with nc.Block() as block:
            for eng in ENGINES:
                ops = self.q[eng]
                if not ops:
                    continue

                def body(E, ops=ops):
                    for f in ops:
                        f(E, sems)
                getattr(block, eng)(body)


class Ctx:

    def __init__(self, nc, prog):
        self.nc = nc
        self.p = prog
        self.t = {}
        self.colreg = {}
        self.bank_guard = [None] * 4
        self.bank_i = 0
        self.wstage_guard = [None, None]
        self.wbf_guard = {}

    def col(self, name, j=0):
        s, n = self.colreg[name]
        assert 0 <= j < n, (name, j, n)
        return self.t["cols"][:, s + j:s + j + 1]

    def col1(self, name, j=0):
        s, n = self.colreg[name]
        return self.t["cols"][0:1, s + j:s + j + 1]

    def next_bank(self):
        b = self.bank_i % 4
        self.bank_i += 1
        return b

    def guard_bank(self, b):
        g = self.bank_guard[b]
        if g is not None:
            self.p.wait("tensor", g[0], g[1])

    def set_bank_guard(self, b, sem):
        self.bank_guard[b] = (sem, self.p.cnt[sem])


SEMNAMES = ("dma0", "dma1", "dma2", "dma3", "stdma", "te", "ve", "sc", "gp")
ENG_SEM = dict(tensor="te", vector="ve", scalar="sc", gpsimd="gp")



class WSrc:

    def __init__(self, dram_ap, Kc, mw=128):
        self.ap = dram_ap
        self.Kc = Kc
        self.mw = mw

    def emit_load(self, ctx, mc, dest_ap_fn):
        p = ctx.p
        kgran = max(1, 512 // self.mw)
        ng = (self.Kc + kgran - 1) // kgran
        for g in range(ng):
            k0 = g * kgran
            kn = min(kgran, self.Kc - k0)
            half = ctx.stage_i % 2
            ctx.stage_i += 1
            stage = ctx.t["wstage"][:, half * 512:half * 512 + kn * self.mw]
            stage_v = stage.rearrange("p (k f) -> p k f", f=self.mw)
            gg = ctx.wstage_guard[half]
            if gg is not None:
                p.wait("sync", gg[0], gg[1])
            src = self.ap[mc, :, k0:k0 + kn, :]
            dm = p.dma("sync", lambda E, S, d=stage_v, s=src: E.dma_start(out=d, in_=s))
            p.wait("vector", dm[0], dm[1])
            dst = dest_ap_fn(k0, kn)
            p.emit("vector", lambda E, S, d=dst, s=stage_v: E.tensor_copy(d, s), mark="ve")
            ctx.wstage_guard[half] = ("ve", p.cnt["ve"])
        return p.cnt["ve"]


class WSinCos:

    def __init__(self, theta_ap, noise_ap, is_cos, Kc=8, mw=128):
        self.a1 = theta_ap
        self.a2 = noise_ap
        self.is_cos = is_cos
        self.Kc = Kc
        self.mw = mw

    def emit_load(self, ctx, mc, dest_ap_fn):
        p = ctx.p
        kgran = 2
        ng = (self.Kc + kgran - 1) // kgran
        for g in range(ng):
            k0 = g * kgran
            kn = min(kgran, self.Kc - k0)
            half = ctx.stage_i % 2
            ctx.stage_i += 1
            base = half * 512
            st1 = ctx.t["wstage"][:, base:base + kn * 128].rearrange("p (k f) -> p k f", f=128)
            st2 = ctx.t["wstage"][:, base + 256:base + 256 + kn * 128].rearrange("p (k f) -> p k f", f=128)
            gg = ctx.wstage_guard[half]
            if gg is not None:
                p.wait("sync", gg[0], gg[1])
            s1 = self.a1[mc, :, k0:k0 + kn, :]
            s2 = self.a2[mc, :, k0:k0 + kn, :]
            dm1 = p.dma("sync", lambda E, S, d=st1, s=s1: E.dma_start(out=d, in_=s))
            dm2 = p.dma("sync", lambda E, S, d=st2, s=s2: E.dma_start(out=d, in_=s))
            p.wait("vector", dm1[0], dm1[1])
            p.wait("vector", dm2[0], dm2[1])
            p.emit("vector", lambda E, S, a=st1, b=st2: E.tensor_add(a, a, b), mark="ve")
            dst = dest_ap_fn(k0, kn)
            p.wait("scalar", "ve", p.cnt["ve"])
            if self.is_cos:
                p.emit("scalar", lambda E, S, d=dst, s=st1:
                       E.activation(d, s, AF.Sin, bias=ctxcol(ctx, "c_pi2")), mark="sc")
            else:
                p.emit("scalar", lambda E, S, d=dst, s=st1:
                       E.activation(d, s, AF.Sin), mark="sc")
            ctx.wstage_guard[half] = ("sc", p.cnt["sc"])
        return None

    uses_sc = True


def ctxcol(ctx, name, j=0):
    return ctx.col(name, j)



def rhs_barrier(ctx):
    p = ctx.p
    p.wait("tensor", "ve", p.cnt["ve"])
    p.wait("tensor", "sc", p.cnt["sc"])
    p.wait("tensor", "gp", p.cnt["gp"])


def war_barrier(ctx, eng):
    ctx.p.wait(eng, "te", ctx.p.cnt["te"])


def emit_linear_fm(ctx, weights, rhs_fn, out_cb, Mc, T=TSTRIP):
    p = ctx.p
    nw = len(weights)
    Kc = weights[0].Kc
    mw = weights[0].mw
    nt = T // 512
    reg = Kc * mw
    assert nw * 2 * reg <= 8192
    ctx.stage_i = 0
    rhs_barrier(ctx)
    ctx.pass_id = getattr(ctx, "pass_id", 0) + 1
    pass_te0 = p.cnt["te"]

    def wreg(wi, parity):
        off = (wi * 2 + parity) * reg
        return ctx.t["wbf"][:, off:off + reg].rearrange("p (k f) -> p k f", f=mw)

    for mc in range(Mc):
        parity = mc % 2
        for wi, w in enumerate(weights):
            g = ctx.wbf_guard.get((wi, parity))
            if g is None or g[0] != ctx.pass_id:
                g = (ctx.pass_id, "te", pass_te0)
            p.wait("vector", g[1], g[2])
            if getattr(w, "uses_sc", False):
                p.wait("scalar", g[1], g[2])

            def dest_fn(k0, kn, wi=wi, parity=parity):
                return wreg(wi, parity)[:, k0:k0 + kn, :]
            w.emit_load(ctx, mc, dest_fn)
            if getattr(w, "uses_sc", False):
                p.wait("tensor", "sc", p.cnt["sc"])
            else:
                p.wait("tensor", "ve", p.cnt["ve"])
        for t in range(nt):
            banks = []
            for wi, w in enumerate(weights):
                b = ctx.next_bank()
                banks.append(b)
                ctx.guard_bank(b)
                ps = ctx.t["psA"][:, b, 0:512]
                wv = wreg(wi, parity)
                for k in range(Kc):
                    p.emit("tensor",
                           lambda E, S, o=ps[0:mw, :], l=wv[:, k, :], r=rhs_fn(wi, k, t),
                                  st=(k == 0), sp=(k == Kc - 1):
                           E.matmul(o, l, r, start=st, stop=sp),
                           mark="te" if k == Kc - 1 else None)
                out_cb(wi, mc, t, ps[0:mw, :], b, banks)
        ctx.wbf_guard[(0, parity)] = (ctx.pass_id, "te", p.cnt["te"])
        if nw > 1:
            ctx.wbf_guard[(1, parity)] = (ctx.pass_id, "te", p.cnt["te"])



def emit_ln(ctx, src_fn, dst_fn, affine=None, src_bf16=False):
    p = ctx.p
    scr = ctx.t["scr"]
    scr2 = ctx.t["scr2"]
    A_t = scr[:, 0:512]
    B_t = scr[:, 512:1024]
    Cs = scr[:, 1024:1536]
    Ds = scr[:, 1536:2048]
    Csl = [Cs, Ds]
    for t in range(NT):
        war_barrier(ctx, "vector")
        p.selfwait("vector")
        p.selfwait("scalar")
        p.wait("vector", "gp", p.cnt["gp"])
        p.wait("scalar", "gp", p.cnt["gp"])
        cast_marks = []
        if src_bf16:
            statsrc = src_fn
            cast_marks = [p.cnt["ve"]] * FC
        else:
            statsrc = dst_fn
            for fc in range(FC):
                p.emit("vector", lambda E, S, d=dst_fn(fc, t), s=src_fn(fc, t):
                       E.tensor_copy(d, s), mark="ve")
                cast_marks.append(p.cnt["ve"])
        sq_marks = []
        qmm_marks = []
        ps_s = ctx.t["psB"][0:1, 0, 0:512]
        ps_q = ctx.t["psB"][0:1, 1, 0:512]
        for fc in range(FC):
            p.wait("tensor", "ve", cast_marks[fc])
            p.emit("tensor", lambda E, S, o=ps_s, l=ctx.t["cons_bf"][:, 0:1], r=statsrc(fc, t),
                          st=(fc == 0), sp=(fc == FC - 1):
                   E.matmul(o, l, r, start=st, stop=sp),
                   mark="te" if fc == FC - 1 else None)
        smark = p.cnt["te"]
        for fc in range(FC):
            sqslot = ctx.t["scr2"][:, (fc % 2) * 512:(fc % 2) * 512 + 512]
            p.wait("scalar", "ve", cast_marks[fc])
            if fc >= 2:
                p.wait("scalar", "te", qmm_marks[fc - 2])
            p.emit("scalar", lambda E, S, d=sqslot, s=statsrc(fc, t):
                   E.activation(d, s, AF.Square), mark="sc")
            p.wait("tensor", "sc", p.cnt["sc"])
            p.emit("tensor", lambda E, S, o=ps_q, l=ctx.t["cons_bf"][:, 0:1], r=sqslot,
                          st=(fc == 0), sp=(fc == FC - 1):
                   E.matmul(o, l, r, start=st, stop=sp), mark="te")
            qmm_marks.append(p.cnt["te"])
        qmark = p.cnt["te"]
        ve2_mark = [0]
        sq_s = Ds[0:1, 0:512]
        var = B_t[0:1, 0:512]
        a_row = Cs[0:1, 0:512]
        p.wait("scalar", "te", smark)
        p.emit("scalar", lambda E, S, d=sq_s, s=ps_s:
               E.activation(d, s, AF.Square, scale=ctx.col1("c_r1024")), mark="sc")
        p.wait("vector", "te", qmark)
        p.wait("vector", "sc", p.cnt["sc"])
        p.emit("vector", lambda E, S, d=var, q=ps_q, ss=sq_s:
               E.scalar_tensor_tensor(d, q, ctx.col1("c_r1024"), ss, ALU.mult, ALU.subtract),
               mark="ve")
        p.wait("scalar", "ve", p.cnt["ve"])
        p.emit("scalar", lambda E, S, d=sq_s, s=var:
               E.activation(d, s, AF.Ln, bias=ctx.col1("c_eps")), mark="sc")
        p.selfwait("scalar")
        p.emit("scalar", lambda E, S, d=a_row, s=sq_s:
               E.activation(d, s, AF.Exp, scale=ctx.col1("c_neghalf")), mark="sc")
        p.wait("vector", "sc", p.cnt["sc"])
        p.emit("vector", lambda E, S, d=var, s=ps_s, ar=a_row:
               E.scalar_tensor_tensor(d, s, ctx.col1("c_nr1024"), ar, ALU.mult, ALU.mult),
               mark="ve")
        b_row = var
        onesrow_f = ctx.t["cons_f"][0:1, CF_UTRI:CF_UTRI + 128]
        p.wait("tensor", "ve", p.cnt["ve"])
        p.emit("tensor", lambda E, S, o=ctx.t["psB"][:, 0, 0:512], l=onesrow_f, r=a_row:
               E.matmul(o, l, r, start=True, stop=True), mark="te")
        p.emit("tensor", lambda E, S, o=ctx.t["psB"][:, 1, 0:512], l=onesrow_f, r=b_row:
               E.matmul(o, l, r, start=True, stop=True), mark="te")
        p.wait("scalar", "te", p.cnt["te"])
        p.emit("scalar", lambda E, S, d=A_t, s=ctx.t["psB"][:, 0, 0:512]:
               E.activation(d, s, AF.Copy), mark="sc")
        p.emit("scalar", lambda E, S, d=B_t, s=ctx.t["psB"][:, 1, 0:512]:
               E.activation(d, s, AF.Copy), mark="sc")
        bc_mark = p.cnt["sc"]
        p.wait("vector", "sc", bc_mark)
        p.wait("gpsimd", "sc", bc_mark)
        gp_marks = [0, 0]
        for fc in range(FC):
            sl = fc % 2
            if affine is None:
                if fc >= 2:
                    p.wait("vector", "gp", gp_marks[sl])
                p.emit("vector", lambda E, S, d=Csl[sl], s=src_fn(fc, t), a=A_t:
                       E.tensor_mul(d, s, a), mark="ve")
                p.wait("gpsimd", "ve", p.cnt["ve"])
                p.emit("gpsimd", lambda E, S, d=dst_fn(fc, t), c=Csl[sl], b=B_t:
                       E.tensor_add(d, c, b), mark="gp")
                gp_marks[sl] = p.cnt["gp"]
            else:
                wn, bn = affine
                if fc > 0:
                    p.wait("vector", "gp", p.cnt["gp"])
                p.emit("vector", lambda E, S, d=Cs, s=src_fn(fc, t), a=A_t:
                       E.tensor_mul(d, s, a), mark="ve")
                p.wait("gpsimd", "ve", p.cnt["ve"])
                if fc > 0:
                    p.wait("gpsimd", "ve", ve2_mark[0])
                p.emit("gpsimd", lambda E, S, d=Ds, c=Cs, b=B_t:
                       E.tensor_add(d, c, b), mark="gp")
                p.wait("vector", "gp", p.cnt["gp"])
                p.emit("vector", lambda E, S, d=dst_fn(fc, t), s=Ds,
                              w=ctx.col(wn, fc), b=ctx.col(bn, fc):
                       E.tensor_scalar(d, s, w, b, ALU.mult, ALU.add), mark="ve")
                ve2_mark[0] = p.cnt["ve"]
        ctx.ln_done = ("gp" if affine is None else "ve",
                       p.cnt["gp" if affine is None else "ve"])



def aslot(ctx, arena, slot, t, n=1):
    a = ctx.t[arena]
    return a[:, slot * 1024 + t * 512: slot * 1024 + t * 512 + 512]


def aslot_tokens(ctx, arena, slot, t0, tl):
    a = ctx.t[arena]
    return a[:, slot * 1024 + t0: slot * 1024 + t0 + tl]


def x1_ap(ctx, fc, t):
    a = ctx.t["arenaA"]
    off = 32768 // 2 + (fc * 1024 + t * 512) * 2
    return a[:, off:off + 1024].bitcast(F32)



def emit_l0ffn(ctx, W):
    p = ctx.p
    silu_slot = [ctx.t["scr2"][:, i * 512:(i + 1) * 512] for i in range(2)]

    def rhs_z(wi, k, t):
        return aslot(ctx, "arenaA", k, t)

    def cb_w12(wi, mc, t, ps, b, banks):
        if wi == 0:
            p.wait("scalar", "te", p.cnt["te"])
            p.wait("scalar", "ve", p.cnt["ve"])
            p.emit("scalar", lambda E, S, d=silu_slot[t], s=ps:
                   E.activation(d, s, AF.Silu, scale=ctx.col("l0f_s1", mc)), mark="sc")
            ctx.set_bank_guard(b, "sc")
        else:
            p.wait("vector", "te", p.cnt["te"])
            p.wait("vector", "sc", p.cnt["sc"])
            dst = aslot(ctx, "arenaB", mc, t)
            p.emit("vector", lambda E, S, d=dst, s=ps, sl=silu_slot[t]:
                   E.scalar_tensor_tensor(d, s, ctx.col("l0f_s2", mc), sl, ALU.mult, ALU.mult),
                   mark="ve")
            ctx.set_bank_guard(b, "ve")
            ctx.set_bank_guard(banks[0], "ve")

    emit_linear_fm(ctx, [W["w1"], W["w2"]], rhs_z, cb_w12, Mc=32)

    def rhs_a(wi, k, t):
        return aslot(ctx, "arenaB", k, t)

    def cb_h1(wi, mc, t, ps, b, banks):
        p.wait("scalar", "te", p.cnt["te"])
        dst = aslot(ctx, "arenaA", mc, t)
        p.emit("scalar", lambda E, S, d=dst, s=ps:
               E.activation(d, s, AF.Gelu, scale=ctx.col("l0f_h1s", mc)), mark="sc")
        ctx.set_bank_guard(b, "sc")

    emit_linear_fm(ctx, [W["h1"]], rhs_a, cb_h1, Mc=32)

    def rhs_b(wi, k, t):
        return aslot(ctx, "arenaA", k, t)

    def cb_h2(wi, mc, t, ps, b, banks):
        p.wait("scalar", "te", p.cnt["te"])
        dst = aslot(ctx, "arenaB", mc, t)
        p.emit("scalar", lambda E, S, d=dst, s=ps:
               E.activation(d, s, AF.Gelu, scale=ctx.col("l0f_h2s", mc)), mark="sc")
        ctx.set_bank_guard(b, "sc")

    emit_linear_fm(ctx, [W["h2"]], rhs_b, cb_h2, Mc=32)

    def rhs_c(wi, k, t):
        return aslot(ctx, "arenaB", k, t)

    def cb_w3(wi, mc, t, ps, b, banks):
        p.wait("vector", "te", p.cnt["te"])
        R = ctx.t["R"][:, mc, t * 512:(t + 1) * 512]
        p.emit("vector", lambda E, S, d=R, s=ps:
               E.scalar_tensor_tensor(d, s, ctx.col("l0f_hs3", mc), d, ALU.mult, ALU.add),
               mark="ve")
        ctx.set_bank_guard(b, "ve")

    emit_linear_fm(ctx, [W["w3"]], rhs_c, cb_w3, Mc=8)



def emit_l1ffn(ctx, W):
    p = ctx.p
    scr = ctx.t["scr"]
    Cs, Ds = scr[:, 1024:1536], scr[:, 1536:2048]

    def rhs_z(wi, k, t):
        return aslot(ctx, "arenaA", k, t)

    def mag_ap(mc, t):
        a = ctx.t["arenaB"]
        off = (mc * 1024 + t * 512) * 2
        return a[:, off:off + 1024].bitcast(F32)

    def cb(wi, mc, t, ps, b, banks):
        if wi == 0:
            return
        p.wait("scalar", "te", p.cnt["te"])
        p.wait("scalar", "ve", p.cnt["ve"])
        ps_re = ctx.t["psA"][:, banks[0], 0:512]
        p.emit("scalar", lambda E, S, d=Cs, s=ps_re:
               E.activation(d, s, AF.Square), mark="sc")
        p.emit("scalar", lambda E, S, d=Ds, s=ps:
               E.activation(d, s, AF.Square), mark="sc")
        p.wait("vector", "sc", p.cnt["sc"])
        p.emit("vector", lambda E, S, d=mag_ap(mc, t), a=Cs, bb=Ds:
               E.tensor_add(d, a, bb), mark="ve")
        ctx.set_bank_guard(b, "ve")
        ctx.set_bank_guard(banks[0], "sc")

    emit_linear_fm(ctx, [W["cos"], W["sin"]], rhs_z, cb, Mc=8)
    p.wait("scalar", "ve", p.cnt["ve"])
    for mc in range(8):
        for t in range(NT):
            p.wait("scalar", "ve", p.cnt["ve"])
            p.selfwait("scalar")
            p.emit("scalar", lambda E, S, d=Ds, s=mag_ap(mc, t):
                   E.activation(d, s, AF.Sqrt, scale=ctx.col("c_quarter")), mark="sc")
            p.wait("vector", "sc", p.cnt["sc"])
            p.selfwait("vector")
            R = ctx.t["R"][:, mc, t * 512:(t + 1) * 512]
            p.emit("vector", lambda E, S, d=R, s=Ds: E.tensor_add(d, d, s), mark="ve")



BLOCKS = ("l0ffn_a", "l0attn", "l0ffn_b", "l1ffn_a", "l1attn", "l1ffn_b")


def make_consts():
    i128 = np.arange(128)
    ident_f = np.eye(128, dtype=np.float32)
    utri_f = (i128[:, None] <= i128[None, :]).astype(np.float32)
    bf = ml_dtypes.bfloat16
    ident_b = ident_f.astype(bf)
    utri_b = utri_f.astype(bf)
    ones_b = np.ones((128, 128), bf)
    maskLneg = -np.tril(np.ones((128, 128), np.float32), -1)
    maskUneg = -np.triu(np.ones((128, 128), np.float32), 1)
    knsel = np.zeros((128, 128), np.float32)
    for fc in range(8):
        for j in range(16):
            if j // 2 == fc:
                rows = slice((j % 2) * 64, (j % 2) * 64 + 64)
                knsel[rows, fc * 16 + j] = 1.0
    BIG = -1e30
    mb_le = np.where(i128[:, None] <= i128[None, :], 0.0, BIG).astype(np.float32)
    mb_lt = np.where(i128[:, None] < i128[None, :], 0.0, BIG).astype(np.float32)
    mb_gt = np.where(i128[:, None] > i128[None, :], 0.0, BIG).astype(np.float32)
    ones_f = np.ones((128, 128), np.float32)
    cons_f = np.concatenate([ident_f, utri_f, ones_f, mb_le, mb_lt, mb_gt], axis=1)
    cons_b = np.concatenate([np.ones((128, 1), np.float32), ident_f, utri_f,
                             np.ones((128, 128), np.float32), maskLneg, maskUneg,
                             knsel], axis=1).astype(bf)
    return cons_f, cons_b

CB_ONESCOL = 0
CB_IDENT = 1
CB_UTRI = 129
CB_ONES = 257
CB_MASKL = 385
CB_MASKU = 513
CB_KNSEL = 641
CF_IDENT = 0
CF_UTRI = 128
CF_ONES = 256
CF_MBLE = 384
CF_MBLT = 512
CF_MBGT = 640


def build_cols(inputs):
    reg = {}
    colsl = []

    def add(name, arr):
        arr = np.asarray(arr, np.float32)
        if arr.ndim == 1:
            if arr.size % 128 == 0:
                c = cols_of(arr)
            else:
                raise ValueError(name)
        else:
            c = arr
        reg[name] = (sum(x.shape[1] for x in colsl), c.shape[1])
        colsl.append(c)

    def addc(name, v):
        add(name, np.full((128, 1), v, np.float32))

    add("l0f_s1", inputs["L0_ffn_w1_s"])
    add("l0f_s2", inputs["L0_ffn_w2_s"])
    add("l0f_h1s", inputs["L0_ffn_h1_s"])
    add("l0f_h2s", inputs["L0_ffn_h2_s"])
    add("l0f_s3", inputs["L0_ffn_w3_s"])
    add("l0f_hs3", inputs["L0_ffn_w3_s"])
    add("l0a_n1w", inputs["L0_attn_norm1_w"])
    add("l0a_n1b", inputs["L0_attn_norm1_b"])
    add("l0a_n2w", inputs["L0_attn_norm2_w"])
    add("l0a_n2b", inputs["L0_attn_norm2_b"])
    add("l0a_gb", inputs["L0_attn_gate_b"])
    add("l0a_ob", inputs["L0_attn_out_b"])
    add("l0a_mb1", inputs["L0_attn_mlp_b1"])
    add("l0a_mb2", inputs["L0_attn_mlp_b2"])
    add("l1a_ln1g", inputs["L1_attn_ln1_g"])
    add("l1a_ln1b", inputs["L1_attn_ln1_b"])
    add("l1a_ln2g", inputs["L1_attn_ln2_g"])
    add("l1a_ln2b", inputs["L1_attn_ln2_b"])
    add("l1a_bo", inputs["L1_attn_bo"])
    add("l1a_b1", inputs["L1_attn_b1"])
    add("l1a_b2", inputs["L1_attn_b2"])
    addc("c_1", 1.0)
    addc("c_quarter", 0.25)
    addc("c_eighth", 0.125)
    addc("c_eps", 1e-5)
    addc("c_r1024", 1.0 / 1024.0)
    addc("c_nr1024", -1.0 / 1024.0)
    addc("c_neg1", -1.0)
    addc("c_neghalf", -0.5)
    addc("c_1e6", 1e-6)
    addc("c_pi2", float(np.pi / 2))
    addc("c_half", 0.5)
    cols = np.concatenate(colsl, axis=1)
    return cols, reg


def prep_host(inputs):
    h = {}
    h["W_l0f_w1"] = tile_w(np.asarray(inputs["L0_ffn_w1_w"]).T)
    h["W_l0f_w2"] = tile_w(np.asarray(inputs["L0_ffn_w2_w"]).T)
    h["W_l0f_h1"] = tile_w(np.asarray(inputs["L0_ffn_h1_w"]).T)
    h["W_l0f_h2"] = tile_w(np.asarray(inputs["L0_ffn_h2_w"]).T)
    h["W_l0f_w3"] = tile_w(np.asarray(inputs["L0_ffn_w3_w"]).T)
    h["W_l0a_gate"] = tile_w(np.asarray(inputs["L0_attn_gate_w"]))
    h["W_l0a_state"] = np.ascontiguousarray(np.asarray(inputs["L0_attn_state_w"]))
    h["W_l0a_out"] = tile_w(np.asarray(inputs["L0_attn_out_w"]))
    h["W_l0a_mlp1"] = tile_w(np.asarray(inputs["L0_attn_mlp_w1"]))
    h["W_l0a_mlp2"] = tile_w(np.asarray(inputs["L0_attn_mlp_w2"]))
    h["W_l1a_q"] = tile_w(np.asarray(inputs["L1_attn_Wq"]))
    h["W_l1a_k"] = tile_w(np.asarray(inputs["L1_attn_Wk"]))
    h["W_l1a_v"] = tile_w(np.asarray(inputs["L1_attn_Wv"]))
    h["W_l1a_b"] = tile_w(np.asarray(inputs["L1_attn_Wb"]))
    h["W_l1a_g"] = tile_w(np.asarray(inputs["L1_attn_Wg"]))
    h["W_l1a_o"] = tile_w(np.asarray(inputs["L1_attn_Wo"]))
    h["W_l1a_m1"] = tile_w(np.asarray(inputs["L1_attn_w1"]))
    h["W_l1a_m2"] = tile_w(np.asarray(inputs["L1_attn_w2"]))
    h["W_l1f_th"] = tile_w(np.asarray(inputs["L1_ffn_theta"]))
    h["W_l1f_thn"] = tile_w(np.asarray(inputs["L1_ffn_theta_noise"]))
    h["W_l1f_ph"] = tile_w(np.asarray(inputs["L1_ffn_phi"]))
    h["W_l1f_phn"] = tile_w(np.asarray(inputs["L1_ffn_phi_noise"]))
    cols, reg = build_cols(inputs)
    h["cols_in"] = cols
    h["rows_in"] = np.asarray(inputs["L0_attn_state_b"], np.float32)[None, :]
    cf, cb = make_consts()
    h["cons_f"] = cf
    h["cons_b"] = cb
    return h, reg


def declare_drams(nc, h, blocks, fused, rin_shape):
    d = {}
    for k, v in h.items():
        d[k] = nc.dram_tensor(k, list(v.shape), BF16 if v.dtype == ml_dtypes.bfloat16 else F32,
                              kind="ExternalInput").ap()
    d["x_fm"] = nc.dram_tensor("x_fm", list(rin_shape), F32, kind="ExternalInput").ap()
    d["y_fm"] = nc.dram_tensor("y_fm", list(rin_shape), F32, kind="ExternalOutput").ap()
    return d


def alloc_tensors(ctx, es):
    nc = ctx.nc
    t = ctx.t
    t["R"] = es.enter_context(nc.sbuf_tensor("sb_R", [128, 8, 1024], F32))
    t["arenaA"] = es.enter_context(nc.sbuf_tensor("sb_arenaA", [128, 32768], BF16))
    t["arenaB"] = es.enter_context(nc.sbuf_tensor("sb_arenaB", [128, 32768], BF16))
    t["wstage"] = es.enter_context(nc.sbuf_tensor("sb_wstage", [128, 1024], F32))
    t["wbf"] = es.enter_context(nc.sbuf_tensor("sb_wbf", [128, 8192], BF16))
    t["scr"] = es.enter_context(nc.sbuf_tensor("sb_scr", [128, 2048], F32))
    t["scr2"] = es.enter_context(nc.sbuf_tensor("sb_scr2", [128, 3584], BF16))
    t["cols"] = es.enter_context(nc.sbuf_tensor("sb_cols", [128, 384], F32))
    t["cons_f"] = es.enter_context(nc.sbuf_tensor("sb_cons_f", [128, 768], F32))
    t["cons_bf"] = es.enter_context(nc.sbuf_tensor("sb_cons_bf", [128, 769], BF16))
    t["Sstate"] = es.enter_context(nc.sbuf_tensor("sb_Sstate", [128, 8, 64], F32))
    t["rows_bf"] = es.enter_context(nc.sbuf_tensor("sb_rows_bf", [1, 1040], BF16))
    t["carry"] = es.enter_context(nc.sbuf_tensor("sb_carry", [1, 1024], BF16))
    t["wbar"] = es.enter_context(nc.sbuf_tensor("sb_wbar", [128, 18], F32))
    t["psA"] = es.enter_context(nc.psum_tensor("sb_psA", [128, 4, 512], F32))
    t["psB"] = es.enter_context(nc.psum_tensor("sb_psB", [128, 3, 512], F32))
    t["psC"] = es.enter_context(nc.psum_tensor("sb_psC", [128, 1024], BF16))


def emit_init(ctx, d):
    p = ctx.p
    rows = ctx.t["scr"][0:1, 0:1024]
    dms = []
    for name, dst in [("cols_in", "cols"), ("cons_f", "cons_f"), ("cons_b", "cons_bf")]:
        nn = d[name].shape[1]
        dms.append(p.dma("sync", lambda E, S, dd=ctx.t[dst][:, 0:nn], s=d[name]:
                         E.dma_start(out=dd, in_=s)))
    dms.append(p.dma("sync", lambda E, S, dd=rows, s=d["rows_in"]: E.dma_start(out=dd, in_=s)))
    for sem, val in dms:
        p.wait("vector", sem, val)
    s, n = ctx.colreg["l0f_hs3"]
    p.emit("vector", lambda E, S: E.tensor_scalar_mul(
        ctx.t["cols"][:, s:s + n], ctx.t["cols"][:, s:s + n], 0.5), mark="ve")
    p.emit("vector", lambda E, S, rows=rows: E.tensor_copy(ctx.t["rows_bf"][:, 0:1024], rows),
           mark="ve")
    bbar_f = ctx.t["scr"][0:1, 1024:1025]
    p.emit("vector", lambda E, S, rows=rows, d=bbar_f: E.reduce_sum(d, rows, axis=mybir.AxisListType.X),
           mark="ve")
    p.selfwait("vector")
    p.emit("vector", lambda E, S, d=ctx.t["rows_bf"][0:1, 1024:1025], s=bbar_f:
           E.tensor_scalar_mul(d, s, 1.0 / 1024.0), mark="ve")
    p.emit("vector", lambda E, S: E.memset(ctx.t["carry"][:], 0.0), mark="ve")
    p.emit("vector", lambda E, S: E.memset(ctx.t["scr2"][:], 0.0), mark="ve")
    p.emit("vector", lambda E, S: E.memset(ctx.t["arenaA"][:], 0.0), mark="ve")
    p.emit("vector", lambda E, S: E.memset(ctx.t["arenaB"][:], 0.0), mark="ve")
    p.selfwait("vector")
    if os.environ.get("BASS_SIM_FP64") == "1" or os.environ.get("KMK_SIM") == "1":
        p.emit("vector", lambda E, S: E.memset(ctx.t["psC"][:], 0.0), mark="ve")
        p.emit("vector", lambda E, S: E.memset(ctx.t["psB"][:], 0.0), mark="ve")
    p.emit("vector", lambda E, S: E.memset(ctx.t["scr"][:], 0.0), mark="ve")
    p.emit("vector", lambda E, S: E.memset(ctx.t["Sstate"][:], 0.0), mark="ve")
    p.emit("vector", lambda E, S: E.memset(ctx.t["wbar"][:], 0.0), mark="ve")
    ctx.init_done = p.cnt["ve"]


def emit_strip(ctx, d, W, strip, blocks):
    p = ctx.p
    t0 = strip * TSTRIP
    ctx.rload = p.dma("sync", lambda E, S: E.dma_start(
        out=ctx.t["R"][:], in_=d["x_fm"][:, :, t0:t0 + TSTRIP]))

    def Rsrc(fc, t):
        return ctx.t["R"][:, fc, t * 512:(t + 1) * 512]

    def zdst(fc, t):
        return aslot(ctx, "arenaA", fc, t)

    first = True
    for blk in blocks:
        if first:
            p.wait("vector", ctx.rload[0], ctx.rload[1])
            first = False
        if blk in ("l0ffn_a", "l0ffn_b"):
            emit_ln(ctx, Rsrc, zdst, affine=None)
            emit_l0ffn(ctx, {k: W[k] for k in ("w1", "w2", "h1", "h2", "w3")})
        elif blk in ("l1ffn_a", "l1ffn_b"):
            emit_ln(ctx, Rsrc, zdst, affine=None)
            emit_l1ffn(ctx, {"cos": W["cos"], "sin": W["sin"]})
        elif blk == "l0attn":
            emit_ln(ctx, Rsrc, zdst, affine=None)
            emit_l0attn(ctx, W)
        elif blk == "l1attn":
            emit_ln(ctx, Rsrc, zdst, affine=None)
            emit_l1attn(ctx, W)
        elif blk == "lntest":
            emit_ln(ctx, Rsrc, zdst, affine=None)
            for t in range(NT):
                for fc in range(FC):
                    p.wait("vector", "gp", p.cnt["gp"])
                    p.emit("vector", lambda E, S, d=Rsrc(fc, t), s=zdst(fc, t):
                           E.tensor_copy(d, s), mark="ve")
        elif blk == "lnafftest":
            emit_ln(ctx, Rsrc, zdst, affine=("l0a_n1w", "l0a_n1b"))
            p.selfwait("vector")
            for t in range(NT):
                for fc in range(FC):
                    p.selfwait("vector")
                    p.emit("vector", lambda E, S, d=Rsrc(fc, t), s=zdst(fc, t):
                           E.tensor_copy(d, s), mark="ve")
    p.wait("gpsimd", "ve", p.cnt["ve"])
    p.wait("gpsimd", "sc", p.cnt["sc"])
    p.wait("gpsimd", "stdma", p.cnt["stdma"])
    p.emit("gpsimd", lambda E, S: E.dma_start(
        out=d["y_fm"][:, :, t0:t0 + TSTRIP], in_=ctx.t["R"][:]), mark="stdma", n=16)
    p.wait("sync", "stdma", p.cnt["stdma"])


def make_wsrcs(ctx, d):
    return dict(
        w1=WSrc(d["W_l0f_w1"], 8), w2=WSrc(d["W_l0f_w2"], 8),
        h1=WSrc(d["W_l0f_h1"], 32), h2=WSrc(d["W_l0f_h2"], 32),
        w3=WSrc(d["W_l0f_w3"], 32),
        gate=WSrc(d["W_l0a_gate"], 8), out=WSrc(d["W_l0a_out"], 8),
        mlp1=WSrc(d["W_l0a_mlp1"], 8), mlp2=WSrc(d["W_l0a_mlp2"], 32),
        state_nat=d["W_l0a_state"],
        q=WSrc(d["W_l1a_q"], 8), k=WSrc(d["W_l1a_k"], 8), v=WSrc(d["W_l1a_v"], 8),
        wb=WSrc(d["W_l1a_b"], 8, mw=16), wg=WSrc(d["W_l1a_g"], 8, mw=16),
        wo=WSrc(d["W_l1a_o"], 8), m1=WSrc(d["W_l1a_m1"], 8), m2=WSrc(d["W_l1a_m2"], 32),
        cos=WSinCos(d["W_l1f_th"], d["W_l1f_thn"], True),
        sin=WSinCos(d["W_l1f_ph"], d["W_l1f_phn"], False),
    )


def build_program(h, reg, blocks, n_strips=NSTRIP):
    nc = bass.Bass("TRN2", target_bir_lowering=False)
    d = declare_drams(nc, h, blocks, True, [128, 8, SEQ])
    prog = Prog()
    ctx = Ctx(nc, prog)
    ctx.colreg = reg
    es = contextlib.ExitStack()
    with es:
        alloc_tensors(ctx, es)
        sems = {}
        for sn in SEMNAMES:
            sems[sn] = es.enter_context(nc.semaphore(sn))
        emit_init(ctx, d)
        W = make_wsrcs(ctx, d)
        for strip in range(n_strips):
            emit_strip(ctx, d, W, strip, blocks)
        prog.wait("gpsimd", "stdma", prog.cnt["stdma"])
        prog.replay(nc, sems)
    return nc


def kernel(**inputs):
    h, reg = prep_host(inputs)
    x = np.asarray(inputs["L0_attn_x"])
    nc = build_program(h, reg, list(BLOCKS))
    in_maps = []
    for c in range(NCORES):
        m = dict(h)
        m["x_fm"] = fm_of(x[c])
        in_maps.append(m)
    res = run_bass_kernel_spmd(nc, in_maps, core_ids=list(range(NCORES)))
    out = np.empty((NCORES, SEQ, D), np.float32)
    for c in range(NCORES):
        y = res.results[c]["y_fm"]
        out[c] = y.transpose(1, 0, 2).reshape(D, SEQ).T
    return out


def emit_tok_state_pass(ctx, W, srcs, dsts, seed_row, wbar=True):
    p = ctx.p
    rhs_barrier(ctx)
    ctx.pass_id = getattr(ctx, "pass_id", 0) + 1
    pass_te0 = p.cnt["te"]
    wk = [ctx.t["wbf"][:, k * 1024:(k + 1) * 1024] for k in range(8)]
    ctx.stage_i = 0
    for k in range(8):
        for half in range(2):
            hs = ctx.stage_i % 2
            ctx.stage_i += 1
            stage = ctx.t["wstage"][:, hs * 512:(hs + 1) * 512]
            gg = ctx.wstage_guard[hs]
            if gg is not None:
                p.wait("sync", gg[0], gg[1])
            src = W[k * 128:(k + 1) * 128, half * 512:(half + 1) * 512]
            dm = p.dma("sync", lambda E, S, d=stage, s=src: E.dma_start(out=d, in_=s))
            p.wait("vector", dm[0], dm[1])
            if k == 0 and half == 0:
                p.wait("vector", "te", pass_te0)
            p.emit("vector", lambda E, S, d=wk[k][:, half * 512:(half + 1) * 512], s=stage:
                   E.tensor_copy(d, s), mark="ve")
            ctx.wstage_guard[hs] = ("ve", p.cnt["ve"])
        if wbar:
            p.selfwait("vector")
            p.emit("vector", lambda E, S, d=ctx.t["wbar"][:, k:k + 1], s=wk[k]:
                   E.reduce_sum(d, s, axis=mybir.AxisListType.X), mark="ve")
    if wbar:
        p.selfwait("vector")
        p.emit("vector", lambda E, S: E.tensor_scalar_mul(
            ctx.t["wbar"][:, 0:8], ctx.t["wbar"][:, 0:8], 1.0 / 1024.0), mark="ve")
        p.selfwait("vector")
        p.emit("vector", lambda E, S: E.tensor_copy(
            ctx.t["scr2"][:, 3520:3528], ctx.t["wbar"][:, 0:8]), mark="ve")
    cast_done = p.cnt["ve"]
    p.wait("tensor", "ve", cast_done)
    for tc in range(8):
        for si, src_fn in enumerate(srcs):
            for half in range(2):
                b = ctx.next_bank()
                ctx.guard_bank(b)
                ps = ctx.t["psA"][:, b, 0:512]
                p.emit("tensor", lambda E, S, o=ps, r=seed_row(half):
                       E.matmul(o, ctx.t["cons_bf"][0:1, CB_ONES:CB_ONES + 128], r,
                                start=True, stop=False))
                for k in range(8):
                    p.emit("tensor", lambda E, S, o=ps, l=src_fn(k, tc),
                                  r=wk[k][:, half * 512:(half + 1) * 512], sp=(k == 7):
                           E.matmul(o, l, r, start=False, stop=sp),
                           mark="te" if k == 7 else None)
                p.wait("scalar", "te", p.cnt["te"])
                p.emit("scalar", lambda E, S, d=dsts[si](tc, half), s=ps:
                       E.activation(d, s, AF.Copy), mark="sc")
                ctx.set_bank_guard(b, "sc")


def emit_l0attn(ctx, W):
    p = ctx.p

    def z_ap(fc, t):
        return aslot(ctx, "arenaA", fc, t)

    def z_tok(kc, tc):
        return ctx.t["arenaA"][:, kc * 1024 + tc * 128: kc * 1024 + tc * 128 + 128]

    war_barrier(ctx, "vector")
    p.selfwait("vector")
    p.wait("vector", "gp", p.cnt["gp"])
    for t in range(NT):
        for fc in range(FC):
            p.emit("vector", lambda E, S, d=ctx.t["R"][:, fc, t * 512:(t + 1) * 512],
                          s=z_ap(fc, t): E.tensor_add(d, d, s), mark="ve")

    def xn1_ap(fc, t):
        return aslot(ctx, "arenaB", fc, t)
    emit_ln(ctx, z_ap, xn1_ap, affine=("l0a_n1w", "l0a_n1b"), src_bf16=True)

    def rhs_xn1(wi, k, t):
        return aslot(ctx, "arenaB", k, t)

    def cb_gate(wi, mc, t, ps, b, banks):
        p.wait("scalar", "te", p.cnt["te"])
        dst = aslot(ctx, "arenaB", 8 + mc, t)
        p.emit("scalar", lambda E, S, d=dst, s=ps:
               E.activation(d, s, AF.Silu, bias=ctx.col("l0a_gb", mc)), mark="sc")
        ctx.set_bank_guard(b, "sc")

    emit_linear_fm(ctx, [W["gate"]], rhs_xn1, cb_gate, Mc=16)

    war_barrier(ctx, "vector")
    p.wait("vector", "sc", p.cnt["sc"])
    for t in range(NT):
        for fc in range(FC):
            p.emit("vector", lambda E, S, d=aslot(ctx, "arenaA", 8 + fc, t),
                          a=aslot(ctx, "arenaB", fc, t), g=aslot(ctx, "arenaB", 8 + fc, t):
                   E.tensor_mul(d, a, g), mark="ve")

    def xg_tok(kc, tc):
        return ctx.t["arenaA"][:, (8 + kc) * 1024 + tc * 128: (8 + kc) * 1024 + tc * 128 + 128]

    def gg_tok(kc, tc):
        return ctx.t["arenaB"][:, (16 + kc) * 1024 + tc * 128: (16 + kc) * 1024 + tc * 128 + 128]

    def u_dst(tc, half):
        return ctx.t["arenaB"][:, (24 + tc) * 1024 + half * 512: (24 + tc) * 1024 + half * 512 + 512]

    def bm_dst(tc, half):
        return ctx.t["arenaA"][:, (16 + tc) * 1024 + half * 512: (16 + tc) * 1024 + half * 512 + 512]

    def seed_row(half):
        return ctx.t["rows_bf"][0:1, half * 512:(half + 1) * 512]

    emit_tok_state_pass(ctx, W["state_nat"], [xg_tok, gg_tok], [u_dst, bm_dst], seed_row)

    scr = ctx.t["scr"]
    dcol = scr[:, 1536:1544]
    ps_d = ctx.t["psB"][:, 2, 0:1]
    for tc in range(8):
        p.wait("tensor", "sc", p.cnt["sc"])
        p.emit("tensor", lambda E, S, o=ps_d:
               E.matmul(o, ctx.t["cons_bf"][0:1, CB_ONES:CB_ONES + 128],
                        ctx.t["rows_bf"][0:1, 1024:1025], start=True, stop=False))
        wb_bf = ctx.t["scr2"][:, 3520:3528]
        for k in range(8):
            p.emit("tensor", lambda E, S, o=ps_d, l=xn1_tok(ctx, k, tc),
                          r=wb_bf[:, k:k + 1], sp=(k == 7):
                   E.matmul(o, l, r, start=False, stop=sp), mark="te" if k == 7 else None)
        p.wait("scalar", "te", p.cnt["te"])
        p.emit("scalar", lambda E, S, d=dcol[:, tc:tc + 1], s=ps_d:
               E.activation(d, s, AF.Copy), mark="sc")
    t1 = scr[:, 1544:1552]
    t2 = scr[:, 1552:1560]
    t3 = scr[:, 1560:1568]
    p.selfwait("scalar")
    p.emit("scalar", lambda E, S: E.activation(t1, dcol, AF.Abs), mark="sc")
    p.selfwait("scalar")
    p.emit("scalar", lambda E, S: E.activation(t2, t1, AF.Exp, scale=ctx.col("c_neg1")), mark="sc")
    p.selfwait("scalar")
    p.emit("scalar", lambda E, S: E.activation(t3, t2, AF.Ln, bias=ctx.col("c_1")), mark="sc")
    p.wait("vector", "sc", p.cnt["sc"])
    p.selfwait("vector")
    p.emit("vector", lambda E, S: E.tensor_relu(t1, dcol), mark="ve")
    p.selfwait("vector")
    p.emit("vector", lambda E, S: E.tensor_add(dcol, t1, t3), mark="ve")

    p.wait("vector", "sc", p.cnt["sc"])
    for tc in range(8):
        dbu = ctx.t["arenaA"][:, (24 + tc) * 1024: (25 + tc) * 1024]
        u = ctx.t["arenaB"][:, (24 + tc) * 1024: (25 + tc) * 1024]
        bm = ctx.t["arenaA"][:, (16 + tc) * 1024: (17 + tc) * 1024]
        p.emit("vector", lambda E, S, d=dbu, a=u, b=bm: E.tensor_mul(d, a, b), mark="ve")
        p.selfwait("vector")
        p.emit("vector", lambda E, S, d=dbu, c=dcol[:, tc:tc + 1]:
               E.tensor_scalar_mul(d, d, c), mark="ve")

    consf = ctx.t["cons_f"]
    utri_f = consf[:, CF_UTRI:CF_UTRI + 128]
    ident_f = consf[:, CF_IDENT:CF_IDENT + 128]
    onesrow_f = consf[0:1, CF_UTRI:CF_UTRI + 128]
    utri_b = ctx.t["cons_bf"][:, CB_UTRI:CB_UTRI + 128]
    scr2 = ctx.t["scr2"]
    M_sb = scr2[:, 2048:2176]
    coef = scr2[0:1, 2176:2304]
    dcol_sb = scr[:, 1568:1569]
    drow_sb = scr[0:1, 1569:1697]
    rhs_barrier(ctx)
    for tc in range(8):
        ps_dc = ctx.t["psB"][:, 2, 0:1]
        ps_dr = ctx.t["psB"][0:1, 2, 128:256]
        ps_bc = ctx.t["psB"][:, 2, 256:384]
        p.wait("tensor", "ve", p.cnt["ve"])
        p.wait("tensor", "sc", p.cnt["sc"])
        p.emit("tensor", lambda E, S, o=ps_dc, r=dcol[:, tc:tc + 1]:
               E.matmul(o, utri_f, r, start=True, stop=True), mark="te")
        p.wait("scalar", "te", p.cnt["te"])
        p.selfwait("scalar")
        p.emit("scalar", lambda E, S, d=dcol_sb, s=ps_dc:
               E.activation(d, s, AF.Copy), mark="sc")
        p.wait("tensor", "sc", p.cnt["sc"])
        p.emit("tensor", lambda E, S, o=ps_dr, i=dcol_sb:
               E.transpose(o, i, ident_f), mark="te")
        p.wait("scalar", "te", p.cnt["te"])
        p.selfwait("scalar")
        p.emit("scalar", lambda E, S, d=drow_sb, s=ps_dr:
               E.activation(d, s, AF.Copy), mark="sc")
        p.wait("tensor", "sc", p.cnt["sc"])
        p.emit("tensor", lambda E, S, o=ps_bc, r=drow_sb:
               E.matmul(o, onesrow_f, r, start=True, stop=True), mark="te")
        p.wait("vector", "te", p.cnt["te"])
        p.selfwait("vector")
        marg = scr[:, 0:128]
        p.emit("vector", lambda E, S, d=marg, s=ps_bc, m=consf[:, CF_MBLE:CF_MBLE + 128]:
               E.tensor_sub(d, s, m), mark="ve")
        p.wait("scalar", "ve", p.cnt["ve"])
        p.wait("scalar", "te", p.cnt["te"])
        p.selfwait("scalar")
        p.emit("scalar", lambda E, S, d=M_sb, s=marg:
               E.activation(d, s, AF.Exp, scale=ctx.col("c_neg1"), bias=dcol_sb), mark="sc")
        p.selfwait("scalar")
        p.emit("scalar", lambda E, S, d=coef, s=ps_dr:
               E.activation(d, s, AF.Exp, scale=ctx.col1("c_neg1")), mark="sc")
        p.wait("vector", "sc", p.cnt["sc"])
        b1 = ctx.next_bank()
        b2 = ctx.next_bank()
        ctx.guard_bank(b1)
        ctx.guard_bank(b2)
        p.wait("tensor", "ve", p.cnt["ve"])
        p.wait("tensor", "sc", p.cnt["sc"])
        for fc in range(8):
            bk = b1 if fc < 4 else b2
            reg = ctx.t["psA"][:, bk, (fc % 4) * 128:(fc % 4) * 128 + 128]
            dbu = ctx.t["arenaA"][:, (24 + tc) * 1024 + fc * 128: (24 + tc) * 1024 + fc * 128 + 128]
            p.emit("tensor", lambda E, S, o=reg, l=dbu, m=M_sb:
                   E.matmul(o, l, m, start=True, stop=False))
            p.emit("tensor", lambda E, S, o=reg, l=ctx.t["carry"][0:1, fc * 128:(fc + 1) * 128],
                          r=coef: E.matmul(o, l, r, start=False, stop=True),
                   mark="te" if fc == 7 else None)
        ymark = p.cnt["te"]
        ps_c0 = ctx.t["psB"][0:1, 0, 0:512]
        ps_c1 = ctx.t["psB"][0:1, 1, 0:512]
        for half in range(2):
            psc = ps_c0 if half == 0 else ps_c1
            dbuh = ctx.t["arenaA"][:, (24 + tc) * 1024 + half * 512: (24 + tc) * 1024 + half * 512 + 512]
            p.emit("tensor", lambda E, S, o=psc, l=M_sb[:, 127:128], r=dbuh:
                   E.matmul(o, l, r, start=True, stop=False))
            p.emit("tensor", lambda E, S, o=psc, l=coef[0:1, 127:128],
                          r=ctx.t["carry"][0:1, half * 512:(half + 1) * 512]:
                   E.matmul(o, l, r, start=False, stop=True), mark="te")
        p.wait("scalar", "te", p.cnt["te"])
        p.selfwait("scalar")
        p.emit("scalar", lambda E, S, d=ctx.t["carry"][0:1, 0:512], s=ps_c0:
               E.activation(d, s, AF.Copy), mark="sc")
        p.emit("scalar", lambda E, S, d=ctx.t["carry"][0:1, 512:1024], s=ps_c1:
               E.activation(d, s, AF.Copy), mark="sc")
        for i, bk in enumerate((b1, b2)):
            ydst = ctx.t["arenaB"][:, 0:8 * 1024].rearrange("p (f c) -> p f c", c=1024)[
                :, i * 4:(i + 1) * 4, tc * 128:(tc + 1) * 128]
            ysrc = ctx.t["psA"][:, bk, 0:512].rearrange("p (f c) -> p f c", c=128)
            p.emit("scalar", lambda E, S, d=ydst, s=ysrc:
                   E.activation(d, s, AF.Copy), mark="sc")
        ctx.set_bank_guard(b1, "sc")
        ctx.set_bank_guard(b2, "sc")

    def rhs_y(wi, k, t):
        return aslot(ctx, "arenaB", k, t)

    def cb_out(wi, mc, t, ps, b, banks):
        p.wait("vector", "te", p.cnt["te"])
        p.selfwait("vector")
        R = ctx.t["R"][:, mc, t * 512:(t + 1) * 512]
        p.emit("vector", lambda E, S, d=R, s=ps, c=ctx.col("l0a_ob", mc):
               E.scalar_tensor_tensor(d, s, c, d, ALU.add, ALU.add), mark="ve")
        p.emit("vector", lambda E, S, d=x1_ap(ctx, mc, t), s=ps,
                      c=ctx.col("l0a_ob", mc), z=z_ap(mc, t):
               E.scalar_tensor_tensor(d, s, c, z, ALU.add, ALU.add), mark="ve")
        ctx.set_bank_guard(b, "ve")

    emit_linear_fm(ctx, [W["out"]], rhs_y, cb_out, Mc=8)

    def x1src(fc, t):
        return x1_ap(ctx, fc, t)

    def xn2_ap(fc, t):
        return aslot(ctx, "arenaB", fc, t)
    emit_ln(ctx, x1src, xn2_ap, affine=("l0a_n2w", "l0a_n2b"))

    def rhs_xn2(wi, k, t):
        return aslot(ctx, "arenaB", k, t)

    def h_ap(kc, t):
        if kc < 16:
            return aslot(ctx, "arenaA", kc, t)
        return aslot(ctx, "arenaB", 8 + (kc - 16), t)

    def cb_m1(wi, mc, t, ps, b, banks):
        p.wait("scalar", "te", p.cnt["te"])
        p.emit("scalar", lambda E, S, d=h_ap(mc, t), s=ps:
               E.activation(d, s, AF.Gelu, bias=ctx.col("l0a_mb1", mc)), mark="sc")
        ctx.set_bank_guard(b, "sc")

    emit_linear_fm(ctx, [W["mlp1"]], rhs_xn2, cb_m1, Mc=32)

    def rhs_h(wi, k, t):
        return h_ap(k, t)

    def cb_m2(wi, mc, t, ps, b, banks):
        p.wait("vector", "te", p.cnt["te"])
        p.selfwait("vector")
        R = ctx.t["R"][:, mc, t * 512:(t + 1) * 512]
        p.emit("vector", lambda E, S, d=R, s=ps, c=ctx.col("l0a_mb2", mc):
               E.scalar_tensor_tensor(d, s, c, d, ALU.add, ALU.add), mark="ve")
        ctx.set_bank_guard(b, "ve")

    emit_linear_fm(ctx, [W["mlp2"]], rhs_h, cb_m2, Mc=8)


def xn1_tok(ctx, kc, tc):
    return ctx.t["arenaB"][:, kc * 1024 + tc * 128: kc * 1024 + tc * 128 + 128]


def emit_l1attn(ctx, W):
    p = ctx.p
    scr = ctx.t["scr"]
    scr2 = ctx.t["scr2"]
    s2f = scr2[:].bitcast(F32)
    consb = ctx.t["cons_bf"]
    consf = ctx.t["cons_f"]
    utri_f = consf[:, CF_UTRI:CF_UTRI + 128]
    ident_f = consf[:, CF_IDENT:CF_IDENT + 128]
    onesrow_f = consf[0:1, CF_UTRI:CF_UTRI + 128]
    ident_b = consb[:, CB_IDENT:CB_IDENT + 128]
    ones_b = consb[:, CB_ONES:CB_ONES + 128]
    utri_b = consb[:, CB_UTRI:CB_UTRI + 128]
    maskL = consb[:, CB_MASKL:CB_MASKL + 128]
    maskU = consb[:, CB_MASKU:CB_MASKU + 128]

    def z_ap(fc, t):
        return aslot(ctx, "arenaA", fc, t)

    war_barrier(ctx, "vector")
    p.selfwait("vector")
    p.wait("vector", "gp", p.cnt["gp"])
    for t in range(NT):
        for fc in range(FC):
            p.emit("vector", lambda E, S, d=ctx.t["R"][:, fc, t * 512:(t + 1) * 512],
                          s=z_ap(fc, t): E.tensor_add(d, d, s), mark="ve")

    def xn1_ap(fc, t):
        return aslot(ctx, "arenaB", fc, t)
    emit_ln(ctx, z_ap, xn1_ap, affine=("l1a_ln1g", "l1a_ln1b"), src_bf16=True)

    def rhs_xn1(wi, k, t):
        return aslot(ctx, "arenaB", k, t)

    def mk_cb(base_slot, scale_col):
        def cb(wi, mc, t, ps, b, banks):
            p.wait("scalar", "te", p.cnt["te"])
            dst = aslot(ctx, "arenaA", base_slot + mc, t)
            if scale_col is None:
                p.emit("scalar", lambda E, S, d=dst, s=ps:
                       E.activation(d, s, AF.Copy), mark="sc")
            else:
                p.emit("scalar", lambda E, S, d=dst, s=ps:
                       E.activation(d, s, AF.Copy, scale=ctx.col(scale_col)), mark="sc")
            ctx.set_bank_guard(b, "sc")
        return cb

    emit_linear_fm(ctx, [W["q"]], rhs_xn1, mk_cb(8, "c_eighth"), Mc=8)
    emit_linear_fm(ctx, [W["k"]], rhs_xn1, mk_cb(16, None), Mc=8)
    emit_linear_fm(ctx, [W["v"]], rhs_xn1, mk_cb(24, None), Mc=8)

    lgT = lambda tc: scr[:, 448 + tc * 16: 448 + tc * 16 + 16]
    bT = lambda tc: scr[:, 576 + tc * 16: 576 + tc * 16 + 16]
    knT = lambda tc: scr[:, 704 + tc * 16: 704 + tc * 16 + 16]
    rowtmp = scr2[0:16, 0:512]
    r1 = scr[0:16, 832:1344]
    r2 = scr[0:16, 1344:1856]
    rowf = scr[0:16, 0:512]

    def cb_bg(wi, mc, t, ps, b, banks):
        p.wait("scalar", "te", p.cnt["te"])
        p.wait("scalar", "ve", p.cnt["ve"])
        p.selfwait("scalar")
        if wi == 0:
            p.emit("scalar", lambda E, S, d=r1, s=ps:
                   E.activation(d, s, AF.Exp, scale=ctx.t["cols"][0:16,
                       ctx.colreg["c_neg1"][0]:ctx.colreg["c_neg1"][0] + 1]), mark="sc")
            p.wait("vector", "sc", p.cnt["sc"])
            p.wait("vector", "te", p.cnt["te"])
            p.selfwait("vector")
            p.emit("vector", lambda E, S, d=r1: E.tensor_scalar_add(d, d, 1.0), mark="ve")
            p.selfwait("vector")
            p.emit("vector", lambda E, S, d=r2, s=r1: E.reciprocal(d, s), mark="ve")
            p.wait("tensor", "ve", p.cnt["ve"])
        else:
            p.emit("scalar", lambda E, S, d=r1, s=ps:
                   E.activation(d, s, AF.Abs), mark="sc")
            p.selfwait("scalar")
            p.emit("scalar", lambda E, S, d=r2, s=r1:
                   E.activation(d, s, AF.Exp, scale=ctx.t["cols"][0:16,
                       ctx.colreg["c_neg1"][0]:ctx.colreg["c_neg1"][0] + 1]), mark="sc")
            p.selfwait("scalar")
            p.emit("scalar", lambda E, S, d=r1, s=r2:
                   E.activation(d, s, AF.Ln, bias=ctx.t["cols"][0:16,
                       ctx.colreg["c_1"][0]:ctx.colreg["c_1"][0] + 1]), mark="sc")
            p.wait("vector", "sc", p.cnt["sc"])
            p.wait("vector", "te", p.cnt["te"])
            p.selfwait("vector")
            p.emit("vector", lambda E, S, d=r2, s=ps:
                   E.tensor_scalar(d, s, -1.0, 0.0, ALU.mult, ALU.max), mark="ve")
            p.selfwait("vector")
            p.emit("vector", lambda E, S, d=rowf, a=r1, bb=r2:
                   E.scalar_tensor_tensor(d, a, -1.0, bb, ALU.mult, ALU.subtract), mark="ve")
            p.wait("tensor", "ve", p.cnt["ve"])
        ctx.set_bank_guard(b, "ve")
        for i in range(4):
            tc = t * 4 + i
            p.wait("tensor", "sc", p.cnt["sc"])
            if wi == 1:
                pf = ctx.t["psB"][:, 0, i * 16:(i + 1) * 16]
                p.emit("tensor", lambda E, S, o=pf, s=rowf[:, i * 128:(i + 1) * 128]:
                       E.transpose(o, s, ident_f[0:16, 0:16]), mark="te")
                p.wait("scalar", "te", p.cnt["te"])
                p.selfwait("scalar")
                p.emit("scalar", lambda E, S, d=lgT(tc), s=pf:
                       E.activation(d, s, AF.Copy), mark="sc")
            else:
                pf = ctx.t["psB"][:, 0, 64 + i * 16:64 + (i + 1) * 16]
                p.emit("tensor", lambda E, S, o=pf, s=r2[:, i * 128:(i + 1) * 128]:
                       E.transpose(o, s, ident_f[0:16, 0:16]), mark="te")
                p.wait("scalar", "te", p.cnt["te"])
                p.selfwait("scalar")
                p.emit("scalar", lambda E, S, d=bT(tc), s=pf:
                       E.activation(d, s, AF.Copy), mark="sc")

    emit_linear_fm(ctx, [W["wb"], W["wg"]], rhs_xn1, cb_bg, Mc=1)

    ksq = [scr2[:, 0:512], scr2[:, 512:1024]]
    ps_kn = ctx.t["psB"][0:16, 0, 0:512]
    for t in range(NT):
        p.wait("scalar", "ve", p.cnt["ve"])
        p.wait("scalar", "te", p.cnt["te"])
        for fc in range(FC):
            p.selfwait("scalar")
            if fc >= 2:
                p.wait("scalar", "te", p.cnt["te"])
            p.emit("scalar", lambda E, S, d=ksq[fc % 2], s=aslot(ctx, "arenaA", 16 + fc, t):
                   E.activation(d, s, AF.Square), mark="sc")
            p.wait("tensor", "sc", p.cnt["sc"])
            p.emit("tensor", lambda E, S, o=ps_kn, l=consb[:, CB_KNSEL + fc * 16:CB_KNSEL + fc * 16 + 16],
                          r=ksq[fc % 2], st=(fc == 0), sp=(fc == FC - 1):
                   E.matmul(o, l, r, start=st, stop=sp), mark="te")
        p.wait("scalar", "te", p.cnt["te"])
        p.selfwait("scalar")
        p.emit("scalar", lambda E, S, d=rowtmp, s=ps_kn:
               E.activation(d, s, AF.Copy), mark="sc")
        for i in range(4):
            tc = t * 4 + i
            pc = ctx.t["psC"][:, i * 16:(i + 1) * 16]
            p.wait("tensor", "sc", p.cnt["sc"])
            p.emit("tensor", lambda E, S, o=pc, s=rowtmp[:, i * 128:(i + 1) * 128]:
                   E.transpose(o, s, ident_b[0:16, 0:16]), mark="te")
            p.wait("scalar", "te", p.cnt["te"])
            p.selfwait("scalar")
            p.emit("scalar", lambda E, S, d=scr[:, 192:208], s=pc:
                   E.activation(d, s, AF.Ln), mark="sc")
            p.selfwait("scalar")
            p.emit("scalar", lambda E, S, d=knT(tc), s=scr[:, 192:208]:
                   E.activation(d, s, AF.Exp, scale=ctx.col("c_half")), mark="sc")
            p.wait("vector", "sc", p.cnt["sc"])
            p.selfwait("vector")
            p.emit("vector", lambda E, S, d=knT(tc):
                   E.tensor_scalar_add(d, d, 1e-6), mark="ve")
            p.selfwait("vector")
            p.emit("vector", lambda E, S, d=knT(tc), s=knT(tc):
                   E.reciprocal(d, s), mark="ve")

    cumlg = scr[:, 832:848]
    gamT = scr[:, 848:864]
    negcl = scr[:, 864:880]
    cumlnb = scr[:, 880:896]
    lnb = scr[:, 896:912]
    negb = scr[:, 912:928]
    gCc = scr[:, 928:944]
    c_sb = scr[:, 944:945]
    gCrow = scr[0:1, 945:961]
    lcrow = scr[0:1, 961:977]
    sC = scr[:, 977:993]
    f32a = [scr[:, 0:64], scr[:, 1024:1088]]
    f32b = [scr[:, 64:192], scr[:, 1088:1216]]
    diagf = [scr[:, 192:320], scr[:, 1216:1344]]
    argf = [scr[:, 320:448], scr[:, 1344:1472]]

    def _pt(base):
        off = [0]
        def take(n):
            o = base + off[0]
            off[0] += n
            return scr2[:, o:o + n]
        return take
    tk = [None, None]
    tiles = [{}, {}]
    for hp in range(2):
        take = _pt(hp * 1216)
        tiles[hp] = dict(
            kh=take(128), Adec=take(128), qh=take(128), B=take(128), BT=take(128),
            Mkq=take(128), U=take(64), kt=take(128), Rm=take(64), S0b=take(64),
            diagb=take(128))
    Opair = scr2[:, 2432:2560]
    vT_t = scr2[:, 2560:2688]

    psA_ = ctx.t["psA"]
    psB_ = ctx.t["psB"]
    P0 = dict(bc=psB_[:, 1, 0:128], gram=psB_[:, 1, 128:256],
              bu=psB_[:, 2, 368:432], b2=psB_[:, 0, 0:128], bt2=psB_[:, 0, 128:256],
              w0=psB_[:, 2, 176:240], O=psB_[:, 2, 240:304], P=psB_[:, 2, 304:368])
    P1 = dict(bc=psA_[:, 2, 0:128], gram=psA_[:, 2, 128:256],
              bu=psA_[:, 1, 192:256], b2=psA_[:, 0, 0:128], bt2=psA_[:, 0, 128:256],
              w0=psA_[:, 1, 0:64], O=psA_[:, 1, 64:128], P=psA_[:, 1, 128:192])
    PS = [P0, P1]
    ps_cg = psB_[:, 2, 0:16]
    ps_cgT = psB_[0:16, 2, 16:144]
    ps_g1 = psB_[0:1, 2, 144:160]
    ps_gc = psB_[:, 2, 160:176]
    pc_kt = [ctx.t["psC"][:, 256:384], ctx.t["psC"][:, 512:640]]
    pc_ot = ctx.t["psC"][:, 384:512]
    onesf = consf[:, CF_ONES:CF_ONES + 128]
    mb_gt = consf[:, CF_MBGT:CF_MBGT + 128]
    mb_lt = consf[:, CF_MBLT:CF_MBLT + 128]
    mb_le = consf[:, CF_MBLE:CF_MBLE + 128]

    Sst = ctx.t["Sstate"]

    for t in range(NT):
        for ci in range(4):
            tc = t * 4 + ci
            rhs_barrier(ctx)
            for b in range(4):
                ctx.guard_bank(b)
            p.emit("tensor", lambda E, S, o=ps_cg, r=lgT(tc):
                   E.matmul(o, utri_f, r, start=True, stop=True), mark="te")
            p.wait("scalar", "te", p.cnt["te"])
            p.selfwait("scalar")
            p.emit("scalar", lambda E, S, d=cumlg, s=ps_cg:
                   E.activation(d, s, AF.Copy), mark="sc")
            p.selfwait("scalar")
            p.emit("scalar", lambda E, S, d=gamT, s=cumlg:
                   E.activation(d, s, AF.Exp), mark="sc")
            p.selfwait("scalar")
            p.emit("scalar", lambda E, S, d=lnb, s=bT(tc):
                   E.activation(d, s, AF.Ln), mark="sc")
            p.wait("vector", "sc", p.cnt["sc"])
            p.selfwait("vector")
            p.emit("vector", lambda E, S, d=negcl, s=cumlg:
                   E.tensor_scalar_mul(d, s, -1.0), mark="ve")
            p.emit("vector", lambda E, S, d=cumlnb, a=cumlg, b=lnb:
                   E.tensor_add(d, a, b), mark="ve")
            p.emit("vector", lambda E, S, d=negb, s=bT(tc):
                   E.tensor_scalar_mul(d, s, -1.0), mark="ve")
            p.wait("tensor", "sc", p.cnt["sc"])
            p.wait("tensor", "ve", p.cnt["ve"])
            p.emit("tensor", lambda E, S, o=ps_cgT, s=cumlg:
                   E.transpose(o, s, ident_f), mark="te")
            p.wait("scalar", "te", p.cnt["te"])
            p.selfwait("scalar")
            p.emit("scalar", lambda E, S, d=c_sb[0:16, :], s=ps_cgT[0:16, 127:128]:
                   E.activation(d, s, AF.Copy), mark="sc")
            p.wait("tensor", "sc", p.cnt["sc"])
            p.emit("tensor", lambda E, S, o=ps_g1, s=c_sb[0:16, :]:
                   E.transpose(o, s, ident_f[0:16, 0:16]), mark="te")
            p.wait("scalar", "te", p.cnt["te"])
            p.selfwait("scalar")
            p.emit("scalar", lambda E, S, d=gCrow, s=ps_g1:
                   E.activation(d, s, AF.Exp), mark="sc")
            p.emit("scalar", lambda E, S, d=lcrow, s=ps_g1:
                   E.activation(d, s, AF.Copy), mark="sc")
            p.wait("tensor", "sc", p.cnt["sc"])
            p.emit("tensor", lambda E, S, o=ps_gc, r=gCrow:
                   E.matmul(o, onesrow_f, r, start=True, stop=True), mark="te")
            p.wait("scalar", "te", p.cnt["te"])
            p.selfwait("scalar")
            p.emit("scalar", lambda E, S, d=gCc, s=ps_gc:
                   E.activation(d, s, AF.Copy), mark="sc")
            p.wait("tensor", "sc", p.cnt["sc"])
            p.emit("tensor", lambda E, S, o=ps_gc, r=lcrow:
                   E.matmul(o, onesrow_f, r, start=True, stop=True), mark="te")
            p.wait("vector", "te", p.cnt["te"])
            p.selfwait("vector")
            p.emit("vector", lambda E, S, d=sC, s=ps_gc, c=cumlg:
                   E.tensor_sub(d, s, c), mark="ve")
            p.wait("scalar", "ve", p.cnt["ve"])
            p.selfwait("scalar")
            p.emit("scalar", lambda E, S, d=sC, s=sC:
                   E.activation(d, s, AF.Exp), mark="sc")

            for fc in range(FC):
                vsl = ctx.t["arenaA"][:, (24 + fc) * 1024 + tc * 128: (24 + fc) * 1024 + tc * 128 + 128]
                qsl = ctx.t["arenaA"][:, (8 + fc) * 1024 + tc * 128: (8 + fc) * 1024 + tc * 128 + 128]
                ksl = ctx.t["arenaA"][:, (16 + fc) * 1024 + tc * 128: (16 + fc) * 1024 + tc * 128 + 128]
                p.wait("tensor", "sc", p.cnt["sc"])
                p.wait("tensor", "ve", p.cnt["ve"])
                p.emit("tensor", lambda E, S, o=pc_kt[0], s=vsl:
                       E.transpose(o, s, ident_b), mark="te")
                p.wait("scalar", "te", p.cnt["te"])
                p.selfwait("scalar")
                p.emit("scalar", lambda E, S, d=vT_t, s=pc_kt[0]:
                       E.activation(d, s, AF.Copy), mark="sc")
                vT_done = p.cnt["sc"]

                def mk(hp):
                    h = 2 * fc + hp
                    b0 = 64 * hp
                    te0 = p.cnt["te"]
                    ve0 = p.cnt["ve"]
                    return dict(h=h, b0=b0, T=tiles[hp], R=PS[hp],
                                ksl_h=ksl[b0:b0 + 64, :], qsl_h=qsl[b0:b0 + 64, :],
                                S_h=Sst[b0:b0 + 64, fc, :],
                                fa=f32a[hp], fb=f32b[hp], df=diagf[hp], af=argf[hp],
                                pk=pc_kt[hp],
                                m_bc=te0, m_bcread=ve0, m_diag=ve0, m_arg=ve0, m_exp=0)
                H = [mk(0), mk(1)]

                def st_diag(c, coltile, identt):
                    p.wait("vector", "te", c["m_bc"])
                    p.selfwait("vector")
                    dt_ = c["T"]["diagb"] if identt is ident_b else c["df"]
                    p.emit("vector", lambda E, S, d=dt_, cc=coltile[:, c["h"]:c["h"] + 1]:
                           E.tensor_scalar_mul(d, identt, cc), mark="ve")
                    c["m_diag"] = p.cnt["ve"]

                def st_bcast(c, identt):
                    src_ = c["T"]["diagb"] if identt is ident_b else c["df"]
                    ones_ = ones_b if identt is ident_b else onesf
                    p.wait("tensor", "ve", max(c["m_diag"], c["m_bcread"]))
                    p.emit("tensor", lambda E, S, o=c["R"]["bc"], dd=src_:
                           E.matmul(o, ones_, dd, start=True, stop=True), mark="te")
                    c["m_bc"] = p.cnt["te"]

                def st_mult(c, dtile_key, srct):
                    T = c["T"]
                    p.wait("vector", "te", c["m_bc"])
                    p.selfwait("vector")
                    p.emit("vector", lambda E, S, d=T[dtile_key][c["b0"]:c["b0"] + 64, :], s=srct,
                                  bc=c["R"]["bc"][c["b0"]:c["b0"] + 64, :]:
                           E.tensor_mul(d, s, bc), mark="ve")
                    c["m_bcread"] = p.cnt["ve"]
                    c[f"m_{dtile_key}"] = p.cnt["ve"]

                def st_arg(c, mask):
                    p.wait("vector", "te", c["m_bc"])
                    p.wait("vector", "sc", c["m_exp"])
                    p.selfwait("vector")
                    p.emit("vector", lambda E, S, d=c["af"], s=c["R"]["bc"], m=mask:
                           E.tensor_add(d, s, m), mark="ve")
                    c["m_bcread"] = p.cnt["ve"]
                    c["m_arg"] = p.cnt["ve"]

                def st_exp(c, dstkey, bias_col):
                    p.wait("scalar", "ve", c["m_arg"])
                    p.selfwait("scalar")
                    p.emit("scalar", lambda E, S, d=c["T"][dstkey], s=c["af"],
                                  b=bias_col[:, c["h"]:c["h"] + 1]:
                           E.activation(d, s, AF.Exp, bias=b), mark="sc")
                    c["m_exp"] = p.cnt["sc"]
                    c[f"m_{dstkey}"] = p.cnt["sc"]

                for key, col, srckey in (("kh", knT(tc), "ksl_h"), ("qh", gamT, "qsl_h")):
                    for c in H:
                        st_diag(c, col, ident_b)
                    for c in H:
                        st_bcast(c, ident_b)
                    for c in H:
                        st_mult(c, key, c[srckey])
                for dstkey, col, bias, mask in (("Adec", negcl, cumlg, mb_gt),
                                                ("BT", cumlnb, negcl, mb_lt),
                                                ("Mkq", cumlg, negcl, mb_le)):
                    for c in H:
                        st_diag(c, col, ident_f)
                    for c in H:
                        st_bcast(c, ident_f)
                    for c in H:
                        st_arg(c, mask)
                    for c in H:
                        st_exp(c, dstkey, bias)
                for c in H:
                    T, R = c["T"], c["R"]
                    b0 = c["b0"]
                    p.wait("tensor", "ve", max(c["m_kh"], c["m_bcread"]))
                    p.emit("tensor", lambda E, S, o=R["gram"], l=T["kh"][b0:b0 + 64, :], r=T["kh"][b0:b0 + 64, :]:
                           E.matmul(o, l, r, start=True, stop=True), mark="te")
                    c["gram_mark"] = p.cnt["te"]
                for c in H:
                    T, R = c["T"], c["R"]
                    p.wait("vector", "te", c["gram_mark"])
                    p.wait("vector", "sc", max(c["m_Adec"], c["m_BT"]))
                    p.selfwait("vector")
                    p.emit("vector", lambda E, S, d=T["B"], s=R["gram"], cc=negb[:, c["h"]:c["h"] + 1], a=T["Adec"]:
                           E.scalar_tensor_tensor(d, s, cc, a, ALU.mult, ALU.mult), mark="ve")
                    p.emit("vector", lambda E, S, d=T["BT"], s=R["gram"], a=T["BT"]:
                           E.scalar_tensor_tensor(d, s, -1.0, a, ALU.mult, ALU.mult), mark="ve")
                    c["bbt_mark"] = p.cnt["ve"]
                for c in H:
                    T, R = c["T"], c["R"]
                    b0 = c["b0"]
                    p.wait("tensor", "ve", c["bbt_mark"])
                    p.emit("tensor", lambda E, S, o=R["gram"], l=T["kh"][b0:b0 + 64, :], r=c["qsl_h"]:
                           E.matmul(o, l, r, start=True, stop=True), mark="te")
                    c["gram2_mark"] = p.cnt["te"]
                for c in H:
                    T, R = c["T"], c["R"]
                    p.wait("vector", "te", c["gram2_mark"])
                    p.wait("vector", "sc", c["m_Mkq"])
                    p.selfwait("vector")
                    p.emit("vector", lambda E, S, d=T["Mkq"], s=R["gram"], a=T["Mkq"]:
                           E.tensor_mul(d, s, a), mark="ve")
                    p.emit("vector", lambda E, S, d=T["S0b"][c["b0"]:c["b0"] + 64, :], s=c["S_h"]:
                           E.tensor_copy(d, s), mark="ve")
                    c["s0_mark"] = p.cnt["ve"]
                for c in H:
                    T, R = c["T"], c["R"]
                    b0 = c["b0"]
                    p.wait("tensor", "ve", c["s0_mark"])
                    p.emit("tensor", lambda E, S, o=R["w0"], l=T["kh"][b0:b0 + 64, :], r=T["S0b"][b0:b0 + 64, :]:
                           E.matmul(o, l, r, start=True, stop=True), mark="te")
                    c["w0_mark"] = p.cnt["te"]
                for c in H:
                    T = c["T"]
                    p.wait("vector", "te", c["w0_mark"])
                    p.wait("vector", "sc", vT_done)
                    p.selfwait("vector")
                    p.emit("vector", lambda E, S, d=c["fa"], s=c["R"]["w0"], cc=gamT[:, c["h"]:c["h"] + 1]:
                           E.tensor_scalar_mul(d, s, cc), mark="ve")
                    p.selfwait("vector")
                    p.emit("vector", lambda E, S, d=T["Rm"], s=vT_t[:, c["b0"]:c["b0"] + 64],
                                  cc=bT(tc)[:, c["h"]:c["h"] + 1], b=c["fa"]:
                           E.scalar_tensor_tensor(d, s, cc, b, ALU.mult, ALU.subtract), mark="ve")
                    p.selfwait("vector")
                    p.emit("vector", lambda E, S, d=T["U"], s=T["Rm"]: E.tensor_copy(d, s), mark="ve")
                    c["u_mark"] = p.cnt["ve"]

                for it in range(7):
                    for c in H:
                        T, R = c["T"], c["R"]
                        p.wait("tensor", "ve", c["u_mark"])
                        if it > 0:
                            p.wait("tensor", "sc", c["evac_mark"])
                        if it < 6:
                            p.emit("tensor", lambda E, S, o=R["bu"], l=T["BT"], r=T["U"]:
                                   E.matmul(o, l, r, start=True, stop=True))
                            p.emit("tensor", lambda E, S, o=R["b2"], l=T["BT"], r=T["B"]:
                                   E.matmul(o, l, r, start=True, stop=True))
                            p.emit("tensor", lambda E, S, o=R["bt2"], l=T["B"], r=T["BT"]:
                                   E.matmul(o, l, r, start=True, stop=True), mark="te")
                        else:
                            p.emit("tensor", lambda E, S, o=R["bu"], l=T["BT"], r=T["U"]:
                                   E.matmul(o, l, r, start=True, stop=True), mark="te")
                        c["mm_mark"] = p.cnt["te"]
                    for c in H:
                        T, R = c["T"], c["R"]
                        p.wait("vector", "te", c["mm_mark"])
                        p.selfwait("vector")
                        p.emit("vector", lambda E, S, d=T["U"], s=R["bu"]:
                               E.tensor_add(d, d, s), mark="ve")
                        c["u_mark"] = p.cnt["ve"]
                        if it < 6:
                            p.wait("scalar", "te", c["mm_mark"])
                            p.selfwait("scalar")
                            p.emit("scalar", lambda E, S, d=T["B"], s=R["b2"]:
                                   E.activation(d, s, AF.Copy), mark="sc")
                            p.emit("scalar", lambda E, S, d=T["BT"], s=R["bt2"]:
                                   E.activation(d, s, AF.Copy), mark="sc")
                            c["evac_mark"] = p.cnt["sc"]
                for c in H:
                    T, R = c["T"], c["R"]
                    b0 = c["b0"]
                    p.wait("tensor", "ve", c["u_mark"])
                    p.emit("tensor", lambda E, S, o=R["O"], l=T["Mkq"], r=T["U"]:
                           E.matmul(o, l, r, start=True, stop=False))
                    p.emit("tensor", lambda E, S, o=R["O"], l=T["qh"][b0:b0 + 64, :], r=T["S0b"][b0:b0 + 64, :]:
                           E.matmul(o, l, r, start=False, stop=True), mark="te")
                    c["o_mark"] = p.cnt["te"]
                for c in H:
                    p.wait("scalar", "te", c["o_mark"])
                    p.selfwait("scalar")
                    p.emit("scalar", lambda E, S, d=Opair[:, c["b0"]:c["b0"] + 64], s=c["R"]["O"]:
                           E.activation(d, s, AF.Copy), mark="sc")
                for c in H:
                    T = c["T"]
                    p.wait("tensor", "ve", p.cnt["ve"])
                    p.wait("tensor", "sc", p.cnt["sc"])
                    p.emit("tensor", lambda E, S, o=c["pk"], s=T["kh"]:
                           E.transpose(o, s, ident_b), mark="te")
                    c["kt_mark"] = p.cnt["te"]
                    p.wait("vector", "te", c["kt_mark"])
                    p.selfwait("vector")
                    p.emit("vector", lambda E, S, d=T["kt"], s=c["pk"], cc=sC[:, c["h"]:c["h"] + 1]:
                           E.tensor_scalar_mul(d, s, cc), mark="ve")
                    c["ktd_mark"] = p.cnt["ve"]
                for c in H:
                    T, R = c["T"], c["R"]
                    p.wait("tensor", "ve", c["ktd_mark"])
                    p.emit("tensor", lambda E, S, o=R["P"], l=T["kt"], r=T["U"]:
                           E.matmul(o, l, r, start=True, stop=True), mark="te")
                    c["p_mark"] = p.cnt["te"]
                for c in H:
                    b0 = c["b0"]
                    p.wait("vector", "te", c["p_mark"])
                    p.selfwait("vector")
                    p.emit("vector", lambda E, S, d=c["fb"][b0:b0 + 64, 0:64], a=c["S_h"],
                                  cc=gCc[b0:b0 + 64, c["h"]:c["h"] + 1]:
                           E.tensor_scalar_mul(d, a, cc), mark="ve")
                    p.selfwait("vector")
                    p.emit("vector", lambda E, S, d=c["S_h"], a=c["fb"][b0:b0 + 64, 0:64],
                                  b=c["R"]["P"][b0:b0 + 64, :]:
                           E.tensor_add(d, a, b), mark="ve")
                p.wait("tensor", "sc", p.cnt["sc"])
                p.wait("tensor", "ve", p.cnt["ve"])
                p.emit("tensor", lambda E, S, o=pc_ot, s=Opair:
                       E.transpose(o, s, ident_b), mark="te")
                p.wait("scalar", "te", p.cnt["te"])
                p.selfwait("scalar")
                odst = ctx.t["arenaB"][:, (8 + fc) * 1024 + tc * 128: (8 + fc) * 1024 + tc * 128 + 128]
                p.emit("scalar", lambda E, S, d=odst, s=pc_ot:
                       E.activation(d, s, AF.Copy), mark="sc")

    if GDN_DEBUG:
        return
    def rhs_o(wi, k, t):
        return aslot(ctx, "arenaB", 8 + k, t)

    def cb_wo(wi, mc, t, ps, b, banks):
        p.wait("vector", "te", p.cnt["te"])
        p.selfwait("vector")
        R = ctx.t["R"][:, mc, t * 512:(t + 1) * 512]
        p.emit("vector", lambda E, S, d=R, s=ps, c=ctx.col("l1a_bo", mc):
               E.scalar_tensor_tensor(d, s, c, d, ALU.add, ALU.add), mark="ve")
        p.emit("vector", lambda E, S, d=x1_ap(ctx, mc, t), s=ps,
                      c=ctx.col("l1a_bo", mc), z=z_ap(mc, t):
               E.scalar_tensor_tensor(d, s, c, z, ALU.add, ALU.add), mark="ve")
        ctx.set_bank_guard(b, "ve")

    emit_linear_fm(ctx, [W["wo"]], rhs_o, cb_wo, Mc=8)

    def x1src(fc, t):
        return x1_ap(ctx, fc, t)

    def xn2_ap(fc, t):
        return aslot(ctx, "arenaB", fc, t)
    emit_ln(ctx, x1src, xn2_ap, affine=("l1a_ln2g", "l1a_ln2b"))

    def rhs_xn2(wi, k, t):
        return aslot(ctx, "arenaB", k, t)

    def h_ap(kc, t):
        if kc < 16:
            return aslot(ctx, "arenaA", kc, t)
        return aslot(ctx, "arenaB", 8 + (kc - 16), t)

    def cb_m1(wi, mc, t, ps, b, banks):
        p.wait("scalar", "te", p.cnt["te"])
        p.emit("scalar", lambda E, S, d=h_ap(mc, t), s=ps:
               E.activation(d, s, AF.Gelu, bias=ctx.col("l1a_b1", mc)), mark="sc")
        ctx.set_bank_guard(b, "sc")

    emit_linear_fm(ctx, [W["m1"]], rhs_xn2, cb_m1, Mc=32)

    def rhs_h(wi, k, t):
        return h_ap(k, t)

    def cb_m2(wi, mc, t, ps, b, banks):
        p.wait("vector", "te", p.cnt["te"])
        p.selfwait("vector")
        R = ctx.t["R"][:, mc, t * 512:(t + 1) * 512]
        p.emit("vector", lambda E, S, d=R, s=ps, c=ctx.col("l1a_b2", mc):
               E.scalar_tensor_tensor(d, s, c, d, ALU.add, ALU.add), mark="ve")
        ctx.set_bank_guard(b, "ve")

    emit_linear_fm(ctx, [W["m2"]], rhs_h, cb_m2, Mc=8)


# version 10 (final)
# speedup vs baseline: 1.2363x; 1.1531x over previous
import contextlib
import os
import numpy as np
import ml_dtypes
import concourse.bass as bass
import concourse.mybir as mybir
from concourse.bass_utils import run_bass_kernel_spmd

F32 = mybir.dt.float32
BF16 = mybir.dt.bfloat16
AF = mybir.ActivationFunctionType
ALU = mybir.AluOpType

NCORES = 8
D = 1024
SEQ = 4096
TSTRIP = 1024
NSTRIP = SEQ // TSTRIP
NT = TSTRIP // 512
FC = D // 128

ENGINES = ("sync", "tensor", "vector", "scalar", "gpsimd")
GDN_DEBUG = False



def tile_w(w):
    K, M = w.shape
    Kc = K // 128
    mw = min(M, 128)
    Mc = M // mw
    return np.ascontiguousarray(
        w.reshape(Kc, 128, Mc, mw).transpose(2, 1, 0, 3))


def fm_of(x):
    T, d = x.shape
    return np.ascontiguousarray(x.T.reshape(d // 128, 128, T).transpose(1, 0, 2))


def cols_of(v):
    return np.ascontiguousarray(v.reshape(-1, 128).T)



class Prog:

    def __init__(self):
        self.q = {e: [] for e in ENGINES}
        self.cnt = dict(dma0=0, dma1=0, dma2=0, dma3=0, stdma=0, te=0, ve=0, sc=0, gp=0)
        self.dma_i = 0

    def wait(self, eng, sem, val):
        if val <= 0:
            return
        self.q[eng].append(lambda E, S, sem=sem, val=val: E.wait_ge(S[sem], val))

    def selfwait(self, eng):
        sem = ENG_SEM[eng]
        self.wait(eng, sem, self.cnt[sem])

    def dma(self, eng, build):
        i = self.dma_i % 4
        self.dma_i += 1
        sem = "dma%d" % i
        self.wait(eng, sem, self.cnt[sem])
        self.emit(eng, build, mark=sem, n=16)
        return (sem, self.cnt[sem])

    def emit(self, eng, build, mark=None, n=1):
        if mark is not None:
            self.cnt[mark] += n

            def f(E, S, build=build, mark=mark, n=n):
                ins = build(E, S)
                ins.then_inc(S[mark], n)
            self.q[eng].append(f)
        else:
            self.q[eng].append(lambda E, S, build=build: build(E, S))

    def replay(self, nc, sems):
        with nc.Block() as block:
            for eng in ENGINES:
                ops = self.q[eng]
                if not ops:
                    continue

                def body(E, ops=ops):
                    for f in ops:
                        f(E, sems)
                getattr(block, eng)(body)


class Ctx:

    def __init__(self, nc, prog):
        self.nc = nc
        self.p = prog
        self.t = {}
        self.colreg = {}
        self.bank_guard = [None] * 4
        self.bank_i = 0
        self.wstage_guard = [None, None]
        self.wbf_guard = {}

    def col(self, name, j=0):
        s, n = self.colreg[name]
        assert 0 <= j < n, (name, j, n)
        return self.t["cols"][:, s + j:s + j + 1]

    def col1(self, name, j=0):
        s, n = self.colreg[name]
        return self.t["cols"][0:1, s + j:s + j + 1]

    def next_bank(self):
        b = self.bank_i % 4
        self.bank_i += 1
        return b

    def guard_bank(self, b):
        g = self.bank_guard[b]
        if g is not None:
            self.p.wait("tensor", g[0], g[1])

    def set_bank_guard(self, b, sem):
        self.bank_guard[b] = (sem, self.p.cnt[sem])


SEMNAMES = ("dma0", "dma1", "dma2", "dma3", "stdma", "te", "ve", "sc", "gp")
ENG_SEM = dict(tensor="te", vector="ve", scalar="sc", gpsimd="gp")



class WSrc:

    def __init__(self, dram_ap, Kc, mw=128):
        self.ap = dram_ap
        self.Kc = Kc
        self.mw = mw

    def emit_load(self, ctx, mc, dest_ap_fn):
        p = ctx.p
        kgran = max(1, 512 // self.mw)
        ng = (self.Kc + kgran - 1) // kgran
        for g in range(ng):
            k0 = g * kgran
            kn = min(kgran, self.Kc - k0)
            half = ctx.stage_i % 2
            ctx.stage_i += 1
            stage = ctx.t["wstage"][:, half * 512:half * 512 + kn * self.mw]
            stage_v = stage.rearrange("p (k f) -> p k f", f=self.mw)
            gg = ctx.wstage_guard[half]
            if gg is not None:
                p.wait("sync", gg[0], gg[1])
            src = self.ap[mc, :, k0:k0 + kn, :]
            dm = p.dma("sync", lambda E, S, d=stage_v, s=src: E.dma_start(out=d, in_=s))
            p.wait("vector", dm[0], dm[1])
            dst = dest_ap_fn(k0, kn)
            p.emit("vector", lambda E, S, d=dst, s=stage_v: E.tensor_copy(d, s), mark="ve")
            ctx.wstage_guard[half] = ("ve", p.cnt["ve"])
        return ("ve", p.cnt["ve"])


class WSinCos:

    def __init__(self, theta_ap, noise_ap, is_cos, Kc=8, mw=128):
        self.a1 = theta_ap
        self.a2 = noise_ap
        self.is_cos = is_cos
        self.Kc = Kc
        self.mw = mw

    def emit_load(self, ctx, mc, dest_ap_fn):
        p = ctx.p
        kgran = 2
        ng = (self.Kc + kgran - 1) // kgran
        for g in range(ng):
            k0 = g * kgran
            kn = min(kgran, self.Kc - k0)
            half = ctx.stage_i % 2
            ctx.stage_i += 1
            base = half * 512
            st1 = ctx.t["wstage"][:, base:base + kn * 128].rearrange("p (k f) -> p k f", f=128)
            st2 = ctx.t["wstage"][:, base + 256:base + 256 + kn * 128].rearrange("p (k f) -> p k f", f=128)
            gg = ctx.wstage_guard[half]
            if gg is not None:
                p.wait("sync", gg[0], gg[1])
            s1 = self.a1[mc, :, k0:k0 + kn, :]
            s2 = self.a2[mc, :, k0:k0 + kn, :]
            dm1 = p.dma("sync", lambda E, S, d=st1, s=s1: E.dma_start(out=d, in_=s))
            dm2 = p.dma("sync", lambda E, S, d=st2, s=s2: E.dma_start(out=d, in_=s))
            p.wait("vector", dm1[0], dm1[1])
            p.wait("vector", dm2[0], dm2[1])
            p.emit("vector", lambda E, S, a=st1, b=st2: E.tensor_add(a, a, b), mark="ve")
            dst = dest_ap_fn(k0, kn)
            p.wait("scalar", "ve", p.cnt["ve"])
            if self.is_cos:
                p.emit("scalar", lambda E, S, d=dst, s=st1:
                       E.activation(d, s, AF.Sin, bias=ctxcol(ctx, "c_pi2")), mark="sc")
            else:
                p.emit("scalar", lambda E, S, d=dst, s=st1:
                       E.activation(d, s, AF.Sin), mark="sc")
            ctx.wstage_guard[half] = ("sc", p.cnt["sc"])
        return ("sc", p.cnt["sc"])

    uses_sc = True


def ctxcol(ctx, name, j=0):
    return ctx.col(name, j)



def rhs_barrier(ctx):
    p = ctx.p
    p.wait("tensor", "ve", p.cnt["ve"])
    p.wait("tensor", "sc", p.cnt["sc"])
    p.wait("tensor", "gp", p.cnt["gp"])


def war_barrier(ctx, eng):
    ctx.p.wait(eng, "te", ctx.p.cnt["te"])


def emit_linear_fm(ctx, weights, rhs_fn, out_cb, Mc, T=TSTRIP):
    p = ctx.p
    nw = len(weights)
    Kc = weights[0].Kc
    mw = weights[0].mw
    nt = T // 512
    reg = Kc * mw
    assert nw * 2 * reg <= 8192
    ctx.stage_i = 0
    rhs_barrier(ctx)
    ctx.pass_id = getattr(ctx, "pass_id", 0) + 1
    pass_te0 = p.cnt["te"]

    def wreg(wi, parity):
        off = (wi * 2 + parity) * reg
        return ctx.t["wbf"][:, off:off + reg].rearrange("p (k f) -> p k f", f=mw)

    def do_load(mc):
        parity = mc % 2
        marks = []
        for wi, w in enumerate(weights):
            g = ctx.wbf_guard.get((wi, parity))
            if g is None or g[0] != ctx.pass_id:
                g = (ctx.pass_id, "te", pass_te0)
            p.wait("vector", g[1], g[2])
            if getattr(w, "uses_sc", False):
                p.wait("scalar", g[1], g[2])

            def dest_fn(k0, kn, wi=wi, parity=parity):
                return wreg(wi, parity)[:, k0:k0 + kn, :]
            marks.append(w.emit_load(ctx, mc, dest_fn))
        return marks

    pend = do_load(0)
    for mc in range(Mc):
        parity = mc % 2
        cur = pend
        if mc + 1 < Mc:
            pend = do_load(mc + 1)
        for eng, c in cur:
            p.wait("tensor", eng, c)
        for t in range(nt):
            banks = []
            for wi, w in enumerate(weights):
                b = ctx.next_bank()
                banks.append(b)
                ctx.guard_bank(b)
                ps = ctx.t["psA"][:, b, 0:512]
                wv = wreg(wi, parity)
                for k in range(Kc):
                    p.emit("tensor",
                           lambda E, S, o=ps[0:mw, :], l=wv[:, k, :], r=rhs_fn(wi, k, t),
                                  st=(k == 0), sp=(k == Kc - 1):
                           E.matmul(o, l, r, start=st, stop=sp),
                           mark="te" if k == Kc - 1 else None)
                out_cb(wi, mc, t, ps[0:mw, :], b, banks)
        ctx.wbf_guard[(0, parity)] = (ctx.pass_id, "te", p.cnt["te"])
        if nw > 1:
            ctx.wbf_guard[(1, parity)] = (ctx.pass_id, "te", p.cnt["te"])



def emit_ln(ctx, src_fn, dst_fn, affine=None, src_bf16=False):
    p = ctx.p
    scr = ctx.t["scr"]
    scr2 = ctx.t["scr2"]
    A_t = scr[:, 0:512]
    B_t = scr[:, 512:1024]
    Cs = scr[:, 1024:1536]
    Ds = scr[:, 1536:2048]
    Csl = [Cs, Ds]
    for t in range(NT):
        war_barrier(ctx, "vector")
        p.selfwait("vector")
        p.selfwait("scalar")
        p.wait("vector", "gp", p.cnt["gp"])
        p.wait("scalar", "gp", p.cnt["gp"])
        cast_marks = []
        if src_bf16:
            statsrc = src_fn
            cast_marks = [p.cnt["ve"]] * FC
        else:
            statsrc = dst_fn
            for fc in range(FC):
                p.emit("vector", lambda E, S, d=dst_fn(fc, t), s=src_fn(fc, t):
                       E.tensor_copy(d, s), mark="ve")
                cast_marks.append(p.cnt["ve"])
        sq_marks = []
        qmm_marks = []
        ps_s = ctx.t["psB"][0:1, 0, 0:512]
        ps_q = ctx.t["psB"][0:1, 1, 0:512]
        for fc in range(FC):
            p.wait("tensor", "ve", cast_marks[fc])
            p.emit("tensor", lambda E, S, o=ps_s, l=ctx.t["cons_bf"][:, 0:1], r=statsrc(fc, t),
                          st=(fc == 0), sp=(fc == FC - 1):
                   E.matmul(o, l, r, start=st, stop=sp),
                   mark="te" if fc == FC - 1 else None)
        smark = p.cnt["te"]
        for fc in range(FC):
            sqslot = ctx.t["scr2"][:, (fc % 2) * 512:(fc % 2) * 512 + 512]
            p.wait("scalar", "ve", cast_marks[fc])
            if fc >= 2:
                p.wait("scalar", "te", qmm_marks[fc - 2])
            p.emit("scalar", lambda E, S, d=sqslot, s=statsrc(fc, t):
                   E.activation(d, s, AF.Square), mark="sc")
            p.wait("tensor", "sc", p.cnt["sc"])
            p.emit("tensor", lambda E, S, o=ps_q, l=ctx.t["cons_bf"][:, 0:1], r=sqslot,
                          st=(fc == 0), sp=(fc == FC - 1):
                   E.matmul(o, l, r, start=st, stop=sp), mark="te")
            qmm_marks.append(p.cnt["te"])
        qmark = p.cnt["te"]
        ve2_mark = [0]
        sq_s = Ds[0:1, 0:512]
        var = B_t[0:1, 0:512]
        a_row = Cs[0:1, 0:512]
        p.wait("scalar", "te", smark)
        p.emit("scalar", lambda E, S, d=sq_s, s=ps_s:
               E.activation(d, s, AF.Square, scale=ctx.col1("c_r1024")), mark="sc")
        p.wait("vector", "te", qmark)
        p.wait("vector", "sc", p.cnt["sc"])
        p.emit("vector", lambda E, S, d=var, q=ps_q, ss=sq_s:
               E.scalar_tensor_tensor(d, q, ctx.col1("c_r1024"), ss, ALU.mult, ALU.subtract),
               mark="ve")
        p.wait("scalar", "ve", p.cnt["ve"])
        p.emit("scalar", lambda E, S, d=sq_s, s=var:
               E.activation(d, s, AF.Ln, bias=ctx.col1("c_eps")), mark="sc")
        p.selfwait("scalar")
        p.emit("scalar", lambda E, S, d=a_row, s=sq_s:
               E.activation(d, s, AF.Exp, scale=ctx.col1("c_neghalf")), mark="sc")
        p.wait("vector", "sc", p.cnt["sc"])
        p.emit("vector", lambda E, S, d=var, s=ps_s, ar=a_row:
               E.scalar_tensor_tensor(d, s, ctx.col1("c_nr1024"), ar, ALU.mult, ALU.mult),
               mark="ve")
        b_row = var
        onesrow_f = ctx.t["cons_f"][0:1, CF_UTRI:CF_UTRI + 128]
        p.wait("tensor", "ve", p.cnt["ve"])
        p.emit("tensor", lambda E, S, o=ctx.t["psB"][:, 0, 0:512], l=onesrow_f, r=a_row:
               E.matmul(o, l, r, start=True, stop=True), mark="te")
        p.emit("tensor", lambda E, S, o=ctx.t["psB"][:, 1, 0:512], l=onesrow_f, r=b_row:
               E.matmul(o, l, r, start=True, stop=True), mark="te")
        p.wait("scalar", "te", p.cnt["te"])
        p.emit("scalar", lambda E, S, d=A_t, s=ctx.t["psB"][:, 0, 0:512]:
               E.activation(d, s, AF.Copy), mark="sc")
        p.emit("scalar", lambda E, S, d=B_t, s=ctx.t["psB"][:, 1, 0:512]:
               E.activation(d, s, AF.Copy), mark="sc")
        bc_mark = p.cnt["sc"]
        p.wait("vector", "sc", bc_mark)
        p.wait("gpsimd", "sc", bc_mark)
        gp_marks = [0, 0]
        for fc in range(FC):
            sl = fc % 2
            if affine is None:
                if fc >= 2:
                    p.wait("vector", "gp", gp_marks[sl])
                p.emit("vector", lambda E, S, d=Csl[sl], s=src_fn(fc, t), a=A_t:
                       E.tensor_mul(d, s, a), mark="ve")
                p.wait("gpsimd", "ve", p.cnt["ve"])
                p.emit("gpsimd", lambda E, S, d=dst_fn(fc, t), c=Csl[sl], b=B_t:
                       E.tensor_add(d, c, b), mark="gp")
                gp_marks[sl] = p.cnt["gp"]
            else:
                wn, bn = affine
                if fc > 0:
                    p.wait("vector", "gp", p.cnt["gp"])
                p.emit("vector", lambda E, S, d=Cs, s=src_fn(fc, t), a=A_t:
                       E.tensor_mul(d, s, a), mark="ve")
                p.wait("gpsimd", "ve", p.cnt["ve"])
                if fc > 0:
                    p.wait("gpsimd", "ve", ve2_mark[0])
                p.emit("gpsimd", lambda E, S, d=Ds, c=Cs, b=B_t:
                       E.tensor_add(d, c, b), mark="gp")
                p.wait("vector", "gp", p.cnt["gp"])
                p.emit("vector", lambda E, S, d=dst_fn(fc, t), s=Ds,
                              w=ctx.col(wn, fc), b=ctx.col(bn, fc):
                       E.tensor_scalar(d, s, w, b, ALU.mult, ALU.add), mark="ve")
                ve2_mark[0] = p.cnt["ve"]
        ctx.ln_done = ("gp" if affine is None else "ve",
                       p.cnt["gp" if affine is None else "ve"])



def aslot(ctx, arena, slot, t, n=1):
    a = ctx.t[arena]
    return a[:, slot * 1024 + t * 512: slot * 1024 + t * 512 + 512]


def aslot_tokens(ctx, arena, slot, t0, tl):
    a = ctx.t[arena]
    return a[:, slot * 1024 + t0: slot * 1024 + t0 + tl]


def x1_ap(ctx, fc, t):
    a = ctx.t["arenaA"]
    off = 32768 // 2 + (fc * 1024 + t * 512) * 2
    return a[:, off:off + 1024].bitcast(F32)



def emit_l0ffn(ctx, W):
    p = ctx.p
    silu_slot = [ctx.t["scr2"][:, i * 512:(i + 1) * 512] for i in range(2)]

    def rhs_z(wi, k, t):
        return aslot(ctx, "arenaA", k, t)

    def cb_w12(wi, mc, t, ps, b, banks):
        if wi == 0:
            p.wait("scalar", "te", p.cnt["te"])
            p.wait("scalar", "ve", p.cnt["ve"])
            p.emit("scalar", lambda E, S, d=silu_slot[t], s=ps:
                   E.activation(d, s, AF.Silu, scale=ctx.col("l0f_s1", mc)), mark="sc")
            ctx.set_bank_guard(b, "sc")
        else:
            p.wait("vector", "te", p.cnt["te"])
            p.wait("vector", "sc", p.cnt["sc"])
            dst = aslot(ctx, "arenaB", mc, t)
            p.emit("vector", lambda E, S, d=dst, s=ps, sl=silu_slot[t]:
                   E.scalar_tensor_tensor(d, s, ctx.col("l0f_s2", mc), sl, ALU.mult, ALU.mult),
                   mark="ve")
            ctx.set_bank_guard(b, "ve")
            ctx.set_bank_guard(banks[0], "ve")

    emit_linear_fm(ctx, [W["w1"], W["w2"]], rhs_z, cb_w12, Mc=32)

    def rhs_a(wi, k, t):
        return aslot(ctx, "arenaB", k, t)

    def cb_h1(wi, mc, t, ps, b, banks):
        p.wait("scalar", "te", p.cnt["te"])
        dst = aslot(ctx, "arenaA", mc, t)
        p.emit("scalar", lambda E, S, d=dst, s=ps:
               E.activation(d, s, AF.Gelu, scale=ctx.col("l0f_h1s", mc)), mark="sc")
        ctx.set_bank_guard(b, "sc")

    emit_linear_fm(ctx, [W["h1"]], rhs_a, cb_h1, Mc=32)

    def rhs_b(wi, k, t):
        return aslot(ctx, "arenaA", k, t)

    def cb_h2(wi, mc, t, ps, b, banks):
        p.wait("scalar", "te", p.cnt["te"])
        dst = aslot(ctx, "arenaB", mc, t)
        p.emit("scalar", lambda E, S, d=dst, s=ps:
               E.activation(d, s, AF.Gelu, scale=ctx.col("l0f_h2s", mc)), mark="sc")
        ctx.set_bank_guard(b, "sc")

    emit_linear_fm(ctx, [W["h2"]], rhs_b, cb_h2, Mc=32)

    def rhs_c(wi, k, t):
        return aslot(ctx, "arenaB", k, t)

    def cb_w3(wi, mc, t, ps, b, banks):
        p.wait("vector", "te", p.cnt["te"])
        R = ctx.t["R"][:, mc, t * 512:(t + 1) * 512]
        p.emit("vector", lambda E, S, d=R, s=ps:
               E.scalar_tensor_tensor(d, s, ctx.col("l0f_hs3", mc), d, ALU.mult, ALU.add),
               mark="ve")
        ctx.set_bank_guard(b, "ve")

    emit_linear_fm(ctx, [W["w3"]], rhs_c, cb_w3, Mc=8)



def emit_l1ffn(ctx, W):
    p = ctx.p
    scr = ctx.t["scr"]
    Cs, Ds = scr[:, 1024:1536], scr[:, 1536:2048]

    def rhs_z(wi, k, t):
        return aslot(ctx, "arenaA", k, t)

    def mag_ap(mc, t):
        a = ctx.t["arenaB"]
        off = (mc * 1024 + t * 512) * 2
        return a[:, off:off + 1024].bitcast(F32)

    def cb(wi, mc, t, ps, b, banks):
        if wi == 0:
            return
        p.wait("scalar", "te", p.cnt["te"])
        p.wait("scalar", "ve", p.cnt["ve"])
        ps_re = ctx.t["psA"][:, banks[0], 0:512]
        p.emit("scalar", lambda E, S, d=Cs, s=ps_re:
               E.activation(d, s, AF.Square), mark="sc")
        p.emit("scalar", lambda E, S, d=Ds, s=ps:
               E.activation(d, s, AF.Square), mark="sc")
        p.wait("vector", "sc", p.cnt["sc"])
        p.emit("vector", lambda E, S, d=mag_ap(mc, t), a=Cs, bb=Ds:
               E.tensor_add(d, a, bb), mark="ve")
        ctx.set_bank_guard(b, "ve")
        ctx.set_bank_guard(banks[0], "sc")

    emit_linear_fm(ctx, [W["cos"], W["sin"]], rhs_z, cb, Mc=8)
    p.wait("scalar", "ve", p.cnt["ve"])
    for mc in range(8):
        for t in range(NT):
            p.wait("scalar", "ve", p.cnt["ve"])
            p.selfwait("scalar")
            p.emit("scalar", lambda E, S, d=Ds, s=mag_ap(mc, t):
                   E.activation(d, s, AF.Sqrt, scale=ctx.col("c_quarter")), mark="sc")
            p.wait("vector", "sc", p.cnt["sc"])
            p.selfwait("vector")
            R = ctx.t["R"][:, mc, t * 512:(t + 1) * 512]
            p.emit("vector", lambda E, S, d=R, s=Ds: E.tensor_add(d, d, s), mark="ve")



BLOCKS = ("l0ffn_a", "l0attn", "l0ffn_b", "l1ffn_a", "l1attn", "l1ffn_b")


def make_consts():
    i128 = np.arange(128)
    ident_f = np.eye(128, dtype=np.float32)
    utri_f = (i128[:, None] <= i128[None, :]).astype(np.float32)
    bf = ml_dtypes.bfloat16
    ident_b = ident_f.astype(bf)
    utri_b = utri_f.astype(bf)
    ones_b = np.ones((128, 128), bf)
    maskLneg = -np.tril(np.ones((128, 128), np.float32), -1)
    maskUneg = -np.triu(np.ones((128, 128), np.float32), 1)
    knsel = np.zeros((128, 128), np.float32)
    for fc in range(8):
        for j in range(16):
            if j // 2 == fc:
                rows = slice((j % 2) * 64, (j % 2) * 64 + 64)
                knsel[rows, fc * 16 + j] = 1.0
    BIG = -1e30
    mb_le = np.where(i128[:, None] <= i128[None, :], 0.0, BIG).astype(np.float32)
    mb_lt = np.where(i128[:, None] < i128[None, :], 0.0, BIG).astype(np.float32)
    mb_gt = np.where(i128[:, None] > i128[None, :], 0.0, BIG).astype(np.float32)
    ones_f = np.ones((128, 128), np.float32)
    cons_f = np.concatenate([ident_f, utri_f, ones_f, mb_le, mb_lt, mb_gt], axis=1)
    cons_b = np.concatenate([np.ones((128, 1), np.float32), ident_f, utri_f,
                             np.ones((128, 128), np.float32), maskLneg, maskUneg,
                             knsel], axis=1).astype(bf)
    return cons_f, cons_b

CB_ONESCOL = 0
CB_IDENT = 1
CB_UTRI = 129
CB_ONES = 257
CB_MASKL = 385
CB_MASKU = 513
CB_KNSEL = 641
CF_IDENT = 0
CF_UTRI = 128
CF_ONES = 256
CF_MBLE = 384
CF_MBLT = 512
CF_MBGT = 640


def build_cols(inputs):
    reg = {}
    colsl = []

    def add(name, arr):
        arr = np.asarray(arr, np.float32)
        if arr.ndim == 1:
            if arr.size % 128 == 0:
                c = cols_of(arr)
            else:
                raise ValueError(name)
        else:
            c = arr
        reg[name] = (sum(x.shape[1] for x in colsl), c.shape[1])
        colsl.append(c)

    def addc(name, v):
        add(name, np.full((128, 1), v, np.float32))

    add("l0f_s1", inputs["L0_ffn_w1_s"])
    add("l0f_s2", inputs["L0_ffn_w2_s"])
    add("l0f_h1s", inputs["L0_ffn_h1_s"])
    add("l0f_h2s", inputs["L0_ffn_h2_s"])
    add("l0f_s3", inputs["L0_ffn_w3_s"])
    add("l0f_hs3", inputs["L0_ffn_w3_s"])
    add("l0a_n1w", inputs["L0_attn_norm1_w"])
    add("l0a_n1b", inputs["L0_attn_norm1_b"])
    add("l0a_n2w", inputs["L0_attn_norm2_w"])
    add("l0a_n2b", inputs["L0_attn_norm2_b"])
    add("l0a_gb", inputs["L0_attn_gate_b"])
    add("l0a_ob", inputs["L0_attn_out_b"])
    add("l0a_mb1", inputs["L0_attn_mlp_b1"])
    add("l0a_mb2", inputs["L0_attn_mlp_b2"])
    add("l1a_ln1g", inputs["L1_attn_ln1_g"])
    add("l1a_ln1b", inputs["L1_attn_ln1_b"])
    add("l1a_ln2g", inputs["L1_attn_ln2_g"])
    add("l1a_ln2b", inputs["L1_attn_ln2_b"])
    add("l1a_bo", inputs["L1_attn_bo"])
    add("l1a_b1", inputs["L1_attn_b1"])
    add("l1a_b2", inputs["L1_attn_b2"])
    addc("c_1", 1.0)
    addc("c_quarter", 0.25)
    addc("c_eighth", 0.125)
    addc("c_eps", 1e-5)
    addc("c_r1024", 1.0 / 1024.0)
    addc("c_nr1024", -1.0 / 1024.0)
    addc("c_neg1", -1.0)
    addc("c_neghalf", -0.5)
    addc("c_1e6", 1e-6)
    addc("c_pi2", float(np.pi / 2))
    addc("c_half", 0.5)
    cols = np.concatenate(colsl, axis=1)
    return cols, reg


def prep_host(inputs):
    h = {}
    h["W_l0f_w1"] = tile_w(np.asarray(inputs["L0_ffn_w1_w"]).T)
    h["W_l0f_w2"] = tile_w(np.asarray(inputs["L0_ffn_w2_w"]).T)
    h["W_l0f_h1"] = tile_w(np.asarray(inputs["L0_ffn_h1_w"]).T)
    h["W_l0f_h2"] = tile_w(np.asarray(inputs["L0_ffn_h2_w"]).T)
    h["W_l0f_w3"] = tile_w(np.asarray(inputs["L0_ffn_w3_w"]).T)
    h["W_l0a_gate"] = tile_w(np.asarray(inputs["L0_attn_gate_w"]))
    h["W_l0a_state"] = np.ascontiguousarray(np.asarray(inputs["L0_attn_state_w"]))
    h["W_l0a_out"] = tile_w(np.asarray(inputs["L0_attn_out_w"]))
    h["W_l0a_mlp1"] = tile_w(np.asarray(inputs["L0_attn_mlp_w1"]))
    h["W_l0a_mlp2"] = tile_w(np.asarray(inputs["L0_attn_mlp_w2"]))
    h["W_l1a_q"] = tile_w(np.asarray(inputs["L1_attn_Wq"]))
    h["W_l1a_k"] = tile_w(np.asarray(inputs["L1_attn_Wk"]))
    h["W_l1a_v"] = tile_w(np.asarray(inputs["L1_attn_Wv"]))
    h["W_l1a_b"] = tile_w(np.asarray(inputs["L1_attn_Wb"]))
    h["W_l1a_g"] = tile_w(np.asarray(inputs["L1_attn_Wg"]))
    h["W_l1a_o"] = tile_w(np.asarray(inputs["L1_attn_Wo"]))
    h["W_l1a_m1"] = tile_w(np.asarray(inputs["L1_attn_w1"]))
    h["W_l1a_m2"] = tile_w(np.asarray(inputs["L1_attn_w2"]))
    h["W_l1f_th"] = tile_w(np.asarray(inputs["L1_ffn_theta"]))
    h["W_l1f_thn"] = tile_w(np.asarray(inputs["L1_ffn_theta_noise"]))
    h["W_l1f_ph"] = tile_w(np.asarray(inputs["L1_ffn_phi"]))
    h["W_l1f_phn"] = tile_w(np.asarray(inputs["L1_ffn_phi_noise"]))
    cols, reg = build_cols(inputs)
    h["cols_in"] = cols
    h["rows_in"] = np.asarray(inputs["L0_attn_state_b"], np.float32)[None, :]
    cf, cb = make_consts()
    h["cons_f"] = cf
    h["cons_b"] = cb
    return h, reg


def declare_drams(nc, h, blocks, fused, rin_shape):
    d = {}
    for k, v in h.items():
        d[k] = nc.dram_tensor(k, list(v.shape), BF16 if v.dtype == ml_dtypes.bfloat16 else F32,
                              kind="ExternalInput").ap()
    d["x_fm"] = nc.dram_tensor("x_fm", list(rin_shape), F32, kind="ExternalInput").ap()
    d["y_fm"] = nc.dram_tensor("y_fm", list(rin_shape), F32, kind="ExternalOutput").ap()
    return d


def alloc_tensors(ctx, es):
    nc = ctx.nc
    t = ctx.t
    t["R"] = es.enter_context(nc.sbuf_tensor("sb_R", [128, 8, 1024], F32))
    t["arenaA"] = es.enter_context(nc.sbuf_tensor("sb_arenaA", [128, 32768], BF16))
    t["arenaB"] = es.enter_context(nc.sbuf_tensor("sb_arenaB", [128, 32768], BF16))
    t["wstage"] = es.enter_context(nc.sbuf_tensor("sb_wstage", [128, 1024], F32))
    t["wbf"] = es.enter_context(nc.sbuf_tensor("sb_wbf", [128, 8192], BF16))
    t["scr"] = es.enter_context(nc.sbuf_tensor("sb_scr", [128, 2048], F32))
    t["scr2"] = es.enter_context(nc.sbuf_tensor("sb_scr2", [128, 3584], BF16))
    t["cols"] = es.enter_context(nc.sbuf_tensor("sb_cols", [128, 384], F32))
    t["cons_f"] = es.enter_context(nc.sbuf_tensor("sb_cons_f", [128, 768], F32))
    t["cons_bf"] = es.enter_context(nc.sbuf_tensor("sb_cons_bf", [128, 769], BF16))
    t["Sstate"] = es.enter_context(nc.sbuf_tensor("sb_Sstate", [128, 8, 64], F32))
    t["rows_bf"] = es.enter_context(nc.sbuf_tensor("sb_rows_bf", [1, 1040], BF16))
    t["carry"] = es.enter_context(nc.sbuf_tensor("sb_carry", [1, 1024], BF16))
    t["wbar"] = es.enter_context(nc.sbuf_tensor("sb_wbar", [128, 18], F32))
    t["psA"] = es.enter_context(nc.psum_tensor("sb_psA", [128, 4, 512], F32))
    t["psB"] = es.enter_context(nc.psum_tensor("sb_psB", [128, 3, 512], F32))
    t["psC"] = es.enter_context(nc.psum_tensor("sb_psC", [128, 1024], BF16))


def emit_init(ctx, d):
    p = ctx.p
    rows = ctx.t["scr"][0:1, 0:1024]
    dms = []
    for name, dst in [("cols_in", "cols"), ("cons_f", "cons_f"), ("cons_b", "cons_bf")]:
        nn = d[name].shape[1]
        dms.append(p.dma("sync", lambda E, S, dd=ctx.t[dst][:, 0:nn], s=d[name]:
                         E.dma_start(out=dd, in_=s)))
    dms.append(p.dma("sync", lambda E, S, dd=rows, s=d["rows_in"]: E.dma_start(out=dd, in_=s)))
    for sem, val in dms:
        p.wait("vector", sem, val)
    s, n = ctx.colreg["l0f_hs3"]
    p.emit("vector", lambda E, S: E.tensor_scalar_mul(
        ctx.t["cols"][:, s:s + n], ctx.t["cols"][:, s:s + n], 0.5), mark="ve")
    p.emit("vector", lambda E, S, rows=rows: E.tensor_copy(ctx.t["rows_bf"][:, 0:1024], rows),
           mark="ve")
    bbar_f = ctx.t["scr"][0:1, 1024:1025]
    p.emit("vector", lambda E, S, rows=rows, d=bbar_f: E.reduce_sum(d, rows, axis=mybir.AxisListType.X),
           mark="ve")
    p.selfwait("vector")
    p.emit("vector", lambda E, S, d=ctx.t["rows_bf"][0:1, 1024:1025], s=bbar_f:
           E.tensor_scalar_mul(d, s, 1.0 / 1024.0), mark="ve")
    p.emit("vector", lambda E, S: E.memset(ctx.t["carry"][:], 0.0), mark="ve")
    p.emit("vector", lambda E, S: E.memset(ctx.t["scr2"][:], 0.0), mark="ve")
    p.emit("vector", lambda E, S: E.memset(ctx.t["arenaA"][:], 0.0), mark="ve")
    p.emit("vector", lambda E, S: E.memset(ctx.t["arenaB"][:], 0.0), mark="ve")
    p.selfwait("vector")
    if os.environ.get("BASS_SIM_FP64") == "1" or os.environ.get("KMK_SIM") == "1":
        p.emit("vector", lambda E, S: E.memset(ctx.t["psC"][:], 0.0), mark="ve")
        p.emit("vector", lambda E, S: E.memset(ctx.t["psB"][:], 0.0), mark="ve")
    p.emit("vector", lambda E, S: E.memset(ctx.t["scr"][:], 0.0), mark="ve")
    p.emit("vector", lambda E, S: E.memset(ctx.t["Sstate"][:], 0.0), mark="ve")
    p.emit("vector", lambda E, S: E.memset(ctx.t["wbar"][:], 0.0), mark="ve")
    ctx.init_done = p.cnt["ve"]


def emit_strip(ctx, d, W, strip, blocks):
    p = ctx.p
    t0 = strip * TSTRIP
    ctx.rload = p.dma("sync", lambda E, S: E.dma_start(
        out=ctx.t["R"][:], in_=d["x_fm"][:, :, t0:t0 + TSTRIP]))

    def Rsrc(fc, t):
        return ctx.t["R"][:, fc, t * 512:(t + 1) * 512]

    def zdst(fc, t):
        return aslot(ctx, "arenaA", fc, t)

    first = True
    for blk in blocks:
        if first:
            p.wait("vector", ctx.rload[0], ctx.rload[1])
            first = False
        if blk in ("l0ffn_a", "l0ffn_b"):
            emit_ln(ctx, Rsrc, zdst, affine=None)
            emit_l0ffn(ctx, {k: W[k] for k in ("w1", "w2", "h1", "h2", "w3")})
        elif blk in ("l1ffn_a", "l1ffn_b"):
            emit_ln(ctx, Rsrc, zdst, affine=None)
            emit_l1ffn(ctx, {"cos": W["cos"], "sin": W["sin"]})
        elif blk == "l0attn":
            emit_ln(ctx, Rsrc, zdst, affine=None)
            emit_l0attn(ctx, W)
        elif blk == "l1attn":
            emit_ln(ctx, Rsrc, zdst, affine=None)
            emit_l1attn(ctx, W)
        elif blk == "lntest":
            emit_ln(ctx, Rsrc, zdst, affine=None)
            for t in range(NT):
                for fc in range(FC):
                    p.wait("vector", "gp", p.cnt["gp"])
                    p.emit("vector", lambda E, S, d=Rsrc(fc, t), s=zdst(fc, t):
                           E.tensor_copy(d, s), mark="ve")
        elif blk == "lnafftest":
            emit_ln(ctx, Rsrc, zdst, affine=("l0a_n1w", "l0a_n1b"))
            p.selfwait("vector")
            for t in range(NT):
                for fc in range(FC):
                    p.selfwait("vector")
                    p.emit("vector", lambda E, S, d=Rsrc(fc, t), s=zdst(fc, t):
                           E.tensor_copy(d, s), mark="ve")
    p.wait("gpsimd", "ve", p.cnt["ve"])
    p.wait("gpsimd", "sc", p.cnt["sc"])
    p.wait("gpsimd", "stdma", p.cnt["stdma"])
    p.emit("gpsimd", lambda E, S: E.dma_start(
        out=d["y_fm"][:, :, t0:t0 + TSTRIP], in_=ctx.t["R"][:]), mark="stdma", n=16)
    p.wait("sync", "stdma", p.cnt["stdma"])


def make_wsrcs(ctx, d):
    return dict(
        w1=WSrc(d["W_l0f_w1"], 8), w2=WSrc(d["W_l0f_w2"], 8),
        h1=WSrc(d["W_l0f_h1"], 32), h2=WSrc(d["W_l0f_h2"], 32),
        w3=WSrc(d["W_l0f_w3"], 32),
        gate=WSrc(d["W_l0a_gate"], 8), out=WSrc(d["W_l0a_out"], 8),
        mlp1=WSrc(d["W_l0a_mlp1"], 8), mlp2=WSrc(d["W_l0a_mlp2"], 32),
        state_nat=d["W_l0a_state"],
        q=WSrc(d["W_l1a_q"], 8), k=WSrc(d["W_l1a_k"], 8), v=WSrc(d["W_l1a_v"], 8),
        wb=WSrc(d["W_l1a_b"], 8, mw=16), wg=WSrc(d["W_l1a_g"], 8, mw=16),
        wo=WSrc(d["W_l1a_o"], 8), m1=WSrc(d["W_l1a_m1"], 8), m2=WSrc(d["W_l1a_m2"], 32),
        cos=WSinCos(d["W_l1f_th"], d["W_l1f_thn"], True),
        sin=WSinCos(d["W_l1f_ph"], d["W_l1f_phn"], False),
    )


def build_program(h, reg, blocks, n_strips=NSTRIP):
    nc = bass.Bass("TRN2", target_bir_lowering=False)
    d = declare_drams(nc, h, blocks, True, [128, 8, SEQ])
    prog = Prog()
    ctx = Ctx(nc, prog)
    ctx.colreg = reg
    es = contextlib.ExitStack()
    with es:
        alloc_tensors(ctx, es)
        sems = {}
        for sn in SEMNAMES:
            sems[sn] = es.enter_context(nc.semaphore(sn))
        emit_init(ctx, d)
        W = make_wsrcs(ctx, d)
        for strip in range(n_strips):
            emit_strip(ctx, d, W, strip, blocks)
        prog.wait("gpsimd", "stdma", prog.cnt["stdma"])
        prog.replay(nc, sems)
    return nc


def kernel(**inputs):
    h, reg = prep_host(inputs)
    x = np.asarray(inputs["L0_attn_x"])
    nc = build_program(h, reg, list(BLOCKS))
    in_maps = []
    for c in range(NCORES):
        m = dict(h)
        m["x_fm"] = fm_of(x[c])
        in_maps.append(m)
    res = run_bass_kernel_spmd(nc, in_maps, core_ids=list(range(NCORES)))
    out = np.empty((NCORES, SEQ, D), np.float32)
    for c in range(NCORES):
        y = res.results[c]["y_fm"]
        out[c] = y.transpose(1, 0, 2).reshape(D, SEQ).T
    return out


def emit_tok_state_pass(ctx, W, srcs, dsts, seed_row, wbar=True):
    p = ctx.p
    rhs_barrier(ctx)
    ctx.pass_id = getattr(ctx, "pass_id", 0) + 1
    pass_te0 = p.cnt["te"]
    wk = [ctx.t["wbf"][:, k * 1024:(k + 1) * 1024] for k in range(8)]
    ctx.stage_i = 0
    for k in range(8):
        for half in range(2):
            hs = ctx.stage_i % 2
            ctx.stage_i += 1
            stage = ctx.t["wstage"][:, hs * 512:(hs + 1) * 512]
            gg = ctx.wstage_guard[hs]
            if gg is not None:
                p.wait("sync", gg[0], gg[1])
            src = W[k * 128:(k + 1) * 128, half * 512:(half + 1) * 512]
            dm = p.dma("sync", lambda E, S, d=stage, s=src: E.dma_start(out=d, in_=s))
            p.wait("vector", dm[0], dm[1])
            if k == 0 and half == 0:
                p.wait("vector", "te", pass_te0)
            p.emit("vector", lambda E, S, d=wk[k][:, half * 512:(half + 1) * 512], s=stage:
                   E.tensor_copy(d, s), mark="ve")
            ctx.wstage_guard[hs] = ("ve", p.cnt["ve"])
        if wbar:
            p.selfwait("vector")
            p.emit("vector", lambda E, S, d=ctx.t["wbar"][:, k:k + 1], s=wk[k]:
                   E.reduce_sum(d, s, axis=mybir.AxisListType.X), mark="ve")
    if wbar:
        p.selfwait("vector")
        p.emit("vector", lambda E, S: E.tensor_scalar_mul(
            ctx.t["wbar"][:, 0:8], ctx.t["wbar"][:, 0:8], 1.0 / 1024.0), mark="ve")
        p.selfwait("vector")
        p.emit("vector", lambda E, S: E.tensor_copy(
            ctx.t["scr2"][:, 3520:3528], ctx.t["wbar"][:, 0:8]), mark="ve")
    cast_done = p.cnt["ve"]
    p.wait("tensor", "ve", cast_done)
    for tc in range(8):
        for si, src_fn in enumerate(srcs):
            for half in range(2):
                b = ctx.next_bank()
                ctx.guard_bank(b)
                ps = ctx.t["psA"][:, b, 0:512]
                p.emit("tensor", lambda E, S, o=ps, r=seed_row(half):
                       E.matmul(o, ctx.t["cons_bf"][0:1, CB_ONES:CB_ONES + 128], r,
                                start=True, stop=False))
                for k in range(8):
                    p.emit("tensor", lambda E, S, o=ps, l=src_fn(k, tc),
                                  r=wk[k][:, half * 512:(half + 1) * 512], sp=(k == 7):
                           E.matmul(o, l, r, start=False, stop=sp),
                           mark="te" if k == 7 else None)
                p.wait("scalar", "te", p.cnt["te"])
                p.emit("scalar", lambda E, S, d=dsts[si](tc, half), s=ps:
                       E.activation(d, s, AF.Copy), mark="sc")
                ctx.set_bank_guard(b, "sc")


def emit_l0attn(ctx, W):
    p = ctx.p

    def z_ap(fc, t):
        return aslot(ctx, "arenaA", fc, t)

    def z_tok(kc, tc):
        return ctx.t["arenaA"][:, kc * 1024 + tc * 128: kc * 1024 + tc * 128 + 128]

    war_barrier(ctx, "vector")
    p.selfwait("vector")
    p.wait("vector", "gp", p.cnt["gp"])
    for t in range(NT):
        for fc in range(FC):
            p.emit("vector", lambda E, S, d=ctx.t["R"][:, fc, t * 512:(t + 1) * 512],
                          s=z_ap(fc, t): E.tensor_add(d, d, s), mark="ve")

    def xn1_ap(fc, t):
        return aslot(ctx, "arenaB", fc, t)
    emit_ln(ctx, z_ap, xn1_ap, affine=("l0a_n1w", "l0a_n1b"), src_bf16=True)

    def rhs_xn1(wi, k, t):
        return aslot(ctx, "arenaB", k, t)

    def cb_gate(wi, mc, t, ps, b, banks):
        p.wait("scalar", "te", p.cnt["te"])
        dst = aslot(ctx, "arenaB", 8 + mc, t)
        p.emit("scalar", lambda E, S, d=dst, s=ps:
               E.activation(d, s, AF.Silu, bias=ctx.col("l0a_gb", mc)), mark="sc")
        ctx.set_bank_guard(b, "sc")

    emit_linear_fm(ctx, [W["gate"]], rhs_xn1, cb_gate, Mc=16)

    war_barrier(ctx, "vector")
    p.wait("vector", "sc", p.cnt["sc"])
    for t in range(NT):
        for fc in range(FC):
            p.emit("vector", lambda E, S, d=aslot(ctx, "arenaA", 8 + fc, t),
                          a=aslot(ctx, "arenaB", fc, t), g=aslot(ctx, "arenaB", 8 + fc, t):
                   E.tensor_mul(d, a, g), mark="ve")

    def xg_tok(kc, tc):
        return ctx.t["arenaA"][:, (8 + kc) * 1024 + tc * 128: (8 + kc) * 1024 + tc * 128 + 128]

    def gg_tok(kc, tc):
        return ctx.t["arenaB"][:, (16 + kc) * 1024 + tc * 128: (16 + kc) * 1024 + tc * 128 + 128]

    def u_dst(tc, half):
        return ctx.t["arenaB"][:, (24 + tc) * 1024 + half * 512: (24 + tc) * 1024 + half * 512 + 512]

    def bm_dst(tc, half):
        return ctx.t["arenaA"][:, (16 + tc) * 1024 + half * 512: (16 + tc) * 1024 + half * 512 + 512]

    def seed_row(half):
        return ctx.t["rows_bf"][0:1, half * 512:(half + 1) * 512]

    emit_tok_state_pass(ctx, W["state_nat"], [xg_tok, gg_tok], [u_dst, bm_dst], seed_row)

    scr = ctx.t["scr"]
    dcol = scr[:, 1536:1544]
    ps_d = ctx.t["psB"][:, 2, 0:1]
    for tc in range(8):
        p.wait("tensor", "sc", p.cnt["sc"])
        p.emit("tensor", lambda E, S, o=ps_d:
               E.matmul(o, ctx.t["cons_bf"][0:1, CB_ONES:CB_ONES + 128],
                        ctx.t["rows_bf"][0:1, 1024:1025], start=True, stop=False))
        wb_bf = ctx.t["scr2"][:, 3520:3528]
        for k in range(8):
            p.emit("tensor", lambda E, S, o=ps_d, l=xn1_tok(ctx, k, tc),
                          r=wb_bf[:, k:k + 1], sp=(k == 7):
                   E.matmul(o, l, r, start=False, stop=sp), mark="te" if k == 7 else None)
        p.wait("scalar", "te", p.cnt["te"])
        p.emit("scalar", lambda E, S, d=dcol[:, tc:tc + 1], s=ps_d:
               E.activation(d, s, AF.Copy), mark="sc")
    t1 = scr[:, 1544:1552]
    t2 = scr[:, 1552:1560]
    t3 = scr[:, 1560:1568]
    p.selfwait("scalar")
    p.emit("scalar", lambda E, S: E.activation(t1, dcol, AF.Abs), mark="sc")
    p.selfwait("scalar")
    p.emit("scalar", lambda E, S: E.activation(t2, t1, AF.Exp, scale=ctx.col("c_neg1")), mark="sc")
    p.selfwait("scalar")
    p.emit("scalar", lambda E, S: E.activation(t3, t2, AF.Ln, bias=ctx.col("c_1")), mark="sc")
    p.wait("vector", "sc", p.cnt["sc"])
    p.selfwait("vector")
    p.emit("vector", lambda E, S: E.tensor_relu(t1, dcol), mark="ve")
    p.selfwait("vector")
    p.emit("vector", lambda E, S: E.tensor_add(dcol, t1, t3), mark="ve")

    p.wait("vector", "sc", p.cnt["sc"])
    for tc in range(8):
        dbu = ctx.t["arenaA"][:, (24 + tc) * 1024: (25 + tc) * 1024]
        u = ctx.t["arenaB"][:, (24 + tc) * 1024: (25 + tc) * 1024]
        bm = ctx.t["arenaA"][:, (16 + tc) * 1024: (17 + tc) * 1024]
        p.emit("vector", lambda E, S, d=dbu, a=u, b=bm: E.tensor_mul(d, a, b), mark="ve")
        p.selfwait("vector")
        p.emit("vector", lambda E, S, d=dbu, c=dcol[:, tc:tc + 1]:
               E.tensor_scalar_mul(d, d, c), mark="ve")

    consf = ctx.t["cons_f"]
    utri_f = consf[:, CF_UTRI:CF_UTRI + 128]
    ident_f = consf[:, CF_IDENT:CF_IDENT + 128]
    onesrow_f = consf[0:1, CF_UTRI:CF_UTRI + 128]
    utri_b = ctx.t["cons_bf"][:, CB_UTRI:CB_UTRI + 128]
    scr2 = ctx.t["scr2"]
    M_sb = scr2[:, 2048:2176]
    coef = scr2[0:1, 2176:2304]
    dcol_sb = scr[:, 1568:1569]
    drow_sb = scr[0:1, 1569:1697]
    rhs_barrier(ctx)
    for tc in range(8):
        ps_dc = ctx.t["psB"][:, 2, 0:1]
        ps_dr = ctx.t["psB"][0:1, 2, 128:256]
        ps_bc = ctx.t["psB"][:, 2, 256:384]
        p.wait("tensor", "ve", p.cnt["ve"])
        p.wait("tensor", "sc", p.cnt["sc"])
        p.emit("tensor", lambda E, S, o=ps_dc, r=dcol[:, tc:tc + 1]:
               E.matmul(o, utri_f, r, start=True, stop=True), mark="te")
        p.wait("scalar", "te", p.cnt["te"])
        p.selfwait("scalar")
        p.emit("scalar", lambda E, S, d=dcol_sb, s=ps_dc:
               E.activation(d, s, AF.Copy), mark="sc")
        p.wait("tensor", "sc", p.cnt["sc"])
        p.emit("tensor", lambda E, S, o=ps_dr, i=dcol_sb:
               E.transpose(o, i, ident_f), mark="te")
        p.wait("scalar", "te", p.cnt["te"])
        p.selfwait("scalar")
        p.emit("scalar", lambda E, S, d=drow_sb, s=ps_dr:
               E.activation(d, s, AF.Copy), mark="sc")
        p.wait("tensor", "sc", p.cnt["sc"])
        p.emit("tensor", lambda E, S, o=ps_bc, r=drow_sb:
               E.matmul(o, onesrow_f, r, start=True, stop=True), mark="te")
        p.wait("vector", "te", p.cnt["te"])
        p.selfwait("vector")
        marg = scr[:, 0:128]
        p.emit("vector", lambda E, S, d=marg, s=ps_bc, m=consf[:, CF_MBLE:CF_MBLE + 128]:
               E.tensor_sub(d, s, m), mark="ve")
        p.wait("scalar", "ve", p.cnt["ve"])
        p.wait("scalar", "te", p.cnt["te"])
        p.selfwait("scalar")
        p.emit("scalar", lambda E, S, d=M_sb, s=marg:
               E.activation(d, s, AF.Exp, scale=ctx.col("c_neg1"), bias=dcol_sb), mark="sc")
        p.selfwait("scalar")
        p.emit("scalar", lambda E, S, d=coef, s=ps_dr:
               E.activation(d, s, AF.Exp, scale=ctx.col1("c_neg1")), mark="sc")
        p.wait("vector", "sc", p.cnt["sc"])
        b1 = ctx.next_bank()
        b2 = ctx.next_bank()
        ctx.guard_bank(b1)
        ctx.guard_bank(b2)
        p.wait("tensor", "ve", p.cnt["ve"])
        p.wait("tensor", "sc", p.cnt["sc"])
        for fc in range(8):
            bk = b1 if fc < 4 else b2
            reg = ctx.t["psA"][:, bk, (fc % 4) * 128:(fc % 4) * 128 + 128]
            dbu = ctx.t["arenaA"][:, (24 + tc) * 1024 + fc * 128: (24 + tc) * 1024 + fc * 128 + 128]
            p.emit("tensor", lambda E, S, o=reg, l=dbu, m=M_sb:
                   E.matmul(o, l, m, start=True, stop=False))
            p.emit("tensor", lambda E, S, o=reg, l=ctx.t["carry"][0:1, fc * 128:(fc + 1) * 128],
                          r=coef: E.matmul(o, l, r, start=False, stop=True),
                   mark="te" if fc == 7 else None)
        ymark = p.cnt["te"]
        ps_c0 = ctx.t["psB"][0:1, 0, 0:512]
        ps_c1 = ctx.t["psB"][0:1, 1, 0:512]
        for half in range(2):
            psc = ps_c0 if half == 0 else ps_c1
            dbuh = ctx.t["arenaA"][:, (24 + tc) * 1024 + half * 512: (24 + tc) * 1024 + half * 512 + 512]
            p.emit("tensor", lambda E, S, o=psc, l=M_sb[:, 127:128], r=dbuh:
                   E.matmul(o, l, r, start=True, stop=False))
            p.emit("tensor", lambda E, S, o=psc, l=coef[0:1, 127:128],
                          r=ctx.t["carry"][0:1, half * 512:(half + 1) * 512]:
                   E.matmul(o, l, r, start=False, stop=True), mark="te")
        p.wait("scalar", "te", p.cnt["te"])
        p.selfwait("scalar")
        p.emit("scalar", lambda E, S, d=ctx.t["carry"][0:1, 0:512], s=ps_c0:
               E.activation(d, s, AF.Copy), mark="sc")
        p.emit("scalar", lambda E, S, d=ctx.t["carry"][0:1, 512:1024], s=ps_c1:
               E.activation(d, s, AF.Copy), mark="sc")
        for i, bk in enumerate((b1, b2)):
            ydst = ctx.t["arenaB"][:, 0:8 * 1024].rearrange("p (f c) -> p f c", c=1024)[
                :, i * 4:(i + 1) * 4, tc * 128:(tc + 1) * 128]
            ysrc = ctx.t["psA"][:, bk, 0:512].rearrange("p (f c) -> p f c", c=128)
            p.emit("scalar", lambda E, S, d=ydst, s=ysrc:
                   E.activation(d, s, AF.Copy), mark="sc")
        ctx.set_bank_guard(b1, "sc")
        ctx.set_bank_guard(b2, "sc")

    def rhs_y(wi, k, t):
        return aslot(ctx, "arenaB", k, t)

    def cb_out(wi, mc, t, ps, b, banks):
        p.wait("vector", "te", p.cnt["te"])
        p.selfwait("vector")
        R = ctx.t["R"][:, mc, t * 512:(t + 1) * 512]
        p.emit("vector", lambda E, S, d=R, s=ps, c=ctx.col("l0a_ob", mc):
               E.scalar_tensor_tensor(d, s, c, d, ALU.add, ALU.add), mark="ve")
        p.emit("vector", lambda E, S, d=x1_ap(ctx, mc, t), s=ps,
                      c=ctx.col("l0a_ob", mc), z=z_ap(mc, t):
               E.scalar_tensor_tensor(d, s, c, z, ALU.add, ALU.add), mark="ve")
        ctx.set_bank_guard(b, "ve")

    emit_linear_fm(ctx, [W["out"]], rhs_y, cb_out, Mc=8)

    def x1src(fc, t):
        return x1_ap(ctx, fc, t)

    def xn2_ap(fc, t):
        return aslot(ctx, "arenaB", fc, t)
    emit_ln(ctx, x1src, xn2_ap, affine=("l0a_n2w", "l0a_n2b"))

    def rhs_xn2(wi, k, t):
        return aslot(ctx, "arenaB", k, t)

    def h_ap(kc, t):
        if kc < 16:
            return aslot(ctx, "arenaA", kc, t)
        return aslot(ctx, "arenaB", 8 + (kc - 16), t)

    def cb_m1(wi, mc, t, ps, b, banks):
        p.wait("scalar", "te", p.cnt["te"])
        p.emit("scalar", lambda E, S, d=h_ap(mc, t), s=ps:
               E.activation(d, s, AF.Gelu, bias=ctx.col("l0a_mb1", mc)), mark="sc")
        ctx.set_bank_guard(b, "sc")

    emit_linear_fm(ctx, [W["mlp1"]], rhs_xn2, cb_m1, Mc=32)

    def rhs_h(wi, k, t):
        return h_ap(k, t)

    def cb_m2(wi, mc, t, ps, b, banks):
        p.wait("vector", "te", p.cnt["te"])
        p.selfwait("vector")
        R = ctx.t["R"][:, mc, t * 512:(t + 1) * 512]
        p.emit("vector", lambda E, S, d=R, s=ps, c=ctx.col("l0a_mb2", mc):
               E.scalar_tensor_tensor(d, s, c, d, ALU.add, ALU.add), mark="ve")
        ctx.set_bank_guard(b, "ve")

    emit_linear_fm(ctx, [W["mlp2"]], rhs_h, cb_m2, Mc=8)


def xn1_tok(ctx, kc, tc):
    return ctx.t["arenaB"][:, kc * 1024 + tc * 128: kc * 1024 + tc * 128 + 128]


def emit_l1attn(ctx, W):
    p = ctx.p
    scr = ctx.t["scr"]
    scr2 = ctx.t["scr2"]
    s2f = scr2[:].bitcast(F32)
    consb = ctx.t["cons_bf"]
    consf = ctx.t["cons_f"]
    utri_f = consf[:, CF_UTRI:CF_UTRI + 128]
    ident_f = consf[:, CF_IDENT:CF_IDENT + 128]
    onesrow_f = consf[0:1, CF_UTRI:CF_UTRI + 128]
    ident_b = consb[:, CB_IDENT:CB_IDENT + 128]
    ones_b = consb[:, CB_ONES:CB_ONES + 128]
    utri_b = consb[:, CB_UTRI:CB_UTRI + 128]
    maskL = consb[:, CB_MASKL:CB_MASKL + 128]
    maskU = consb[:, CB_MASKU:CB_MASKU + 128]

    def z_ap(fc, t):
        return aslot(ctx, "arenaA", fc, t)

    war_barrier(ctx, "vector")
    p.selfwait("vector")
    p.wait("vector", "gp", p.cnt["gp"])
    for t in range(NT):
        for fc in range(FC):
            p.emit("vector", lambda E, S, d=ctx.t["R"][:, fc, t * 512:(t + 1) * 512],
                          s=z_ap(fc, t): E.tensor_add(d, d, s), mark="ve")

    def xn1_ap(fc, t):
        return aslot(ctx, "arenaB", fc, t)
    emit_ln(ctx, z_ap, xn1_ap, affine=("l1a_ln1g", "l1a_ln1b"), src_bf16=True)

    def rhs_xn1(wi, k, t):
        return aslot(ctx, "arenaB", k, t)

    def mk_cb(base_slot, scale_col):
        def cb(wi, mc, t, ps, b, banks):
            p.wait("scalar", "te", p.cnt["te"])
            dst = aslot(ctx, "arenaA", base_slot + mc, t)
            if scale_col is None:
                p.emit("scalar", lambda E, S, d=dst, s=ps:
                       E.activation(d, s, AF.Copy), mark="sc")
            else:
                p.emit("scalar", lambda E, S, d=dst, s=ps:
                       E.activation(d, s, AF.Copy, scale=ctx.col(scale_col)), mark="sc")
            ctx.set_bank_guard(b, "sc")
        return cb

    emit_linear_fm(ctx, [W["q"]], rhs_xn1, mk_cb(8, "c_eighth"), Mc=8)
    emit_linear_fm(ctx, [W["k"]], rhs_xn1, mk_cb(16, None), Mc=8)
    emit_linear_fm(ctx, [W["v"]], rhs_xn1, mk_cb(24, None), Mc=8)

    lgT = lambda tc: scr[:, 448 + tc * 16: 448 + tc * 16 + 16]
    bT = lambda tc: scr[:, 576 + tc * 16: 576 + tc * 16 + 16]
    knT = lambda tc: scr[:, 704 + tc * 16: 704 + tc * 16 + 16]
    rowtmp = scr2[0:16, 0:512]
    r1 = scr[0:16, 832:1344]
    r2 = scr[0:16, 1344:1856]
    rowf = scr[0:16, 0:512]

    def cb_bg(wi, mc, t, ps, b, banks):
        p.wait("scalar", "te", p.cnt["te"])
        p.wait("scalar", "ve", p.cnt["ve"])
        p.selfwait("scalar")
        if wi == 0:
            p.emit("scalar", lambda E, S, d=r1, s=ps:
                   E.activation(d, s, AF.Exp, scale=ctx.t["cols"][0:16,
                       ctx.colreg["c_neg1"][0]:ctx.colreg["c_neg1"][0] + 1]), mark="sc")
            p.wait("vector", "sc", p.cnt["sc"])
            p.wait("vector", "te", p.cnt["te"])
            p.selfwait("vector")
            p.emit("vector", lambda E, S, d=r1: E.tensor_scalar_add(d, d, 1.0), mark="ve")
            p.selfwait("vector")
            p.emit("vector", lambda E, S, d=r2, s=r1: E.reciprocal(d, s), mark="ve")
            p.wait("tensor", "ve", p.cnt["ve"])
        else:
            p.emit("scalar", lambda E, S, d=r1, s=ps:
                   E.activation(d, s, AF.Abs), mark="sc")
            p.selfwait("scalar")
            p.emit("scalar", lambda E, S, d=r2, s=r1:
                   E.activation(d, s, AF.Exp, scale=ctx.t["cols"][0:16,
                       ctx.colreg["c_neg1"][0]:ctx.colreg["c_neg1"][0] + 1]), mark="sc")
            p.selfwait("scalar")
            p.emit("scalar", lambda E, S, d=r1, s=r2:
                   E.activation(d, s, AF.Ln, bias=ctx.t["cols"][0:16,
                       ctx.colreg["c_1"][0]:ctx.colreg["c_1"][0] + 1]), mark="sc")
            p.wait("vector", "sc", p.cnt["sc"])
            p.wait("vector", "te", p.cnt["te"])
            p.selfwait("vector")
            p.emit("vector", lambda E, S, d=r2, s=ps:
                   E.tensor_scalar(d, s, -1.0, 0.0, ALU.mult, ALU.max), mark="ve")
            p.selfwait("vector")
            p.emit("vector", lambda E, S, d=rowf, a=r1, bb=r2:
                   E.scalar_tensor_tensor(d, a, -1.0, bb, ALU.mult, ALU.subtract), mark="ve")
            p.wait("tensor", "ve", p.cnt["ve"])
        ctx.set_bank_guard(b, "ve")
        for i in range(4):
            tc = t * 4 + i
            p.wait("tensor", "sc", p.cnt["sc"])
            if wi == 1:
                pf = ctx.t["psB"][:, 0, i * 16:(i + 1) * 16]
                p.emit("tensor", lambda E, S, o=pf, s=rowf[:, i * 128:(i + 1) * 128]:
                       E.transpose(o, s, ident_f[0:16, 0:16]), mark="te")
                p.wait("scalar", "te", p.cnt["te"])
                p.selfwait("scalar")
                p.emit("scalar", lambda E, S, d=lgT(tc), s=pf:
                       E.activation(d, s, AF.Copy), mark="sc")
            else:
                pf = ctx.t["psB"][:, 0, 64 + i * 16:64 + (i + 1) * 16]
                p.emit("tensor", lambda E, S, o=pf, s=r2[:, i * 128:(i + 1) * 128]:
                       E.transpose(o, s, ident_f[0:16, 0:16]), mark="te")
                p.wait("scalar", "te", p.cnt["te"])
                p.selfwait("scalar")
                p.emit("scalar", lambda E, S, d=bT(tc), s=pf:
                       E.activation(d, s, AF.Copy), mark="sc")

    emit_linear_fm(ctx, [W["wb"], W["wg"]], rhs_xn1, cb_bg, Mc=1)

    ksq = [scr2[:, 0:512], scr2[:, 512:1024]]
    ps_kn = ctx.t["psB"][0:16, 0, 0:512]
    for t in range(NT):
        p.wait("scalar", "ve", p.cnt["ve"])
        p.wait("scalar", "te", p.cnt["te"])
        for fc in range(FC):
            p.selfwait("scalar")
            if fc >= 2:
                p.wait("scalar", "te", p.cnt["te"])
            p.emit("scalar", lambda E, S, d=ksq[fc % 2], s=aslot(ctx, "arenaA", 16 + fc, t):
                   E.activation(d, s, AF.Square), mark="sc")
            p.wait("tensor", "sc", p.cnt["sc"])
            p.emit("tensor", lambda E, S, o=ps_kn, l=consb[:, CB_KNSEL + fc * 16:CB_KNSEL + fc * 16 + 16],
                          r=ksq[fc % 2], st=(fc == 0), sp=(fc == FC - 1):
                   E.matmul(o, l, r, start=st, stop=sp), mark="te")
        p.wait("scalar", "te", p.cnt["te"])
        p.selfwait("scalar")
        p.emit("scalar", lambda E, S, d=rowtmp, s=ps_kn:
               E.activation(d, s, AF.Copy), mark="sc")
        for i in range(4):
            tc = t * 4 + i
            pc = ctx.t["psC"][:, i * 16:(i + 1) * 16]
            p.wait("tensor", "sc", p.cnt["sc"])
            p.emit("tensor", lambda E, S, o=pc, s=rowtmp[:, i * 128:(i + 1) * 128]:
                   E.transpose(o, s, ident_b[0:16, 0:16]), mark="te")
            p.wait("scalar", "te", p.cnt["te"])
            p.selfwait("scalar")
            p.emit("scalar", lambda E, S, d=scr[:, 192:208], s=pc:
                   E.activation(d, s, AF.Ln), mark="sc")
            p.selfwait("scalar")
            p.emit("scalar", lambda E, S, d=knT(tc), s=scr[:, 192:208]:
                   E.activation(d, s, AF.Exp, scale=ctx.col("c_half")), mark="sc")
            p.wait("vector", "sc", p.cnt["sc"])
            p.selfwait("vector")
            p.emit("vector", lambda E, S, d=knT(tc):
                   E.tensor_scalar_add(d, d, 1e-6), mark="ve")
            p.selfwait("vector")
            p.emit("vector", lambda E, S, d=knT(tc), s=knT(tc):
                   E.reciprocal(d, s), mark="ve")

    cumlg = scr[:, 832:848]
    gamT = scr[:, 848:864]
    negcl = scr[:, 864:880]
    cumlnb = scr[:, 880:896]
    lnb = scr[:, 896:912]
    negb = scr[:, 912:928]
    gCc = scr[:, 928:944]
    c_sb = scr[:, 944:945]
    gCrow = scr[0:1, 945:961]
    lcrow = scr[0:1, 961:977]
    sC = scr[:, 977:993]
    f32a = [scr[:, 0:64], scr[:, 1024:1088]]
    f32b = [scr[:, 64:192], scr[:, 1088:1216]]
    diagf = [scr[:, 192:320], scr[:, 1216:1344]]
    argf = [scr[:, 320:448], scr[:, 1344:1472]]

    def _pt(base):
        off = [0]
        def take(n):
            o = base + off[0]
            off[0] += n
            return scr2[:, o:o + n]
        return take
    tk = [None, None]
    tiles = [{}, {}]
    for hp in range(2):
        take = _pt(hp * 1216)
        tiles[hp] = dict(
            kh=take(128), Adec=take(128), qh=take(128), B=take(128), BT=take(128),
            Mkq=take(128), U=take(64), kt=take(128), Rm=take(64), S0b=take(64),
            diagb=take(128))
    Opair = scr2[:, 2432:2560]
    vT_t = scr2[:, 2560:2688]

    psA_ = ctx.t["psA"]
    psB_ = ctx.t["psB"]
    P0 = dict(bc=psB_[:, 1, 0:128], gram=psB_[:, 1, 128:256],
              bu=psB_[:, 2, 368:432], b2=psB_[:, 0, 0:128], bt2=psB_[:, 0, 128:256],
              w0=psB_[:, 2, 176:240], O=psB_[:, 2, 240:304], P=psB_[:, 2, 304:368])
    P1 = dict(bc=psA_[:, 2, 0:128], gram=psA_[:, 2, 128:256],
              bu=psA_[:, 1, 192:256], b2=psA_[:, 0, 0:128], bt2=psA_[:, 0, 128:256],
              w0=psA_[:, 1, 0:64], O=psA_[:, 1, 64:128], P=psA_[:, 1, 128:192])
    PS = [P0, P1]
    ps_cg = psB_[:, 2, 0:16]
    ps_cgT = psB_[0:16, 2, 16:144]
    ps_g1 = psB_[0:1, 2, 144:160]
    ps_gc = psB_[:, 2, 160:176]
    pc_kt = [ctx.t["psC"][:, 256:384], ctx.t["psC"][:, 512:640]]
    pc_ot = ctx.t["psC"][:, 384:512]
    onesf = consf[:, CF_ONES:CF_ONES + 128]
    mb_gt = consf[:, CF_MBGT:CF_MBGT + 128]
    mb_lt = consf[:, CF_MBLT:CF_MBLT + 128]
    mb_le = consf[:, CF_MBLE:CF_MBLE + 128]

    Sst = ctx.t["Sstate"]

    for t in range(NT):
        for ci in range(4):
            tc = t * 4 + ci
            rhs_barrier(ctx)
            for b in range(4):
                ctx.guard_bank(b)
            p.emit("tensor", lambda E, S, o=ps_cg, r=lgT(tc):
                   E.matmul(o, utri_f, r, start=True, stop=True), mark="te")
            p.wait("scalar", "te", p.cnt["te"])
            p.selfwait("scalar")
            p.emit("scalar", lambda E, S, d=cumlg, s=ps_cg:
                   E.activation(d, s, AF.Copy), mark="sc")
            p.selfwait("scalar")
            p.emit("scalar", lambda E, S, d=gamT, s=cumlg:
                   E.activation(d, s, AF.Exp), mark="sc")
            p.selfwait("scalar")
            p.emit("scalar", lambda E, S, d=lnb, s=bT(tc):
                   E.activation(d, s, AF.Ln), mark="sc")
            p.wait("vector", "sc", p.cnt["sc"])
            p.selfwait("vector")
            p.emit("vector", lambda E, S, d=negcl, s=cumlg:
                   E.tensor_scalar_mul(d, s, -1.0), mark="ve")
            p.emit("vector", lambda E, S, d=cumlnb, a=cumlg, b=lnb:
                   E.tensor_add(d, a, b), mark="ve")
            p.emit("vector", lambda E, S, d=negb, s=bT(tc):
                   E.tensor_scalar_mul(d, s, -1.0), mark="ve")
            p.wait("tensor", "sc", p.cnt["sc"])
            p.wait("tensor", "ve", p.cnt["ve"])
            p.emit("tensor", lambda E, S, o=ps_cgT, s=cumlg:
                   E.transpose(o, s, ident_f), mark="te")
            p.wait("scalar", "te", p.cnt["te"])
            p.selfwait("scalar")
            p.emit("scalar", lambda E, S, d=c_sb[0:16, :], s=ps_cgT[0:16, 127:128]:
                   E.activation(d, s, AF.Copy), mark="sc")
            p.wait("tensor", "sc", p.cnt["sc"])
            p.emit("tensor", lambda E, S, o=ps_g1, s=c_sb[0:16, :]:
                   E.transpose(o, s, ident_f[0:16, 0:16]), mark="te")
            p.wait("scalar", "te", p.cnt["te"])
            p.selfwait("scalar")
            p.emit("scalar", lambda E, S, d=gCrow, s=ps_g1:
                   E.activation(d, s, AF.Exp), mark="sc")
            p.emit("scalar", lambda E, S, d=lcrow, s=ps_g1:
                   E.activation(d, s, AF.Copy), mark="sc")
            p.wait("tensor", "sc", p.cnt["sc"])
            p.emit("tensor", lambda E, S, o=ps_gc, r=gCrow:
                   E.matmul(o, onesrow_f, r, start=True, stop=True), mark="te")
            p.wait("scalar", "te", p.cnt["te"])
            p.selfwait("scalar")
            p.emit("scalar", lambda E, S, d=gCc, s=ps_gc:
                   E.activation(d, s, AF.Copy), mark="sc")
            p.wait("tensor", "sc", p.cnt["sc"])
            p.emit("tensor", lambda E, S, o=ps_gc, r=lcrow:
                   E.matmul(o, onesrow_f, r, start=True, stop=True), mark="te")
            p.wait("vector", "te", p.cnt["te"])
            p.selfwait("vector")
            p.emit("vector", lambda E, S, d=sC, s=ps_gc, c=cumlg:
                   E.tensor_sub(d, s, c), mark="ve")
            p.wait("scalar", "ve", p.cnt["ve"])
            p.selfwait("scalar")
            p.emit("scalar", lambda E, S, d=sC, s=sC:
                   E.activation(d, s, AF.Exp), mark="sc")

            for fc in range(FC):
                vsl = ctx.t["arenaA"][:, (24 + fc) * 1024 + tc * 128: (24 + fc) * 1024 + tc * 128 + 128]
                qsl = ctx.t["arenaA"][:, (8 + fc) * 1024 + tc * 128: (8 + fc) * 1024 + tc * 128 + 128]
                ksl = ctx.t["arenaA"][:, (16 + fc) * 1024 + tc * 128: (16 + fc) * 1024 + tc * 128 + 128]
                p.wait("tensor", "sc", p.cnt["sc"])
                p.wait("tensor", "ve", p.cnt["ve"])
                p.emit("tensor", lambda E, S, o=pc_kt[0], s=vsl:
                       E.transpose(o, s, ident_b), mark="te")
                p.wait("scalar", "te", p.cnt["te"])
                p.selfwait("scalar")
                p.emit("scalar", lambda E, S, d=vT_t, s=pc_kt[0]:
                       E.activation(d, s, AF.Copy), mark="sc")
                vT_done = p.cnt["sc"]

                def mk(hp):
                    h = 2 * fc + hp
                    b0 = 64 * hp
                    te0 = p.cnt["te"]
                    ve0 = p.cnt["ve"]
                    return dict(h=h, b0=b0, T=tiles[hp], R=PS[hp],
                                ksl_h=ksl[b0:b0 + 64, :], qsl_h=qsl[b0:b0 + 64, :],
                                S_h=Sst[b0:b0 + 64, fc, :],
                                fa=f32a[hp], fb=f32b[hp], df=diagf[hp], af=argf[hp],
                                pk=pc_kt[hp],
                                m_bc=te0, m_bcread=ve0, m_diag=ve0, m_arg=ve0, m_exp=0)
                H = [mk(0), mk(1)]

                def st_diag(c, coltile, identt):
                    p.wait("vector", "te", c["m_bc"])
                    p.selfwait("vector")
                    dt_ = c["T"]["diagb"] if identt is ident_b else c["df"]
                    p.emit("vector", lambda E, S, d=dt_, cc=coltile[:, c["h"]:c["h"] + 1]:
                           E.tensor_scalar_mul(d, identt, cc), mark="ve")
                    c["m_diag"] = p.cnt["ve"]

                def st_bcast(c, identt):
                    src_ = c["T"]["diagb"] if identt is ident_b else c["df"]
                    ones_ = ones_b if identt is ident_b else onesf
                    p.wait("tensor", "ve", max(c["m_diag"], c["m_bcread"]))
                    p.emit("tensor", lambda E, S, o=c["R"]["bc"], dd=src_:
                           E.matmul(o, ones_, dd, start=True, stop=True), mark="te")
                    c["m_bc"] = p.cnt["te"]

                def st_mult(c, dtile_key, srct):
                    T = c["T"]
                    p.wait("vector", "te", c["m_bc"])
                    p.selfwait("vector")
                    p.emit("vector", lambda E, S, d=T[dtile_key][c["b0"]:c["b0"] + 64, :], s=srct,
                                  bc=c["R"]["bc"][c["b0"]:c["b0"] + 64, :]:
                           E.tensor_mul(d, s, bc), mark="ve")
                    c["m_bcread"] = p.cnt["ve"]
                    c[f"m_{dtile_key}"] = p.cnt["ve"]

                def st_arg(c, mask):
                    p.wait("vector", "te", c["m_bc"])
                    p.wait("vector", "sc", c["m_exp"])
                    p.selfwait("vector")
                    p.emit("vector", lambda E, S, d=c["af"], s=c["R"]["bc"], m=mask:
                           E.tensor_add(d, s, m), mark="ve")
                    c["m_bcread"] = p.cnt["ve"]
                    c["m_arg"] = p.cnt["ve"]

                def st_exp(c, dstkey, bias_col):
                    p.wait("scalar", "ve", c["m_arg"])
                    p.selfwait("scalar")
                    p.emit("scalar", lambda E, S, d=c["T"][dstkey], s=c["af"],
                                  b=bias_col[:, c["h"]:c["h"] + 1]:
                           E.activation(d, s, AF.Exp, bias=b), mark="sc")
                    c["m_exp"] = p.cnt["sc"]
                    c[f"m_{dstkey}"] = p.cnt["sc"]

                for key, col, srckey in (("kh", knT(tc), "ksl_h"), ("qh", gamT, "qsl_h")):
                    for c in H:
                        st_diag(c, col, ident_b)
                    for c in H:
                        st_bcast(c, ident_b)
                    for c in H:
                        st_mult(c, key, c[srckey])
                for dstkey, col, bias, mask in (("Adec", negcl, cumlg, mb_gt),
                                                ("BT", cumlnb, negcl, mb_lt),
                                                ("Mkq", cumlg, negcl, mb_le)):
                    for c in H:
                        st_diag(c, col, ident_f)
                    for c in H:
                        st_bcast(c, ident_f)
                    for c in H:
                        st_arg(c, mask)
                    for c in H:
                        st_exp(c, dstkey, bias)
                for c in H:
                    T, R = c["T"], c["R"]
                    b0 = c["b0"]
                    p.wait("tensor", "ve", max(c["m_kh"], c["m_bcread"]))
                    p.emit("tensor", lambda E, S, o=R["gram"], l=T["kh"][b0:b0 + 64, :], r=T["kh"][b0:b0 + 64, :]:
                           E.matmul(o, l, r, start=True, stop=True), mark="te")
                    c["gram_mark"] = p.cnt["te"]
                for c in H:
                    T, R = c["T"], c["R"]
                    p.wait("vector", "te", c["gram_mark"])
                    p.wait("vector", "sc", max(c["m_Adec"], c["m_BT"]))
                    p.selfwait("vector")
                    p.emit("vector", lambda E, S, d=T["B"], s=R["gram"], cc=negb[:, c["h"]:c["h"] + 1], a=T["Adec"]:
                           E.scalar_tensor_tensor(d, s, cc, a, ALU.mult, ALU.mult), mark="ve")
                    p.emit("vector", lambda E, S, d=T["BT"], s=R["gram"], a=T["BT"]:
                           E.scalar_tensor_tensor(d, s, -1.0, a, ALU.mult, ALU.mult), mark="ve")
                    c["bbt_mark"] = p.cnt["ve"]
                for c in H:
                    T, R = c["T"], c["R"]
                    b0 = c["b0"]
                    p.wait("tensor", "ve", c["bbt_mark"])
                    p.emit("tensor", lambda E, S, o=R["gram"], l=T["kh"][b0:b0 + 64, :], r=c["qsl_h"]:
                           E.matmul(o, l, r, start=True, stop=True), mark="te")
                    c["gram2_mark"] = p.cnt["te"]
                for c in H:
                    T, R = c["T"], c["R"]
                    p.wait("vector", "te", c["gram2_mark"])
                    p.wait("vector", "sc", c["m_Mkq"])
                    p.selfwait("vector")
                    p.emit("vector", lambda E, S, d=T["Mkq"], s=R["gram"], a=T["Mkq"]:
                           E.tensor_mul(d, s, a), mark="ve")
                    p.emit("vector", lambda E, S, d=T["S0b"][c["b0"]:c["b0"] + 64, :], s=c["S_h"]:
                           E.tensor_copy(d, s), mark="ve")
                    c["s0_mark"] = p.cnt["ve"]
                for c in H:
                    T, R = c["T"], c["R"]
                    b0 = c["b0"]
                    p.wait("tensor", "ve", c["s0_mark"])
                    p.emit("tensor", lambda E, S, o=R["w0"], l=T["kh"][b0:b0 + 64, :], r=T["S0b"][b0:b0 + 64, :]:
                           E.matmul(o, l, r, start=True, stop=True), mark="te")
                    c["w0_mark"] = p.cnt["te"]
                for c in H:
                    T = c["T"]
                    p.wait("vector", "te", c["w0_mark"])
                    p.wait("vector", "sc", vT_done)
                    p.selfwait("vector")
                    p.emit("vector", lambda E, S, d=c["fa"], s=c["R"]["w0"], cc=gamT[:, c["h"]:c["h"] + 1]:
                           E.tensor_scalar_mul(d, s, cc), mark="ve")
                    p.selfwait("vector")
                    p.emit("vector", lambda E, S, d=T["Rm"], s=vT_t[:, c["b0"]:c["b0"] + 64],
                                  cc=bT(tc)[:, c["h"]:c["h"] + 1], b=c["fa"]:
                           E.scalar_tensor_tensor(d, s, cc, b, ALU.mult, ALU.subtract), mark="ve")
                    p.selfwait("vector")
                    p.emit("vector", lambda E, S, d=T["U"], s=T["Rm"]: E.tensor_copy(d, s), mark="ve")
                    c["u_mark"] = p.cnt["ve"]

                for it in range(7):
                    for c in H:
                        T, R = c["T"], c["R"]
                        p.wait("tensor", "ve", c["u_mark"])
                        if it > 0:
                            p.wait("tensor", "sc", c["evac_mark"])
                        if it < 6:
                            p.emit("tensor", lambda E, S, o=R["bu"], l=T["BT"], r=T["U"]:
                                   E.matmul(o, l, r, start=True, stop=True))
                            p.emit("tensor", lambda E, S, o=R["b2"], l=T["BT"], r=T["B"]:
                                   E.matmul(o, l, r, start=True, stop=True))
                            p.emit("tensor", lambda E, S, o=R["bt2"], l=T["B"], r=T["BT"]:
                                   E.matmul(o, l, r, start=True, stop=True), mark="te")
                        else:
                            p.emit("tensor", lambda E, S, o=R["bu"], l=T["BT"], r=T["U"]:
                                   E.matmul(o, l, r, start=True, stop=True), mark="te")
                        c["mm_mark"] = p.cnt["te"]
                    for c in H:
                        T, R = c["T"], c["R"]
                        p.wait("vector", "te", c["mm_mark"])
                        p.selfwait("vector")
                        p.emit("vector", lambda E, S, d=T["U"], s=R["bu"]:
                               E.tensor_add(d, d, s), mark="ve")
                        c["u_mark"] = p.cnt["ve"]
                        if it < 6:
                            p.wait("scalar", "te", c["mm_mark"])
                            p.selfwait("scalar")
                            p.emit("scalar", lambda E, S, d=T["B"], s=R["b2"]:
                                   E.activation(d, s, AF.Copy), mark="sc")
                            p.emit("scalar", lambda E, S, d=T["BT"], s=R["bt2"]:
                                   E.activation(d, s, AF.Copy), mark="sc")
                            c["evac_mark"] = p.cnt["sc"]
                for c in H:
                    T, R = c["T"], c["R"]
                    b0 = c["b0"]
                    p.wait("tensor", "ve", c["u_mark"])
                    p.emit("tensor", lambda E, S, o=R["O"], l=T["Mkq"], r=T["U"]:
                           E.matmul(o, l, r, start=True, stop=False))
                    p.emit("tensor", lambda E, S, o=R["O"], l=T["qh"][b0:b0 + 64, :], r=T["S0b"][b0:b0 + 64, :]:
                           E.matmul(o, l, r, start=False, stop=True), mark="te")
                    c["o_mark"] = p.cnt["te"]
                for c in H:
                    p.wait("scalar", "te", c["o_mark"])
                    p.selfwait("scalar")
                    p.emit("scalar", lambda E, S, d=Opair[:, c["b0"]:c["b0"] + 64], s=c["R"]["O"]:
                           E.activation(d, s, AF.Copy), mark="sc")
                for c in H:
                    T = c["T"]
                    p.wait("tensor", "ve", p.cnt["ve"])
                    p.wait("tensor", "sc", p.cnt["sc"])
                    p.emit("tensor", lambda E, S, o=c["pk"], s=T["kh"]:
                           E.transpose(o, s, ident_b), mark="te")
                    c["kt_mark"] = p.cnt["te"]
                    p.wait("vector", "te", c["kt_mark"])
                    p.selfwait("vector")
                    p.emit("vector", lambda E, S, d=T["kt"], s=c["pk"], cc=sC[:, c["h"]:c["h"] + 1]:
                           E.tensor_scalar_mul(d, s, cc), mark="ve")
                    c["ktd_mark"] = p.cnt["ve"]
                for c in H:
                    T, R = c["T"], c["R"]
                    p.wait("tensor", "ve", c["ktd_mark"])
                    p.emit("tensor", lambda E, S, o=R["P"], l=T["kt"], r=T["U"]:
                           E.matmul(o, l, r, start=True, stop=True), mark="te")
                    c["p_mark"] = p.cnt["te"]
                for c in H:
                    b0 = c["b0"]
                    p.wait("vector", "te", c["p_mark"])
                    p.selfwait("vector")
                    p.emit("vector", lambda E, S, d=c["fb"][b0:b0 + 64, 0:64], a=c["S_h"],
                                  cc=gCc[b0:b0 + 64, c["h"]:c["h"] + 1]:
                           E.tensor_scalar_mul(d, a, cc), mark="ve")
                    p.selfwait("vector")
                    p.emit("vector", lambda E, S, d=c["S_h"], a=c["fb"][b0:b0 + 64, 0:64],
                                  b=c["R"]["P"][b0:b0 + 64, :]:
                           E.tensor_add(d, a, b), mark="ve")
                p.wait("tensor", "sc", p.cnt["sc"])
                p.wait("tensor", "ve", p.cnt["ve"])
                p.emit("tensor", lambda E, S, o=pc_ot, s=Opair:
                       E.transpose(o, s, ident_b), mark="te")
                p.wait("scalar", "te", p.cnt["te"])
                p.selfwait("scalar")
                odst = ctx.t["arenaB"][:, (8 + fc) * 1024 + tc * 128: (8 + fc) * 1024 + tc * 128 + 128]
                p.emit("scalar", lambda E, S, d=odst, s=pc_ot:
                       E.activation(d, s, AF.Copy), mark="sc")

    if GDN_DEBUG:
        return
    def rhs_o(wi, k, t):
        return aslot(ctx, "arenaB", 8 + k, t)

    def cb_wo(wi, mc, t, ps, b, banks):
        p.wait("vector", "te", p.cnt["te"])
        p.selfwait("vector")
        R = ctx.t["R"][:, mc, t * 512:(t + 1) * 512]
        p.emit("vector", lambda E, S, d=R, s=ps, c=ctx.col("l1a_bo", mc):
               E.scalar_tensor_tensor(d, s, c, d, ALU.add, ALU.add), mark="ve")
        p.emit("vector", lambda E, S, d=x1_ap(ctx, mc, t), s=ps,
                      c=ctx.col("l1a_bo", mc), z=z_ap(mc, t):
               E.scalar_tensor_tensor(d, s, c, z, ALU.add, ALU.add), mark="ve")
        ctx.set_bank_guard(b, "ve")

    emit_linear_fm(ctx, [W["wo"]], rhs_o, cb_wo, Mc=8)

    def x1src(fc, t):
        return x1_ap(ctx, fc, t)

    def xn2_ap(fc, t):
        return aslot(ctx, "arenaB", fc, t)
    emit_ln(ctx, x1src, xn2_ap, affine=("l1a_ln2g", "l1a_ln2b"))

    def rhs_xn2(wi, k, t):
        return aslot(ctx, "arenaB", k, t)

    def h_ap(kc, t):
        if kc < 16:
            return aslot(ctx, "arenaA", kc, t)
        return aslot(ctx, "arenaB", 8 + (kc - 16), t)

    def cb_m1(wi, mc, t, ps, b, banks):
        p.wait("scalar", "te", p.cnt["te"])
        p.emit("scalar", lambda E, S, d=h_ap(mc, t), s=ps:
               E.activation(d, s, AF.Gelu, bias=ctx.col("l1a_b1", mc)), mark="sc")
        ctx.set_bank_guard(b, "sc")

    emit_linear_fm(ctx, [W["m1"]], rhs_xn2, cb_m1, Mc=32)

    def rhs_h(wi, k, t):
        return h_ap(k, t)

    def cb_m2(wi, mc, t, ps, b, banks):
        p.wait("vector", "te", p.cnt["te"])
        p.selfwait("vector")
        R = ctx.t["R"][:, mc, t * 512:(t + 1) * 512]
        p.emit("vector", lambda E, S, d=R, s=ps, c=ctx.col("l1a_b2", mc):
               E.scalar_tensor_tensor(d, s, c, d, ALU.add, ALU.add), mark="ve")
        ctx.set_bank_guard(b, "ve")

    emit_linear_fm(ctx, [W["m2"]], rhs_h, cb_m2, Mc=8)
